# version 06
# baseline (speedup 1.0000x reference)
import contextlib
import numpy as np
import ml_dtypes
import concourse.bass as bass
import concourse.mybir as mybir
from concourse.bass_utils import run_bass_kernel_spmd

F32 = mybir.dt.float32
BF16 = mybir.dt.bfloat16
AF = mybir.ActivationFunctionType
ALU = mybir.AluOpType
BF = ml_dtypes.bfloat16

T = 4096
D = 1024
CH = 8
TT = 512
NTT = T // TT
NQB = 8
KCN = 32
DH = 64
NEG = -1.0e30
HID = 2730
HIDP = 2816
MA = 22
OFFA = 384
WA = 896
OFFF = 384
WF = 1408
BW_B = 8064
WBB = 4480

ENGS = ("t", "v", "s", "p", "q")


def _host_consts():
    c = {}
    f32 = np.float32
    half = DH // 2
    theta = (1.0 / (10000.0 ** (np.arange(half, dtype=f32) / f32(half)))).astype(f32)
    pos = np.arange(T, dtype=f32)
    freqs = (pos[:, None] * theta[None, :]).astype(f32)
    cosr = np.repeat(np.cos(freqs, dtype=f32), 2, axis=1)
    sinr = np.repeat(np.sin(freqs, dtype=f32), 2, axis=1)
    cqa = np.tile(cosr.T, (2, 1)).astype(f32)
    sqa = np.tile(sinr.T, (2, 1)).astype(f32)
    c["tab_a_cq"] = (cqa * f32(0.125)).astype(f32)
    c["tab_a_sq"] = (sqa * f32(0.125)).astype(f32)
    c["tab_a_ck"] = cqa.copy()
    c["tab_a_sk"] = sqa.copy()
    WINDOW = 512
    half_idx = np.arange(0, DH, 2, dtype=f32)
    inv_freq = (1.0 / (10000.0 ** (half_idx / f32(DH)))).astype(f32)
    tpos = np.arange(T, dtype=f32)
    fr = (tpos[:, None] * inv_freq[None, :]).astype(f32)
    fr2 = np.concatenate([fr, fr], axis=1)
    base = ((half_idx + f32(0.4 * DH)) / f32(1.4 * DH)).astype(f32)
    power = ((tpos - f32(T // 2)) / f32(WINDOW // 2)).astype(f32)
    scale = (base[None, :] ** power[:, None]).astype(f32)
    scale2 = np.concatenate([scale, scale], axis=1)
    cosf = np.cos(fr2, dtype=f32)
    sinf = np.sin(fr2, dtype=f32)
    fcq = (cosf * scale2).astype(f32)
    fsq = (sinf * scale2).astype(f32)
    fck = (cosf / scale2).astype(f32)
    fsk = (sinf / scale2).astype(f32)
    c["tab_f_cq"] = np.tile((fcq * f32(8.0)).T, (2, 1)).astype(f32)
    c["tab_f_sq"] = np.tile((fsq * f32(8.0)).T, (2, 1)).astype(f32)
    c["tab_f_ck"] = np.tile(fck.T, (2, 1)).astype(f32)
    c["tab_f_sk"] = np.tile(fsk.T, (2, 1)).astype(f32)
    m = np.arange(WA)[None, :] - np.arange(128)[:, None] - OFFA
    c["band_a"] = np.where(m >= 0, f32(0), f32(NEG)).astype(f32)
    m = np.arange(WF)[None, :] - np.arange(128)[:, None] - OFFF
    c["band_f"] = np.where((m >= 0) & (m <= WINDOW), f32(0), f32(NEG)).astype(f32)
    c["ident_bf"] = np.eye(128, dtype=f32).astype(BF)
    c["ident_f32"] = np.eye(128, dtype=f32)
    c["perm_bf"] = _perm_matrix().astype(BF)
    c["ones_r_f32"] = np.ones((1, 128), dtype=f32)
    c["ones_c_bf"] = np.ones((128, 1), dtype=f32).astype(BF)
    on2 = np.zeros((128, 2), dtype=f32)
    on2[0:64, 0] = 1.0
    on2[64:128, 1] = 1.0
    c["ones2_bf"] = on2.astype(BF)
    b2 = np.zeros((2, 128), dtype=f32)
    b2[0, 0:64] = 1.0
    b2[1, 64:128] = 1.0
    c["b2_f32"] = b2
    return c


def _perm_matrix():
    P = np.zeros((128, 128), dtype=np.float32)
    for blk in (0, 64):
        for d in range(32):
            P[blk + d + 32, blk + d] = -1.0
        for d in range(32, 64):
            P[blk + d - 32, blk + d] = 1.0
    return P


def _host_layout(inputs):
    f32 = np.float32
    L = {}
    x = np.asarray(inputs["L0_attn_a_x"], dtype=f32)
    L["xT"] = [np.ascontiguousarray(x[i].T) for i in range(8)]
    def ocmajor(w, pad_to=None):
        w = np.asarray(w, dtype=f32)
        K, M = w.shape
        if pad_to is not None and M < pad_to:
            w = np.concatenate([w, np.zeros((K, pad_to - M), f32)], axis=1)
            M = pad_to
        n = M // 128
        return np.ascontiguousarray(w.reshape(K, n, 128).transpose(1, 0, 2))
    L["wa_qkv"] = ocmajor(inputs["L0_attn_a_w_qkv"])
    L["wa_proj"] = ocmajor(inputs["L0_attn_a_w_proj"])
    for nm, key in (("wb_q", "L0_attn_b_Wq"), ("wb_k", "L0_attn_b_Wk"),
                    ("wb_v", "L0_attn_b_Wv"), ("wb_o", "L0_attn_b_Wo")):
        L[nm] = ocmajor(np.ascontiguousarray(np.asarray(inputs[key], f32).T))
    wq = np.asarray(inputs["L0_ffn_wqkv"], f32)
    L["wf_qkv"] = np.stack([ocmajor(wq[l]) for l in range(2)])
    wo = np.asarray(inputs["L0_ffn_wo"], f32)
    L["wf_wo"] = np.stack([ocmajor(wo[l]) for l in range(2)])
    w1 = np.asarray(inputs["L0_ffn_w1"], f32)
    L["wf_w1a"] = np.stack([ocmajor(w1[l][:, :HID], pad_to=HIDP) for l in range(2)])
    L["wf_w1g"] = np.stack([ocmajor(w1[l][:, HID:], pad_to=HIDP) for l in range(2)])
    w2 = np.asarray(inputs["L0_ffn_w2"], f32)
    w2p = np.concatenate([w2, np.zeros((2, HIDP - HID, D), f32)], axis=1)
    L["wf_w2"] = np.stack([ocmajor(w2p[l]) for l in range(2)])
    b1 = np.asarray(inputs["L0_ffn_b1"], f32)
    def vec_lma(a):
        return np.ascontiguousarray(a.reshape(2, MA, 128).transpose(2, 0, 1))
    L["b1a"] = vec_lma(np.concatenate(
        [b1[:, :HID], np.zeros((2, HIDP - HID), f32)], axis=1))
    L["b1g"] = vec_lma(np.concatenate(
        [b1[:, HID:], np.zeros((2, HIDP - HID), f32)], axis=1))
    pb = np.asarray(inputs["L0_attn_b_pos_bias"], f32)
    mm = np.arange(BW_B)[None, :] - np.arange(128)[:, None] + 127
    idx = np.minimum(mm, 4094)
    L["band_b"] = np.ascontiguousarray(pb[:, idx])
    return L


class Dep:
    __slots__ = ("key", "c", "g", "k", "e")
    def __init__(self, key, c, g=0, k="cmp", e=None):
        self.key, self.c, self.g, self.k, self.e = key, c, g, k, e


class Sched:
    def __init__(self, nc, es):
        self.nc = nc
        self.es = es
        self.ops = {e: [] for e in ENGS}
        self.cnt = {e: 0 for e in ENGS}
        self.sems = {}
        self.slot_cnt = {}
        self.waited = {}
        self.semgen = 0
        self.dbg_names = None
        self.dbg_tensors = {}
        self.dbg_deps = []
        self._new_semset()

    def _new_semset(self):
        self.semgen += 1
        self.gen = self.semgen
        if self.gen == 1:
            for e in ENGS:
                self.sems[("e", e)] = self.es.enter_context(
                    self.nc.semaphore(f"sm{e}"))
                self.cnt[e] = 0

    def new_phase_sems(self):
        self.barrier()
        self._new_semset()

    def slot_sem(self, slot):
        key = ("s", slot)
        if key not in self.sems:
            self.sems[key] = self.es.enter_context(
                self.nc.semaphore("sl_" + slot))
            self.slot_cnt[key] = 0
        return key

    def dep(self, e, kind=None):
        k = kind or "cmp"
        return Dep(("e", e), self.cnt[e], self.gen, k, e)

    def dma(self, e, slot, fn, waits=()):
        key = self.slot_sem(slot)
        self.slot_cnt[key] += 16
        cnt = self.slot_cnt[key]
        sem = self.sems[key]
        wl = self._translate(e, waits)

        def op(eng):
            for s, c in wl:
                eng.wait_ge(s, c)
            fn(eng).then_inc(sem, 16)
        self.ops[e].append(op)
        return Dep(key, cnt, self.gen, "dma", e)

    def _translate(self, e, waits):
        wl = []
        for d in waits:
            if d is None or d.c <= 0:
                continue
            if d.g != self.gen:
                continue
            wkey = (e, d.key)
            if self.waited.get(wkey, 0) >= d.c:
                continue
            self.waited[wkey] = d.c
            wl.append((self.sems[d.key], d.c))
        return wl

    def emit(self, e, fn, waits=(), inc=0):
        wl = self._translate(e, waits)
        if inc:
            self.cnt[e] += inc
        sem = self.sems[("e", e)] if inc else None
        amt = inc

        def op(eng):
            for s, c in wl:
                eng.wait_ge(s, c)
            ins = fn(eng)
            if amt:
                if ins is None:
                    eng.nop().then_inc(sem, amt)
                else:
                    ins.then_inc(sem, amt)
        self.ops[e].append(op)
        return Dep(("e", e), self.cnt[e], self.gen, "cmp", e) if inc else None

    def wait_only(self, e, waits):
        self.emit(e, lambda eng: None, waits=waits, inc=0)

    def barrier(self):
        deps = [Dep(("e", e), self.cnt[e], self.gen, "bar", e)
                for e in ENGS]
        deps += [Dep(key, self.slot_cnt[key], self.gen, "bar", None)
                 for key in self.slot_cnt]
        for e in ENGS:
            self.wait_only(e, deps)

    def dbg(self, name, shape, dtype, src_ap_fn, waits):
        if self.dbg_names is None or name not in self.dbg_names:
            return
        t = self.nc.dram_tensor("dbg_" + name, list(shape), dtype,
                                kind="ExternalOutput")
        self.dbg_tensors[name] = t
        d = self.dma("q", "dbg_" + name,
                     lambda eng: eng.dma_start(out=t.ap(), in_=src_ap_fn()),
                     waits=waits)
        self.dbg_deps.append(d)
        return d

    def replay(self, block):
        for ename, attr in (("t", "tensor"), ("v", "vector"), ("s", "scalar"),
                            ("p", "gpsimd"), ("q", "sync")):
            def mk(e):
                def body(eng):
                    for f in self.ops[e]:
                        f(eng)
                return body
            getattr(block, attr)(mk(ename))


class Arena:
    def __init__(self, nc, es, nbytes):
        self.h = es.enter_context(nc.sbuf_tensor("arena", [128, nbytes],
                                                 mybir.dt.uint8))
        self.nbytes = nbytes
        self.off = 0
        self.stack = []

    def push(self):
        self.stack.append(self.off)

    def pop(self):
        self.off = self.stack.pop()

    def alloc(self, shape, dtype, name=""):
        bpe = {F32: 4, BF16: 2}[dtype]
        n = 1
        for s in shape[1:]:
            n *= s
        self.off = (self.off + 63) & ~63
        start = self.off
        self.off += n * bpe
        assert self.off <= self.nbytes, \
            f"arena overflow {name}: {self.off} > {self.nbytes}"
        base = self.h[:].bitcast(dtype)
        ap = base[:, start // bpe: start // bpe + n]
        if len(shape) > 2:
            pat = "p (" + " ".join(f"a{i}" for i in range(len(shape) - 1)) + ") -> p " + \
                  " ".join(f"a{i}" for i in range(len(shape) - 1))
            kw = {f"a{i}": shape[i + 1] for i in range(len(shape) - 1)}
            ap = ap.rearrange(pat, **kw)
        return ap


def build_program(dbg_names=None, phase_limit=99):
    nc = bass.Bass("TRN2", target_bir_lowering=False)
    es = contextlib.ExitStack()
    C = {}

    def din(name, shape, dtype=F32):
        C[name] = nc.dram_tensor(name, list(shape), dtype, kind="ExternalInput")
        return C[name]

    def dint(name, shape, dtype):
        C[name] = nc.dram_tensor(name, list(shape), dtype)
        return C[name]

    din("xT", [D, T])
    din("wa_qkv", [24, D, 128]); din("wa_proj", [8, D, 128])
    for nm in ("wb_q", "wb_k", "wb_v", "wb_o"):
        din(nm, [8, D, 128])
    din("bq", [128, CH]); din("bk", [128, CH]); din("bv", [128, CH])
    din("bo", [128, CH])
    din("pos_bias", [16, 4095])
    din("band_b", [16, 128, BW_B])
    din("wf_qkv", [2, 12, D, 128]); din("wf_wo", [2, 8, 512, 128])
    din("wf_w1a", [2, MA, D, 128]); din("wf_w1g", [2, MA, D, 128])
    din("wf_w2", [2, 8, HIDP, 128])
    din("b1a", [128, 2, MA]); din("b1g", [128, 2, MA]); din("b2", [128, 2, CH])
    din("ln_g", [128, 2, CH]); din("ln_b", [128, 2, CH])
    din("qs", [DH, 2]); din("ks", [DH, 2])
    for nm in ("tab_a_cq", "tab_a_sq", "tab_a_ck", "tab_a_sk",
               "tab_f_cq", "tab_f_sq", "tab_f_ck", "tab_f_sk"):
        din(nm, [128, T])
    din("band_a", [128, WA]); din("band_f", [128, WF])
    din("ident_bf", [128, 128], BF16); din("ident_f32", [128, 128], F32)
    din("perm_bf", [128, 128], BF16)
    din("ones_r_f32", [1, 128], F32); din("ones_c_bf", [128, 1], BF16)
    din("ones2_bf", [128, 2], BF16); din("b2_f32", [2, 128], F32)

    C["yT"] = nc.dram_tensor("yT", [D, T], F32, kind="ExternalOutput")
    dint("x1T", [D, T], F32)
    dint("x2T", [D, T], F32)
    dint("x3T", [D, T], F32)
    dint("uT", [D, T], F32)
    dint("u1T", [D, T], F32)
    dint("c_wa_qkv", [24, D, 128], BF16); dint("c_wa_proj", [8, D, 128], BF16)
    for nm in ("wb_q", "wb_k", "wb_v", "wb_o"):
        dint("c_" + nm, [8, D, 128], BF16)
    dint("c_wf_qkv", [2, 12, D, 128], BF16); dint("c_wf_wo", [2, 8, 512, 128], BF16)
    dint("c_wf_w1a", [2, MA, D, 128], BF16); dint("c_wf_w1g", [2, MA, D, 128], BF16)
    dint("c_wf_w2", [2, 8, HIDP, 128], BF16)

    S = Sched(nc, es)
    S.dbg_names = set(dbg_names) if dbg_names else None
    A = Arena(nc, es, 188 * 1024)

    K = {}
    def sball(name, shape, dtype):
        K[name] = es.enter_context(nc.sbuf_tensor(name, list(shape), dtype))
        return K[name]

    for nm, sh, dt in (
        ("ident_bf", [128, 128], BF16), ("ident_f32", [128, 128], F32),
        ("perm_bf", [128, 128], BF16), ("ones_r_f32", [1, 128], F32),
        ("ones_c_bf", [128, 1], BF16), ("ones2_bf", [128, 2], BF16),
        ("b2_f32", [2, 128], F32),
        ("band_a", [128, WA], F32), ("band_f", [128, WF], F32),
        ("bq", [128, CH], F32), ("bk", [128, CH], F32), ("bv", [128, CH], F32),
        ("bo", [128, CH], F32), ("b2b", [128, 2, CH], F32),
        ("lng", [128, 2, CH], F32), ("lnb", [128, 2, CH], F32),
        ("b1a_sb", [128, 2, MA], F32), ("b1g_sb", [128, 2, MA], F32),
        ("qs_sb", [128, 2], F32), ("ks_sb", [128, 2], F32),
        ("pbrow", [1, 16], F32),
        ("mu_row", [1, TT], F32), ("r_row", [1, TT], F32), ("mur_row", [1, TT], F32),
        ("eps1", [1, 1], F32),
    ):
        K[nm] = es.enter_context(nc.sbuf_tensor("sb_" + nm, list(sh), dt))

    PS = [es.enter_context(nc.psum_tensor(f"ps{i}", [128, 512], F32))
          for i in range(8)]

    meta = dict(nc=nc, S=S, A=A, K=K, PS=PS, C=C)

    _emit_wprep(meta)
    _emit_const_loads(meta)
    if phase_limit >= 1:
        A.push()
        xn = A.alloc([128, CH, T], BF16, "xn")
        meta["xn"] = xn
        _emit_ln_stats_apply(meta, src_dram=C["xT"], xn_out=xn, phase_tag="ln1")
    if phase_limit >= 2:
        meta["Oa"] = A.alloc([128, CH, T], BF16, "Oa")
        _emit_phase_attn_a(meta)
        S.dbg("a_o", [128, CH * T], BF16,
              lambda: meta["Oa"].rearrange("p a b -> p (a b)"),
              [meta["attn_a_done"]])
        S.new_phase_sems()
    if phase_limit >= 3:
        _emit_phase_proj_res(meta, meta["Oa"],
                             lambda oc: C["c_wa_proj"].ap()[oc], None,
                             C["xT"], C["x1T"], o_dep=None)
        S.dbg("x1", [D, T], F32, lambda: C["x1T"].ap(), [])
        S.new_phase_sems()
        A.pop()
    if phase_limit >= 4:
        _emit_ln_stats_apply(meta, C["x1T"], None, "ln2", out_dram=C["uT"])
        S.new_phase_sems()
        S.dbg("u_dram", [D, T], F32, lambda: C["uT"].ap(), [])
        _emit_ffn_sublayer(meta, 0, C["uT"], C["u1T"], C["u1T"], None,
                           C["uT"], "f1l0")
        _emit_ffn_sublayer(meta, 1, C["uT"], C["u1T"], C["u1T"], C["x1T"],
                           C["x2T"], "f1l1")
        S.new_phase_sems()
        S.dbg("x2", [D, T], F32, lambda: C["x2T"].ap(), [])
    if phase_limit >= 5:
        A.push()
        xn3 = A.alloc([128, CH, T], BF16, "xn3")
        _emit_ln_stats_apply(meta, C["x2T"], xn3, "ln3")
        meta["Ob"] = A.alloc([128, CH, T], BF16, "Ob")
        S.new_phase_sems()
        _emit_phase_attn_b(meta, xn3)
        S.new_phase_sems()
        _emit_phase_proj_res(meta, meta["Ob"],
                             lambda oc: C["c_wb_o"].ap()[oc], "bo",
                             C["x2T"], C["x3T"], o_dep=None)
        S.new_phase_sems()
        S.dbg("x3", [D, T], F32, lambda: C["x3T"].ap(), [])
        A.pop()
    if phase_limit >= 6:
        _emit_ln_stats_apply(meta, C["x3T"], None, "ln4", out_dram=C["uT"])
        S.new_phase_sems()
        _emit_ffn_sublayer(meta, 0, C["uT"], C["u1T"], C["u1T"], None,
                           C["uT"], "f2l0")
        _emit_ffn_sublayer(meta, 1, C["uT"], C["u1T"], C["u1T"], C["x3T"],
                           C["yT"], "f2l1")
        S.new_phase_sems()
    return nc, es, meta


def _q_dma(S, dst_ap, src_ap, waits=(), slot="misc"):
    return S.dma("q", slot, lambda e: e.dma_start(out=dst_ap, in_=src_ap),
                 waits=waits)


def _p_dma(S, dst_ap, src_ap, waits=(), slot="pmisc"):
    return S.dma("p", slot, lambda e: e.dma_start(out=dst_ap, in_=src_ap),
                 waits=waits)


def _emit_wprep(meta):
    S, C = meta["S"], meta["C"]
    deps = []
    pairs = [("wa_qkv", "c_wa_qkv"), ("wa_proj", "c_wa_proj"),
             ("wb_q", "c_wb_q"), ("wb_k", "c_wb_k"), ("wb_v", "c_wb_v"),
             ("wb_o", "c_wb_o"), ("wf_qkv", "c_wf_qkv"), ("wf_wo", "c_wf_wo"),
             ("wf_w1a", "c_wf_w1a"), ("wf_w1g", "c_wf_w1g"), ("wf_w2", "c_wf_w2")]
    for src, dst in pairs:
        sap = C[src].ap()
        dap = C[dst].ap()
        n = 1
        for s in sap.shape:
            n *= s
        sflat = sap.rearrange(" ".join(f"a{i}" for i in range(len(sap.shape)))
                              + " -> (" + " ".join(f"a{i}" for i in range(len(sap.shape))) + ")")
        dflat = dap.rearrange(" ".join(f"a{i}" for i in range(len(dap.shape)))
                              + " -> (" + " ".join(f"a{i}" for i in range(len(dap.shape))) + ")")
        CHUNK = 1 << 20
        for o in range(0, n, CHUNK):
            ln = min(CHUNK, n - o)
            deps.append(_p_dma(S, dflat[o:o + ln], sflat[o:o + ln],
                               slot="wprep"))
    meta["wprep_dep"] = deps[-1]


def _emit_const_loads(meta):
    S, C, K = meta["S"], meta["C"], meta["K"]
    nc = meta["nc"]
    deps = []
    for nm in ("ident_bf", "ident_f32", "perm_bf", "ones_r_f32", "ones_c_bf",
               "ones2_bf", "b2_f32", "band_a", "band_f"):
        deps.append(_q_dma(S, K[nm][:], C[nm].ap(), slot="const"))
    for nm, dn in (("bq", "bq"), ("bk", "bk"), ("bv", "bv"), ("bo", "bo")):
        deps.append(_q_dma(S, K[nm][:], C[dn].ap(), slot="const"))
    for nm, dn in (("b2b", "b2"), ("lng", "ln_g"), ("lnb", "ln_b")):
        deps.append(_q_dma(S, K[nm][:], C[dn].ap(), slot="const"))
    for nm, dn in (("b1a_sb", "b1a"), ("b1g_sb", "b1g")):
        deps.append(_q_dma(S, K[nm][:], C[dn].ap(), slot="const"))
    for nm in ("qs", "ks"):
        deps.append(_q_dma(S, K[nm + "_sb"][0:64, :], C[nm].ap(), slot="const"))
    meta["A"].push()
    pb_sb = meta["A"].alloc([128, 4095], F32, "pbtmp")
    pbm = meta["A"].alloc([128, 1], F32, "pbm")
    deps.append(_q_dma(S, pb_sb[0:16, :], C["pos_bias"].ap(), slot="const"))
    d = deps[-1]
    dcl = d
    for nm in ("qs_sb", "ks_sb"):
        S.emit("v", lambda e, t=K[nm]: e.tensor_copy(t[64:128, :], t[0:64, :]),
               waits=[d], inc=1)
    meta["eps_dep"] = S.emit("p", lambda e: e.memset(K["eps1"][:, :], 1e-5),
                             inc=1)
    dm = S.emit("v", lambda e: e.reduce_max(pbm[0:16, :], pb_sb[0:16, :],
                                            mybir.AxisListType.X),
                waits=[d], inc=1)
    PS = meta["PS"]
    dt = S.emit("t", lambda e: e.transpose(PS[5][0:1, 0:16], pbm[0:16, 0:1],
                                           K["ident_f32"][0:16, 0:16]),
                waits=[dm, dcl], inc=1)
    S.emit("v", lambda e: e.tensor_copy(K["pbrow"][0:1, 0:16], PS[5][0:1, 0:16]),
           waits=[dt], inc=1)
    meta["A"].pop()
    meta["const_dep"] = dcl
    meta["const_v_dep"] = S.dep("v")


def _emit_ln_stats_apply(meta, src_dram, xn_out, phase_tag, gains_l=None,
                         out_dram=None):
    S, A, K, PS, C = meta["S"], meta["A"], meta["K"], meta["PS"], meta["C"]
    A.push()
    xf = A.alloc([128, 2, CH, TT], F32, "xf")
    xbt = A.alloc([128, 2, CH, TT], BF16, "xbt")
    sq = A.alloc([128, 2, 2, TT], BF16, "sq")
    src = src_dram.ap()
    cdep = meta["const_dep"]
    war_v = {}
    war_t = {}
    prev_apply = {}
    for tt in range(NTT):
        b = tt % 2
        dl = None
        for c in range(CH):
            dl = _q_dma(S, xf[:, b, c, :],
                        src[c * 128:(c + 1) * 128, tt * TT:(tt + 1) * TT],
                        waits=([war_v.get(tt - 2)] if c == 0 else ()),
                        slot=f"lnxf{b}")
        dcast = None
        for c in range(CH):
            dcast = S.emit("p", lambda e, c=c, b=b: e.tensor_copy(
                xbt[:, b, c, :], xf[:, b, c, :]), waits=[dl], inc=1)
        war = prev_apply.get(tt - 1, meta.get("const_v_dep"))
        sqmm = {}
        for c in range(CH):
            dsq = S.emit("s", lambda e, c=c, b=b: e.activation(
                sq[:, b, c % 2, :], xbt[:, b, c, :], AF.Square),
                waits=[dcast, war, sqmm.get(c - 2)], inc=1)
            S.emit("t", lambda e, c=c, b=b: e.matmul(
                PS[6][0:1, :], K["ones_c_bf"][:, :], xbt[:, b, c, :],
                start=(c == 0), stop=(c == CH - 1)),
                waits=[dcast, cdep, war], inc=(1 if c == CH - 1 else 0))
            sqmm[c] = S.emit("t", lambda e, c=c, b=b: e.matmul(
                PS[7][0:1, :], K["ones_c_bf"][:, :], sq[:, b, c % 2, :],
                start=(c == 0), stop=(c == CH - 1)),
                waits=[dsq, war], inc=1)
        dstat = S.dep("t")
        d1 = S.emit("v", lambda e: e.tensor_scalar_mul(
            K["mu_row"][0:1, :], PS[6][0:1, :], 1.0 / D), waits=[dstat], inc=1)
        d2 = S.emit("v", lambda e: e.tensor_scalar_mul(
            K["r_row"][0:1, :], PS[7][0:1, :], 1.0 / D), waits=[dstat], inc=1)
        d3 = S.emit("v", lambda e: e.tensor_mul(
            K["mur_row"][0:1, :], K["mu_row"][0:1, :], K["mu_row"][0:1, :]),
            waits=[d2], inc=1)
        d4 = S.emit("v", lambda e: e.tensor_sub(
            K["r_row"][0:1, :], K["r_row"][0:1, :], K["mur_row"][0:1, :]),
            waits=[d3], inc=1)
        d5 = S.emit("v", lambda e: e.tensor_scalar_max(
            K["r_row"][0:1, :], K["r_row"][0:1, :], 0.0), waits=[d4], inc=1)
        d6 = S.emit("s", lambda e: e.activation(
            K["r_row"][0:1, :], K["r_row"][0:1, :], AF.Ln,
            bias=K["eps1"][0:1, :], scale=1.0),
            waits=[d5, meta.get("eps_dep")], inc=1)
        d7 = S.emit("s", lambda e: e.activation(
            K["r_row"][0:1, :], K["r_row"][0:1, :], AF.Exp, scale=-0.5),
            waits=[d6], inc=1)
        d8 = S.emit("v", lambda e: e.tensor_mul(
            K["mur_row"][0:1, :], K["mu_row"][0:1, :], K["r_row"][0:1, :]),
            waits=[d7], inc=1)
        if tt == 0:
            S.dbg(phase_tag + "_r", [1, TT], F32, lambda: K["r_row"][0:1, :], [d8])
            S.dbg(phase_tag + "_mur", [1, TT], F32,
                  lambda: K["mur_row"][0:1, :], [d8])
        dr = S.emit("t", lambda e: e.matmul(
            PS[6][:, :], K["ones_r_f32"][:, :], K["r_row"][0:1, :],
            start=True, stop=True), waits=[d8], inc=1)
        dm = S.emit("t", lambda e: e.matmul(
            PS[7][:, :], K["ones_r_f32"][:, :], K["mur_row"][0:1, :],
            start=True, stop=True), waits=[d8], inc=1)
        dv = None
        for c in range(CH):
            dv1 = S.emit("v", lambda e, c=c, b=b: e.tensor_mul(
                xf[:, b, c, :], xbt[:, b, c, :], PS[6][:, :]),
                waits=[dr], inc=1)
            if out_dram is not None:
                dv = S.emit("v", lambda e, c=c, b=b: e.tensor_sub(
                    xf[:, b, c, :], xf[:, b, c, :], PS[7][:, :]),
                    waits=[dm, dv1], inc=1)
                dv = _q_dma(S, out_dram.ap()[c * 128:(c + 1) * 128,
                                             tt * TT:(tt + 1) * TT],
                            xf[:, b, c, :], waits=[dv], slot=f"lnout{b}")
            elif gains_l is None:
                dv = S.emit("v", lambda e, c=c, b=b, tt=tt: e.tensor_sub(
                    xn_out[:, c, tt * TT:(tt + 1) * TT], xf[:, b, c, :],
                    PS[7][:, :]), waits=[dm, dv1], inc=1)
            else:
                dv2 = S.emit("v", lambda e, c=c, b=b: e.tensor_sub(
                    xf[:, b, c, :], xf[:, b, c, :], PS[7][:, :]),
                    waits=[dm, dv1], inc=1)
                dv = S.emit("v", lambda e, c=c, b=b, tt=tt, l=gains_l:
                    e.tensor_scalar(
                        xn_out[:, c, tt * TT:(tt + 1) * TT], xf[:, b, c, :],
                        K["lng"][:, l, c:c + 1], K["lnb"][:, l, c:c + 1],
                        ALU.mult, ALU.add), waits=[dv2], inc=1)
        war_v[tt] = dv
        prev_apply[tt] = dv
    if xn_out is not None:
        S.dbg(phase_tag + "_xn", [128, CH * T], BF16,
              lambda: xn_out.rearrange("p a b -> p (a b)"), [S.dep("v")])
    A.pop()
    meta[phase_tag + "_done"] = S.dep("v")


def _emit_pair_qkv_a(meta, P, hp):
    S, A, K, PS, C = meta["S"], meta["A"], meta["K"], meta["PS"], meta["C"]
    cdep = meta["const_dep"]
    wdep = None
    for j in range(3):
        srcw = P["wsrc"](j)
        sap = bass.AP(srcw.tensor, srcw.offset,
                      [[128, 128], [128 * 128, CH], [1, 128]])
        wdep = _q_dma(S, P["wq"][:, j, :, :], sap,
                      waits=[meta["wprep_dep"], P.get("scope_war")],
                      slot="wq")
    if P.get("dbg_pair"):
        dd = S.dbg("pp_wq", [128, 3 * CH * 128], BF16,
                   lambda: P["wq"].rearrange("p a b c -> p (a b c)"), [wdep])
        dd2 = S.dbg("pp_ub0", [128, CH, TT], BF16,
                    lambda: P["rhs_dbg"](), [P["rhs_dep"]])
    V4 = P["V"]
    for tt in range(NTT):
        ts = slice(tt * TT, (tt + 1) * TT)
        b = tt % 2

        qbank = 0 if tt % 2 == 0 else 6
        kbank = 1 if tt % 2 == 0 else 7
        for psn, j in ((qbank, 0), (kbank, 1), (2, 2)):
            for kc in range(CH):
                S.emit("t", lambda e, psn=psn, j=j, kc=kc, tt=tt: e.matmul(
                    PS[psn][:, :], P["wq"][:, j, kc, :],
                    P["rhs_fn"](kc, tt),
                    start=(kc == 0), stop=(kc == CH - 1)),
                    waits=[wdep, P["rhs_dep"], P["bank"].get(psn)],
                    inc=(1 if kc == CH - 1 else 0))
            if j == 0:
                dqp = S.dep("t")
            elif j == 1:
                dkp = S.dep("t")
            else:
                dvp = S.dep("t")
        dvf = S.emit("v", lambda e: e.tensor_copy(P["vt_f32"][:, 0, :],
                                                  PS[2][:, :]),
                     waits=[dvp, P["war_vt"].get(tt - 1)], inc=1)
        P["bank"][2] = dvf
        dtr = None
        for j in range(4):
            dtr = S.emit("t", lambda e, j=j, b=b: e.transpose(
                PS[3][:, j * 128:(j + 1) * 128],
                P["vt_f32"][:, 0, j * 128:(j + 1) * 128],
                K["ident_f32"][:, :]),
                waits=[dvf, cdep, P["war_ps3"].get(tt - 1),
                       P["bank"].get(3)], inc=1)
        P["war_vt"][tt] = dtr
        dve = None
        for h in range(2):
            vout = V4[:, h, tt * 4:(tt + 1) * 4, 0:64]
            vin = bass.AP(PS[3], 64 * h, [[512, 128], [128, 4], [1, 64]])
            dve = S.emit("v", lambda e, vout=vout, vin=vin: e.tensor_copy(
                vout, vin), waits=[dtr, P.get("war_v4")], inc=1)
        P["war_ps3"][tt] = dve
        P["bank"][3] = dve
        for name, psn, tcos, tsin, dst, nrm in (
                ("q", qbank, 0, None, P["Qt"], P["qn"]),
                ("k", kbank, 2, None, P["Kt"], P["kn"])):
            dep_p = dqp if name == "q" else dkp
            tdep = None
            for j in range(2):
                tdep = _q_dma(S, P["tab"][:, j, :],
                              C[P["tabs"][tcos + j]].ap()[:, ts],
                              waits=[P.get("war_tab_l"), P.get("scope_war")],
                              slot="tab")
            dcast = S.emit("v", lambda e, psn=psn, b=b: e.tensor_copy(
                P["qb"][:, b, :], PS[psn][:, :]),
                waits=[dep_p, P.get("war_qb_l")], inc=1)
            P["bank"][psn] = dcast
            if P.get("dbg_pair") and tt == 0:
                dd = S.dbg(f"pp_{name}_raw", [128, TT], BF16,
                           lambda: P["qb"][:, 0, :], [dcast])
                if dd: dcast = dd
            if P.get("l2norm"):
                dcast = _emit_l2norm(meta, P, dcast, b, name)
                if P.get("dbg_pair") and tt == 0:
                    dd = S.dbg(f"pp_{name}_l2", [128, TT], BF16,
                               lambda: P["qb"][:, 0, :], [dcast])
                    if dd: dcast = dd
            drot = S.emit("t", lambda e, b=b: e.matmul(
                PS[4][:, :], K["perm_bf"][:, :], P["qb"][:, b, :],
                start=True, stop=True),
                waits=[dcast, cdep, P.get("war_rot_l"), P["bank"].get(4)],
                inc=1)
            dt1 = S.emit("v", lambda e, b=b: e.tensor_mul(
                P["t1"][:, :], P["qb"][:, b, :], P["tab"][:, 0, :]),
                waits=[tdep, dcast], inc=1)
            dt2 = S.emit("v", lambda e, b=b: e.tensor_mul(
                P["t2"][:, :], PS[4][:, :], P["tab"][:, 1, :]),
                waits=[drot, dt1], inc=1)
            P["war_rot_l"] = dt2
            P["war_tab_l"] = dt2
            P["bank"][4] = dt2
            dqt = S.emit("v", lambda e, dst=dst, tt=tt: e.tensor_add(
                dst[:, tt * TT:(tt + 1) * TT], P["t1"][:, :], P["t2"][:, :]),
                waits=[dt2, P.get("war_qt")], inc=1)
            if P.get("dbg_pair") and tt == 0:
                dd = S.dbg(f"pp_{name}_t1", [128, TT], F32,
                           lambda: P["t1"][:, :], [dt2])
                dd = S.dbg(f"pp_{name}_t2", [128, TT], F32,
                           lambda: P["t2"][:, :], [dqt])
                if dd:
                    P["war_rot_l"] = dd
                    P["war_qb_l"] = dd
                    P["war_tab_l"] = dd
            P["war_qb_l"] = drot
            dsqn = S.emit("s", lambda e, dst=dst, tt=tt, b=b: e.activation(
                P["sqn"][:, b, :], dst[:, tt * TT:(tt + 1) * TT], AF.Square),
                waits=[dqt, P.get("war_sqn_l")], inc=1)
            dnm = S.emit("t", lambda e, b=b: e.matmul(
                PS[5][0:2, :], K["ones2_bf"][:, :], P["sqn"][:, b, :],
                start=True, stop=True),
                waits=[dsqn, cdep, P.get("war_ps5_l"), P["bank"].get(5)],
                inc=1)
            P["war_sqn_l"] = dnm
            if P.get("l2norm"):
                n32 = P["qn32"] if name == "q" else P["kn32"]
                for j4 in range(4):
                    dnv = S.emit("v", lambda e, n32=n32, tt=tt, j4=j4:
                        e.reduce_max(n32[0:2, 4 + 4 * tt + j4:5 + 4 * tt + j4],
                                     PS[5][0:2, j4 * 128:(j4 + 1) * 128],
                                     mybir.AxisListType.X),
                        waits=[dnm, P.get("war_norm_v"),
                               P.get("war_norm_s")], inc=1)
            else:
                dnv = S.emit("v", lambda e, nrm=nrm, tt=tt: e.reduce_max(
                    nrm[0:2, tt:tt + 1], PS[5][0:2, :], mybir.AxisListType.X),
                    waits=[dnm, P.get("war_norm_v"),
                           P.get("war_norm_s")], inc=1)
            P["war_ps5_l"] = dnv
            P["bank"][5] = dnv


def _emit_l2norm(meta, P, dcast, b, name):
    S, K, PS = meta["S"], meta["K"], meta["PS"]
    cdep = meta["const_dep"]
    svec = K["qs_sb"] if name == "q" else K["ks_sb"]
    l = P["ffn_l"]
    dsq = S.emit("s", lambda e, b=b: e.activation(
        P["sqn"][:, b, :], P["qb"][:, b, :], AF.Square),
        waits=[dcast, P.get("war_sqn_l")], inc=1)
    dnm = S.emit("t", lambda e, b=b: e.matmul(
        PS[5][0:2, :], K["ones2_bf"][:, :], P["sqn"][:, b, :],
        start=True, stop=True),
        waits=[dsq, cdep, P.get("war_ps5_l"), P["bank"].get(5)], inc=1)
    P["war_sqn_l"] = dnm
    dmx = S.emit("v", lambda e: e.tensor_scalar_max(
        P["ssc"][0:2, :], PS[5][0:2, :], 1e-24),
        waits=[dnm, P.get("war_ssc")], inc=1)
    P["war_ps5_l"] = dmx
    dln = S.emit("s", lambda e: e.activation(P["ssc"][0:2, :], P["ssc"][0:2, :],
                                             AF.Ln), waits=[dmx], inc=1)
    dex = S.emit("s", lambda e: e.activation(P["ssc"][0:2, :], P["ssc"][0:2, :],
                                             AF.Exp, scale=-0.5),
                 waits=[dln], inc=1)
    drb = S.emit("t", lambda e: e.matmul(
        PS[5][:, :], K["b2_f32"][0:2, :], P["ssc"][0:2, :],
        start=True, stop=True),
        waits=[dex, cdep, P.get("war_ps5_l"), P["bank"].get(5)], inc=1)
    P["war_ssc"] = drb
    dsc = S.emit("v", lambda e, b=b, svec=svec, l=l: e.scalar_tensor_tensor(
        P["qb"][:, b, :], P["qb"][:, b, :], svec[:, l:l + 1], PS[5][:, :],
        ALU.mult, ALU.mult), waits=[drb, meta["const_v_dep"]], inc=1)
    P["war_ps5_l"] = dsc
    P["bank"][5] = dsc
    return dsc


def _emit_attention(meta, P, hp, kind):
    S, A, K, PS, C = meta["S"], meta["A"], meta["K"], meta["PS"], meta["C"]
    cdep = meta["const_dep"]
    for qb in range(NQB):
        qs_ = slice(qb * TT, (qb + 1) * TT)
        dbc = {}

        def bcast_negc(dneg):
            for h in range(2):
                db = S.emit("t", lambda e, h=h: e.matmul(
                    PS[7][:, 500 + h:501 + h], K["ones_r_f32"][:, :],
                    P["negc_row"][0:1, h:h + 1], start=True, stop=True),
                    waits=[dneg, P.get("war_negsb"), P["bank"].get(7)],
                    inc=1)
                dbc[h] = S.emit("v", lambda e, h=h: e.tensor_copy(
                    P["negc_sb"][:, h:h + 1], PS[7][:, 500 + h:501 + h]),
                    waits=[db], inc=1)

        if kind != "f":
            if kind == "a":
                if qb == 0:
                    dk = S.emit("v", lambda e: e.tensor_copy(
                        P["knr"][0:2, :], P["kn"][0:2, 0:1]),
                        waits=[P["norm_dep"], P.get("war_norm_s")], inc=1)
                else:
                    dk = S.emit("v", lambda e, qb=qb: e.tensor_max(
                        P["knr"][0:2, :], P["knr"][0:2, :],
                        P["kn"][0:2, qb:qb + 1]),
                        waits=[P["norm_dep"], P.get("knr_war")], inc=1)
            else:
                if qb == 0:
                    dk = S.emit("v", lambda e: e.reduce_max(
                        P["knr"][0:2, :], P["kn"][0:2, :],
                        mybir.AxisListType.X),
                        waits=[P["norm_dep"], P.get("war_norm_s")], inc=1)
                else:
                    dk = P["knr_dep"]
            P["knr_dep"] = dk
            d1 = S.emit("s", lambda e, qb=qb: e.activation(
                P["c1"][0:2, 0:1], P["qn"][0:2, qb:qb + 1], AF.Ln),
                waits=[dk, P["norm_dep"]], inc=1)
            d2 = S.emit("s", lambda e: e.activation(
                P["c2"][0:2, 0:1], P["knr"][0:2, :], AF.Ln),
                waits=[dk, d1], inc=1)
            d3 = S.emit("v", lambda e: e.tensor_add(P["c1"][0:2, 0:1],
                                                    P["c1"][0:2, 0:1],
                                                    P["c2"][0:2, 0:1]),
                        waits=[d1, d2], inc=1)
            P["knr_war"] = d3
            d4 = S.emit("s", lambda e: e.activation(P["c1"][0:2, 0:1],
                                                    P["c1"][0:2, 0:1],
                                                    AF.Exp, scale=0.5),
                        waits=[d3], inc=1)
            dtr = S.emit("t", lambda e: e.transpose(PS[7][0:1, 504:506],
                                                    P["c1"][0:2, 0:1],
                                                    K["ident_f32"][0:2, 0:2]),
                         waits=[d4, cdep, P.get("war_negps"),
                                P["bank"].get(7)], inc=1)
            if kind == "b":
                dtr = S.emit("v", lambda e, hp=hp: e.tensor_add(
                    PS[7][0:1, 504:506], PS[7][0:1, 504:506],
                    K["pbrow"][0:1, 2 * hp:2 * hp + 2]),
                    waits=[dtr, meta["const_v_dep"]], inc=1)
            dneg = S.emit("v", lambda e: e.tensor_scalar_mul(
                P["negc_row"][0:1, 0:2], PS[7][0:1, 504:506], -1.0),
                waits=[dtr], inc=1)
            P["war_negps"] = dneg
            bcast_negc(dneg)
            P["war_negsb"] = dbc[1]
            P["bank"][7] = dbc[1]

        if kind == "a":
            kcs = [(kc, "vis" if kc < 4 * qb else "band",
                    512 * qb - 128 * kc + OFFA) for kc in range(4 * qb + 4)]
            nk = len(kcs)
            dS_l, dP_l = {}, {}
            SBK = ((0, 2, 6), (1, 3, 7))

            def emit_S(i):
                kc, kd, c0 = kcs[i]
                for h in range(2):
                    psn = SBK[h][i % 3]
                    dS_l[(i, h)] = S.emit(
                        "t", lambda e, psn=psn, h=h, kc=kc, qs_=qs_: e.matmul(
                            PS[psn][:, :], P["Kt"][64 * h:64 * (h + 1),
                                                   kc * 128:(kc + 1) * 128],
                            P["Qt"][64 * h:64 * (h + 1), qs_],
                            start=True, stop=True),
                        waits=[P["bank"].get(psn), P["qkv_dep"]], inc=1)

            def emit_PX(i):
                kc, kd, c0 = kcs[i]
                p = i % 2
                for h in range(2):
                    psn = SBK[h][i % 3]
                    dS = dS_l[(i, h)]
                    if kd == "vis":
                        dP = S.emit("s", lambda e, psn=psn, h=h, p=p:
                            e.activation(
                                P["pt"][:, 2 * h + p, :], PS[psn][:, :], AF.Exp,
                                bias=P["negc_sb"][:, h:h + 1], scale=1.0),
                            waits=[dS, dbc[h], P["war_pt"].get((h, p))], inc=1)
                    else:
                        band_ap = K["band_a"][:, c0:c0 + TT]
                        dsm = S.emit("v", lambda e, psn=psn, p=p,
                                     band_ap=band_ap:
                            e.tensor_add(P["sm"][:, p, :], PS[psn][:, :],
                                         band_ap),
                            waits=[dS, cdep, P["war_sm"].get(p)], inc=1)
                        dP = S.emit("s", lambda e, h=h, p=p: e.activation(
                            P["pt"][:, 2 * h + p, :], P["sm"][:, p, :],
                            AF.Exp, bias=P["negc_sb"][:, h:h + 1], scale=1.0),
                            waits=[dsm, dbc[h], P["war_pt"].get((h, p))], inc=1)
                        P["war_sm"][p] = dP
                    P["bank"][psn] = dP
                    dP_l[(i, h)] = dP

            def emit_AV(i):
                kc, kd, c0 = kcs[i]
                p = i % 2
                for h in range(2):
                    dav = S.emit("t", lambda e, h=h, p=p, kc=kc, i=i, nk=nk:
                        e.matmul(
                            PS[4 + h][0:65, :], P["V"][:, h, kc, :],
                            P["pt"][:, 2 * h + p, :],
                            start=(i == 0), stop=(i == nk - 1)),
                        waits=[dP_l[(i, h)], P["bank"].get(4 + h)
                               if i == 0 else None], inc=1)
                    P["war_pt"][(h, p)] = dav

            emit_S(0)
            if nk > 1:
                emit_S(1)
            for i in range(nk):
                emit_PX(i)
                if i + 2 < nk:
                    emit_S(i + 2)
                emit_AV(i)
        elif kind == "f":
            kcs = [(kc, 512 * qb - 128 * kc + OFFF)
                   for kc in range(max(0, 4 * (qb - 1)), 4 * (qb + 1))]
            nk = len(kcs)
            dmx = None
            for i, (kc, c0) in enumerate(kcs):
                for h in range(2):
                    psn = 2 * h + (i % 2)
                    dS = S.emit("t", lambda e, psn=psn, h=h, kc=kc, qs_=qs_:
                        e.matmul(
                            PS[psn][:, :], P["Kt"][64 * h:64 * (h + 1),
                                                   kc * 128:(kc + 1) * 128],
                            P["Qt"][64 * h:64 * (h + 1), qs_],
                            start=True, stop=True),
                        waits=[P["bank"].get(psn), P["qkv_dep"]], inc=1)
                    dsm = S.emit("v", lambda e, psn=psn, h=h, i=i, c0=c0:
                        e.tensor_add(P["smB"][:, h, i, :], PS[psn][:, :],
                                     K["band_f"][:, c0:c0 + TT]),
                        waits=[dS, cdep, P["war_smB"].get((h, i))], inc=1)
                    P["bank"][psn] = dsm
                    dmx = S.emit("v", lambda e, h=h, i=i: e.reduce_max(
                        P["rmx"][:, h, i:i + 1], P["smB"][:, h, i, :],
                        mybir.AxisListType.X), waits=[dsm], inc=1)
            for h in range(2):
                dm2 = S.emit("v", lambda e, h=h, nk=nk: e.reduce_max(
                    P["rm1"][:, h:h + 1], P["rmx"][:, h, 0:nk],
                    mybir.AxisListType.X), waits=[dmx], inc=1)
                dtr = S.emit("t", lambda e, h=h: e.transpose(
                    PS[7][0:1, 360:488],
                    P["rm1"][:, h:h + 1], K["ident_f32"][:, :]),
                    waits=[dm2, cdep, P.get("war_negps"),
                           P["bank"].get(7)], inc=1)
                dcx = S.emit("v", lambda e, h=h: e.reduce_max(
                    P["negc_row"][0:1, h:h + 1], PS[7][0:1, 360:488],
                    mybir.AxisListType.X), waits=[dtr], inc=1)
                P["war_negps"] = dcx
                P["bank"][7] = dcx
            dneg = S.emit("v", lambda e: e.tensor_scalar_mul(
                P["negc_row"][0:1, 0:2], P["negc_row"][0:1, 0:2], -1.0),
                waits=[P["war_negps"]], inc=1)
            P["war_negps"] = dneg
            bcast_negc(dneg)
            P["war_negsb"] = dbc[1]
            for i, (kc, c0) in enumerate(kcs):
                for h in range(2):
                    p = i % 2
                    dP = S.emit("s", lambda e, h=h, i=i, p=p: e.activation(
                        P["pt"][:, 2 * h + p, :], P["smB"][:, h, i, :],
                        AF.Exp, bias=P["negc_sb"][:, h:h + 1], scale=1.0),
                        waits=[dbc[h], P["war_pt"].get((h, p))], inc=1)
                    P["war_smB"][(h, i)] = dP
                    dav = S.emit("t", lambda e, h=h, p=p, kc=kc, i=i, nk=nk:
                        e.matmul(
                            PS[4 + h][0:65, :], P["V"][:, h, kc, :],
                            P["pt"][:, 2 * h + p, :],
                            start=(i == 0), stop=(i == nk - 1)),
                        waits=[dP, P["war_o"].get(h)], inc=1)
                    P["war_pt"][(h, p)] = dav
        else:
            dS_l, dP_l, dband_l = {}, {}, {}

            def b_band(h, ei):
                base = 3584 - 512 * ei
                srcb = C["band_b"].ap()[2 * hp + h]
                bap = bass.AP(srcb.tensor, srcb.offset + qb * TT + base,
                              [[BW_B, 128], [1, 896]])
                dband_l[(h, ei)] = _q_dma(
                    S, P["wband"][:, h, :], bap,
                    waits=[P["war_band"].get(h)], slot=f"band{h}")

            SBK = ((0, 2, 6), (1, 3, 7))

            def b_S(kc):
                for h in range(2):
                    psn = SBK[h][kc % 3]
                    dS_l[(kc, h)] = S.emit(
                        "t", lambda e, psn=psn, h=h, kc=kc, qs_=qs_: e.matmul(
                            PS[psn][:, :], P["Kt"][64 * h:64 * (h + 1),
                                                   kc * 128:(kc + 1) * 128],
                            P["Qt"][64 * h:64 * (h + 1), qs_],
                            start=True, stop=True),
                        waits=[P["bank"].get(psn), P["qkv_dep"]], inc=1)

            def b_PX(kc):
                p = kc % 2
                ei = kc // 4
                base = 3584 - 512 * ei
                c0 = 3968 - 128 * kc - base
                for h in range(2):
                    psn = SBK[h][kc % 3]
                    dsm = S.emit("v", lambda e, psn=psn, h=h, p=p, c0=c0:
                        e.tensor_add(P["sm"][:, 2 * h + p, :], PS[psn][:, :],
                                     P["wband"][:, h, c0:c0 + TT]),
                        waits=[dS_l[(kc, h)], dband_l[(h, ei)],
                               P["war_sm"].get((h, p))], inc=1)
                    P["war_band"][h] = dsm
                    dP = S.emit("s", lambda e, h=h, p=p: e.activation(
                        P["pt"][:, 2 * h + p, :], P["sm"][:, 2 * h + p, :],
                        AF.Exp, bias=P["negc_sb"][:, h:h + 1], scale=1.0),
                        waits=[dsm, dbc[h], P["war_pt"].get((h, p))], inc=1)
                    P["war_sm"][(h, p)] = dP
                    P["bank"][psn] = dP
                    dP_l[(kc, h)] = dP

            def b_AV(kc):
                p = kc % 2
                for h in range(2):
                    dav = S.emit("t", lambda e, h=h, p=p, kc=kc: e.matmul(
                        PS[4 + h][0:65, :], P["V"][:, h, kc, :],
                        P["pt"][:, 2 * h + p, :],
                        start=(kc == 0), stop=(kc == KCN - 1)),
                        waits=[dP_l[(kc, h)], P["bank"].get(4 + h)
                               if kc == 0 else None], inc=1)
                    P["war_pt"][(h, p)] = dav

            b_band(0, 0)
            b_band(1, 0)
            b_S(0)
            b_S(1)
            for kc in range(KCN):
                if kc > 0 and kc % 4 == 0:
                    b_band(0, kc // 4)
                    b_band(1, kc // 4)
                b_PX(kc)
                if kc + 2 < KCN:
                    b_S(kc + 2)
                b_AV(kc)
        davs = S.dep("t")
        for h in range(2):
            drr = S.emit("v", lambda e, h=h: e.reciprocal(
                P["rrow"][0:1, :], PS[4 + h][64:65, :]), waits=[davs], inc=1)
            dg = S.emit("t", lambda e: e.matmul(
                PS[6][0:64, :], K["ones_r_f32"][0:1, 0:64], P["rrow"][0:1, :],
                start=True, stop=True),
                waits=[drr, P.get("war_gam"), P["bank"].get(6)], inc=1)
            dgc = S.emit("v", lambda e: e.tensor_copy(P["gam"][0:64, :],
                                                      PS[6][0:64, :]),
                         waits=[dg], inc=1)
            P["war_gam"] = dgc
            P["bank"][6] = dgc
            dst_ap = P["o_dst"](qb, h)
            dout = S.emit("v", lambda e, h=h, dst_ap=dst_ap: e.tensor_mul(
                dst_ap, PS[4 + h][0:64, :], P["gam"][0:64, :]),
                waits=[dgc], inc=1)
            P["bank"][4 + h] = dout
        P["war_negsb"] = S.dep("v")
        P["o_done"] = S.dep("v")


def _mk_attn_bufs(meta, nkc=KCN, rope=True, nkin=CH, fine=False):
    A = meta["A"]
    P = {}
    if fine:
        P["smB"] = A.alloc([128, 2, 8, TT], BF16, "smB")
        P["rmx"] = A.alloc([128, 2, 8], F32, "rmx")
        P["rm1"] = A.alloc([128, 2], F32, "rm1")
        P["war_smB"] = {}
    P["wq"] = A.alloc([128, 3, nkin, 128], BF16, "wq")
    if rope:
        P["tab"] = A.alloc([128, 2, TT], F32, "tab")
        P["t1"] = A.alloc([128, TT], F32, "t1")
        P["t2"] = A.alloc([128, TT], F32, "t2")
        P["ssc"] = A.alloc([128, TT], F32, "ssc")
    P["vt_f32"] = A.alloc([128, 1, TT], F32, "vtf")
    if rope:
        P["qb"] = A.alloc([128, 2, TT], BF16, "qbb")
    P["sqn"] = A.alloc([128, 2, TT], BF16, "sqn")
    P["sm"] = A.alloc([128, 2 if rope else 4, TT], F32, "sm")
    P["Qt"] = A.alloc([128, T], BF16, "Qt")
    P["Kt"] = A.alloc([128, T], BF16, "Kt")
    P["V"] = A.alloc([128, 2, nkc, 65], BF16, "Vsb")
    P["qn"] = A.alloc([128, NTT], F32, "qn")
    P["kn"] = A.alloc([128, NTT], F32, "kn")
    P["qn32"] = A.alloc([128, 40], F32, "qn32")
    P["kn32"] = A.alloc([128, 40], F32, "kn32")
    P["acc4"] = A.alloc([128, 4], F32, "acc4")
    P["t4"] = A.alloc([128, 4], F32, "t4")
    P["knr"] = A.alloc([128, 1], F32, "knr")
    P["c1"] = A.alloc([128, 1], F32, "c1")
    P["c2"] = A.alloc([128, 1], F32, "c2")
    P["negc_row"] = A.alloc([128, 2], F32, "negcr")
    P["negc_sb"] = A.alloc([128, 2], F32, "negcs")
    P["rrow"] = A.alloc([128, TT], F32, "rrow")
    P["gam"] = A.alloc([128, TT], F32, "gam")
    P["pt"] = A.alloc([128, 4, TT], BF16, "pt")
    for nmm in ("war_tab", "bank", "war_vt", "war_ps3", "war_sm",
                "war_pt", "war_o"):
        P[nmm] = {}
    return P


def _emit_phase_attn_a(meta):
    S, A, C = meta["S"], meta["A"], meta["C"]
    O = meta["Oa"]
    A.push()
    P = _mk_attn_bufs(meta)
    P["tabs"] = ("tab_a_cq", "tab_a_sq", "tab_a_ck", "tab_a_sk")
    xn = meta["xn"]
    P["rhs_fn"] = lambda kc, tt: xn[:, kc, tt * TT:(tt + 1) * TT]
    P["rhs_dbg"] = lambda: xn[:, :, 0:TT]
    P["rhs_dep"] = meta["ln1_done"]
    P["scope_war"] = meta["ln1_done"]
    S.emit("p", lambda e: e.memset(P["V"][:, :, :, 64:65], 1.0),
           waits=[P["scope_war"]], inc=1)
    donev = S.dep("p")
    S.wait_only("t", [donev])
    for hp in range(CH):
        P["wsrc"] = (lambda hp: lambda j: C["c_wa_qkv"].ap()[
            (hp, 8 + hp, 16 + hp)[j]])(hp)
        _emit_pair_qkv_a(meta, P, hp)
        P["qkv_dep"] = S.dep("v")
        P["norm_dep"] = S.dep("v")
        if hp == 0:
            S.dbg("a_qt0", [128, T], BF16, lambda: P["Qt"][:, :], [P["qkv_dep"]])
            S.dbg("a_kt0", [128, T], BF16, lambda: P["Kt"][:, :], [P["qkv_dep"]])
            S.dbg("a_v0", [128, 2 * KCN * 65], BF16,
                  lambda: P["V"].rearrange("p a b c -> p (a b c)"),
                  [P["qkv_dep"]])
            S.dbg("a_qn0", [128, NTT], F32, lambda: P["qn"][:, :],
                  [P["norm_dep"]])
        P["o_dst"] = (lambda hp: lambda qb, h: O[64 * h:64 * h + 64, hp,
                                                 qb * TT:(qb + 1) * TT])(hp)
        _emit_attention(meta, P, hp, "a")
        P["war_qt"] = S.dep("t")
        P["war_v4"] = S.dep("t")
        P["war_norm_v"] = S.dep("v")
        P["war_norm_s"] = S.dep("s")
        P.pop("knr_dep", None)
        P.pop("knr_war", None)
    S.barrier()
    A.pop()
    meta["attn_a_done"] = S.dep("v")


def _emit_phase_proj_res(meta, O, w_oc_fn, bias_name, src_dram, dst_dram,
                         kcs=CH, o_dep=None):
    S, A, K, PS, C = meta["S"], meta["A"], meta["K"], meta["PS"], meta["C"]
    A.push()
    wp = A.alloc([128, CH, kcs, 128], BF16, "wp")
    xold = A.alloc([128, 2, TT], F32, "xold")
    xnew = A.alloc([128, 2, TT], F32, "xnew")
    wdep = None
    for oc in range(CH):
        srcw = w_oc_fn(oc)
        sap = bass.AP(srcw.tensor, srcw.offset,
                      [[128, 128], [128 * 128, kcs], [1, 128]])
        wdep = _q_dma(S, wp[:, oc, :, :], sap, waits=[meta["wprep_dep"]],
                      slot="wp")
    war = {}
    for tt in range(NTT):
        ts = slice(tt * TT, (tt + 1) * TT)
        for oc in range(CH):
            idx = tt * CH + oc
            b = idx % 4
            xb = idx % 2
            dl = _q_dma(S, xold[:, xb, :],
                        src_dram.ap()[oc * 128:(oc + 1) * 128, ts],
                        waits=[war.get((xb, "x"))], slot=f"xold{xb}")
            for kc in range(kcs):
                S.emit("t", lambda e, oc=oc, kc=kc, tt=tt, b=b: e.matmul(
                    PS[b][:, :], wp[:, oc, kc, :], O[:, kc, tt * TT:(tt + 1) * TT],
                    start=(kc == 0), stop=(kc == kcs - 1)),
                    waits=[wdep, o_dep, war.get((b, "ps"))],
                    inc=(1 if kc == kcs - 1 else 0))
            dmm = S.dep("t")
            if bias_name:
                dv0 = S.emit("v", lambda e, b=b, oc=oc: e.tensor_scalar_add(
                    PS[b][:, :], PS[b][:, :], K[bias_name][:, oc:oc + 1]),
                    waits=[dmm, meta["const_dep"]], inc=1)
            else:
                dv0 = dmm
            dv = S.emit("v", lambda e, b=b, xb=xb: e.tensor_add(
                xnew[:, xb, :], PS[b][:, :], xold[:, xb, :]),
                waits=[dv0, dl, war.get((xb, "xn"))], inc=1)
            war[(b, "ps")] = dv
            dst = _q_dma(S, dst_dram.ap()[oc * 128:(oc + 1) * 128, ts],
                         xnew[:, xb, :], waits=[dv], slot=f"pro{xb}")
            war[(xb, "x")] = dst
            war[(xb, "xn")] = dst
    A.pop()
    return S.dep("q")


def _emit_ub_halo(meta, u_dram, ub):
    S, A = meta["S"], meta["A"]
    A.push()
    uf = A.alloc([128, 2, TT], F32, "uf")
    dz = S.emit("p", lambda e: e.memset(ub[:, :, 0:1], 0.0), inc=1)
    war = {}
    for c in range(CH):
        for tt in range(NTT):
            b = (c * NTT + tt) % 2
            dl = _q_dma(S, uf[:, b, :],
                        u_dram.ap()[c * 128:(c + 1) * 128,
                                    tt * TT:(tt + 1) * TT],
                        waits=[war.get(b), meta.get("u_ready"), dz],
                        slot=f"ub{b}")
            war[b] = S.emit("p", lambda e, c=c, tt=tt, b=b: e.tensor_copy(
                ub[:, c, 1 + tt * TT:1 + (tt + 1) * TT], uf[:, b, :]),
                waits=[dl], inc=1)
    A.pop()
    return S.dep("v")


def _emit_ffn_sublayer(meta, l, u_in, u_mid, res_in, extra_res, dst, tag):
    S, A, K, PS, C = meta["S"], meta["A"], meta["K"], meta["PS"], meta["C"]
    cdep = meta["const_dep"]
    A.push()
    ub = A.alloc([128, CH, T + 1], BF16, "ub")
    Of = A.alloc([128, 4, T], BF16, "Of")
    meta["u_ready"] = None
    dub = _emit_ub_halo(meta, u_in, ub)
    P = _mk_attn_bufs(meta, rope=True, fine=True)
    P["tabs"] = ("tab_f_cq", "tab_f_sq", "tab_f_ck", "tab_f_sk")
    P["l2norm"] = True
    P["ffn_l"] = l
    P["rhs_fn"] = (lambda ub: lambda kc, tt:
                   ub[:, kc, 1 + tt * TT:1 + (tt + 1) * TT] if kc < 4 else
                   ub[:, kc, tt * TT:(tt + 1) * TT])(ub)
    P["rhs_dbg"] = (lambda ub: lambda: ub[:, :, 0:TT])(ub)
    P["rhs_dep"] = dub
    P["scope_war"] = dub
    S.emit("p", lambda e: e.memset(P["V"][:, :, :, 64:65], 1.0),
           waits=[dub], inc=1)
    S.wait_only("t", [S.dep("p")])
    for hp in range(4):
        P["wsrc"] = (lambda hp: lambda j: C["c_wf_qkv"].ap()[
            l, (hp, 4 + hp, 8 + hp)[j]])(hp)
        P["dbg_pair"] = (tag == "f1l0" and hp == 0)
        _emit_pair_qkv_a(meta, P, hp)
        P["qkv_dep"] = S.dep("v")
        P["norm_dep"] = S.dep("v")
        if tag == "f1l0" and hp == 0:
            S.dbg("f_qt0", [128, T], BF16, lambda: P["Qt"][:, :], [P["qkv_dep"]])
            S.dbg("f_kt0", [128, T], BF16, lambda: P["Kt"][:, :], [P["qkv_dep"]])
        P["o_dst"] = (lambda hp: lambda qb, h: Of[64 * h:64 * h + 64, hp,
                                                  qb * TT:(qb + 1) * TT])(hp)
        _emit_attention(meta, P, hp, "f")
        P["war_qt"] = S.dep("t")
        P["war_v4"] = S.dep("t")
        P["war_norm_v"] = S.dep("v")
        P["war_norm_s"] = S.dep("s")
        for nmm in ("knr_dep", "knr_war"):
            P.pop(nmm, None)
    if tag == "f1l0":
        S.dbg("f_o0", [128, 4 * T], BF16,
              lambda: Of.rearrange("p a b -> p (a b)"), [S.dep("v")])
    _emit_phase_proj_res(meta, Of, lambda oc: C["c_wf_wo"].ap()[l, oc], None,
                         u_in, u_mid, kcs=4)
    S.new_phase_sems()
    A.pop()
    if tag == "f1l0":
        S.dbg("f_u1", [D, T], F32, lambda: u_mid.ap(), [])
    A.push()
    hb = A.alloc([128, CH, T], BF16, "hb")
    _emit_ln_shift_apply(meta, u_mid, hb, l)
    if tag == "f1l0":
        S.dbg("f_hb", [128, CH * T], BF16,
              lambda: hb.rearrange("p a b -> p (a b)"), [S.dep("v")])
    _emit_w1_glu_w2(meta, l, hb, res_in, extra_res, dst, tag)
    S.new_phase_sems()
    A.pop()


def _emit_ln_shift_apply(meta, u_dram, hb, l):
    S, A, K, PS, C = meta["S"], meta["A"], meta["K"], meta["PS"], meta["C"]
    A.push()
    uf = A.alloc([128, 2, CH, TT + 1], F32, "uf2")
    xbt = A.alloc([128, 2, CH, TT + 1], BF16, "xbt2")
    sq = A.alloc([128, 2, 2, TT], BF16, "sq2")
    cdep = meta["const_dep"]
    war_v = {}
    prev_apply = {}
    dz = S.emit("p", lambda e: e.memset(uf[:, :, :, 0:1], 0.0), inc=1)

    def chv(buf, b, c):
        if c < 4:
            return buf[:, b, c, 1:TT + 1]
        return buf[:, b, c, 0:TT]

    for tt in range(NTT):
        b = tt % 2
        dl = None
        for c in range(CH):
            if tt == 0:
                dl = _q_dma(S, uf[:, b, c, 1:TT + 1],
                            u_dram.ap()[c * 128:(c + 1) * 128, 0:TT],
                            waits=([war_v.get(tt - 2), dz] if c == 0 else ()),
                            slot=f"lsf{b}")
            else:
                dl = _q_dma(S, uf[:, b, c, 0:TT + 1],
                            u_dram.ap()[c * 128:(c + 1) * 128,
                                        tt * TT - 1:(tt + 1) * TT],
                            waits=([war_v.get(tt - 2), dz] if c == 0 else ()),
                            slot=f"lsf{b}")
        dcast = None
        for c in range(CH):
            dcast = S.emit("p", lambda e, c=c, b=b: e.tensor_copy(
                xbt[:, b, c, :], uf[:, b, c, :]), waits=[dl], inc=1)
        war = prev_apply.get(tt - 1, meta.get("const_v_dep"))
        sqmm = {}
        for c in range(CH):
            dsq = S.emit("s", lambda e, c=c, b=b: e.activation(
                sq[:, b, c % 2, :], chv(xbt, b, c), AF.Square),
                waits=[dcast, war, sqmm.get(c - 2)], inc=1)
            S.emit("t", lambda e, c=c, b=b: e.matmul(
                PS[6][0:1, :], K["ones_c_bf"][:, :], chv(xbt, b, c),
                start=(c == 0), stop=(c == CH - 1)),
                waits=[dcast, cdep, war], inc=(1 if c == CH - 1 else 0))
            sqmm[c] = S.emit("t", lambda e, c=c, b=b: e.matmul(
                PS[7][0:1, :], K["ones_c_bf"][:, :], sq[:, b, c % 2, :],
                start=(c == 0), stop=(c == CH - 1)),
                waits=[dsq, war], inc=1)
        dstat = S.dep("t")
        d1 = S.emit("v", lambda e: e.tensor_scalar_mul(
            K["mu_row"][0:1, :], PS[6][0:1, :], 1.0 / D), waits=[dstat], inc=1)
        d2 = S.emit("v", lambda e: e.tensor_scalar_mul(
            K["r_row"][0:1, :], PS[7][0:1, :], 1.0 / D), waits=[dstat], inc=1)
        d3 = S.emit("v", lambda e: e.tensor_mul(
            K["mur_row"][0:1, :], K["mu_row"][0:1, :], K["mu_row"][0:1, :]),
            waits=[d2], inc=1)
        d4 = S.emit("v", lambda e: e.tensor_sub(
            K["r_row"][0:1, :], K["r_row"][0:1, :], K["mur_row"][0:1, :]),
            waits=[d3], inc=1)
        d5 = S.emit("v", lambda e: e.tensor_scalar_max(
            K["r_row"][0:1, :], K["r_row"][0:1, :], 0.0), waits=[d4], inc=1)
        d6 = S.emit("s", lambda e: e.activation(
            K["r_row"][0:1, :], K["r_row"][0:1, :], AF.Ln,
            bias=K["eps1"][0:1, :], scale=1.0),
            waits=[d5, meta.get("eps_dep")], inc=1)
        d7 = S.emit("s", lambda e: e.activation(
            K["r_row"][0:1, :], K["r_row"][0:1, :], AF.Exp, scale=-0.5),
            waits=[d6], inc=1)
        d8 = S.emit("v", lambda e: e.tensor_mul(
            K["mur_row"][0:1, :], K["mu_row"][0:1, :], K["r_row"][0:1, :]),
            waits=[d7], inc=1)
        dr = S.emit("t", lambda e: e.matmul(
            PS[6][:, :], K["ones_r_f32"][:, :], K["r_row"][0:1, :],
            start=True, stop=True), waits=[d8], inc=1)
        dm = S.emit("t", lambda e: e.matmul(
            PS[7][:, :], K["ones_r_f32"][:, :], K["mur_row"][0:1, :],
            start=True, stop=True), waits=[d8], inc=1)
        dv = None
        for c in range(CH):
            dv1 = S.emit("v", lambda e, c=c, b=b: e.tensor_mul(
                uf[:, b, c, 1:TT + 1], chv(xbt, b, c), PS[6][:, :]),
                waits=[dr], inc=1)
            dv2 = S.emit("v", lambda e, c=c, b=b: e.tensor_sub(
                uf[:, b, c, 1:TT + 1], uf[:, b, c, 1:TT + 1], PS[7][:, :]),
                waits=[dm, dv1], inc=1)
            dv = S.emit("v", lambda e, c=c, b=b, tt=tt, l=l: e.tensor_scalar(
                hb[:, c, tt * TT:(tt + 1) * TT], uf[:, b, c, 1:TT + 1],
                K["lng"][:, l, c:c + 1], K["lnb"][:, l, c:c + 1],
                ALU.mult, ALU.add), waits=[dv2, meta["const_dep"]], inc=1)
        war_v[tt] = dv
        prev_apply[tt] = dv
    A.pop()
    meta["hb_done"] = S.dep("v")


def _emit_w1_glu_w2(meta, l, hb, res_in, extra_res, dst, tag):
    S, A, K, PS, C = meta["S"], meta["A"], meta["K"], meta["PS"], meta["C"]
    cdep = meta["const_dep"]
    A.push()
    TH = 1024
    prod = A.alloc([128, MA, TH], BF16, "prod")
    a_sb = A.alloc([128, 2, TT], F32, "asb")
    gel = A.alloc([128, 2, TT], F32, "gel")
    w1t = A.alloc([128, 2, 2, CH, 128], BF16, "w1t")
    w2t = A.alloc([128, 2, MA, 128], BF16, "w2t")
    ures = A.alloc([128, 2, TT], F32, "ures")
    xres = A.alloc([128, 2, TT], F32, "xres")
    yout = A.alloc([128, 2, TT], F32, "yout")
    hb_dep = meta["hb_done"]
    for th in range(T // TH):
        hs = th * TH
        war_w1 = {}
        war_prod = {}
        for m in range(MA):
            bw = m % 2
            dwa = _q_dma(S, w1t[:, bw, 0, :, :],
                         bass.AP(C["c_wf_w1a"].ap()[l, m].tensor,
                                 C["c_wf_w1a"].ap()[l, m].offset,
                                 [[128, 128], [128 * 128, CH], [1, 128]]),
                         waits=[meta["wprep_dep"], war_w1.get(m - 2), hb_dep],
                         slot=f"w1a{bw}")
            dwg = _q_dma(S, w1t[:, bw, 1, :, :],
                         bass.AP(C["c_wf_w1g"].ap()[l, m].tensor,
                                 C["c_wf_w1g"].ap()[l, m].offset,
                                 [[128, 128], [128 * 128, CH], [1, 128]]),
                         waits=[war_w1.get(m - 2), hb_dep], slot=f"w1g{bw}")
            for sub in range(TH // TT):
                ss = slice(hs + sub * TT, hs + (sub + 1) * TT)
                ab = 0 if sub % 2 == 0 else 6
                gb = 1 if sub % 2 == 0 else 7
                sb2 = sub % 2
                for kc in range(CH):
                    S.emit("t", lambda e, kc=kc, bw=bw, ss=ss, ab=ab: e.matmul(
                        PS[ab][:, :], w1t[:, bw, 0, kc, :], hb[:, kc, ss],
                        start=(kc == 0), stop=(kc == CH - 1)),
                        waits=[dwa, hb_dep, war_prod.get(ab)],
                        inc=(1 if kc == CH - 1 else 0))
                da = S.dep("t")
                for kc in range(CH):
                    S.emit("t", lambda e, kc=kc, bw=bw, ss=ss, gb=gb: e.matmul(
                        PS[gb][:, :], w1t[:, bw, 1, kc, :], hb[:, kc, ss],
                        start=(kc == 0), stop=(kc == CH - 1)),
                        waits=[dwg, hb_dep, war_prod.get(gb)],
                        inc=(1 if kc == CH - 1 else 0))
                dg = S.dep("t")
                dae = S.emit("v", lambda e, m=m, ab=ab, sb2=sb2: e.tensor_scalar_add(
                    a_sb[:, sb2, :], PS[ab][:, :], K["b1a_sb"][:, l, m:m + 1]),
                    waits=[da, cdep, war_prod.get(("a", sb2))], inc=1)
                war_prod[ab] = dae
                dge = S.emit("s", lambda e, m=m, gb=gb, sb2=sb2: e.activation(
                    gel[:, sb2, :], PS[gb][:, :], AF.Gelu,
                    bias=K["b1g_sb"][:, l, m:m + 1], scale=1.0),
                    waits=[dg, cdep, war_prod.get(("g", sb2))], inc=1)
                war_prod[gb] = dge
                dp = S.emit("v", lambda e, m=m, sub=sub, sb2=sb2: e.tensor_mul(
                    prod[:, m, sub * TT:(sub + 1) * TT], a_sb[:, sb2, :],
                    gel[:, sb2, :]),
                    waits=[dae, dge], inc=1)
                war_prod[("a", sb2)] = dp
                war_prod[("g", sb2)] = dp
            war_w1[m] = dp
        dprod = S.dep("v")
        war_ps = {}
        war_res = {}
        for oc in range(CH):
            bw = oc % 2
            dw2 = _q_dma(S, w2t[:, bw, :, :],
                         bass.AP(C["c_wf_w2"].ap()[l, oc].tensor,
                                 C["c_wf_w2"].ap()[l, oc].offset,
                                 [[128, 128], [128 * 128, MA], [1, 128]]),
                         waits=[meta["wprep_dep"], war_ps.get(("w", bw)),
                                hb_dep],
                         slot=f"w2t{bw}")
            for sub in range(TH // TT):
                ss = slice(hs + sub * TT, hs + (sub + 1) * TT)
                bb = sub % 2
                rs = slice(sub * TT, (sub + 1) * TT)
                for m in range(MA):
                    S.emit("t", lambda e, m=m, bw=bw, rs=rs, bb=bb: e.matmul(
                        PS[2 + bb][:, :], w2t[:, bw, m, :],
                        prod[:, m, rs],
                        start=(m == 0), stop=(m == MA - 1)),
                        waits=[dw2, dprod, war_ps.get(bb)],
                        inc=(1 if m == MA - 1 else 0))
                dy = S.dep("t")
                dlu = _q_dma(S, ures[:, bb, :],
                             res_in.ap()[oc * 128:(oc + 1) * 128, ss],
                             waits=[war_res.get((bb, "u")), hb_dep],
                             slot=f"ures{bb}")
                dv0 = S.emit("v", lambda e, bb=bb, oc=oc: e.tensor_scalar_add(
                    PS[2 + bb][:, :], PS[2 + bb][:, :],
                    K["b2b"][:, l, oc:oc + 1]), waits=[dy, cdep], inc=1)
                dv1 = S.emit("v", lambda e, bb=bb: e.tensor_add(
                    yout[:, bb, :], PS[2 + bb][:, :], ures[:, bb, :]),
                    waits=[dv0, dlu], inc=1)
                if extra_res is not None:
                    dlx = _q_dma(S, xres[:, bb, :],
                                 extra_res.ap()[oc * 128:(oc + 1) * 128, ss],
                                 waits=[war_res.get((bb, "x")), hb_dep],
                                 slot=f"xres{bb}")
                    dv1 = S.emit("v", lambda e, bb=bb: e.tensor_add(
                        yout[:, bb, :], yout[:, bb, :], xres[:, bb, :]),
                        waits=[dv1, dlx], inc=1)
                war_ps[bb] = dv1
                dso = _q_dma(S, dst.ap()[oc * 128:(oc + 1) * 128, ss],
                             yout[:, bb, :], waits=[dv1], slot=f"w2o{bb}")
                war_res[(bb, "u")] = dso
                war_res[(bb, "x")] = dso
            war_ps[("w", bw)] = S.dep("t")
        S.barrier()
    A.pop()


def _emit_pair_qkv_b(meta, P, hp):
    S, A, K, PS, C = meta["S"], meta["A"], meta["K"], meta["PS"], meta["C"]
    cdep = meta["const_dep"]
    xn = P["xn"]
    wdep = None
    for j, wn in enumerate(("c_wb_q", "c_wb_k", "c_wb_v")):
        srcw = C[wn].ap()[hp]
        sap = bass.AP(srcw.tensor, srcw.offset,
                      [[128, 128], [128 * 128, CH], [1, 128]])
        wdep = _q_dma(S, P["wq"][:, j, :, :], sap,
                      waits=[meta["wprep_dep"], P.get("scope_war")],
                      slot="wq")
    for tt in range(NTT):
        ts = slice(tt * TT, (tt + 1) * TT)
        b = tt % 2
        qbank = 0 if tt % 2 == 0 else 6
        kbank = 1 if tt % 2 == 0 else 7
        for psn, j in ((qbank, 0), (kbank, 1), (2, 2)):
            for kc in range(CH):
                S.emit("t", lambda e, psn=psn, j=j, kc=kc, tt=tt: e.matmul(
                    PS[psn][:, :], P["wq"][:, j, kc, :],
                    xn[:, kc, tt * TT:(tt + 1) * TT],
                    start=(kc == 0), stop=(kc == CH - 1)),
                    waits=[wdep, P["rhs_dep"], P["bank"].get(psn)],
                    inc=(1 if kc == CH - 1 else 0))
            if j == 0:
                dqp = S.dep("t")
            elif j == 1:
                dkp = S.dep("t")
            else:
                dvp = S.dep("t")
        dq = S.emit("v", lambda e, tt=tt, hp=hp, qbank=qbank: e.tensor_scalar(
            P["Qt"][:, tt * TT:(tt + 1) * TT], PS[qbank][:, :],
            K["bq"][:, hp:hp + 1], 0.125, ALU.add, ALU.mult),
            waits=[dqp, cdep, P.get("war_qt")], inc=1)
        P["bank"][qbank] = dq
        dk = S.emit("v", lambda e, tt=tt, hp=hp, kbank=kbank: e.tensor_scalar_add(
            P["Kt"][:, tt * TT:(tt + 1) * TT], PS[kbank][:, :],
            K["bk"][:, hp:hp + 1]), waits=[dkp, cdep, P.get("war_qt")], inc=1)
        P["bank"][kbank] = dk
        dvf = S.emit("v", lambda e, hp=hp: e.tensor_scalar_add(
            P["vt_f32"][:, 0, :], PS[2][:, :], K["bv"][:, hp:hp + 1]),
            waits=[dvp, cdep, P["war_vt"].get(tt - 1)], inc=1)
        P["bank"][2] = dvf
        dtr = None
        for j in range(4):
            dtr = S.emit("t", lambda e, j=j: e.transpose(
                PS[3][:, j * 128:(j + 1) * 128],
                P["vt_f32"][:, 0, j * 128:(j + 1) * 128],
                K["ident_f32"][:, :]),
                waits=[dvf, cdep, P["war_ps3"].get(tt - 1),
                       P["bank"].get(3)], inc=1)
        P["war_vt"][tt] = dtr
        dve = None
        V4b = P["V"]
        for h in range(2):
            vout = V4b[:, h, tt * 4:(tt + 1) * 4, 0:64]
            vin = bass.AP(PS[3], 64 * h, [[512, 128], [128, 4], [1, 64]])
            dve = S.emit("v", lambda e, vout=vout, vin=vin: e.tensor_copy(
                vout, vin), waits=[dtr, P.get("war_v4")], inc=1)
        P["war_ps3"][tt] = dve
        P["bank"][3] = dve
        for dst_t, nrm, dd in ((P["Qt"], P["qn"], dq), (P["Kt"], P["kn"], dk)):
            dsqn = S.emit("s", lambda e, dst_t=dst_t, tt=tt, b=b: e.activation(
                P["sqn"][:, b, :], dst_t[:, tt * TT:(tt + 1) * TT], AF.Square),
                waits=[dd, P.get("war_sqn_l")], inc=1)
            dnm = S.emit("t", lambda e, b=b: e.matmul(
                PS[5][0:2, :], K["ones2_bf"][:, :], P["sqn"][:, b, :],
                start=True, stop=True),
                waits=[dsqn, cdep, P.get("war_ps5_l"), P["bank"].get(5)],
                inc=1)
            P["war_sqn_l"] = dnm
            dnv = S.emit("v", lambda e, nrm=nrm, tt=tt: e.reduce_max(
                nrm[0:2, tt:tt + 1], PS[5][0:2, :], mybir.AxisListType.X),
                waits=[dnm], inc=1)
            P["war_ps5_l"] = dnv


def _emit_phase_attn_b(meta, xn3):
    S, A, C, K = meta["S"], meta["A"], meta["C"], meta["K"]
    O = meta["Ob"]
    A.push()
    P = _mk_attn_bufs(meta, rope=False)
    P["xn"] = xn3
    P["rhs_dep"] = meta["ln3_done"]
    P["scope_war"] = meta["ln3_done"]
    P["wband"] = A.alloc([128, 2, 896], F32, "wband")
    P["war_band"] = {}
    S.emit("p", lambda e: e.memset(P["V"][:, :, :, 64:65], 1.0),
           waits=[P["scope_war"]], inc=1)
    S.wait_only("t", [S.dep("p")])
    for hp in range(CH):
        _emit_pair_qkv_b(meta, P, hp)
        P["qkv_dep"] = S.dep("v")
        P["norm_dep"] = S.dep("v")
        if hp == 0:
            S.dbg("b_qt0", [128, T], BF16, lambda: P["Qt"][:, :], [P["qkv_dep"]])
        P["o_dst"] = (lambda hp: lambda qb, h: O[64 * h:64 * h + 64, hp,
                                                 qb * TT:(qb + 1) * TT])(hp)
        _emit_attention(meta, P, hp, "b")
        P["war_qt"] = S.dep("t")
        P["war_v4"] = S.dep("t")
        P["war_norm_v"] = S.dep("v")
        P["war_norm_s"] = S.dep("s")
        for nmm in ("knr_dep", "knr_war"):
            P.pop(nmm, None)
    S.barrier()
    A.pop()
    meta["attn_b_done"] = S.dep("v")


def _build_full(dbg_names=None, phase_limit=99):
    nc, es, meta = build_program(dbg_names=dbg_names, phase_limit=phase_limit)
    S = meta["S"]
    if phase_limit < 6:
        C = meta["C"]
        srcn = {3: "x1T", 4: "x2T", 5: "x3T"}.get(phase_limit)
        if srcn:
            _q_dma(S, C["yT"].ap(), C[srcn].ap(), waits=(), slot="out")
        elif meta.get("xn") is not None:
            for c in range(CH):
                _p_dma(S, C["yT"].ap()[c * 128:(c + 1) * 128, :],
                       meta["xn"][:, c, :], waits=[meta.get("ln1_done")])
    S.barrier()
    with nc.Block() as block:
        S.replay(block)
    es.close()
    return nc, meta


def kernel(**inputs):
    consts = _host_consts()
    L = _host_layout(inputs)
    nc, meta = _build_full()
    in_maps = []
    for i in range(8):
        m = {"xT": L["xT"][i]}
        for k in ("wa_qkv", "wa_proj", "wb_q", "wb_k", "wb_v", "wb_o",
                  "wf_qkv", "wf_wo", "wf_w1a", "wf_w1g", "wf_w2",
                  "b1a", "b1g", "band_b"):
            m[k] = L[k]
        def vec_d(a):
            return np.ascontiguousarray(
                np.asarray(a, np.float32).reshape(CH, 128).T)
        def vec_ld(a):
            return np.ascontiguousarray(
                np.asarray(a, np.float32).reshape(2, CH, 128).transpose(2, 0, 1))
        m["bq"] = vec_d(inputs["L0_attn_b_bq"])
        m["bk"] = vec_d(inputs["L0_attn_b_bk"])
        m["bv"] = vec_d(inputs["L0_attn_b_bv"])
        m["bo"] = vec_d(inputs["L0_attn_b_bo"])
        m["pos_bias"] = np.asarray(inputs["L0_attn_b_pos_bias"], np.float32)
        m["b2"] = vec_ld(inputs["L0_ffn_b2"])
        m["ln_g"] = vec_ld(inputs["L0_ffn_ln_g"])
        m["ln_b"] = vec_ld(inputs["L0_ffn_ln_b"])
        m["qs"] = np.ascontiguousarray(
            np.asarray(inputs["L0_ffn_q_scale"], np.float32).T)
        m["ks"] = np.ascontiguousarray(
            np.asarray(inputs["L0_ffn_k_scale"], np.float32).T)
        m.update(consts)
        in_maps.append(m)
    res = run_bass_kernel_spmd(nc, in_maps, core_ids=list(range(8)))
    out = np.stack([np.ascontiguousarray(r["yT"].T) for r in res.results], axis=0)
    return out


# version 07
# speedup vs baseline: 1.0006x; 1.0006x over previous
import contextlib
import numpy as np
import ml_dtypes
import concourse.bass as bass
import concourse.mybir as mybir
from concourse.bass_utils import run_bass_kernel_spmd

F32 = mybir.dt.float32
BF16 = mybir.dt.bfloat16
AF = mybir.ActivationFunctionType
ALU = mybir.AluOpType
BF = ml_dtypes.bfloat16

T = 4096
D = 1024
CH = 8
TT = 512
NTT = T // TT
NQB = 8
KCN = 32
DH = 64
NEG = -1.0e30
HID = 2730
HIDP = 2816
MA = 22
OFFA = 384
WA = 896
OFFF = 384
WF = 1408
BW_B = 8064
WBB = 4480

ENGS = ("t", "v", "s", "p", "q")


def _host_consts():
    c = {}
    f32 = np.float32
    half = DH // 2
    theta = (1.0 / (10000.0 ** (np.arange(half, dtype=f32) / f32(half)))).astype(f32)
    pos = np.arange(T, dtype=f32)
    freqs = (pos[:, None] * theta[None, :]).astype(f32)
    cosr = np.repeat(np.cos(freqs, dtype=f32), 2, axis=1)
    sinr = np.repeat(np.sin(freqs, dtype=f32), 2, axis=1)
    cqa = np.tile(cosr.T, (2, 1)).astype(f32)
    sqa = np.tile(sinr.T, (2, 1)).astype(f32)
    c["tab_a_cq"] = (cqa * f32(0.125)).astype(f32)
    c["tab_a_sq"] = (sqa * f32(0.125)).astype(f32)
    c["tab_a_ck"] = cqa.copy()
    c["tab_a_sk"] = sqa.copy()
    WINDOW = 512
    half_idx = np.arange(0, DH, 2, dtype=f32)
    inv_freq = (1.0 / (10000.0 ** (half_idx / f32(DH)))).astype(f32)
    tpos = np.arange(T, dtype=f32)
    fr = (tpos[:, None] * inv_freq[None, :]).astype(f32)
    fr2 = np.concatenate([fr, fr], axis=1)
    base = ((half_idx + f32(0.4 * DH)) / f32(1.4 * DH)).astype(f32)
    power = ((tpos - f32(T // 2)) / f32(WINDOW // 2)).astype(f32)
    scale = (base[None, :] ** power[:, None]).astype(f32)
    scale2 = np.concatenate([scale, scale], axis=1)
    cosf = np.cos(fr2, dtype=f32)
    sinf = np.sin(fr2, dtype=f32)
    fcq = (cosf * scale2).astype(f32)
    fsq = (sinf * scale2).astype(f32)
    fck = (cosf / scale2).astype(f32)
    fsk = (sinf / scale2).astype(f32)
    c["tab_f_cq"] = np.tile((fcq * f32(8.0)).T, (2, 1)).astype(f32)
    c["tab_f_sq"] = np.tile((fsq * f32(8.0)).T, (2, 1)).astype(f32)
    c["tab_f_ck"] = np.tile(fck.T, (2, 1)).astype(f32)
    c["tab_f_sk"] = np.tile(fsk.T, (2, 1)).astype(f32)
    m = np.arange(WA)[None, :] - np.arange(128)[:, None] - OFFA
    c["band_a"] = np.where(m >= 0, f32(0), f32(NEG)).astype(f32)
    m = np.arange(WF)[None, :] - np.arange(128)[:, None] - OFFF
    c["band_f"] = np.where((m >= 0) & (m <= WINDOW), f32(0), f32(NEG)).astype(f32)
    c["ident_bf"] = np.eye(128, dtype=f32).astype(BF)
    c["ident_f32"] = np.eye(128, dtype=f32)
    c["perm_bf"] = _perm_matrix().astype(BF)
    c["ones_r_f32"] = np.ones((1, 128), dtype=f32)
    c["ones_c_bf"] = np.ones((128, 1), dtype=f32).astype(BF)
    on2 = np.zeros((128, 2), dtype=f32)
    on2[0:64, 0] = 1.0
    on2[64:128, 1] = 1.0
    c["ones2_bf"] = on2.astype(BF)
    b2 = np.zeros((2, 128), dtype=f32)
    b2[0, 0:64] = 1.0
    b2[1, 64:128] = 1.0
    c["b2_f32"] = b2
    return c


def _perm_matrix():
    P = np.zeros((128, 128), dtype=np.float32)
    for blk in (0, 64):
        for d in range(32):
            P[blk + d + 32, blk + d] = -1.0
        for d in range(32, 64):
            P[blk + d - 32, blk + d] = 1.0
    return P


def _host_layout(inputs):
    f32 = np.float32
    L = {}
    x = np.asarray(inputs["L0_attn_a_x"], dtype=f32)
    L["xT"] = [np.ascontiguousarray(x[i].T) for i in range(8)]
    def ocmajor(w, pad_to=None):
        w = np.asarray(w, dtype=f32)
        K, M = w.shape
        if pad_to is not None and M < pad_to:
            w = np.concatenate([w, np.zeros((K, pad_to - M), f32)], axis=1)
            M = pad_to
        n = M // 128
        return np.ascontiguousarray(w.reshape(K, n, 128).transpose(1, 0, 2))
    L["wa_qkv"] = ocmajor(inputs["L0_attn_a_w_qkv"])
    L["wa_proj"] = ocmajor(inputs["L0_attn_a_w_proj"])
    for nm, key in (("wb_q", "L0_attn_b_Wq"), ("wb_k", "L0_attn_b_Wk"),
                    ("wb_v", "L0_attn_b_Wv"), ("wb_o", "L0_attn_b_Wo")):
        L[nm] = ocmajor(np.ascontiguousarray(np.asarray(inputs[key], f32).T))
    wq = np.asarray(inputs["L0_ffn_wqkv"], f32)
    L["wf_qkv"] = np.stack([ocmajor(wq[l]) for l in range(2)])
    wo = np.asarray(inputs["L0_ffn_wo"], f32)
    L["wf_wo"] = np.stack([ocmajor(wo[l]) for l in range(2)])
    w1 = np.asarray(inputs["L0_ffn_w1"], f32)
    L["wf_w1a"] = np.stack([ocmajor(w1[l][:, :HID], pad_to=HIDP) for l in range(2)])
    L["wf_w1g"] = np.stack([ocmajor(w1[l][:, HID:], pad_to=HIDP) for l in range(2)])
    w2 = np.asarray(inputs["L0_ffn_w2"], f32)
    w2p = np.concatenate([w2, np.zeros((2, HIDP - HID, D), f32)], axis=1)
    L["wf_w2"] = np.stack([ocmajor(w2p[l]) for l in range(2)])
    b1 = np.asarray(inputs["L0_ffn_b1"], f32)
    def vec_lma(a):
        return np.ascontiguousarray(a.reshape(2, MA, 128).transpose(2, 0, 1))
    L["b1a"] = vec_lma(np.concatenate(
        [b1[:, :HID], np.zeros((2, HIDP - HID), f32)], axis=1))
    L["b1g"] = vec_lma(np.concatenate(
        [b1[:, HID:], np.zeros((2, HIDP - HID), f32)], axis=1))
    pb = np.asarray(inputs["L0_attn_b_pos_bias"], f32)
    mm = np.arange(BW_B)[None, :] - np.arange(128)[:, None] + 127
    idx = np.minimum(mm, 4094)
    L["band_b"] = np.ascontiguousarray(pb[:, idx])
    return L


class Dep:
    __slots__ = ("key", "c", "g", "k", "e")
    def __init__(self, key, c, g=0, k="cmp", e=None):
        self.key, self.c, self.g, self.k, self.e = key, c, g, k, e


class Sched:
    def __init__(self, nc, es):
        self.nc = nc
        self.es = es
        self.ops = {e: [] for e in ENGS}
        self.cnt = {e: 0 for e in ENGS}
        self.sems = {}
        self.slot_cnt = {}
        self.waited = {}
        self.semgen = 0
        self.dbg_names = None
        self.dbg_tensors = {}
        self.dbg_deps = []
        self._new_semset()

    def _new_semset(self):
        self.semgen += 1
        self.gen = self.semgen
        if self.gen == 1:
            for e in ENGS:
                self.sems[("e", e)] = self.es.enter_context(
                    self.nc.semaphore(f"sm{e}"))
                self.cnt[e] = 0

    def new_phase_sems(self):
        self.barrier()
        self._new_semset()

    def slot_sem(self, slot):
        key = ("s", slot)
        if key not in self.sems:
            self.sems[key] = self.es.enter_context(
                self.nc.semaphore("sl_" + slot))
            self.slot_cnt[key] = 0
        return key

    def dep(self, e, kind=None):
        k = kind or "cmp"
        return Dep(("e", e), self.cnt[e], self.gen, k, e)

    def dma(self, e, slot, fn, waits=()):
        key = self.slot_sem(slot)
        self.slot_cnt[key] += 16
        cnt = self.slot_cnt[key]
        sem = self.sems[key]
        wl = self._translate(e, waits)

        def op(eng):
            for s, c in wl:
                eng.wait_ge(s, c)
            fn(eng).then_inc(sem, 16)
        self.ops[e].append(op)
        return Dep(key, cnt, self.gen, "dma", e)

    def _translate(self, e, waits):
        wl = []
        for d in waits:
            if d is None or d.c <= 0:
                continue
            if d.g != self.gen:
                continue
            wkey = (e, d.key)
            if self.waited.get(wkey, 0) >= d.c:
                continue
            self.waited[wkey] = d.c
            wl.append((self.sems[d.key], d.c))
        return wl

    def emit(self, e, fn, waits=(), inc=0):
        wl = self._translate(e, waits)
        if inc:
            self.cnt[e] += inc
        sem = self.sems[("e", e)] if inc else None
        amt = inc

        def op(eng):
            for s, c in wl:
                eng.wait_ge(s, c)
            ins = fn(eng)
            if amt:
                if ins is None:
                    eng.nop().then_inc(sem, amt)
                else:
                    ins.then_inc(sem, amt)
        self.ops[e].append(op)
        return Dep(("e", e), self.cnt[e], self.gen, "cmp", e) if inc else None

    def wait_only(self, e, waits):
        self.emit(e, lambda eng: None, waits=waits, inc=0)

    def barrier(self):
        deps = [Dep(("e", e), self.cnt[e], self.gen, "bar", e)
                for e in ENGS]
        deps += [Dep(key, self.slot_cnt[key], self.gen, "bar", None)
                 for key in self.slot_cnt]
        for e in ENGS:
            self.wait_only(e, deps)

    def dbg(self, name, shape, dtype, src_ap_fn, waits):
        if self.dbg_names is None or name not in self.dbg_names:
            return
        t = self.nc.dram_tensor("dbg_" + name, list(shape), dtype,
                                kind="ExternalOutput")
        self.dbg_tensors[name] = t
        d = self.dma("q", "dbg_" + name,
                     lambda eng: eng.dma_start(out=t.ap(), in_=src_ap_fn()),
                     waits=waits)
        self.dbg_deps.append(d)
        return d

    def replay(self, block):
        for ename, attr in (("t", "tensor"), ("v", "vector"), ("s", "scalar"),
                            ("p", "gpsimd"), ("q", "sync")):
            def mk(e):
                def body(eng):
                    for f in self.ops[e]:
                        f(eng)
                return body
            getattr(block, attr)(mk(ename))


class Arena:
    def __init__(self, nc, es, nbytes):
        self.h = es.enter_context(nc.sbuf_tensor("arena", [128, nbytes],
                                                 mybir.dt.uint8))
        self.nbytes = nbytes
        self.off = 0
        self.stack = []

    def push(self):
        self.stack.append(self.off)

    def pop(self):
        self.off = self.stack.pop()

    def alloc(self, shape, dtype, name=""):
        bpe = {F32: 4, BF16: 2}[dtype]
        n = 1
        for s in shape[1:]:
            n *= s
        self.off = (self.off + 63) & ~63
        start = self.off
        self.off += n * bpe
        assert self.off <= self.nbytes, \
            f"arena overflow {name}: {self.off} > {self.nbytes}"
        base = self.h[:].bitcast(dtype)
        ap = base[:, start // bpe: start // bpe + n]
        if len(shape) > 2:
            pat = "p (" + " ".join(f"a{i}" for i in range(len(shape) - 1)) + ") -> p " + \
                  " ".join(f"a{i}" for i in range(len(shape) - 1))
            kw = {f"a{i}": shape[i + 1] for i in range(len(shape) - 1)}
            ap = ap.rearrange(pat, **kw)
        return ap


def build_program(dbg_names=None, phase_limit=99):
    nc = bass.Bass("TRN2", target_bir_lowering=False)
    es = contextlib.ExitStack()
    C = {}

    def din(name, shape, dtype=F32):
        C[name] = nc.dram_tensor(name, list(shape), dtype, kind="ExternalInput")
        return C[name]

    def dint(name, shape, dtype):
        C[name] = nc.dram_tensor(name, list(shape), dtype)
        return C[name]

    din("xT", [D, T])
    din("wa_qkv", [24, D, 128]); din("wa_proj", [8, D, 128])
    for nm in ("wb_q", "wb_k", "wb_v", "wb_o"):
        din(nm, [8, D, 128])
    din("bq", [128, CH]); din("bk", [128, CH]); din("bv", [128, CH])
    din("bo", [128, CH])
    din("pos_bias", [16, 4095])
    din("band_b", [16, 128, BW_B])
    din("wf_qkv", [2, 12, D, 128]); din("wf_wo", [2, 8, 512, 128])
    din("wf_w1a", [2, MA, D, 128]); din("wf_w1g", [2, MA, D, 128])
    din("wf_w2", [2, 8, HIDP, 128])
    din("b1a", [128, 2, MA]); din("b1g", [128, 2, MA]); din("b2", [128, 2, CH])
    din("ln_g", [128, 2, CH]); din("ln_b", [128, 2, CH])
    din("qs", [DH, 2]); din("ks", [DH, 2])
    for nm in ("tab_a_cq", "tab_a_sq", "tab_a_ck", "tab_a_sk",
               "tab_f_cq", "tab_f_sq", "tab_f_ck", "tab_f_sk"):
        din(nm, [128, T])
    din("band_a", [128, WA]); din("band_f", [128, WF])
    din("ident_bf", [128, 128], BF16); din("ident_f32", [128, 128], F32)
    din("perm_bf", [128, 128], BF16)
    din("ones_r_f32", [1, 128], F32); din("ones_c_bf", [128, 1], BF16)
    din("ones2_bf", [128, 2], BF16); din("b2_f32", [2, 128], F32)

    C["yT"] = nc.dram_tensor("yT", [D, T], F32, kind="ExternalOutput")
    dint("x1T", [D, T], F32)
    dint("x2T", [D, T], F32)
    dint("x3T", [D, T], F32)
    dint("uT", [D, T], F32)
    dint("u1T", [D, T], F32)
    dint("c_wa_qkv", [24, D, 128], BF16); dint("c_wa_proj", [8, D, 128], BF16)
    for nm in ("wb_q", "wb_k", "wb_v", "wb_o"):
        dint("c_" + nm, [8, D, 128], BF16)
    dint("c_wf_qkv", [2, 12, D, 128], BF16); dint("c_wf_wo", [2, 8, 512, 128], BF16)
    dint("c_wf_w1a", [2, MA, D, 128], BF16); dint("c_wf_w1g", [2, MA, D, 128], BF16)
    dint("c_wf_w2", [2, 8, HIDP, 128], BF16)

    S = Sched(nc, es)
    S.dbg_names = set(dbg_names) if dbg_names else None
    A = Arena(nc, es, 188 * 1024)

    K = {}
    def sball(name, shape, dtype):
        K[name] = es.enter_context(nc.sbuf_tensor(name, list(shape), dtype))
        return K[name]

    for nm, sh, dt in (
        ("ident_bf", [128, 128], BF16), ("ident_f32", [128, 128], F32),
        ("perm_bf", [128, 128], BF16), ("ones_r_f32", [1, 128], F32),
        ("ones_c_bf", [128, 1], BF16), ("ones2_bf", [128, 2], BF16),
        ("b2_f32", [2, 128], F32),
        ("band_a", [128, WA], F32), ("band_f", [128, WF], F32),
        ("bq", [128, CH], F32), ("bk", [128, CH], F32), ("bv", [128, CH], F32),
        ("bo", [128, CH], F32), ("b2b", [128, 2, CH], F32),
        ("lng", [128, 2, CH], F32), ("lnb", [128, 2, CH], F32),
        ("b1a_sb", [128, 2, MA], F32), ("b1g_sb", [128, 2, MA], F32),
        ("qs_sb", [128, 2], F32), ("ks_sb", [128, 2], F32),
        ("pbrow", [1, 16], F32),
        ("mu_row", [1, TT], F32), ("r_row", [1, TT], F32), ("mur_row", [1, TT], F32),
        ("eps1", [1, 1], F32),
    ):
        K[nm] = es.enter_context(nc.sbuf_tensor("sb_" + nm, list(sh), dt))

    PS = [es.enter_context(nc.psum_tensor(f"ps{i}", [128, 512], F32))
          for i in range(8)]

    meta = dict(nc=nc, S=S, A=A, K=K, PS=PS, C=C)

    _emit_wprep(meta)
    _emit_const_loads(meta)
    if phase_limit >= 1:
        A.push()
        xn = A.alloc([128, CH, T], BF16, "xn")
        meta["xn"] = xn
        _emit_ln_stats_apply(meta, src_dram=C["xT"], xn_out=xn, phase_tag="ln1")
    if phase_limit >= 2:
        meta["Oa"] = A.alloc([128, CH, T], BF16, "Oa")
        _emit_phase_attn_a(meta)
        S.dbg("a_o", [128, CH * T], BF16,
              lambda: meta["Oa"].rearrange("p a b -> p (a b)"),
              [meta["attn_a_done"]])
        S.new_phase_sems()
    if phase_limit >= 3:
        _emit_phase_proj_res(meta, meta["Oa"],
                             lambda oc: C["c_wa_proj"].ap()[oc], None,
                             C["xT"], C["x1T"], o_dep=None)
        S.dbg("x1", [D, T], F32, lambda: C["x1T"].ap(), [])
        S.new_phase_sems()
        A.pop()
    if phase_limit >= 4:
        _emit_ln_stats_apply(meta, C["x1T"], None, "ln2", out_dram=C["uT"])
        S.new_phase_sems()
        S.dbg("u_dram", [D, T], F32, lambda: C["uT"].ap(), [])
        _emit_ffn_sublayer(meta, 0, C["uT"], C["u1T"], C["u1T"], None,
                           C["uT"], "f1l0")
        _emit_ffn_sublayer(meta, 1, C["uT"], C["u1T"], C["u1T"], C["x1T"],
                           C["x2T"], "f1l1")
        S.new_phase_sems()
        S.dbg("x2", [D, T], F32, lambda: C["x2T"].ap(), [])
    if phase_limit >= 5:
        A.push()
        xn3 = A.alloc([128, CH, T], BF16, "xn3")
        _emit_ln_stats_apply(meta, C["x2T"], xn3, "ln3")
        meta["Ob"] = A.alloc([128, CH, T], BF16, "Ob")
        S.new_phase_sems()
        _emit_phase_attn_b(meta, xn3)
        S.new_phase_sems()
        _emit_phase_proj_res(meta, meta["Ob"],
                             lambda oc: C["c_wb_o"].ap()[oc], "bo",
                             C["x2T"], C["x3T"], o_dep=None)
        S.new_phase_sems()
        S.dbg("x3", [D, T], F32, lambda: C["x3T"].ap(), [])
        A.pop()
    if phase_limit >= 6:
        _emit_ln_stats_apply(meta, C["x3T"], None, "ln4", out_dram=C["uT"])
        S.new_phase_sems()
        _emit_ffn_sublayer(meta, 0, C["uT"], C["u1T"], C["u1T"], None,
                           C["uT"], "f2l0")
        _emit_ffn_sublayer(meta, 1, C["uT"], C["u1T"], C["u1T"], C["x3T"],
                           C["yT"], "f2l1")
        S.new_phase_sems()
    return nc, es, meta


def _q_dma(S, dst_ap, src_ap, waits=(), slot="misc"):
    return S.dma("q", slot, lambda e: e.dma_start(out=dst_ap, in_=src_ap),
                 waits=waits)


def _p_dma(S, dst_ap, src_ap, waits=(), slot="pmisc"):
    return S.dma("p", slot, lambda e: e.dma_start(out=dst_ap, in_=src_ap),
                 waits=waits)


def _emit_wprep(meta):
    S, C = meta["S"], meta["C"]
    deps = []
    pairs = [("wa_qkv", "c_wa_qkv"), ("wa_proj", "c_wa_proj"),
             ("wb_q", "c_wb_q"), ("wb_k", "c_wb_k"), ("wb_v", "c_wb_v"),
             ("wb_o", "c_wb_o"), ("wf_qkv", "c_wf_qkv"), ("wf_wo", "c_wf_wo"),
             ("wf_w1a", "c_wf_w1a"), ("wf_w1g", "c_wf_w1g"), ("wf_w2", "c_wf_w2")]
    for src, dst in pairs:
        sap = C[src].ap()
        dap = C[dst].ap()
        n = 1
        for s in sap.shape:
            n *= s
        sflat = sap.rearrange(" ".join(f"a{i}" for i in range(len(sap.shape)))
                              + " -> (" + " ".join(f"a{i}" for i in range(len(sap.shape))) + ")")
        dflat = dap.rearrange(" ".join(f"a{i}" for i in range(len(dap.shape)))
                              + " -> (" + " ".join(f"a{i}" for i in range(len(dap.shape))) + ")")
        CHUNK = 1 << 20
        for o in range(0, n, CHUNK):
            ln = min(CHUNK, n - o)
            deps.append(_p_dma(S, dflat[o:o + ln], sflat[o:o + ln],
                               slot="wprep"))
    meta["wprep_dep"] = deps[-1]


def _emit_const_loads(meta):
    S, C, K = meta["S"], meta["C"], meta["K"]
    nc = meta["nc"]
    deps = []
    for nm in ("ident_bf", "ident_f32", "perm_bf", "ones_r_f32", "ones_c_bf",
               "ones2_bf", "b2_f32", "band_a", "band_f"):
        deps.append(_q_dma(S, K[nm][:], C[nm].ap(), slot="const"))
    for nm, dn in (("bq", "bq"), ("bk", "bk"), ("bv", "bv"), ("bo", "bo")):
        deps.append(_q_dma(S, K[nm][:], C[dn].ap(), slot="const"))
    for nm, dn in (("b2b", "b2"), ("lng", "ln_g"), ("lnb", "ln_b")):
        deps.append(_q_dma(S, K[nm][:], C[dn].ap(), slot="const"))
    for nm, dn in (("b1a_sb", "b1a"), ("b1g_sb", "b1g")):
        deps.append(_q_dma(S, K[nm][:], C[dn].ap(), slot="const"))
    for nm in ("qs", "ks"):
        deps.append(_q_dma(S, K[nm + "_sb"][0:64, :], C[nm].ap(), slot="const"))
    meta["A"].push()
    pb_sb = meta["A"].alloc([128, 4095], F32, "pbtmp")
    pbm = meta["A"].alloc([128, 1], F32, "pbm")
    deps.append(_q_dma(S, pb_sb[0:16, :], C["pos_bias"].ap(), slot="const"))
    d = deps[-1]
    dcl = d
    for nm in ("qs_sb", "ks_sb"):
        S.emit("v", lambda e, t=K[nm]: e.tensor_copy(t[64:128, :], t[0:64, :]),
               waits=[d], inc=1)
    meta["eps_dep"] = S.emit("p", lambda e: e.memset(K["eps1"][:, :], 1e-5),
                             inc=1)
    dm = S.emit("v", lambda e: e.reduce_max(pbm[0:16, :], pb_sb[0:16, :],
                                            mybir.AxisListType.X),
                waits=[d], inc=1)
    PS = meta["PS"]
    dt = S.emit("t", lambda e: e.transpose(PS[5][0:1, 0:16], pbm[0:16, 0:1],
                                           K["ident_f32"][0:16, 0:16]),
                waits=[dm, dcl], inc=1)
    S.emit("v", lambda e: e.tensor_copy(K["pbrow"][0:1, 0:16], PS[5][0:1, 0:16]),
           waits=[dt], inc=1)
    meta["A"].pop()
    meta["const_dep"] = dcl
    meta["const_v_dep"] = S.dep("v")


def _emit_ln_stats_apply(meta, src_dram, xn_out, phase_tag, gains_l=None,
                         out_dram=None):
    S, A, K, PS, C = meta["S"], meta["A"], meta["K"], meta["PS"], meta["C"]
    A.push()
    xf = A.alloc([128, 2, CH, TT], F32, "xf")
    xbt = A.alloc([128, 2, CH, TT], BF16, "xbt")
    sq = A.alloc([128, 2, 2, TT], BF16, "sq")
    src = src_dram.ap()
    cdep = meta["const_dep"]
    war_v = {}
    war_t = {}
    prev_apply = {}
    for tt in range(NTT):
        b = tt % 2
        dl = None
        for c in range(CH):
            dl = _q_dma(S, xf[:, b, c, :],
                        src[c * 128:(c + 1) * 128, tt * TT:(tt + 1) * TT],
                        waits=([war_v.get(tt - 2)] if c == 0 else ()),
                        slot=f"lnxf{b}")
        dcast = None
        for c in range(CH):
            dcast = S.emit("v", lambda e, c=c, b=b: e.tensor_copy(
                xbt[:, b, c, :], xf[:, b, c, :]), waits=[dl], inc=1)
        war = prev_apply.get(tt - 1, meta.get("const_v_dep"))
        sqmm = {}
        for c in range(CH):
            dsq = S.emit("s", lambda e, c=c, b=b: e.activation(
                sq[:, b, c % 2, :], xbt[:, b, c, :], AF.Square),
                waits=[dcast, war, sqmm.get(c - 2)], inc=1)
            S.emit("t", lambda e, c=c, b=b: e.matmul(
                PS[6][0:1, :], K["ones_c_bf"][:, :], xbt[:, b, c, :],
                start=(c == 0), stop=(c == CH - 1)),
                waits=[dcast, cdep, war], inc=(1 if c == CH - 1 else 0))
            sqmm[c] = S.emit("t", lambda e, c=c, b=b: e.matmul(
                PS[7][0:1, :], K["ones_c_bf"][:, :], sq[:, b, c % 2, :],
                start=(c == 0), stop=(c == CH - 1)),
                waits=[dsq, war], inc=1)
        dstat = S.dep("t")
        d1 = S.emit("v", lambda e: e.tensor_scalar_mul(
            K["mu_row"][0:1, :], PS[6][0:1, :], 1.0 / D), waits=[dstat], inc=1)
        d2 = S.emit("v", lambda e: e.tensor_scalar_mul(
            K["r_row"][0:1, :], PS[7][0:1, :], 1.0 / D), waits=[dstat], inc=1)
        d3 = S.emit("v", lambda e: e.tensor_mul(
            K["mur_row"][0:1, :], K["mu_row"][0:1, :], K["mu_row"][0:1, :]),
            waits=[d2], inc=1)
        d4 = S.emit("v", lambda e: e.tensor_sub(
            K["r_row"][0:1, :], K["r_row"][0:1, :], K["mur_row"][0:1, :]),
            waits=[d3], inc=1)
        d5 = S.emit("v", lambda e: e.tensor_scalar_max(
            K["r_row"][0:1, :], K["r_row"][0:1, :], 0.0), waits=[d4], inc=1)
        d6 = S.emit("s", lambda e: e.activation(
            K["r_row"][0:1, :], K["r_row"][0:1, :], AF.Ln,
            bias=K["eps1"][0:1, :], scale=1.0),
            waits=[d5, meta.get("eps_dep")], inc=1)
        d7 = S.emit("s", lambda e: e.activation(
            K["r_row"][0:1, :], K["r_row"][0:1, :], AF.Exp, scale=-0.5),
            waits=[d6], inc=1)
        d8 = S.emit("v", lambda e: e.tensor_mul(
            K["mur_row"][0:1, :], K["mu_row"][0:1, :], K["r_row"][0:1, :]),
            waits=[d7], inc=1)
        if tt == 0:
            S.dbg(phase_tag + "_r", [1, TT], F32, lambda: K["r_row"][0:1, :], [d8])
            S.dbg(phase_tag + "_mur", [1, TT], F32,
                  lambda: K["mur_row"][0:1, :], [d8])
        dr = S.emit("t", lambda e: e.matmul(
            PS[6][:, :], K["ones_r_f32"][:, :], K["r_row"][0:1, :],
            start=True, stop=True), waits=[d8], inc=1)
        dm = S.emit("t", lambda e: e.matmul(
            PS[7][:, :], K["ones_r_f32"][:, :], K["mur_row"][0:1, :],
            start=True, stop=True), waits=[d8], inc=1)
        dv = None
        for c in range(CH):
            dv1 = S.emit("v", lambda e, c=c, b=b: e.tensor_mul(
                xf[:, b, c, :], xbt[:, b, c, :], PS[6][:, :]),
                waits=[dr], inc=1)
            if out_dram is not None:
                dv = S.emit("v", lambda e, c=c, b=b: e.tensor_sub(
                    xf[:, b, c, :], xf[:, b, c, :], PS[7][:, :]),
                    waits=[dm, dv1], inc=1)
                dv = _q_dma(S, out_dram.ap()[c * 128:(c + 1) * 128,
                                             tt * TT:(tt + 1) * TT],
                            xf[:, b, c, :], waits=[dv], slot=f"lnout{b}")
            elif gains_l is None:
                dv = S.emit("v", lambda e, c=c, b=b, tt=tt: e.tensor_sub(
                    xn_out[:, c, tt * TT:(tt + 1) * TT], xf[:, b, c, :],
                    PS[7][:, :]), waits=[dm, dv1], inc=1)
            else:
                dv2 = S.emit("v", lambda e, c=c, b=b: e.tensor_sub(
                    xf[:, b, c, :], xf[:, b, c, :], PS[7][:, :]),
                    waits=[dm, dv1], inc=1)
                dv = S.emit("v", lambda e, c=c, b=b, tt=tt, l=gains_l:
                    e.tensor_scalar(
                        xn_out[:, c, tt * TT:(tt + 1) * TT], xf[:, b, c, :],
                        K["lng"][:, l, c:c + 1], K["lnb"][:, l, c:c + 1],
                        ALU.mult, ALU.add), waits=[dv2], inc=1)
        war_v[tt] = dv
        prev_apply[tt] = dv
    if xn_out is not None:
        S.dbg(phase_tag + "_xn", [128, CH * T], BF16,
              lambda: xn_out.rearrange("p a b -> p (a b)"), [S.dep("v")])
    A.pop()
    meta[phase_tag + "_done"] = S.dep("v")


def _emit_pair_qkv_a(meta, P, hp):
    S, A, K, PS, C = meta["S"], meta["A"], meta["K"], meta["PS"], meta["C"]
    cdep = meta["const_dep"]
    wdep = None
    for j in range(3):
        srcw = P["wsrc"](j)
        sap = bass.AP(srcw.tensor, srcw.offset,
                      [[128, 128], [128 * 128, CH], [1, 128]])
        wdep = _q_dma(S, P["wq"][:, j, :, :], sap,
                      waits=[meta["wprep_dep"], P.get("scope_war")],
                      slot="wq")
    if P.get("dbg_pair"):
        dd = S.dbg("pp_wq", [128, 3 * CH * 128], BF16,
                   lambda: P["wq"].rearrange("p a b c -> p (a b c)"), [wdep])
        dd2 = S.dbg("pp_ub0", [128, CH, TT], BF16,
                    lambda: P["rhs_dbg"](), [P["rhs_dep"]])
    V4 = P["V"]
    for tt in range(NTT):
        ts = slice(tt * TT, (tt + 1) * TT)
        b = tt % 2

        qbank = 0 if tt % 2 == 0 else 6
        kbank = 1 if tt % 2 == 0 else 7
        for psn, j in ((qbank, 0), (kbank, 1), (2, 2)):
            for kc in range(CH):
                S.emit("t", lambda e, psn=psn, j=j, kc=kc, tt=tt: e.matmul(
                    PS[psn][:, :], P["wq"][:, j, kc, :],
                    P["rhs_fn"](kc, tt),
                    start=(kc == 0), stop=(kc == CH - 1)),
                    waits=[wdep, P["rhs_dep"], P["bank"].get(psn)],
                    inc=(1 if kc == CH - 1 else 0))
            if j == 0:
                dqp = S.dep("t")
            elif j == 1:
                dkp = S.dep("t")
            else:
                dvp = S.dep("t")
        dvf = S.emit("v", lambda e: e.tensor_copy(P["vt_f32"][:, 0, :],
                                                  PS[2][:, :]),
                     waits=[dvp, P["war_vt"].get(tt - 1)], inc=1)
        P["bank"][2] = dvf
        dtr = None
        for j in range(4):
            dtr = S.emit("t", lambda e, j=j, b=b: e.transpose(
                PS[3][:, j * 128:(j + 1) * 128],
                P["vt_f32"][:, 0, j * 128:(j + 1) * 128],
                K["ident_f32"][:, :]),
                waits=[dvf, cdep, P["war_ps3"].get(tt - 1),
                       P["bank"].get(3)], inc=1)
        P["war_vt"][tt] = dtr
        dve = None
        for h in range(2):
            vout = V4[:, h, tt * 4:(tt + 1) * 4, 0:64]
            vin = bass.AP(PS[3], 64 * h, [[512, 128], [128, 4], [1, 64]])
            dve = S.emit("v", lambda e, vout=vout, vin=vin: e.tensor_copy(
                vout, vin), waits=[dtr, P.get("war_v4")], inc=1)
        P["war_ps3"][tt] = dve
        P["bank"][3] = dve
        for name, psn, tcos, tsin, dst, nrm in (
                ("q", qbank, 0, None, P["Qt"], P["qn"]),
                ("k", kbank, 2, None, P["Kt"], P["kn"])):
            dep_p = dqp if name == "q" else dkp
            tdep = None
            for j in range(2):
                tdep = _q_dma(S, P["tab"][:, j, :],
                              C[P["tabs"][tcos + j]].ap()[:, ts],
                              waits=[P.get("war_tab_l"), P.get("scope_war")],
                              slot="tab")
            dcast = S.emit("v", lambda e, psn=psn, b=b: e.tensor_copy(
                P["qb"][:, b, :], PS[psn][:, :]),
                waits=[dep_p, P.get("war_qb_l")], inc=1)
            P["bank"][psn] = dcast
            if P.get("dbg_pair") and tt == 0:
                dd = S.dbg(f"pp_{name}_raw", [128, TT], BF16,
                           lambda: P["qb"][:, 0, :], [dcast])
                if dd: dcast = dd
            if P.get("l2norm"):
                dcast = _emit_l2norm(meta, P, dcast, b, name)
                if P.get("dbg_pair") and tt == 0:
                    dd = S.dbg(f"pp_{name}_l2", [128, TT], BF16,
                               lambda: P["qb"][:, 0, :], [dcast])
                    if dd: dcast = dd
            drot = S.emit("t", lambda e, b=b: e.matmul(
                PS[4][:, :], K["perm_bf"][:, :], P["qb"][:, b, :],
                start=True, stop=True),
                waits=[dcast, cdep, P.get("war_rot_l"), P["bank"].get(4)],
                inc=1)
            dt1 = S.emit("v", lambda e, b=b: e.tensor_mul(
                P["t1"][:, :], P["qb"][:, b, :], P["tab"][:, 0, :]),
                waits=[tdep, dcast], inc=1)
            dt2 = S.emit("v", lambda e, b=b: e.tensor_mul(
                P["t2"][:, :], PS[4][:, :], P["tab"][:, 1, :]),
                waits=[drot, dt1], inc=1)
            P["war_rot_l"] = dt2
            P["war_tab_l"] = dt2
            P["bank"][4] = dt2
            dqt = S.emit("v", lambda e, dst=dst, tt=tt: e.tensor_add(
                dst[:, tt * TT:(tt + 1) * TT], P["t1"][:, :], P["t2"][:, :]),
                waits=[dt2, P.get("war_qt")], inc=1)
            if P.get("dbg_pair") and tt == 0:
                dd = S.dbg(f"pp_{name}_t1", [128, TT], F32,
                           lambda: P["t1"][:, :], [dt2])
                dd = S.dbg(f"pp_{name}_t2", [128, TT], F32,
                           lambda: P["t2"][:, :], [dqt])
                if dd:
                    P["war_rot_l"] = dd
                    P["war_qb_l"] = dd
                    P["war_tab_l"] = dd
            P["war_qb_l"] = drot
            dsqn = S.emit("s", lambda e, dst=dst, tt=tt, b=b: e.activation(
                P["sqn"][:, b, :], dst[:, tt * TT:(tt + 1) * TT], AF.Square),
                waits=[dqt, P.get("war_sqn_l")], inc=1)
            dnm = S.emit("t", lambda e, b=b: e.matmul(
                PS[5][0:2, :], K["ones2_bf"][:, :], P["sqn"][:, b, :],
                start=True, stop=True),
                waits=[dsqn, cdep, P.get("war_ps5_l"), P["bank"].get(5)],
                inc=1)
            P["war_sqn_l"] = dnm
            if P.get("l2norm"):
                n32 = P["qn32"] if name == "q" else P["kn32"]
                for j4 in range(4):
                    dnv = S.emit("v", lambda e, n32=n32, tt=tt, j4=j4:
                        e.reduce_max(n32[0:2, 4 + 4 * tt + j4:5 + 4 * tt + j4],
                                     PS[5][0:2, j4 * 128:(j4 + 1) * 128],
                                     mybir.AxisListType.X),
                        waits=[dnm, P.get("war_norm_v"),
                               P.get("war_norm_s")], inc=1)
            else:
                dnv = S.emit("v", lambda e, nrm=nrm, tt=tt: e.reduce_max(
                    nrm[0:2, tt:tt + 1], PS[5][0:2, :], mybir.AxisListType.X),
                    waits=[dnm, P.get("war_norm_v"),
                           P.get("war_norm_s")], inc=1)
            P["war_ps5_l"] = dnv
            P["bank"][5] = dnv


def _emit_l2norm(meta, P, dcast, b, name):
    S, K, PS = meta["S"], meta["K"], meta["PS"]
    cdep = meta["const_dep"]
    svec = K["qs_sb"] if name == "q" else K["ks_sb"]
    l = P["ffn_l"]
    dsq = S.emit("s", lambda e, b=b: e.activation(
        P["sqn"][:, b, :], P["qb"][:, b, :], AF.Square),
        waits=[dcast, P.get("war_sqn_l")], inc=1)
    dnm = S.emit("t", lambda e, b=b: e.matmul(
        PS[5][0:2, :], K["ones2_bf"][:, :], P["sqn"][:, b, :],
        start=True, stop=True),
        waits=[dsq, cdep, P.get("war_ps5_l"), P["bank"].get(5)], inc=1)
    P["war_sqn_l"] = dnm
    dmx = S.emit("v", lambda e: e.tensor_scalar_max(
        P["ssc"][0:2, :], PS[5][0:2, :], 1e-24),
        waits=[dnm, P.get("war_ssc")], inc=1)
    P["war_ps5_l"] = dmx
    dln = S.emit("s", lambda e: e.activation(P["ssc"][0:2, :], P["ssc"][0:2, :],
                                             AF.Ln), waits=[dmx], inc=1)
    dex = S.emit("s", lambda e: e.activation(P["ssc"][0:2, :], P["ssc"][0:2, :],
                                             AF.Exp, scale=-0.5),
                 waits=[dln], inc=1)
    drb = S.emit("t", lambda e: e.matmul(
        PS[5][:, :], K["b2_f32"][0:2, :], P["ssc"][0:2, :],
        start=True, stop=True),
        waits=[dex, cdep, P.get("war_ps5_l"), P["bank"].get(5)], inc=1)
    P["war_ssc"] = drb
    dsc = S.emit("v", lambda e, b=b, svec=svec, l=l: e.scalar_tensor_tensor(
        P["qb"][:, b, :], P["qb"][:, b, :], svec[:, l:l + 1], PS[5][:, :],
        ALU.mult, ALU.mult), waits=[drb, meta["const_v_dep"]], inc=1)
    P["war_ps5_l"] = dsc
    P["bank"][5] = dsc
    return dsc


def _emit_attention(meta, P, hp, kind):
    S, A, K, PS, C = meta["S"], meta["A"], meta["K"], meta["PS"], meta["C"]
    cdep = meta["const_dep"]
    for qb in range(NQB):
        qs_ = slice(qb * TT, (qb + 1) * TT)
        dbc = {}

        def bcast_negc(dneg):
            for h in range(2):
                db = S.emit("t", lambda e, h=h: e.matmul(
                    PS[7][:, 500 + h:501 + h], K["ones_r_f32"][:, :],
                    P["negc_row"][0:1, h:h + 1], start=True, stop=True),
                    waits=[dneg, P.get("war_negsb"), P["bank"].get(7)],
                    inc=1)
                dbc[h] = S.emit("v", lambda e, h=h: e.tensor_copy(
                    P["negc_sb"][:, h:h + 1], PS[7][:, 500 + h:501 + h]),
                    waits=[db], inc=1)

        if kind != "f":
            if kind == "a":
                if qb == 0:
                    dk = S.emit("v", lambda e: e.tensor_copy(
                        P["knr"][0:2, :], P["kn"][0:2, 0:1]),
                        waits=[P["norm_dep"], P.get("war_norm_s")], inc=1)
                else:
                    dk = S.emit("v", lambda e, qb=qb: e.tensor_max(
                        P["knr"][0:2, :], P["knr"][0:2, :],
                        P["kn"][0:2, qb:qb + 1]),
                        waits=[P["norm_dep"], P.get("knr_war")], inc=1)
            else:
                if qb == 0:
                    dk = S.emit("v", lambda e: e.reduce_max(
                        P["knr"][0:2, :], P["kn"][0:2, :],
                        mybir.AxisListType.X),
                        waits=[P["norm_dep"], P.get("war_norm_s")], inc=1)
                else:
                    dk = P["knr_dep"]
            P["knr_dep"] = dk
            d1 = S.emit("s", lambda e, qb=qb: e.activation(
                P["c1"][0:2, 0:1], P["qn"][0:2, qb:qb + 1], AF.Ln),
                waits=[dk, P["norm_dep"]], inc=1)
            d2 = S.emit("s", lambda e: e.activation(
                P["c2"][0:2, 0:1], P["knr"][0:2, :], AF.Ln),
                waits=[dk, d1], inc=1)
            d3 = S.emit("v", lambda e: e.tensor_add(P["c1"][0:2, 0:1],
                                                    P["c1"][0:2, 0:1],
                                                    P["c2"][0:2, 0:1]),
                        waits=[d1, d2], inc=1)
            P["knr_war"] = d3
            d4 = S.emit("s", lambda e: e.activation(P["c1"][0:2, 0:1],
                                                    P["c1"][0:2, 0:1],
                                                    AF.Exp, scale=0.5),
                        waits=[d3], inc=1)
            dtr = S.emit("t", lambda e: e.transpose(PS[7][0:1, 504:506],
                                                    P["c1"][0:2, 0:1],
                                                    K["ident_f32"][0:2, 0:2]),
                         waits=[d4, cdep, P.get("war_negps"),
                                P["bank"].get(7)], inc=1)
            if kind == "b":
                dtr = S.emit("v", lambda e, hp=hp: e.tensor_add(
                    PS[7][0:1, 504:506], PS[7][0:1, 504:506],
                    K["pbrow"][0:1, 2 * hp:2 * hp + 2]),
                    waits=[dtr, meta["const_v_dep"]], inc=1)
            dneg = S.emit("v", lambda e: e.tensor_scalar_mul(
                P["negc_row"][0:1, 0:2], PS[7][0:1, 504:506], -1.0),
                waits=[dtr], inc=1)
            P["war_negps"] = dneg
            bcast_negc(dneg)
            P["war_negsb"] = dbc[1]
            P["bank"][7] = dbc[1]

        if kind == "a":
            kcs = [(kc, "vis" if kc < 4 * qb else "band",
                    512 * qb - 128 * kc + OFFA) for kc in range(4 * qb + 4)]
            nk = len(kcs)
            dS_l, dP_l = {}, {}
            SBK = ((0, 2, 6), (1, 3, 7))

            def emit_S(i):
                kc, kd, c0 = kcs[i]
                for h in range(2):
                    psn = SBK[h][i % 3]
                    dS_l[(i, h)] = S.emit(
                        "t", lambda e, psn=psn, h=h, kc=kc, qs_=qs_: e.matmul(
                            PS[psn][:, :], P["Kt"][64 * h:64 * (h + 1),
                                                   kc * 128:(kc + 1) * 128],
                            P["Qt"][64 * h:64 * (h + 1), qs_],
                            start=True, stop=True),
                        waits=[P["bank"].get(psn), P["qkv_dep"]], inc=1)

            def emit_PX(i):
                kc, kd, c0 = kcs[i]
                p = i % 2
                for h in range(2):
                    psn = SBK[h][i % 3]
                    dS = dS_l[(i, h)]
                    if kd == "vis":
                        dP = S.emit("s", lambda e, psn=psn, h=h, p=p:
                            e.activation(
                                P["pt"][:, 2 * h + p, :], PS[psn][:, :], AF.Exp,
                                bias=P["negc_sb"][:, h:h + 1], scale=1.0),
                            waits=[dS, dbc[h], P["war_pt"].get((h, p))], inc=1)
                    else:
                        band_ap = K["band_a"][:, c0:c0 + TT]
                        dsm = S.emit("v", lambda e, psn=psn, p=p,
                                     band_ap=band_ap:
                            e.tensor_add(P["sm"][:, p, :], PS[psn][:, :],
                                         band_ap),
                            waits=[dS, cdep, P["war_sm"].get(p)], inc=1)
                        dP = S.emit("s", lambda e, h=h, p=p: e.activation(
                            P["pt"][:, 2 * h + p, :], P["sm"][:, p, :],
                            AF.Exp, bias=P["negc_sb"][:, h:h + 1], scale=1.0),
                            waits=[dsm, dbc[h], P["war_pt"].get((h, p))], inc=1)
                        P["war_sm"][p] = dP
                    P["bank"][psn] = dP
                    dP_l[(i, h)] = dP

            def emit_AV(i):
                kc, kd, c0 = kcs[i]
                p = i % 2
                for h in range(2):
                    dav = S.emit("t", lambda e, h=h, p=p, kc=kc, i=i, nk=nk:
                        e.matmul(
                            PS[4 + h][0:65, :], P["V"][:, h, kc, :],
                            P["pt"][:, 2 * h + p, :],
                            start=(i == 0), stop=(i == nk - 1)),
                        waits=[dP_l[(i, h)], P["bank"].get(4 + h)
                               if i == 0 else None], inc=1)
                    P["war_pt"][(h, p)] = dav

            emit_S(0)
            if nk > 1:
                emit_S(1)
            for i in range(nk):
                emit_PX(i)
                if i + 2 < nk:
                    emit_S(i + 2)
                emit_AV(i)
        elif kind == "f":
            kcs = [(kc, 512 * qb - 128 * kc + OFFF)
                   for kc in range(max(0, 4 * (qb - 1)), 4 * (qb + 1))]
            nk = len(kcs)
            dmx = None
            for i, (kc, c0) in enumerate(kcs):
                for h in range(2):
                    psn = 2 * h + (i % 2)
                    dS = S.emit("t", lambda e, psn=psn, h=h, kc=kc, qs_=qs_:
                        e.matmul(
                            PS[psn][:, :], P["Kt"][64 * h:64 * (h + 1),
                                                   kc * 128:(kc + 1) * 128],
                            P["Qt"][64 * h:64 * (h + 1), qs_],
                            start=True, stop=True),
                        waits=[P["bank"].get(psn), P["qkv_dep"]], inc=1)
                    dsm = S.emit("v", lambda e, psn=psn, h=h, i=i, c0=c0:
                        e.tensor_add(P["smB"][:, h, i, :], PS[psn][:, :],
                                     K["band_f"][:, c0:c0 + TT]),
                        waits=[dS, cdep, P["war_smB"].get((h, i))], inc=1)
                    P["bank"][psn] = dsm
                    dmx = S.emit("v", lambda e, h=h, i=i: e.reduce_max(
                        P["rmx"][:, h, i:i + 1], P["smB"][:, h, i, :],
                        mybir.AxisListType.X), waits=[dsm], inc=1)
            for h in range(2):
                dm2 = S.emit("v", lambda e, h=h, nk=nk: e.reduce_max(
                    P["rm1"][:, h:h + 1], P["rmx"][:, h, 0:nk],
                    mybir.AxisListType.X), waits=[dmx], inc=1)
                dtr = S.emit("t", lambda e, h=h: e.transpose(
                    PS[7][0:1, 360:488],
                    P["rm1"][:, h:h + 1], K["ident_f32"][:, :]),
                    waits=[dm2, cdep, P.get("war_negps"),
                           P["bank"].get(7)], inc=1)
                dcx = S.emit("v", lambda e, h=h: e.reduce_max(
                    P["negc_row"][0:1, h:h + 1], PS[7][0:1, 360:488],
                    mybir.AxisListType.X), waits=[dtr], inc=1)
                P["war_negps"] = dcx
                P["bank"][7] = dcx
            dneg = S.emit("v", lambda e: e.tensor_scalar_mul(
                P["negc_row"][0:1, 0:2], P["negc_row"][0:1, 0:2], -1.0),
                waits=[P["war_negps"]], inc=1)
            P["war_negps"] = dneg
            bcast_negc(dneg)
            P["war_negsb"] = dbc[1]
            for i, (kc, c0) in enumerate(kcs):
                for h in range(2):
                    p = i % 2
                    dP = S.emit("s", lambda e, h=h, i=i, p=p: e.activation(
                        P["pt"][:, 2 * h + p, :], P["smB"][:, h, i, :],
                        AF.Exp, bias=P["negc_sb"][:, h:h + 1], scale=1.0),
                        waits=[dbc[h], P["war_pt"].get((h, p))], inc=1)
                    P["war_smB"][(h, i)] = dP
                    dav = S.emit("t", lambda e, h=h, p=p, kc=kc, i=i, nk=nk:
                        e.matmul(
                            PS[4 + h][0:65, :], P["V"][:, h, kc, :],
                            P["pt"][:, 2 * h + p, :],
                            start=(i == 0), stop=(i == nk - 1)),
                        waits=[dP, P["war_o"].get(h)], inc=1)
                    P["war_pt"][(h, p)] = dav
        else:
            dS_l, dP_l, dband_l = {}, {}, {}

            def b_band(h, ei):
                base = 3584 - 512 * ei
                srcb = C["band_b"].ap()[2 * hp + h]
                bap = bass.AP(srcb.tensor, srcb.offset + qb * TT + base,
                              [[BW_B, 128], [1, 896]])
                dband_l[(h, ei)] = _q_dma(
                    S, P["wband"][:, h, :], bap,
                    waits=[P["war_band"].get(h)], slot=f"band{h}")

            SBK = ((0, 2, 6), (1, 3, 7))

            def b_S(kc):
                for h in range(2):
                    psn = SBK[h][kc % 3]
                    dS_l[(kc, h)] = S.emit(
                        "t", lambda e, psn=psn, h=h, kc=kc, qs_=qs_: e.matmul(
                            PS[psn][:, :], P["Kt"][64 * h:64 * (h + 1),
                                                   kc * 128:(kc + 1) * 128],
                            P["Qt"][64 * h:64 * (h + 1), qs_],
                            start=True, stop=True),
                        waits=[P["bank"].get(psn), P["qkv_dep"]], inc=1)

            def b_PX(kc):
                p = kc % 2
                ei = kc // 4
                base = 3584 - 512 * ei
                c0 = 3968 - 128 * kc - base
                for h in range(2):
                    psn = SBK[h][kc % 3]
                    dsm = S.emit("v", lambda e, psn=psn, h=h, p=p, c0=c0:
                        e.tensor_add(P["sm"][:, 2 * h + p, :], PS[psn][:, :],
                                     P["wband"][:, h, c0:c0 + TT]),
                        waits=[dS_l[(kc, h)], dband_l[(h, ei)],
                               P["war_sm"].get((h, p))], inc=1)
                    P["war_band"][h] = dsm
                    dP = S.emit("s", lambda e, h=h, p=p: e.activation(
                        P["pt"][:, 2 * h + p, :], P["sm"][:, 2 * h + p, :],
                        AF.Exp, bias=P["negc_sb"][:, h:h + 1], scale=1.0),
                        waits=[dsm, dbc[h], P["war_pt"].get((h, p))], inc=1)
                    P["war_sm"][(h, p)] = dP
                    P["bank"][psn] = dP
                    dP_l[(kc, h)] = dP

            def b_AV(kc):
                p = kc % 2
                for h in range(2):
                    dav = S.emit("t", lambda e, h=h, p=p, kc=kc: e.matmul(
                        PS[4 + h][0:65, :], P["V"][:, h, kc, :],
                        P["pt"][:, 2 * h + p, :],
                        start=(kc == 0), stop=(kc == KCN - 1)),
                        waits=[dP_l[(kc, h)], P["bank"].get(4 + h)
                               if kc == 0 else None], inc=1)
                    P["war_pt"][(h, p)] = dav

            b_band(0, 0)
            b_band(1, 0)
            b_S(0)
            b_S(1)
            for kc in range(KCN):
                if kc > 0 and kc % 4 == 0:
                    b_band(0, kc // 4)
                    b_band(1, kc // 4)
                b_PX(kc)
                if kc + 2 < KCN:
                    b_S(kc + 2)
                b_AV(kc)
        davs = S.dep("t")
        for h in range(2):
            drr = S.emit("v", lambda e, h=h: e.reciprocal(
                P["rrow"][0:1, :], PS[4 + h][64:65, :]), waits=[davs], inc=1)
            dg = S.emit("t", lambda e: e.matmul(
                PS[6][0:64, :], K["ones_r_f32"][0:1, 0:64], P["rrow"][0:1, :],
                start=True, stop=True),
                waits=[drr, P.get("war_gam"), P["bank"].get(6)], inc=1)
            dgc = S.emit("v", lambda e: e.tensor_copy(P["gam"][0:64, :],
                                                      PS[6][0:64, :]),
                         waits=[dg], inc=1)
            P["war_gam"] = dgc
            P["bank"][6] = dgc
            dst_ap = P["o_dst"](qb, h)
            dout = S.emit("v", lambda e, h=h, dst_ap=dst_ap: e.tensor_mul(
                dst_ap, PS[4 + h][0:64, :], P["gam"][0:64, :]),
                waits=[dgc], inc=1)
            P["bank"][4 + h] = dout
        P["war_negsb"] = S.dep("v")
        P["o_done"] = S.dep("v")


def _mk_attn_bufs(meta, nkc=KCN, rope=True, nkin=CH, fine=False):
    A = meta["A"]
    P = {}
    if fine:
        P["smB"] = A.alloc([128, 2, 8, TT], BF16, "smB")
        P["rmx"] = A.alloc([128, 2, 8], F32, "rmx")
        P["rm1"] = A.alloc([128, 2], F32, "rm1")
        P["war_smB"] = {}
    P["wq"] = A.alloc([128, 3, nkin, 128], BF16, "wq")
    if rope:
        P["tab"] = A.alloc([128, 2, TT], F32, "tab")
        P["t1"] = A.alloc([128, TT], F32, "t1")
        P["t2"] = A.alloc([128, TT], F32, "t2")
        P["ssc"] = A.alloc([128, TT], F32, "ssc")
    P["vt_f32"] = A.alloc([128, 1, TT], F32, "vtf")
    if rope:
        P["qb"] = A.alloc([128, 2, TT], BF16, "qbb")
    P["sqn"] = A.alloc([128, 2, TT], BF16, "sqn")
    P["sm"] = A.alloc([128, 2 if rope else 4, TT], F32, "sm")
    P["Qt"] = A.alloc([128, T], BF16, "Qt")
    P["Kt"] = A.alloc([128, T], BF16, "Kt")
    P["V"] = A.alloc([128, 2, nkc, 65], BF16, "Vsb")
    P["qn"] = A.alloc([128, NTT], F32, "qn")
    P["kn"] = A.alloc([128, NTT], F32, "kn")
    P["qn32"] = A.alloc([128, 40], F32, "qn32")
    P["kn32"] = A.alloc([128, 40], F32, "kn32")
    P["acc4"] = A.alloc([128, 4], F32, "acc4")
    P["t4"] = A.alloc([128, 4], F32, "t4")
    P["knr"] = A.alloc([128, 1], F32, "knr")
    P["c1"] = A.alloc([128, 1], F32, "c1")
    P["c2"] = A.alloc([128, 1], F32, "c2")
    P["negc_row"] = A.alloc([128, 2], F32, "negcr")
    P["negc_sb"] = A.alloc([128, 2], F32, "negcs")
    P["rrow"] = A.alloc([128, TT], F32, "rrow")
    P["gam"] = A.alloc([128, TT], F32, "gam")
    P["pt"] = A.alloc([128, 4, TT], BF16, "pt")
    for nmm in ("war_tab", "bank", "war_vt", "war_ps3", "war_sm",
                "war_pt", "war_o"):
        P[nmm] = {}
    return P


def _emit_phase_attn_a(meta):
    S, A, C = meta["S"], meta["A"], meta["C"]
    O = meta["Oa"]
    A.push()
    P = _mk_attn_bufs(meta)
    P["tabs"] = ("tab_a_cq", "tab_a_sq", "tab_a_ck", "tab_a_sk")
    xn = meta["xn"]
    P["rhs_fn"] = lambda kc, tt: xn[:, kc, tt * TT:(tt + 1) * TT]
    P["rhs_dbg"] = lambda: xn[:, :, 0:TT]
    P["rhs_dep"] = meta["ln1_done"]
    P["scope_war"] = meta["ln1_done"]
    S.emit("p", lambda e: e.memset(P["V"][:, :, :, 64:65], 1.0),
           waits=[P["scope_war"]], inc=1)
    donev = S.dep("p")
    S.wait_only("t", [donev])
    for hp in range(CH):
        P["wsrc"] = (lambda hp: lambda j: C["c_wa_qkv"].ap()[
            (hp, 8 + hp, 16 + hp)[j]])(hp)
        _emit_pair_qkv_a(meta, P, hp)
        P["qkv_dep"] = S.dep("v")
        P["norm_dep"] = S.dep("v")
        if hp == 0:
            S.dbg("a_qt0", [128, T], BF16, lambda: P["Qt"][:, :], [P["qkv_dep"]])
            S.dbg("a_kt0", [128, T], BF16, lambda: P["Kt"][:, :], [P["qkv_dep"]])
            S.dbg("a_v0", [128, 2 * KCN * 65], BF16,
                  lambda: P["V"].rearrange("p a b c -> p (a b c)"),
                  [P["qkv_dep"]])
            S.dbg("a_qn0", [128, NTT], F32, lambda: P["qn"][:, :],
                  [P["norm_dep"]])
        P["o_dst"] = (lambda hp: lambda qb, h: O[64 * h:64 * h + 64, hp,
                                                 qb * TT:(qb + 1) * TT])(hp)
        _emit_attention(meta, P, hp, "a")
        P["war_qt"] = S.dep("t")
        P["war_v4"] = S.dep("t")
        P["war_norm_v"] = S.dep("v")
        P["war_norm_s"] = S.dep("s")
        P.pop("knr_dep", None)
        P.pop("knr_war", None)
    S.barrier()
    A.pop()
    meta["attn_a_done"] = S.dep("v")


def _emit_phase_proj_res(meta, O, w_oc_fn, bias_name, src_dram, dst_dram,
                         kcs=CH, o_dep=None):
    S, A, K, PS, C = meta["S"], meta["A"], meta["K"], meta["PS"], meta["C"]
    A.push()
    wp = A.alloc([128, CH, kcs, 128], BF16, "wp")
    xold = A.alloc([128, 2, TT], F32, "xold")
    xnew = A.alloc([128, 2, TT], F32, "xnew")
    wdep = None
    for oc in range(CH):
        srcw = w_oc_fn(oc)
        sap = bass.AP(srcw.tensor, srcw.offset,
                      [[128, 128], [128 * 128, kcs], [1, 128]])
        wdep = _q_dma(S, wp[:, oc, :, :], sap, waits=[meta["wprep_dep"]],
                      slot="wp")
    war = {}
    for tt in range(NTT):
        ts = slice(tt * TT, (tt + 1) * TT)
        for oc in range(CH):
            idx = tt * CH + oc
            b = idx % 4
            xb = idx % 2
            dl = _q_dma(S, xold[:, xb, :],
                        src_dram.ap()[oc * 128:(oc + 1) * 128, ts],
                        waits=[war.get((xb, "x"))], slot=f"xold{xb}")
            for kc in range(kcs):
                S.emit("t", lambda e, oc=oc, kc=kc, tt=tt, b=b: e.matmul(
                    PS[b][:, :], wp[:, oc, kc, :], O[:, kc, tt * TT:(tt + 1) * TT],
                    start=(kc == 0), stop=(kc == kcs - 1)),
                    waits=[wdep, o_dep, war.get((b, "ps"))],
                    inc=(1 if kc == kcs - 1 else 0))
            dmm = S.dep("t")
            if bias_name:
                dv0 = S.emit("v", lambda e, b=b, oc=oc: e.tensor_scalar_add(
                    PS[b][:, :], PS[b][:, :], K[bias_name][:, oc:oc + 1]),
                    waits=[dmm, meta["const_dep"]], inc=1)
            else:
                dv0 = dmm
            dv = S.emit("v", lambda e, b=b, xb=xb: e.tensor_add(
                xnew[:, xb, :], PS[b][:, :], xold[:, xb, :]),
                waits=[dv0, dl, war.get((xb, "xn"))], inc=1)
            war[(b, "ps")] = dv
            dst = _q_dma(S, dst_dram.ap()[oc * 128:(oc + 1) * 128, ts],
                         xnew[:, xb, :], waits=[dv], slot=f"pro{xb}")
            war[(xb, "x")] = dst
            war[(xb, "xn")] = dst
    A.pop()
    return S.dep("q")


def _emit_ub_halo(meta, u_dram, ub):
    S, A = meta["S"], meta["A"]
    A.push()
    uf = A.alloc([128, 2, TT], F32, "uf")
    dz = S.emit("p", lambda e: e.memset(ub[:, :, 0:1], 0.0), inc=1)
    war = {}
    for c in range(CH):
        for tt in range(NTT):
            b = (c * NTT + tt) % 2
            dl = _q_dma(S, uf[:, b, :],
                        u_dram.ap()[c * 128:(c + 1) * 128,
                                    tt * TT:(tt + 1) * TT],
                        waits=[war.get(b), meta.get("u_ready"), dz],
                        slot=f"ub{b}")
            war[b] = S.emit("v", lambda e, c=c, tt=tt, b=b: e.tensor_copy(
                ub[:, c, 1 + tt * TT:1 + (tt + 1) * TT], uf[:, b, :]),
                waits=[dl], inc=1)
    A.pop()
    return S.dep("v")


def _emit_ffn_sublayer(meta, l, u_in, u_mid, res_in, extra_res, dst, tag):
    S, A, K, PS, C = meta["S"], meta["A"], meta["K"], meta["PS"], meta["C"]
    cdep = meta["const_dep"]
    A.push()
    ub = A.alloc([128, CH, T + 1], BF16, "ub")
    Of = A.alloc([128, 4, T], BF16, "Of")
    meta["u_ready"] = None
    dub = _emit_ub_halo(meta, u_in, ub)
    P = _mk_attn_bufs(meta, rope=True, fine=True)
    P["tabs"] = ("tab_f_cq", "tab_f_sq", "tab_f_ck", "tab_f_sk")
    P["l2norm"] = True
    P["ffn_l"] = l
    P["rhs_fn"] = (lambda ub: lambda kc, tt:
                   ub[:, kc, 1 + tt * TT:1 + (tt + 1) * TT] if kc < 4 else
                   ub[:, kc, tt * TT:(tt + 1) * TT])(ub)
    P["rhs_dbg"] = (lambda ub: lambda: ub[:, :, 0:TT])(ub)
    P["rhs_dep"] = dub
    P["scope_war"] = dub
    S.emit("p", lambda e: e.memset(P["V"][:, :, :, 64:65], 1.0),
           waits=[dub], inc=1)
    S.wait_only("t", [S.dep("p")])
    for hp in range(4):
        P["wsrc"] = (lambda hp: lambda j: C["c_wf_qkv"].ap()[
            l, (hp, 4 + hp, 8 + hp)[j]])(hp)
        P["dbg_pair"] = (tag == "f1l0" and hp == 0)
        _emit_pair_qkv_a(meta, P, hp)
        P["qkv_dep"] = S.dep("v")
        P["norm_dep"] = S.dep("v")
        if tag == "f1l0" and hp == 0:
            S.dbg("f_qt0", [128, T], BF16, lambda: P["Qt"][:, :], [P["qkv_dep"]])
            S.dbg("f_kt0", [128, T], BF16, lambda: P["Kt"][:, :], [P["qkv_dep"]])
        P["o_dst"] = (lambda hp: lambda qb, h: Of[64 * h:64 * h + 64, hp,
                                                  qb * TT:(qb + 1) * TT])(hp)
        _emit_attention(meta, P, hp, "f")
        P["war_qt"] = S.dep("t")
        P["war_v4"] = S.dep("t")
        P["war_norm_v"] = S.dep("v")
        P["war_norm_s"] = S.dep("s")
        for nmm in ("knr_dep", "knr_war"):
            P.pop(nmm, None)
    if tag == "f1l0":
        S.dbg("f_o0", [128, 4 * T], BF16,
              lambda: Of.rearrange("p a b -> p (a b)"), [S.dep("v")])
    _emit_phase_proj_res(meta, Of, lambda oc: C["c_wf_wo"].ap()[l, oc], None,
                         u_in, u_mid, kcs=4)
    S.new_phase_sems()
    A.pop()
    if tag == "f1l0":
        S.dbg("f_u1", [D, T], F32, lambda: u_mid.ap(), [])
    A.push()
    hb = A.alloc([128, CH, T], BF16, "hb")
    _emit_ln_shift_apply(meta, u_mid, hb, l)
    if tag == "f1l0":
        S.dbg("f_hb", [128, CH * T], BF16,
              lambda: hb.rearrange("p a b -> p (a b)"), [S.dep("v")])
    _emit_w1_glu_w2(meta, l, hb, res_in, extra_res, dst, tag)
    S.new_phase_sems()
    A.pop()


def _emit_ln_shift_apply(meta, u_dram, hb, l):
    S, A, K, PS, C = meta["S"], meta["A"], meta["K"], meta["PS"], meta["C"]
    A.push()
    uf = A.alloc([128, 2, CH, TT + 1], F32, "uf2")
    xbt = A.alloc([128, 2, CH, TT + 1], BF16, "xbt2")
    sq = A.alloc([128, 2, 2, TT], BF16, "sq2")
    cdep = meta["const_dep"]
    war_v = {}
    prev_apply = {}
    dz = S.emit("p", lambda e: e.memset(uf[:, :, :, 0:1], 0.0), inc=1)

    def chv(buf, b, c):
        if c < 4:
            return buf[:, b, c, 1:TT + 1]
        return buf[:, b, c, 0:TT]

    for tt in range(NTT):
        b = tt % 2
        dl = None
        for c in range(CH):
            if tt == 0:
                dl = _q_dma(S, uf[:, b, c, 1:TT + 1],
                            u_dram.ap()[c * 128:(c + 1) * 128, 0:TT],
                            waits=([war_v.get(tt - 2), dz] if c == 0 else ()),
                            slot=f"lsf{b}")
            else:
                dl = _q_dma(S, uf[:, b, c, 0:TT + 1],
                            u_dram.ap()[c * 128:(c + 1) * 128,
                                        tt * TT - 1:(tt + 1) * TT],
                            waits=([war_v.get(tt - 2), dz] if c == 0 else ()),
                            slot=f"lsf{b}")
        dcast = None
        for c in range(CH):
            dcast = S.emit("v", lambda e, c=c, b=b: e.tensor_copy(
                xbt[:, b, c, :], uf[:, b, c, :]), waits=[dl], inc=1)
        war = prev_apply.get(tt - 1, meta.get("const_v_dep"))
        sqmm = {}
        for c in range(CH):
            dsq = S.emit("s", lambda e, c=c, b=b: e.activation(
                sq[:, b, c % 2, :], chv(xbt, b, c), AF.Square),
                waits=[dcast, war, sqmm.get(c - 2)], inc=1)
            S.emit("t", lambda e, c=c, b=b: e.matmul(
                PS[6][0:1, :], K["ones_c_bf"][:, :], chv(xbt, b, c),
                start=(c == 0), stop=(c == CH - 1)),
                waits=[dcast, cdep, war], inc=(1 if c == CH - 1 else 0))
            sqmm[c] = S.emit("t", lambda e, c=c, b=b: e.matmul(
                PS[7][0:1, :], K["ones_c_bf"][:, :], sq[:, b, c % 2, :],
                start=(c == 0), stop=(c == CH - 1)),
                waits=[dsq, war], inc=1)
        dstat = S.dep("t")
        d1 = S.emit("v", lambda e: e.tensor_scalar_mul(
            K["mu_row"][0:1, :], PS[6][0:1, :], 1.0 / D), waits=[dstat], inc=1)
        d2 = S.emit("v", lambda e: e.tensor_scalar_mul(
            K["r_row"][0:1, :], PS[7][0:1, :], 1.0 / D), waits=[dstat], inc=1)
        d3 = S.emit("v", lambda e: e.tensor_mul(
            K["mur_row"][0:1, :], K["mu_row"][0:1, :], K["mu_row"][0:1, :]),
            waits=[d2], inc=1)
        d4 = S.emit("v", lambda e: e.tensor_sub(
            K["r_row"][0:1, :], K["r_row"][0:1, :], K["mur_row"][0:1, :]),
            waits=[d3], inc=1)
        d5 = S.emit("v", lambda e: e.tensor_scalar_max(
            K["r_row"][0:1, :], K["r_row"][0:1, :], 0.0), waits=[d4], inc=1)
        d6 = S.emit("s", lambda e: e.activation(
            K["r_row"][0:1, :], K["r_row"][0:1, :], AF.Ln,
            bias=K["eps1"][0:1, :], scale=1.0),
            waits=[d5, meta.get("eps_dep")], inc=1)
        d7 = S.emit("s", lambda e: e.activation(
            K["r_row"][0:1, :], K["r_row"][0:1, :], AF.Exp, scale=-0.5),
            waits=[d6], inc=1)
        d8 = S.emit("v", lambda e: e.tensor_mul(
            K["mur_row"][0:1, :], K["mu_row"][0:1, :], K["r_row"][0:1, :]),
            waits=[d7], inc=1)
        dr = S.emit("t", lambda e: e.matmul(
            PS[6][:, :], K["ones_r_f32"][:, :], K["r_row"][0:1, :],
            start=True, stop=True), waits=[d8], inc=1)
        dm = S.emit("t", lambda e: e.matmul(
            PS[7][:, :], K["ones_r_f32"][:, :], K["mur_row"][0:1, :],
            start=True, stop=True), waits=[d8], inc=1)
        dv = None
        for c in range(CH):
            dv1 = S.emit("v", lambda e, c=c, b=b: e.tensor_mul(
                uf[:, b, c, 1:TT + 1], chv(xbt, b, c), PS[6][:, :]),
                waits=[dr], inc=1)
            dv2 = S.emit("v", lambda e, c=c, b=b: e.tensor_sub(
                uf[:, b, c, 1:TT + 1], uf[:, b, c, 1:TT + 1], PS[7][:, :]),
                waits=[dm, dv1], inc=1)
            dv = S.emit("v", lambda e, c=c, b=b, tt=tt, l=l: e.tensor_scalar(
                hb[:, c, tt * TT:(tt + 1) * TT], uf[:, b, c, 1:TT + 1],
                K["lng"][:, l, c:c + 1], K["lnb"][:, l, c:c + 1],
                ALU.mult, ALU.add), waits=[dv2, meta["const_dep"]], inc=1)
        war_v[tt] = dv
        prev_apply[tt] = dv
    A.pop()
    meta["hb_done"] = S.dep("v")


def _emit_w1_glu_w2(meta, l, hb, res_in, extra_res, dst, tag):
    S, A, K, PS, C = meta["S"], meta["A"], meta["K"], meta["PS"], meta["C"]
    cdep = meta["const_dep"]
    A.push()
    TH = 1024
    prod = A.alloc([128, MA, TH], BF16, "prod")
    a_sb = A.alloc([128, 2, TT], F32, "asb")
    gel = A.alloc([128, 2, TT], F32, "gel")
    w1t = A.alloc([128, 2, 2, CH, 128], BF16, "w1t")
    w2t = A.alloc([128, 2, MA, 128], BF16, "w2t")
    ures = A.alloc([128, 2, TT], F32, "ures")
    xres = A.alloc([128, 2, TT], F32, "xres")
    yout = A.alloc([128, 2, TT], F32, "yout")
    hb_dep = meta["hb_done"]
    for th in range(T // TH):
        hs = th * TH
        war_w1 = {}
        war_prod = {}
        for m in range(MA):
            bw = m % 2
            dwa = _q_dma(S, w1t[:, bw, 0, :, :],
                         bass.AP(C["c_wf_w1a"].ap()[l, m].tensor,
                                 C["c_wf_w1a"].ap()[l, m].offset,
                                 [[128, 128], [128 * 128, CH], [1, 128]]),
                         waits=[meta["wprep_dep"], war_w1.get(m - 2), hb_dep],
                         slot=f"w1a{bw}")
            dwg = _q_dma(S, w1t[:, bw, 1, :, :],
                         bass.AP(C["c_wf_w1g"].ap()[l, m].tensor,
                                 C["c_wf_w1g"].ap()[l, m].offset,
                                 [[128, 128], [128 * 128, CH], [1, 128]]),
                         waits=[war_w1.get(m - 2), hb_dep], slot=f"w1g{bw}")
            for sub in range(TH // TT):
                ss = slice(hs + sub * TT, hs + (sub + 1) * TT)
                ab = 0 if sub % 2 == 0 else 6
                gb = 1 if sub % 2 == 0 else 7
                sb2 = sub % 2
                for kc in range(CH):
                    S.emit("t", lambda e, kc=kc, bw=bw, ss=ss, ab=ab: e.matmul(
                        PS[ab][:, :], w1t[:, bw, 0, kc, :], hb[:, kc, ss],
                        start=(kc == 0), stop=(kc == CH - 1)),
                        waits=[dwa, hb_dep, war_prod.get(ab)],
                        inc=(1 if kc == CH - 1 else 0))
                da = S.dep("t")
                for kc in range(CH):
                    S.emit("t", lambda e, kc=kc, bw=bw, ss=ss, gb=gb: e.matmul(
                        PS[gb][:, :], w1t[:, bw, 1, kc, :], hb[:, kc, ss],
                        start=(kc == 0), stop=(kc == CH - 1)),
                        waits=[dwg, hb_dep, war_prod.get(gb)],
                        inc=(1 if kc == CH - 1 else 0))
                dg = S.dep("t")
                dae = S.emit("v", lambda e, m=m, ab=ab, sb2=sb2: e.tensor_scalar_add(
                    a_sb[:, sb2, :], PS[ab][:, :], K["b1a_sb"][:, l, m:m + 1]),
                    waits=[da, cdep, war_prod.get(("a", sb2))], inc=1)
                war_prod[ab] = dae
                dge = S.emit("s", lambda e, m=m, gb=gb, sb2=sb2: e.activation(
                    gel[:, sb2, :], PS[gb][:, :], AF.Gelu,
                    bias=K["b1g_sb"][:, l, m:m + 1], scale=1.0),
                    waits=[dg, cdep, war_prod.get(("g", sb2))], inc=1)
                war_prod[gb] = dge
                dp = S.emit("v", lambda e, m=m, sub=sub, sb2=sb2: e.tensor_mul(
                    prod[:, m, sub * TT:(sub + 1) * TT], a_sb[:, sb2, :],
                    gel[:, sb2, :]),
                    waits=[dae, dge], inc=1)
                war_prod[("a", sb2)] = dp
                war_prod[("g", sb2)] = dp
            war_w1[m] = dp
        dprod = S.dep("v")
        war_ps = {}
        war_res = {}
        for oc in range(CH):
            bw = oc % 2
            dw2 = _q_dma(S, w2t[:, bw, :, :],
                         bass.AP(C["c_wf_w2"].ap()[l, oc].tensor,
                                 C["c_wf_w2"].ap()[l, oc].offset,
                                 [[128, 128], [128 * 128, MA], [1, 128]]),
                         waits=[meta["wprep_dep"], war_ps.get(("w", bw)),
                                hb_dep],
                         slot=f"w2t{bw}")
            for sub in range(TH // TT):
                ss = slice(hs + sub * TT, hs + (sub + 1) * TT)
                bb = sub % 2
                rs = slice(sub * TT, (sub + 1) * TT)
                for m in range(MA):
                    S.emit("t", lambda e, m=m, bw=bw, rs=rs, bb=bb: e.matmul(
                        PS[2 + bb][:, :], w2t[:, bw, m, :],
                        prod[:, m, rs],
                        start=(m == 0), stop=(m == MA - 1)),
                        waits=[dw2, dprod, war_ps.get(bb)],
                        inc=(1 if m == MA - 1 else 0))
                dy = S.dep("t")
                dlu = _q_dma(S, ures[:, bb, :],
                             res_in.ap()[oc * 128:(oc + 1) * 128, ss],
                             waits=[war_res.get((bb, "u")), hb_dep],
                             slot=f"ures{bb}")
                dv0 = S.emit("v", lambda e, bb=bb, oc=oc: e.tensor_scalar_add(
                    PS[2 + bb][:, :], PS[2 + bb][:, :],
                    K["b2b"][:, l, oc:oc + 1]), waits=[dy, cdep], inc=1)
                dv1 = S.emit("v", lambda e, bb=bb: e.tensor_add(
                    yout[:, bb, :], PS[2 + bb][:, :], ures[:, bb, :]),
                    waits=[dv0, dlu], inc=1)
                if extra_res is not None:
                    dlx = _q_dma(S, xres[:, bb, :],
                                 extra_res.ap()[oc * 128:(oc + 1) * 128, ss],
                                 waits=[war_res.get((bb, "x")), hb_dep],
                                 slot=f"xres{bb}")
                    dv1 = S.emit("v", lambda e, bb=bb: e.tensor_add(
                        yout[:, bb, :], yout[:, bb, :], xres[:, bb, :]),
                        waits=[dv1, dlx], inc=1)
                war_ps[bb] = dv1
                dso = _q_dma(S, dst.ap()[oc * 128:(oc + 1) * 128, ss],
                             yout[:, bb, :], waits=[dv1], slot=f"w2o{bb}")
                war_res[(bb, "u")] = dso
                war_res[(bb, "x")] = dso
            war_ps[("w", bw)] = S.dep("t")
        S.barrier()
    A.pop()


def _emit_pair_qkv_b(meta, P, hp):
    S, A, K, PS, C = meta["S"], meta["A"], meta["K"], meta["PS"], meta["C"]
    cdep = meta["const_dep"]
    xn = P["xn"]
    wdep = None
    for j, wn in enumerate(("c_wb_q", "c_wb_k", "c_wb_v")):
        srcw = C[wn].ap()[hp]
        sap = bass.AP(srcw.tensor, srcw.offset,
                      [[128, 128], [128 * 128, CH], [1, 128]])
        wdep = _q_dma(S, P["wq"][:, j, :, :], sap,
                      waits=[meta["wprep_dep"], P.get("scope_war")],
                      slot="wq")
    for tt in range(NTT):
        ts = slice(tt * TT, (tt + 1) * TT)
        b = tt % 2
        qbank = 0 if tt % 2 == 0 else 6
        kbank = 1 if tt % 2 == 0 else 7
        for psn, j in ((qbank, 0), (kbank, 1), (2, 2)):
            for kc in range(CH):
                S.emit("t", lambda e, psn=psn, j=j, kc=kc, tt=tt: e.matmul(
                    PS[psn][:, :], P["wq"][:, j, kc, :],
                    xn[:, kc, tt * TT:(tt + 1) * TT],
                    start=(kc == 0), stop=(kc == CH - 1)),
                    waits=[wdep, P["rhs_dep"], P["bank"].get(psn)],
                    inc=(1 if kc == CH - 1 else 0))
            if j == 0:
                dqp = S.dep("t")
            elif j == 1:
                dkp = S.dep("t")
            else:
                dvp = S.dep("t")
        dq = S.emit("v", lambda e, tt=tt, hp=hp, qbank=qbank: e.tensor_scalar(
            P["Qt"][:, tt * TT:(tt + 1) * TT], PS[qbank][:, :],
            K["bq"][:, hp:hp + 1], 0.125, ALU.add, ALU.mult),
            waits=[dqp, cdep, P.get("war_qt")], inc=1)
        P["bank"][qbank] = dq
        dk = S.emit("v", lambda e, tt=tt, hp=hp, kbank=kbank: e.tensor_scalar_add(
            P["Kt"][:, tt * TT:(tt + 1) * TT], PS[kbank][:, :],
            K["bk"][:, hp:hp + 1]), waits=[dkp, cdep, P.get("war_qt")], inc=1)
        P["bank"][kbank] = dk
        dvf = S.emit("v", lambda e, hp=hp: e.tensor_scalar_add(
            P["vt_f32"][:, 0, :], PS[2][:, :], K["bv"][:, hp:hp + 1]),
            waits=[dvp, cdep, P["war_vt"].get(tt - 1)], inc=1)
        P["bank"][2] = dvf
        dtr = None
        for j in range(4):
            dtr = S.emit("t", lambda e, j=j: e.transpose(
                PS[3][:, j * 128:(j + 1) * 128],
                P["vt_f32"][:, 0, j * 128:(j + 1) * 128],
                K["ident_f32"][:, :]),
                waits=[dvf, cdep, P["war_ps3"].get(tt - 1),
                       P["bank"].get(3)], inc=1)
        P["war_vt"][tt] = dtr
        dve = None
        V4b = P["V"]
        for h in range(2):
            vout = V4b[:, h, tt * 4:(tt + 1) * 4, 0:64]
            vin = bass.AP(PS[3], 64 * h, [[512, 128], [128, 4], [1, 64]])
            dve = S.emit("v", lambda e, vout=vout, vin=vin: e.tensor_copy(
                vout, vin), waits=[dtr, P.get("war_v4")], inc=1)
        P["war_ps3"][tt] = dve
        P["bank"][3] = dve
        for dst_t, nrm, dd in ((P["Qt"], P["qn"], dq), (P["Kt"], P["kn"], dk)):
            dsqn = S.emit("s", lambda e, dst_t=dst_t, tt=tt, b=b: e.activation(
                P["sqn"][:, b, :], dst_t[:, tt * TT:(tt + 1) * TT], AF.Square),
                waits=[dd, P.get("war_sqn_l")], inc=1)
            dnm = S.emit("t", lambda e, b=b: e.matmul(
                PS[5][0:2, :], K["ones2_bf"][:, :], P["sqn"][:, b, :],
                start=True, stop=True),
                waits=[dsqn, cdep, P.get("war_ps5_l"), P["bank"].get(5)],
                inc=1)
            P["war_sqn_l"] = dnm
            dnv = S.emit("v", lambda e, nrm=nrm, tt=tt: e.reduce_max(
                nrm[0:2, tt:tt + 1], PS[5][0:2, :], mybir.AxisListType.X),
                waits=[dnm], inc=1)
            P["war_ps5_l"] = dnv


def _emit_phase_attn_b(meta, xn3):
    S, A, C, K = meta["S"], meta["A"], meta["C"], meta["K"]
    O = meta["Ob"]
    A.push()
    P = _mk_attn_bufs(meta, rope=False)
    P["xn"] = xn3
    P["rhs_dep"] = meta["ln3_done"]
    P["scope_war"] = meta["ln3_done"]
    P["wband"] = A.alloc([128, 2, 896], F32, "wband")
    P["war_band"] = {}
    S.emit("p", lambda e: e.memset(P["V"][:, :, :, 64:65], 1.0),
           waits=[P["scope_war"]], inc=1)
    S.wait_only("t", [S.dep("p")])
    for hp in range(CH):
        _emit_pair_qkv_b(meta, P, hp)
        P["qkv_dep"] = S.dep("v")
        P["norm_dep"] = S.dep("v")
        if hp == 0:
            S.dbg("b_qt0", [128, T], BF16, lambda: P["Qt"][:, :], [P["qkv_dep"]])
        P["o_dst"] = (lambda hp: lambda qb, h: O[64 * h:64 * h + 64, hp,
                                                 qb * TT:(qb + 1) * TT])(hp)
        _emit_attention(meta, P, hp, "b")
        P["war_qt"] = S.dep("t")
        P["war_v4"] = S.dep("t")
        P["war_norm_v"] = S.dep("v")
        P["war_norm_s"] = S.dep("s")
        for nmm in ("knr_dep", "knr_war"):
            P.pop(nmm, None)
    S.barrier()
    A.pop()
    meta["attn_b_done"] = S.dep("v")


def _build_full(dbg_names=None, phase_limit=99):
    nc, es, meta = build_program(dbg_names=dbg_names, phase_limit=phase_limit)
    S = meta["S"]
    if phase_limit < 6:
        C = meta["C"]
        srcn = {3: "x1T", 4: "x2T", 5: "x3T"}.get(phase_limit)
        if srcn:
            _q_dma(S, C["yT"].ap(), C[srcn].ap(), waits=(), slot="out")
        elif meta.get("xn") is not None:
            for c in range(CH):
                _p_dma(S, C["yT"].ap()[c * 128:(c + 1) * 128, :],
                       meta["xn"][:, c, :], waits=[meta.get("ln1_done")])
    S.barrier()
    with nc.Block() as block:
        S.replay(block)
    es.close()
    return nc, meta


def kernel(**inputs):
    consts = _host_consts()
    L = _host_layout(inputs)
    nc, meta = _build_full()
    in_maps = []
    for i in range(8):
        m = {"xT": L["xT"][i]}
        for k in ("wa_qkv", "wa_proj", "wb_q", "wb_k", "wb_v", "wb_o",
                  "wf_qkv", "wf_wo", "wf_w1a", "wf_w1g", "wf_w2",
                  "b1a", "b1g", "band_b"):
            m[k] = L[k]
        def vec_d(a):
            return np.ascontiguousarray(
                np.asarray(a, np.float32).reshape(CH, 128).T)
        def vec_ld(a):
            return np.ascontiguousarray(
                np.asarray(a, np.float32).reshape(2, CH, 128).transpose(2, 0, 1))
        m["bq"] = vec_d(inputs["L0_attn_b_bq"])
        m["bk"] = vec_d(inputs["L0_attn_b_bk"])
        m["bv"] = vec_d(inputs["L0_attn_b_bv"])
        m["bo"] = vec_d(inputs["L0_attn_b_bo"])
        m["pos_bias"] = np.asarray(inputs["L0_attn_b_pos_bias"], np.float32)
        m["b2"] = vec_ld(inputs["L0_ffn_b2"])
        m["ln_g"] = vec_ld(inputs["L0_ffn_ln_g"])
        m["ln_b"] = vec_ld(inputs["L0_ffn_ln_b"])
        m["qs"] = np.ascontiguousarray(
            np.asarray(inputs["L0_ffn_q_scale"], np.float32).T)
        m["ks"] = np.ascontiguousarray(
            np.asarray(inputs["L0_ffn_k_scale"], np.float32).T)
        m.update(consts)
        in_maps.append(m)
    res = run_bass_kernel_spmd(nc, in_maps, core_ids=list(range(8)))
    out = np.stack([np.ascontiguousarray(r["yT"].T) for r in res.results], axis=0)
    return out


# version 09
# speedup vs baseline: 1.0467x; 1.0462x over previous
import contextlib
import numpy as np
import ml_dtypes
import concourse.bass as bass
import concourse.mybir as mybir
from concourse.bass_utils import run_bass_kernel_spmd

F32 = mybir.dt.float32
BF16 = mybir.dt.bfloat16
AF = mybir.ActivationFunctionType
ALU = mybir.AluOpType
BF = ml_dtypes.bfloat16

T = 4096
D = 1024
CH = 8
TT = 512
NTT = T // TT
NQB = 8
KCN = 32
DH = 64
NEG = -1.0e30
HID = 2730
HIDP = 2816
MA = 22
OFFA = 384
WA = 896
OFFF = 384
WF = 1408
BW_B = 8064
WBB = 4480

ENGS = ("t", "v", "s", "p", "q")


def _host_consts():
    c = {}
    f32 = np.float32
    half = DH // 2
    theta = (1.0 / (10000.0 ** (np.arange(half, dtype=f32) / f32(half)))).astype(f32)
    pos = np.arange(T, dtype=f32)
    freqs = (pos[:, None] * theta[None, :]).astype(f32)
    cosr = np.repeat(np.cos(freqs, dtype=f32), 2, axis=1)
    sinr = np.repeat(np.sin(freqs, dtype=f32), 2, axis=1)
    cqa = np.tile(cosr.T, (2, 1)).astype(f32)
    sqa = np.tile(sinr.T, (2, 1)).astype(f32)
    c["tab_a_cq"] = (cqa * f32(0.125)).astype(f32)
    c["tab_a_sq"] = (sqa * f32(0.125)).astype(f32)
    c["tab_a_ck"] = cqa.copy()
    c["tab_a_sk"] = sqa.copy()
    WINDOW = 512
    half_idx = np.arange(0, DH, 2, dtype=f32)
    inv_freq = (1.0 / (10000.0 ** (half_idx / f32(DH)))).astype(f32)
    tpos = np.arange(T, dtype=f32)
    fr = (tpos[:, None] * inv_freq[None, :]).astype(f32)
    fr2 = np.concatenate([fr, fr], axis=1)
    base = ((half_idx + f32(0.4 * DH)) / f32(1.4 * DH)).astype(f32)
    power = ((tpos - f32(T // 2)) / f32(WINDOW // 2)).astype(f32)
    scale = (base[None, :] ** power[:, None]).astype(f32)
    scale2 = np.concatenate([scale, scale], axis=1)
    cosf = np.cos(fr2, dtype=f32)
    sinf = np.sin(fr2, dtype=f32)
    fcq = (cosf * scale2).astype(f32)
    fsq = (sinf * scale2).astype(f32)
    fck = (cosf / scale2).astype(f32)
    fsk = (sinf / scale2).astype(f32)
    c["tab_f_cq"] = np.tile((fcq * f32(8.0)).T, (2, 1)).astype(f32)
    c["tab_f_sq"] = np.tile((fsq * f32(8.0)).T, (2, 1)).astype(f32)
    c["tab_f_ck"] = np.tile(fck.T, (2, 1)).astype(f32)
    c["tab_f_sk"] = np.tile(fsk.T, (2, 1)).astype(f32)
    m = np.arange(WA)[None, :] - np.arange(128)[:, None] - OFFA
    c["band_a"] = np.where(m >= 0, f32(0), f32(NEG)).astype(f32)
    m = np.arange(WF)[None, :] - np.arange(128)[:, None] - OFFF
    c["band_f"] = np.where((m >= 0) & (m <= WINDOW), f32(0), f32(NEG)).astype(f32)
    c["ident_bf"] = np.eye(128, dtype=f32).astype(BF)
    c["ident_f32"] = np.eye(128, dtype=f32)
    c["perm_bf"] = _perm_matrix().astype(BF)
    c["ones_r_f32"] = np.ones((1, 128), dtype=f32)
    c["ones_c_bf"] = np.ones((128, 1), dtype=f32).astype(BF)
    on2 = np.zeros((128, 2), dtype=f32)
    on2[0:64, 0] = 1.0
    on2[64:128, 1] = 1.0
    c["ones2_bf"] = on2.astype(BF)
    b2 = np.zeros((2, 128), dtype=f32)
    b2[0, 0:64] = 1.0
    b2[1, 64:128] = 1.0
    c["b2_f32"] = b2
    return c


def _perm_matrix():
    P = np.zeros((128, 128), dtype=np.float32)
    for blk in (0, 64):
        for d in range(32):
            P[blk + d + 32, blk + d] = -1.0
        for d in range(32, 64):
            P[blk + d - 32, blk + d] = 1.0
    return P


def _host_layout(inputs):
    f32 = np.float32
    L = {}
    x = np.asarray(inputs["L0_attn_a_x"], dtype=f32)
    L["xT"] = [np.ascontiguousarray(x[i].T) for i in range(8)]
    def ocmajor(w, pad_to=None):
        w = np.asarray(w, dtype=f32)
        K, M = w.shape
        if pad_to is not None and M < pad_to:
            w = np.concatenate([w, np.zeros((K, pad_to - M), f32)], axis=1)
            M = pad_to
        n = M // 128
        return np.ascontiguousarray(w.reshape(K, n, 128).transpose(1, 0, 2))
    L["wa_qkv"] = ocmajor(inputs["L0_attn_a_w_qkv"])
    L["wa_proj"] = ocmajor(inputs["L0_attn_a_w_proj"])
    for nm, key in (("wb_q", "L0_attn_b_Wq"), ("wb_k", "L0_attn_b_Wk"),
                    ("wb_v", "L0_attn_b_Wv"), ("wb_o", "L0_attn_b_Wo")):
        L[nm] = ocmajor(np.ascontiguousarray(np.asarray(inputs[key], f32).T))
    wq = np.asarray(inputs["L0_ffn_wqkv"], f32)
    L["wf_qkv"] = np.stack([ocmajor(wq[l]) for l in range(2)])
    wo = np.asarray(inputs["L0_ffn_wo"], f32)
    L["wf_wo"] = np.stack([ocmajor(wo[l]) for l in range(2)])
    w1 = np.asarray(inputs["L0_ffn_w1"], f32)
    L["wf_w1a"] = np.stack([ocmajor(w1[l][:, :HID], pad_to=HIDP) for l in range(2)])
    L["wf_w1g"] = np.stack([ocmajor(w1[l][:, HID:], pad_to=HIDP) for l in range(2)])
    w2 = np.asarray(inputs["L0_ffn_w2"], f32)
    w2p = np.concatenate([w2, np.zeros((2, HIDP - HID, D), f32)], axis=1)
    L["wf_w2"] = np.stack([ocmajor(w2p[l]) for l in range(2)])
    b1 = np.asarray(inputs["L0_ffn_b1"], f32)
    def vec_lma(a):
        return np.ascontiguousarray(a.reshape(2, MA, 128).transpose(2, 0, 1))
    L["b1a"] = vec_lma(np.concatenate(
        [b1[:, :HID], np.zeros((2, HIDP - HID), f32)], axis=1))
    L["b1g"] = vec_lma(np.concatenate(
        [b1[:, HID:], np.zeros((2, HIDP - HID), f32)], axis=1))
    pb = np.asarray(inputs["L0_attn_b_pos_bias"], f32)
    mm = np.arange(BW_B)[None, :] - np.arange(128)[:, None] + 127
    idx = np.minimum(mm, 4094)
    L["band_b"] = np.ascontiguousarray(pb[:, idx])
    return L


class Dep:
    __slots__ = ("key", "c", "g", "k", "e")
    def __init__(self, key, c, g=0, k="cmp", e=None):
        self.key, self.c, self.g, self.k, self.e = key, c, g, k, e


class Sched:
    def __init__(self, nc, es):
        self.nc = nc
        self.es = es
        self.ops = {e: [] for e in ENGS}
        self.cnt = {e: 0 for e in ENGS}
        self.sems = {}
        self.slot_cnt = {}
        self.waited = {}
        self.semgen = 0
        self.dbg_names = None
        self.dbg_tensors = {}
        self.dbg_deps = []
        self._new_semset()

    def _new_semset(self):
        self.semgen += 1
        self.gen = self.semgen
        if self.gen == 1:
            for e in ENGS:
                self.sems[("e", e)] = self.es.enter_context(
                    self.nc.semaphore(f"sm{e}"))
                self.cnt[e] = 0

    def new_phase_sems(self):
        self.barrier()
        self._new_semset()

    def slot_sem(self, slot):
        key = ("s", slot)
        if key not in self.sems:
            self.sems[key] = self.es.enter_context(
                self.nc.semaphore("sl_" + slot))
            self.slot_cnt[key] = 0
        return key

    def dep(self, e, kind=None):
        k = kind or "cmp"
        return Dep(("e", e), self.cnt[e], self.gen, k, e)

    def dma(self, e, slot, fn, waits=()):
        key = self.slot_sem(slot)
        self.slot_cnt[key] += 16
        cnt = self.slot_cnt[key]
        sem = self.sems[key]
        wl = self._translate(e, waits)

        def op(eng):
            for s, c in wl:
                eng.wait_ge(s, c)
            fn(eng).then_inc(sem, 16)
        self.ops[e].append(op)
        return Dep(key, cnt, self.gen, "dma", e)

    def _translate(self, e, waits):
        wl = []
        for d in waits:
            if d is None or d.c <= 0:
                continue
            if d.g != self.gen:
                continue
            wkey = (e, d.key)
            if self.waited.get(wkey, 0) >= d.c:
                continue
            self.waited[wkey] = d.c
            wl.append((self.sems[d.key], d.c))
        return wl

    def emit(self, e, fn, waits=(), inc=0):
        wl = self._translate(e, waits)
        if inc:
            self.cnt[e] += inc
        sem = self.sems[("e", e)] if inc else None
        amt = inc

        def op(eng):
            for s, c in wl:
                eng.wait_ge(s, c)
            ins = fn(eng)
            if amt:
                if ins is None:
                    eng.nop().then_inc(sem, amt)
                else:
                    ins.then_inc(sem, amt)
        self.ops[e].append(op)
        return Dep(("e", e), self.cnt[e], self.gen, "cmp", e) if inc else None

    def wait_only(self, e, waits):
        self.emit(e, lambda eng: None, waits=waits, inc=0)

    def barrier(self):
        deps = [Dep(("e", e), self.cnt[e], self.gen, "bar", e)
                for e in ENGS]
        deps += [Dep(key, self.slot_cnt[key], self.gen, "bar", None)
                 for key in self.slot_cnt]
        for e in ENGS:
            self.wait_only(e, deps)

    def dbg(self, name, shape, dtype, src_ap_fn, waits):
        if self.dbg_names is None or name not in self.dbg_names:
            return
        t = self.nc.dram_tensor("dbg_" + name, list(shape), dtype,
                                kind="ExternalOutput")
        self.dbg_tensors[name] = t
        d = self.dma("q", "dbg_" + name,
                     lambda eng: eng.dma_start(out=t.ap(), in_=src_ap_fn()),
                     waits=waits)
        self.dbg_deps.append(d)
        return d

    def replay(self, block):
        for ename, attr in (("t", "tensor"), ("v", "vector"), ("s", "scalar"),
                            ("p", "gpsimd"), ("q", "sync")):
            def mk(e):
                def body(eng):
                    for f in self.ops[e]:
                        f(eng)
                return body
            getattr(block, attr)(mk(ename))


class Arena:
    def __init__(self, nc, es, nbytes):
        self.h = es.enter_context(nc.sbuf_tensor("arena", [128, nbytes],
                                                 mybir.dt.uint8))
        self.nbytes = nbytes
        self.off = 0
        self.stack = []

    def push(self):
        self.stack.append(self.off)

    def pop(self):
        self.off = self.stack.pop()

    def alloc(self, shape, dtype, name=""):
        bpe = {F32: 4, BF16: 2}[dtype]
        n = 1
        for s in shape[1:]:
            n *= s
        self.off = (self.off + 63) & ~63
        start = self.off
        self.off += n * bpe
        assert self.off <= self.nbytes, \
            f"arena overflow {name}: {self.off} > {self.nbytes}"
        base = self.h[:].bitcast(dtype)
        ap = base[:, start // bpe: start // bpe + n]
        if len(shape) > 2:
            pat = "p (" + " ".join(f"a{i}" for i in range(len(shape) - 1)) + ") -> p " + \
                  " ".join(f"a{i}" for i in range(len(shape) - 1))
            kw = {f"a{i}": shape[i + 1] for i in range(len(shape) - 1)}
            ap = ap.rearrange(pat, **kw)
        return ap


def build_program(dbg_names=None, phase_limit=99):
    nc = bass.Bass("TRN2", target_bir_lowering=False)
    es = contextlib.ExitStack()
    C = {}

    def din(name, shape, dtype=F32):
        C[name] = nc.dram_tensor(name, list(shape), dtype, kind="ExternalInput")
        return C[name]

    def dint(name, shape, dtype):
        C[name] = nc.dram_tensor(name, list(shape), dtype)
        return C[name]

    din("xT", [D, T])
    din("wa_qkv", [24, D, 128]); din("wa_proj", [8, D, 128])
    for nm in ("wb_q", "wb_k", "wb_v", "wb_o"):
        din(nm, [8, D, 128])
    din("bq", [128, CH]); din("bk", [128, CH]); din("bv", [128, CH])
    din("bo", [128, CH])
    din("pos_bias", [16, 4095])
    din("band_b", [16, 128, BW_B])
    din("wf_qkv", [2, 12, D, 128]); din("wf_wo", [2, 8, 512, 128])
    din("wf_w1a", [2, MA, D, 128]); din("wf_w1g", [2, MA, D, 128])
    din("wf_w2", [2, 8, HIDP, 128])
    din("b1a", [128, 2, MA]); din("b1g", [128, 2, MA]); din("b2", [128, 2, CH])
    din("ln_g", [128, 2, CH]); din("ln_b", [128, 2, CH])
    din("qs", [DH, 2]); din("ks", [DH, 2])
    for nm in ("tab_a_cq", "tab_a_sq", "tab_a_ck", "tab_a_sk",
               "tab_f_cq", "tab_f_sq", "tab_f_ck", "tab_f_sk"):
        din(nm, [128, T])
    din("band_a", [128, WA]); din("band_f", [128, WF])
    din("ident_bf", [128, 128], BF16); din("ident_f32", [128, 128], F32)
    din("perm_bf", [128, 128], BF16)
    din("ones_r_f32", [1, 128], F32); din("ones_c_bf", [128, 1], BF16)
    din("ones2_bf", [128, 2], BF16); din("b2_f32", [2, 128], F32)

    C["yT"] = nc.dram_tensor("yT", [D, T], F32, kind="ExternalOutput")
    dint("x1T", [D, T], F32)
    dint("x2T", [D, T], F32)
    dint("x3T", [D, T], F32)
    dint("uT", [D, T], F32)
    dint("u1T", [D, T], F32)
    dint("c_wa_qkv", [24, D, 128], BF16); dint("c_wa_proj", [8, D, 128], BF16)
    for nm in ("wb_q", "wb_k", "wb_v", "wb_o"):
        dint("c_" + nm, [8, D, 128], BF16)
    dint("c_wf_qkv", [2, 12, D, 128], BF16); dint("c_wf_wo", [2, 8, 512, 128], BF16)
    dint("c_wf_w1a", [2, MA, D, 128], BF16); dint("c_wf_w1g", [2, MA, D, 128], BF16)
    dint("c_wf_w2", [2, 8, HIDP, 128], BF16)

    S = Sched(nc, es)
    S.dbg_names = set(dbg_names) if dbg_names else None
    A = Arena(nc, es, 188 * 1024)

    K = {}
    def sball(name, shape, dtype):
        K[name] = es.enter_context(nc.sbuf_tensor(name, list(shape), dtype))
        return K[name]

    for nm, sh, dt in (
        ("ident_bf", [128, 128], BF16), ("ident_f32", [128, 128], F32),
        ("perm_bf", [128, 128], BF16), ("ones_r_f32", [1, 128], F32),
        ("ones_c_bf", [128, 1], BF16), ("ones2_bf", [128, 2], BF16),
        ("b2_f32", [2, 128], F32),
        ("band_a", [128, WA], F32), ("band_f", [128, WF], F32),
        ("bq", [128, CH], F32), ("bk", [128, CH], F32), ("bv", [128, CH], F32),
        ("bo", [128, CH], F32), ("b2b", [128, 2, CH], F32),
        ("lng", [128, 2, CH], F32), ("lnb", [128, 2, CH], F32),
        ("b1a_sb", [128, 2, MA], F32), ("b1g_sb", [128, 2, MA], F32),
        ("qs_sb", [128, 2], F32), ("ks_sb", [128, 2], F32),
        ("pbrow", [1, 16], F32),
        ("mu_row", [1, TT], F32), ("r_row", [1, TT], F32), ("mur_row", [1, TT], F32),
        ("eps1", [1, 1], F32),
    ):
        K[nm] = es.enter_context(nc.sbuf_tensor("sb_" + nm, list(sh), dt))

    PS = [es.enter_context(nc.psum_tensor(f"ps{i}", [128, 512], F32))
          for i in range(8)]

    meta = dict(nc=nc, S=S, A=A, K=K, PS=PS, C=C)

    _emit_wprep(meta)
    _emit_const_loads(meta)
    if phase_limit >= 1:
        A.push()
        xn = A.alloc([128, CH, T], BF16, "xn")
        meta["xn"] = xn
        _emit_ln_stats_apply(meta, src_dram=C["xT"], xn_out=xn, phase_tag="ln1")
    if phase_limit >= 2:
        meta["Oa"] = A.alloc([128, CH, T], BF16, "Oa")
        _emit_phase_attn_a(meta)
        S.dbg("a_o", [128, CH * T], BF16,
              lambda: meta["Oa"].rearrange("p a b -> p (a b)"),
              [meta["attn_a_done"]])
        S.new_phase_sems()
    if phase_limit >= 3:
        _emit_phase_proj_res(meta, meta["Oa"],
                             lambda oc: C["c_wa_proj"].ap()[oc], None,
                             C["xT"], C["x1T"], o_dep=None)
        S.dbg("x1", [D, T], F32, lambda: C["x1T"].ap(), [])
        S.new_phase_sems()
        A.pop()
    if phase_limit >= 4:
        _emit_ln_stats_apply(meta, C["x1T"], None, "ln2", out_dram=C["uT"])
        S.new_phase_sems()
        S.dbg("u_dram", [D, T], F32, lambda: C["uT"].ap(), [])
        _emit_ffn_sublayer(meta, 0, C["uT"], C["u1T"], C["u1T"], None,
                           C["uT"], "f1l0")
        _emit_ffn_sublayer(meta, 1, C["uT"], C["u1T"], C["u1T"], C["x1T"],
                           C["x2T"], "f1l1")
        S.new_phase_sems()
        S.dbg("x2", [D, T], F32, lambda: C["x2T"].ap(), [])
    if phase_limit >= 5:
        A.push()
        xn3 = A.alloc([128, CH, T], BF16, "xn3")
        _emit_ln_stats_apply(meta, C["x2T"], xn3, "ln3")
        meta["Ob"] = A.alloc([128, CH, T], BF16, "Ob")
        S.new_phase_sems()
        _emit_phase_attn_b(meta, xn3)
        S.new_phase_sems()
        _emit_phase_proj_res(meta, meta["Ob"],
                             lambda oc: C["c_wb_o"].ap()[oc], "bo",
                             C["x2T"], C["x3T"], o_dep=None)
        S.new_phase_sems()
        S.dbg("x3", [D, T], F32, lambda: C["x3T"].ap(), [])
        A.pop()
    if phase_limit >= 6:
        _emit_ln_stats_apply(meta, C["x3T"], None, "ln4", out_dram=C["uT"])
        S.new_phase_sems()
        _emit_ffn_sublayer(meta, 0, C["uT"], C["u1T"], C["u1T"], None,
                           C["uT"], "f2l0")
        _emit_ffn_sublayer(meta, 1, C["uT"], C["u1T"], C["u1T"], C["x3T"],
                           C["yT"], "f2l1")
        S.new_phase_sems()
    return nc, es, meta


def _q_dma(S, dst_ap, src_ap, waits=(), slot="misc"):
    return S.dma("q", slot, lambda e: e.dma_start(out=dst_ap, in_=src_ap),
                 waits=waits)


def _p_dma(S, dst_ap, src_ap, waits=(), slot="pmisc"):
    return S.dma("p", slot, lambda e: e.dma_start(out=dst_ap, in_=src_ap),
                 waits=waits)


def _emit_wprep(meta):
    S, C = meta["S"], meta["C"]
    deps = []
    pairs = [("wa_qkv", "c_wa_qkv"), ("wa_proj", "c_wa_proj"),
             ("wb_q", "c_wb_q"), ("wb_k", "c_wb_k"), ("wb_v", "c_wb_v"),
             ("wb_o", "c_wb_o"), ("wf_qkv", "c_wf_qkv"), ("wf_wo", "c_wf_wo"),
             ("wf_w1a", "c_wf_w1a"), ("wf_w1g", "c_wf_w1g"), ("wf_w2", "c_wf_w2")]
    for src, dst in pairs:
        sap = C[src].ap()
        dap = C[dst].ap()
        n = 1
        for s in sap.shape:
            n *= s
        sflat = sap.rearrange(" ".join(f"a{i}" for i in range(len(sap.shape)))
                              + " -> (" + " ".join(f"a{i}" for i in range(len(sap.shape))) + ")")
        dflat = dap.rearrange(" ".join(f"a{i}" for i in range(len(dap.shape)))
                              + " -> (" + " ".join(f"a{i}" for i in range(len(dap.shape))) + ")")
        CHUNK = 1 << 20
        for o in range(0, n, CHUNK):
            ln = min(CHUNK, n - o)
            deps.append(_p_dma(S, dflat[o:o + ln], sflat[o:o + ln],
                               slot="wprep"))
    meta["wprep_dep"] = deps[-1]


def _emit_const_loads(meta):
    S, C, K = meta["S"], meta["C"], meta["K"]
    nc = meta["nc"]
    deps = []
    for nm in ("ident_bf", "ident_f32", "perm_bf", "ones_r_f32", "ones_c_bf",
               "ones2_bf", "b2_f32", "band_a", "band_f"):
        deps.append(_q_dma(S, K[nm][:], C[nm].ap(), slot="const"))
    for nm, dn in (("bq", "bq"), ("bk", "bk"), ("bv", "bv"), ("bo", "bo")):
        deps.append(_q_dma(S, K[nm][:], C[dn].ap(), slot="const"))
    for nm, dn in (("b2b", "b2"), ("lng", "ln_g"), ("lnb", "ln_b")):
        deps.append(_q_dma(S, K[nm][:], C[dn].ap(), slot="const"))
    for nm, dn in (("b1a_sb", "b1a"), ("b1g_sb", "b1g")):
        deps.append(_q_dma(S, K[nm][:], C[dn].ap(), slot="const"))
    for nm in ("qs", "ks"):
        deps.append(_q_dma(S, K[nm + "_sb"][0:64, :], C[nm].ap(), slot="const"))
    meta["A"].push()
    pb_sb = meta["A"].alloc([128, 4095], F32, "pbtmp")
    pbm = meta["A"].alloc([128, 1], F32, "pbm")
    deps.append(_q_dma(S, pb_sb[0:16, :], C["pos_bias"].ap(), slot="const"))
    d = deps[-1]
    dcl = d
    for nm in ("qs_sb", "ks_sb"):
        S.emit("v", lambda e, t=K[nm]: e.tensor_copy(t[64:128, :], t[0:64, :]),
               waits=[d], inc=1)
    meta["eps_dep"] = S.emit("p", lambda e: e.memset(K["eps1"][:, :], 1e-5),
                             inc=1)
    dm = S.emit("v", lambda e: e.reduce_max(pbm[0:16, :], pb_sb[0:16, :],
                                            mybir.AxisListType.X),
                waits=[d], inc=1)
    PS = meta["PS"]
    dt = S.emit("t", lambda e: e.transpose(PS[5][0:1, 0:16], pbm[0:16, 0:1],
                                           K["ident_f32"][0:16, 0:16]),
                waits=[dm, dcl], inc=1)
    S.emit("v", lambda e: e.tensor_copy(K["pbrow"][0:1, 0:16], PS[5][0:1, 0:16]),
           waits=[dt], inc=1)
    meta["A"].pop()
    meta["const_dep"] = dcl
    meta["const_v_dep"] = S.dep("v")


def _emit_ln_stats_apply(meta, src_dram, xn_out, phase_tag, gains_l=None,
                         out_dram=None):
    S, A, K, PS, C = meta["S"], meta["A"], meta["K"], meta["PS"], meta["C"]
    A.push()
    xf = A.alloc([128, 2, CH, TT], F32, "xf")
    xbt = A.alloc([128, 2, CH, TT], BF16, "xbt")
    sq = A.alloc([128, 2, 2, TT], BF16, "sq")
    src = src_dram.ap()
    cdep = meta["const_dep"]
    war_v = {}
    war_t = {}
    prev_apply = {}
    for tt in range(NTT):
        b = tt % 2
        dl = None
        for c in range(CH):
            dl = _q_dma(S, xf[:, b, c, :],
                        src[c * 128:(c + 1) * 128, tt * TT:(tt + 1) * TT],
                        waits=([war_v.get(tt - 2)] if c == 0 else ()),
                        slot=f"lnxf{b}")
        dcast = None
        for c in range(CH):
            dcast = S.emit("v", lambda e, c=c, b=b: e.tensor_copy(
                xbt[:, b, c, :], xf[:, b, c, :]), waits=[dl], inc=1)
        war = prev_apply.get(tt - 1, meta.get("const_v_dep"))
        sqmm = {}
        for c in range(CH):
            dsq = S.emit("s", lambda e, c=c, b=b: e.activation(
                sq[:, b, c % 2, :], xbt[:, b, c, :], AF.Square),
                waits=[dcast, war, sqmm.get(c - 2)], inc=1)
            S.emit("t", lambda e, c=c, b=b: e.matmul(
                PS[6][0:1, :], K["ones_c_bf"][:, :], xbt[:, b, c, :],
                start=(c == 0), stop=(c == CH - 1)),
                waits=[dcast, cdep, war], inc=(1 if c == CH - 1 else 0))
            sqmm[c] = S.emit("t", lambda e, c=c, b=b: e.matmul(
                PS[7][0:1, :], K["ones_c_bf"][:, :], sq[:, b, c % 2, :],
                start=(c == 0), stop=(c == CH - 1)),
                waits=[dsq, war], inc=1)
        dstat = S.dep("t")
        d1 = S.emit("v", lambda e: e.tensor_scalar_mul(
            K["mu_row"][0:1, :], PS[6][0:1, :], 1.0 / D), waits=[dstat], inc=1)
        d2 = S.emit("v", lambda e: e.tensor_scalar_mul(
            K["r_row"][0:1, :], PS[7][0:1, :], 1.0 / D), waits=[dstat], inc=1)
        d3 = S.emit("v", lambda e: e.tensor_mul(
            K["mur_row"][0:1, :], K["mu_row"][0:1, :], K["mu_row"][0:1, :]),
            waits=[d2], inc=1)
        d4 = S.emit("v", lambda e: e.tensor_sub(
            K["r_row"][0:1, :], K["r_row"][0:1, :], K["mur_row"][0:1, :]),
            waits=[d3], inc=1)
        d5 = S.emit("v", lambda e: e.tensor_scalar_max(
            K["r_row"][0:1, :], K["r_row"][0:1, :], 0.0), waits=[d4], inc=1)
        d6 = S.emit("s", lambda e: e.activation(
            K["r_row"][0:1, :], K["r_row"][0:1, :], AF.Ln,
            bias=K["eps1"][0:1, :], scale=1.0),
            waits=[d5, meta.get("eps_dep")], inc=1)
        d7 = S.emit("s", lambda e: e.activation(
            K["r_row"][0:1, :], K["r_row"][0:1, :], AF.Exp, scale=-0.5),
            waits=[d6], inc=1)
        d8 = S.emit("v", lambda e: e.tensor_mul(
            K["mur_row"][0:1, :], K["mu_row"][0:1, :], K["r_row"][0:1, :]),
            waits=[d7], inc=1)
        if tt == 0:
            S.dbg(phase_tag + "_r", [1, TT], F32, lambda: K["r_row"][0:1, :], [d8])
            S.dbg(phase_tag + "_mur", [1, TT], F32,
                  lambda: K["mur_row"][0:1, :], [d8])
        dr = S.emit("t", lambda e: e.matmul(
            PS[6][:, :], K["ones_r_f32"][:, :], K["r_row"][0:1, :],
            start=True, stop=True), waits=[d8], inc=1)
        dm = S.emit("t", lambda e: e.matmul(
            PS[7][:, :], K["ones_r_f32"][:, :], K["mur_row"][0:1, :],
            start=True, stop=True), waits=[d8], inc=1)
        dv = None
        for c in range(CH):
            dv1 = S.emit("v", lambda e, c=c, b=b: e.tensor_mul(
                xf[:, b, c, :], xbt[:, b, c, :], PS[6][:, :]),
                waits=[dr], inc=1)
            if out_dram is not None:
                dv = S.emit("v", lambda e, c=c, b=b: e.tensor_sub(
                    xf[:, b, c, :], xf[:, b, c, :], PS[7][:, :]),
                    waits=[dm, dv1], inc=1)
                dv = _q_dma(S, out_dram.ap()[c * 128:(c + 1) * 128,
                                             tt * TT:(tt + 1) * TT],
                            xf[:, b, c, :], waits=[dv], slot=f"lnout{b}")
            elif gains_l is None:
                dv = S.emit("v", lambda e, c=c, b=b, tt=tt: e.tensor_sub(
                    xn_out[:, c, tt * TT:(tt + 1) * TT], xf[:, b, c, :],
                    PS[7][:, :]), waits=[dm, dv1], inc=1)
            else:
                dv2 = S.emit("v", lambda e, c=c, b=b: e.tensor_sub(
                    xf[:, b, c, :], xf[:, b, c, :], PS[7][:, :]),
                    waits=[dm, dv1], inc=1)
                dv = S.emit("v", lambda e, c=c, b=b, tt=tt, l=gains_l:
                    e.tensor_scalar(
                        xn_out[:, c, tt * TT:(tt + 1) * TT], xf[:, b, c, :],
                        K["lng"][:, l, c:c + 1], K["lnb"][:, l, c:c + 1],
                        ALU.mult, ALU.add), waits=[dv2], inc=1)
        war_v[tt] = dv
        prev_apply[tt] = dv
        meta.setdefault(phase_tag + "_tiles", {})[tt] = dv
    if xn_out is not None:
        S.dbg(phase_tag + "_xn", [128, CH * T], BF16,
              lambda: xn_out.rearrange("p a b -> p (a b)"), [S.dep("v")])
    A.pop()
    meta[phase_tag + "_done"] = S.dep("v")


def _emit_pair_qkv_a(meta, P, hp):
    S, A, K, PS, C = meta["S"], meta["A"], meta["K"], meta["PS"], meta["C"]
    cdep = meta["const_dep"]
    wdep = None
    for j in range(3):
        srcw = P["wsrc"](j)
        sap = bass.AP(srcw.tensor, srcw.offset,
                      [[128, 128], [128 * 128, CH], [1, 128]])
        wdep = _q_dma(S, P["wq"][:, j, :, :], sap,
                      waits=[meta["wprep_dep"], P.get("scope_war")],
                      slot="wq")
    if P.get("dbg_pair"):
        dd = S.dbg("pp_wq", [128, 3 * CH * 128], BF16,
                   lambda: P["wq"].rearrange("p a b c -> p (a b c)"), [wdep])
        dd2 = S.dbg("pp_ub0", [128, CH, TT], BF16,
                    lambda: P["rhs_dbg"](), [P["rhs_dep"]])
    V4 = P["V"]
    for tt in range(NTT):
        ts = slice(tt * TT, (tt + 1) * TT)
        b = tt % 2

        qbank = 0 if tt % 2 == 0 else 6
        kbank = 1 if tt % 2 == 0 else 7
        for psn, j in ((qbank, 0), (kbank, 1), (2, 2)):
            for kc in range(CH):
                rdep = (P["rhs_tiles"][min(tt + 1, NTT - 1)]
                        if P.get("rhs_tiles") else P["rhs_dep"])
                S.emit("t", lambda e, psn=psn, j=j, kc=kc, tt=tt: e.matmul(
                    PS[psn][:, :], P["wq"][:, j, kc, :],
                    P["rhs_fn"](kc, tt),
                    start=(kc == 0), stop=(kc == CH - 1)),
                    waits=[wdep, rdep, P["bank"].get(psn)],
                    inc=(1 if kc == CH - 1 else 0))
            if j == 0:
                dqp = S.dep("t")
            elif j == 1:
                dkp = S.dep("t")
            else:
                dvp = S.dep("t")
        dvf = S.emit("v", lambda e: e.tensor_copy(P["vt_f32"][:, 0, :],
                                                  PS[2][:, :]),
                     waits=[dvp, P["war_vt"].get(tt - 1)], inc=1)
        P["bank"][2] = dvf
        dtr = None
        for j in range(4):
            dtr = S.emit("t", lambda e, j=j, b=b: e.transpose(
                PS[3][:, j * 128:(j + 1) * 128],
                P["vt_f32"][:, 0, j * 128:(j + 1) * 128],
                K["ident_f32"][:, :]),
                waits=[dvf, cdep, P["war_ps3"].get(tt - 1),
                       P["bank"].get(3)], inc=1)
        P["war_vt"][tt] = dtr
        dve = None
        for h in range(2):
            vout = V4[:, h, tt * 4:(tt + 1) * 4, 0:64]
            vin = bass.AP(PS[3], 64 * h, [[512, 128], [128, 4], [1, 64]])
            dve = S.emit("v", lambda e, vout=vout, vin=vin: e.tensor_copy(
                vout, vin), waits=[dtr, P.get("war_v4")], inc=1)
        P["war_ps3"][tt] = dve
        P["bank"][3] = dve
        for name, psn, tcos, tsin, dst, nrm in (
                ("q", qbank, 0, None, P["Qt"], P["qn"]),
                ("k", kbank, 2, None, P["Kt"], P["kn"])):
            dep_p = dqp if name == "q" else dkp
            tdep = None
            for j in range(2):
                tdep = _q_dma(S, P["tab"][:, j, :],
                              C[P["tabs"][tcos + j]].ap()[:, ts],
                              waits=[P.get("war_tab_l"), P.get("scope_war")],
                              slot="tab")
            dcast = S.emit("v", lambda e, psn=psn, b=b: e.tensor_copy(
                P["qb"][:, b, :], PS[psn][:, :]),
                waits=[dep_p, P.get("war_qb_l")], inc=1)
            P["bank"][psn] = dcast
            if P.get("dbg_pair") and tt == 0:
                dd = S.dbg(f"pp_{name}_raw", [128, TT], BF16,
                           lambda: P["qb"][:, 0, :], [dcast])
                if dd: dcast = dd
            if P.get("l2norm"):
                dcast = _emit_l2norm(meta, P, dcast, b, name)
                if P.get("dbg_pair") and tt == 0:
                    dd = S.dbg(f"pp_{name}_l2", [128, TT], BF16,
                               lambda: P["qb"][:, 0, :], [dcast])
                    if dd: dcast = dd
            drot = S.emit("t", lambda e, b=b: e.matmul(
                PS[4][:, :], K["perm_bf"][:, :], P["qb"][:, b, :],
                start=True, stop=True),
                waits=[dcast, cdep, P.get("war_rot_l"), P["bank"].get(4)],
                inc=1)
            dt1 = S.emit("v", lambda e, b=b: e.tensor_mul(
                P["t1"][:, :], P["qb"][:, b, :], P["tab"][:, 0, :]),
                waits=[tdep, dcast], inc=1)
            dt2 = S.emit("v", lambda e, b=b: e.tensor_mul(
                P["t2"][:, :], PS[4][:, :], P["tab"][:, 1, :]),
                waits=[drot, dt1], inc=1)
            P["war_rot_l"] = dt2
            P["war_tab_l"] = dt2
            P["bank"][4] = dt2
            dqt = S.emit("v", lambda e, dst=dst, tt=tt: e.tensor_add(
                dst[:, tt * TT:(tt + 1) * TT], P["t1"][:, :], P["t2"][:, :]),
                waits=[dt2, P.get("war_qt")], inc=1)
            if P.get("dbg_pair") and tt == 0:
                dd = S.dbg(f"pp_{name}_t1", [128, TT], F32,
                           lambda: P["t1"][:, :], [dt2])
                dd = S.dbg(f"pp_{name}_t2", [128, TT], F32,
                           lambda: P["t2"][:, :], [dqt])
                if dd:
                    P["war_rot_l"] = dd
                    P["war_qb_l"] = dd
                    P["war_tab_l"] = dd
            P["war_qb_l"] = drot
            dsqn = S.emit("s", lambda e, dst=dst, tt=tt, b=b: e.activation(
                P["sqn"][:, b, :], dst[:, tt * TT:(tt + 1) * TT], AF.Square),
                waits=[dqt, P.get("war_sqn_l")], inc=1)
            dnm = S.emit("t", lambda e, b=b: e.matmul(
                PS[5][0:2, :], K["ones2_bf"][:, :], P["sqn"][:, b, :],
                start=True, stop=True),
                waits=[dsqn, cdep, P.get("war_ps5_l"), P["bank"].get(5)],
                inc=1)
            P["war_sqn_l"] = dnm
            if P.get("l2norm"):
                n32 = P["qn32"] if name == "q" else P["kn32"]
                for j4 in range(4):
                    dnv = S.emit("v", lambda e, n32=n32, tt=tt, j4=j4:
                        e.reduce_max(n32[0:2, 4 + 4 * tt + j4:5 + 4 * tt + j4],
                                     PS[5][0:2, j4 * 128:(j4 + 1) * 128],
                                     mybir.AxisListType.X),
                        waits=[dnm, P.get("war_norm_v"),
                               P.get("war_norm_s")], inc=1)
            else:
                dnv = S.emit("v", lambda e, nrm=nrm, tt=tt: e.reduce_max(
                    nrm[0:2, tt:tt + 1], PS[5][0:2, :], mybir.AxisListType.X),
                    waits=[dnm, P.get("war_norm_v"),
                           P.get("war_norm_s")], inc=1)
            P["war_ps5_l"] = dnv
            P["bank"][5] = dnv


def _emit_l2norm(meta, P, dcast, b, name):
    S, K, PS = meta["S"], meta["K"], meta["PS"]
    cdep = meta["const_dep"]
    svec = K["qs_sb"] if name == "q" else K["ks_sb"]
    l = P["ffn_l"]
    dsq = S.emit("s", lambda e, b=b: e.activation(
        P["sqn"][:, b, :], P["qb"][:, b, :], AF.Square),
        waits=[dcast, P.get("war_sqn_l")], inc=1)
    dnm = S.emit("t", lambda e, b=b: e.matmul(
        PS[5][0:2, :], K["ones2_bf"][:, :], P["sqn"][:, b, :],
        start=True, stop=True),
        waits=[dsq, cdep, P.get("war_ps5_l"), P["bank"].get(5)], inc=1)
    P["war_sqn_l"] = dnm
    dmx = S.emit("v", lambda e: e.tensor_scalar_max(
        P["ssc"][0:2, :], PS[5][0:2, :], 1e-24),
        waits=[dnm, P.get("war_ssc")], inc=1)
    P["war_ps5_l"] = dmx
    dln = S.emit("s", lambda e: e.activation(P["ssc"][0:2, :], P["ssc"][0:2, :],
                                             AF.Ln), waits=[dmx], inc=1)
    dex = S.emit("s", lambda e: e.activation(P["ssc"][0:2, :], P["ssc"][0:2, :],
                                             AF.Exp, scale=-0.5),
                 waits=[dln], inc=1)
    drb = S.emit("t", lambda e: e.matmul(
        PS[5][:, :], K["b2_f32"][0:2, :], P["ssc"][0:2, :],
        start=True, stop=True),
        waits=[dex, cdep, P.get("war_ps5_l"), P["bank"].get(5)], inc=1)
    P["war_ssc"] = drb
    dsc = S.emit("v", lambda e, b=b, svec=svec, l=l: e.scalar_tensor_tensor(
        P["qb"][:, b, :], P["qb"][:, b, :], svec[:, l:l + 1], PS[5][:, :],
        ALU.mult, ALU.mult), waits=[drb, meta["const_v_dep"]], inc=1)
    P["war_ps5_l"] = dsc
    P["bank"][5] = dsc
    return dsc


def _emit_attention(meta, P, hp, kind):
    S, A, K, PS, C = meta["S"], meta["A"], meta["K"], meta["PS"], meta["C"]
    cdep = meta["const_dep"]
    for qb in range(NQB):
        qs_ = slice(qb * TT, (qb + 1) * TT)
        dbc = {}

        def bcast_negc(dneg):
            for h in range(2):
                db = S.emit("t", lambda e, h=h: e.matmul(
                    PS[7][:, 500 + h:501 + h], K["ones_r_f32"][:, :],
                    P["negc_row"][0:1, h:h + 1], start=True, stop=True),
                    waits=[dneg, P.get("war_negsb"), P["bank"].get(7)],
                    inc=1)
                dbc[h] = S.emit("v", lambda e, h=h: e.tensor_copy(
                    P["negc_sb"][:, h:h + 1], PS[7][:, 500 + h:501 + h]),
                    waits=[db], inc=1)

        if kind != "f":
            if kind == "a":
                if qb == 0:
                    dk = S.emit("v", lambda e: e.tensor_copy(
                        P["knr"][0:2, :], P["kn"][0:2, 0:1]),
                        waits=[P["norm_dep"], P.get("war_norm_s")], inc=1)
                else:
                    dk = S.emit("v", lambda e, qb=qb: e.tensor_max(
                        P["knr"][0:2, :], P["knr"][0:2, :],
                        P["kn"][0:2, qb:qb + 1]),
                        waits=[P["norm_dep"], P.get("knr_war")], inc=1)
            else:
                if qb == 0:
                    dk = S.emit("v", lambda e: e.reduce_max(
                        P["knr"][0:2, :], P["kn"][0:2, :],
                        mybir.AxisListType.X),
                        waits=[P["norm_dep"], P.get("war_norm_s")], inc=1)
                else:
                    dk = P["knr_dep"]
            P["knr_dep"] = dk
            d1 = S.emit("s", lambda e, qb=qb: e.activation(
                P["c1"][0:2, 0:1], P["qn"][0:2, qb:qb + 1], AF.Ln),
                waits=[dk, P["norm_dep"]], inc=1)
            d2 = S.emit("s", lambda e: e.activation(
                P["c2"][0:2, 0:1], P["knr"][0:2, :], AF.Ln),
                waits=[dk, d1], inc=1)
            d3 = S.emit("v", lambda e: e.tensor_add(P["c1"][0:2, 0:1],
                                                    P["c1"][0:2, 0:1],
                                                    P["c2"][0:2, 0:1]),
                        waits=[d1, d2], inc=1)
            P["knr_war"] = d3
            d4 = S.emit("s", lambda e: e.activation(P["c1"][0:2, 0:1],
                                                    P["c1"][0:2, 0:1],
                                                    AF.Exp, scale=0.5),
                        waits=[d3], inc=1)
            dtr = S.emit("t", lambda e: e.transpose(PS[7][0:1, 504:506],
                                                    P["c1"][0:2, 0:1],
                                                    K["ident_f32"][0:2, 0:2]),
                         waits=[d4, cdep, P.get("war_negps"),
                                P["bank"].get(7)], inc=1)
            if kind == "b":
                dtr = S.emit("v", lambda e, hp=hp: e.tensor_add(
                    PS[7][0:1, 504:506], PS[7][0:1, 504:506],
                    K["pbrow"][0:1, 2 * hp:2 * hp + 2]),
                    waits=[dtr, meta["const_v_dep"]], inc=1)
            dneg = S.emit("v", lambda e: e.tensor_scalar_mul(
                P["negc_row"][0:1, 0:2], PS[7][0:1, 504:506], -1.0),
                waits=[dtr], inc=1)
            P["war_negps"] = dneg
            bcast_negc(dneg)
            P["war_negsb"] = dbc[1]
            P["bank"][7] = dbc[1]

        if kind == "a":
            kcs = [(kc, "vis" if kc < 4 * qb else "band",
                    512 * qb - 128 * kc + OFFA) for kc in range(4 * qb + 4)]
            nk = len(kcs)
            dS_l, dP_l = {}, {}
            SBK = ((0, 2, 6), (1, 3, 7))

            def emit_S(i):
                kc, kd, c0 = kcs[i]
                for h in range(2):
                    psn = SBK[h][i % 3]
                    dS_l[(i, h)] = S.emit(
                        "t", lambda e, psn=psn, h=h, kc=kc, qs_=qs_: e.matmul(
                            PS[psn][:, :], P["Kt"][64 * h:64 * (h + 1),
                                                   kc * 128:(kc + 1) * 128],
                            P["Qt"][64 * h:64 * (h + 1), qs_],
                            start=True, stop=True),
                        waits=[P["bank"].get(psn), P["qkv_dep"]], inc=1)

            def emit_PX(i):
                kc, kd, c0 = kcs[i]
                p = i % 2
                for h in range(2):
                    psn = SBK[h][i % 3]
                    dS = dS_l[(i, h)]
                    if kd == "vis":
                        dP = S.emit("s", lambda e, psn=psn, h=h, p=p:
                            e.activation(
                                P["pt"][:, 2 * h + p, :], PS[psn][:, :], AF.Exp,
                                bias=P["negc_sb"][:, h:h + 1], scale=1.0),
                            waits=[dS, dbc[h], P["war_pt"].get((h, p))], inc=1)
                    else:
                        band_ap = K["band_a"][:, c0:c0 + TT]
                        dsm = S.emit("v", lambda e, psn=psn, p=p,
                                     band_ap=band_ap:
                            e.tensor_add(P["sm"][:, p, :], PS[psn][:, :],
                                         band_ap),
                            waits=[dS, cdep, P["war_sm"].get(p)], inc=1)
                        dP = S.emit("s", lambda e, h=h, p=p: e.activation(
                            P["pt"][:, 2 * h + p, :], P["sm"][:, p, :],
                            AF.Exp, bias=P["negc_sb"][:, h:h + 1], scale=1.0),
                            waits=[dsm, dbc[h], P["war_pt"].get((h, p))], inc=1)
                        P["war_sm"][p] = dP
                    P["bank"][psn] = dP
                    dP_l[(i, h)] = dP

            def emit_AV(i):
                kc, kd, c0 = kcs[i]
                p = i % 2
                for h in range(2):
                    dav = S.emit("t", lambda e, h=h, p=p, kc=kc, i=i, nk=nk:
                        e.matmul(
                            PS[4 + h][0:65, :], P["V"][:, h, kc, :],
                            P["pt"][:, 2 * h + p, :],
                            start=(i == 0), stop=(i == nk - 1)),
                        waits=[dP_l[(i, h)], P["bank"].get(4 + h)
                               if i == 0 else None], inc=1)
                    P["war_pt"][(h, p)] = dav

            emit_S(0)
            if nk > 1:
                emit_S(1)
            for i in range(nk):
                emit_PX(i)
                if i + 2 < nk:
                    emit_S(i + 2)
                emit_AV(i)
        elif kind == "f":
            kcs = [(kc, 512 * qb - 128 * kc + OFFF)
                   for kc in range(max(0, 4 * (qb - 1)), 4 * (qb + 1))]
            nk = len(kcs)
            dmx = None
            for i, (kc, c0) in enumerate(kcs):
                for h in range(2):
                    psn = 2 * h + (i % 2)
                    dS = S.emit("t", lambda e, psn=psn, h=h, kc=kc, qs_=qs_:
                        e.matmul(
                            PS[psn][:, :], P["Kt"][64 * h:64 * (h + 1),
                                                   kc * 128:(kc + 1) * 128],
                            P["Qt"][64 * h:64 * (h + 1), qs_],
                            start=True, stop=True),
                        waits=[P["bank"].get(psn), P["qkv_dep"]], inc=1)
                    dsm = S.emit("v", lambda e, psn=psn, h=h, i=i, c0=c0:
                        e.tensor_add(P["smB"][:, h, i, :], PS[psn][:, :],
                                     K["band_f"][:, c0:c0 + TT]),
                        waits=[dS, cdep, P["war_smB"].get((h, i))], inc=1)
                    P["bank"][psn] = dsm
                    dmx = S.emit("v", lambda e, h=h, i=i: e.reduce_max(
                        P["rmx"][:, h, i:i + 1], P["smB"][:, h, i, :],
                        mybir.AxisListType.X), waits=[dsm], inc=1)
            for h in range(2):
                dm2 = S.emit("v", lambda e, h=h, nk=nk: e.reduce_max(
                    P["rm1"][:, h:h + 1], P["rmx"][:, h, 0:nk],
                    mybir.AxisListType.X), waits=[dmx], inc=1)
                dtr = S.emit("t", lambda e, h=h: e.transpose(
                    PS[7][0:1, 360:488],
                    P["rm1"][:, h:h + 1], K["ident_f32"][:, :]),
                    waits=[dm2, cdep, P.get("war_negps"),
                           P["bank"].get(7)], inc=1)
                dcx = S.emit("v", lambda e, h=h: e.reduce_max(
                    P["negc_row"][0:1, h:h + 1], PS[7][0:1, 360:488],
                    mybir.AxisListType.X), waits=[dtr], inc=1)
                P["war_negps"] = dcx
                P["bank"][7] = dcx
            dneg = S.emit("v", lambda e: e.tensor_scalar_mul(
                P["negc_row"][0:1, 0:2], P["negc_row"][0:1, 0:2], -1.0),
                waits=[P["war_negps"]], inc=1)
            P["war_negps"] = dneg
            bcast_negc(dneg)
            P["war_negsb"] = dbc[1]
            for i, (kc, c0) in enumerate(kcs):
                for h in range(2):
                    p = i % 2
                    dP = S.emit("s", lambda e, h=h, i=i, p=p: e.activation(
                        P["pt"][:, 2 * h + p, :], P["smB"][:, h, i, :],
                        AF.Exp, bias=P["negc_sb"][:, h:h + 1], scale=1.0),
                        waits=[dbc[h], P["war_pt"].get((h, p))], inc=1)
                    P["war_smB"][(h, i)] = dP
                    dav = S.emit("t", lambda e, h=h, p=p, kc=kc, i=i, nk=nk:
                        e.matmul(
                            PS[4 + h][0:65, :], P["V"][:, h, kc, :],
                            P["pt"][:, 2 * h + p, :],
                            start=(i == 0), stop=(i == nk - 1)),
                        waits=[dP, P["war_o"].get(h)], inc=1)
                    P["war_pt"][(h, p)] = dav
        else:
            dS_l, dP_l, dband_l = {}, {}, {}

            def b_band(h, si):
                base = 3840 - 256 * si
                srcb = C["band_b"].ap()[2 * hp + h]
                bap = bass.AP(srcb.tensor, srcb.offset + qb * TT + base,
                              [[BW_B, 128], [1, 640]])
                dband_l[(h, si)] = _q_dma(
                    S, P["wband"][:, h, si % 2, :], bap,
                    waits=[P["war_band"].get((h, si % 2))],
                    slot=f"band{h}{si % 2}")

            SBK = ((0, 2, 6), (1, 3, 7))

            def b_S(kc):
                for h in range(2):
                    psn = SBK[h][kc % 3]
                    dS_l[(kc, h)] = S.emit(
                        "t", lambda e, psn=psn, h=h, kc=kc, qs_=qs_: e.matmul(
                            PS[psn][:, :], P["Kt"][64 * h:64 * (h + 1),
                                                   kc * 128:(kc + 1) * 128],
                            P["Qt"][64 * h:64 * (h + 1), qs_],
                            start=True, stop=True),
                        waits=[P["bank"].get(psn), P["qkv_dep"]], inc=1)

            def b_PX(kc):
                p = kc % 2
                si = kc // 2
                base = 3840 - 256 * si
                c0 = 3968 - 128 * kc - base
                for h in range(2):
                    psn = SBK[h][kc % 3]
                    dsm = S.emit("v", lambda e, psn=psn, c0=c0, h=h, si=si:
                        e.tensor_add(PS[psn][:, :], PS[psn][:, :],
                                     P["wband"][:, h, si % 2, c0:c0 + TT]),
                        waits=[dS_l[(kc, h)], dband_l[(h, si)]], inc=1)
                    P["war_band"][(h, si % 2)] = dsm
                    dP = S.emit("s", lambda e, psn=psn, h=h, p=p: e.activation(
                        P["pt"][:, 2 * h + p, :], PS[psn][:, :],
                        AF.Exp, bias=P["negc_sb"][:, h:h + 1], scale=1.0),
                        waits=[dsm, dbc[h], P["war_pt"].get((h, p))], inc=1)
                    P["bank"][psn] = dP
                    dP_l[(kc, h)] = dP

            def b_AV(kc):
                p = kc % 2
                for h in range(2):
                    dav = S.emit("t", lambda e, h=h, p=p, kc=kc: e.matmul(
                        PS[4 + h][0:65, :], P["V"][:, h, kc, :],
                        P["pt"][:, 2 * h + p, :],
                        start=(kc == 0), stop=(kc == KCN - 1)),
                        waits=[dP_l[(kc, h)], P["bank"].get(4 + h)
                               if kc == 0 else None], inc=1)
                    P["war_pt"][(h, p)] = dav

            b_band(0, 0)
            b_band(1, 0)
            b_band(0, 1)
            b_band(1, 1)
            b_S(0)
            b_S(1)
            for kc in range(KCN):
                if kc % 2 == 0 and kc >= 2 and kc // 2 + 1 <= 15:
                    b_band(0, kc // 2 + 1)
                    b_band(1, kc // 2 + 1)
                b_PX(kc)
                if kc + 2 < KCN:
                    b_S(kc + 2)
                b_AV(kc)
        davs = S.dep("t")
        for h in range(2):
            drr = S.emit("v", lambda e, h=h: e.reciprocal(
                P["rrow"][0:1, :], PS[4 + h][64:65, :]), waits=[davs], inc=1)
            dg = S.emit("t", lambda e: e.matmul(
                PS[6][0:64, :], K["ones_r_f32"][0:1, 0:64], P["rrow"][0:1, :],
                start=True, stop=True),
                waits=[drr, P.get("war_gam"), P["bank"].get(6)], inc=1)
            dgc = S.emit("v", lambda e: e.tensor_copy(P["gam"][0:64, :],
                                                      PS[6][0:64, :]),
                         waits=[dg], inc=1)
            P["war_gam"] = dgc
            P["bank"][6] = dgc
            dst_ap = P["o_dst"](qb, h)
            dout = S.emit("v", lambda e, h=h, dst_ap=dst_ap: e.tensor_mul(
                dst_ap, PS[4 + h][0:64, :], P["gam"][0:64, :]),
                waits=[dgc], inc=1)
            P["bank"][4 + h] = dout
        P["war_negsb"] = S.dep("v")
        P["o_done"] = S.dep("v")


def _mk_attn_bufs(meta, nkc=KCN, rope=True, nkin=CH, fine=False):
    A = meta["A"]
    P = {}
    if fine:
        P["smB"] = A.alloc([128, 2, 8, TT], BF16, "smB")
        P["rmx"] = A.alloc([128, 2, 8], F32, "rmx")
        P["rm1"] = A.alloc([128, 2], F32, "rm1")
        P["war_smB"] = {}
    P["wq"] = A.alloc([128, 3, nkin, 128], BF16, "wq")
    if rope:
        P["tab"] = A.alloc([128, 2, TT], F32, "tab")
        P["t1"] = A.alloc([128, TT], F32, "t1")
        P["t2"] = A.alloc([128, TT], F32, "t2")
        P["ssc"] = A.alloc([128, TT], F32, "ssc")
    P["vt_f32"] = A.alloc([128, 1, TT], F32, "vtf")
    if rope:
        P["qb"] = A.alloc([128, 2, TT], BF16, "qbb")
    P["sqn"] = A.alloc([128, 2, TT], BF16, "sqn")
    if rope:
        P["sm"] = A.alloc([128, 2, TT], F32, "sm")
    P["Qt"] = A.alloc([128, T], BF16, "Qt")
    P["Kt"] = A.alloc([128, T], BF16, "Kt")
    P["V"] = A.alloc([128, 2, nkc, 65], BF16, "Vsb")
    sm_all = A.alloc([128, 112], F32, "smalls")
    P["qn"] = sm_all[:, 0:8]
    P["kn"] = sm_all[:, 8:16]
    P["qn32"] = sm_all[:, 16:56]
    P["kn32"] = sm_all[:, 56:96]
    P["acc4"] = sm_all[:, 96:100]
    P["t4"] = sm_all[:, 100:104]
    P["knr"] = sm_all[:, 104:105]
    P["c1"] = sm_all[:, 105:106]
    P["c2"] = sm_all[:, 106:107]
    P["negc_row"] = sm_all[:, 107:109]
    P["negc_sb"] = sm_all[:, 109:111]
    P["rrow"] = A.alloc([128, TT], F32, "rrow")
    P["gam"] = A.alloc([128, TT], F32, "gam")
    P["pt"] = A.alloc([128, 4, TT], BF16, "pt")
    for nmm in ("war_tab", "bank", "war_vt", "war_ps3", "war_sm",
                "war_pt", "war_o"):
        P[nmm] = {}
    return P


def _emit_phase_attn_a(meta):
    S, A, C = meta["S"], meta["A"], meta["C"]
    O = meta["Oa"]
    A.push()
    P = _mk_attn_bufs(meta)
    P["tabs"] = ("tab_a_cq", "tab_a_sq", "tab_a_ck", "tab_a_sk")
    xn = meta["xn"]
    P["rhs_fn"] = lambda kc, tt: xn[:, kc, tt * TT:(tt + 1) * TT]
    P["rhs_dbg"] = lambda: xn[:, :, 0:TT]
    P["rhs_dep"] = meta["ln1_done"]
    P["rhs_tiles"] = meta.get("ln1_tiles")
    P["scope_war"] = meta["ln1_done"]
    for i_ in range(8):
        P["bank"][i_] = meta["ln1_done"]
    S.emit("p", lambda e: e.memset(P["V"][:, :, :, 64:65], 1.0),
           waits=[P["scope_war"]], inc=1)
    donev = S.dep("p")
    S.wait_only("t", [donev])
    for hp in range(CH):
        P["wsrc"] = (lambda hp: lambda j: C["c_wa_qkv"].ap()[
            (hp, 8 + hp, 16 + hp)[j]])(hp)
        _emit_pair_qkv_a(meta, P, hp)
        P["qkv_dep"] = S.dep("v")
        P["norm_dep"] = S.dep("v")
        if hp == 0:
            S.dbg("a_qt0", [128, T], BF16, lambda: P["Qt"][:, :], [P["qkv_dep"]])
            S.dbg("a_kt0", [128, T], BF16, lambda: P["Kt"][:, :], [P["qkv_dep"]])
            S.dbg("a_v0", [128, 2 * KCN * 65], BF16,
                  lambda: P["V"].rearrange("p a b c -> p (a b c)"),
                  [P["qkv_dep"]])
            S.dbg("a_qn0", [128, NTT], F32, lambda: P["qn"][:, :],
                  [P["norm_dep"]])
        P["o_dst"] = (lambda hp: lambda qb, h: O[64 * h:64 * h + 64, hp,
                                                 qb * TT:(qb + 1) * TT])(hp)
        _emit_attention(meta, P, hp, "a")
        P["war_qt"] = S.dep("t")
        P["war_v4"] = S.dep("t")
        P["war_norm_v"] = S.dep("v")
        P["war_norm_s"] = S.dep("s")
        P.pop("knr_dep", None)
        P.pop("knr_war", None)
    S.barrier()
    A.pop()
    meta["attn_a_done"] = S.dep("v")


def _emit_phase_proj_res(meta, O, w_oc_fn, bias_name, src_dram, dst_dram,
                         kcs=CH, o_dep=None):
    S, A, K, PS, C = meta["S"], meta["A"], meta["K"], meta["PS"], meta["C"]
    A.push()
    wp = A.alloc([128, CH, kcs, 128], BF16, "wp")
    xold = A.alloc([128, 2, TT], F32, "xold")
    xnew = A.alloc([128, 2, TT], F32, "xnew")
    wdep = None
    for oc in range(CH):
        srcw = w_oc_fn(oc)
        sap = bass.AP(srcw.tensor, srcw.offset,
                      [[128, 128], [128 * 128, kcs], [1, 128]])
        wdep = _q_dma(S, wp[:, oc, :, :], sap, waits=[meta["wprep_dep"]],
                      slot="wp")
    war = {}
    for tt in range(NTT):
        ts = slice(tt * TT, (tt + 1) * TT)
        for oc in range(CH):
            idx = tt * CH + oc
            b = idx % 4
            xb = idx % 2
            dl = _q_dma(S, xold[:, xb, :],
                        src_dram.ap()[oc * 128:(oc + 1) * 128, ts],
                        waits=[war.get((xb, "x"))], slot=f"xold{xb}")
            for kc in range(kcs):
                S.emit("t", lambda e, oc=oc, kc=kc, tt=tt, b=b: e.matmul(
                    PS[b][:, :], wp[:, oc, kc, :], O[:, kc, tt * TT:(tt + 1) * TT],
                    start=(kc == 0), stop=(kc == kcs - 1)),
                    waits=[wdep, o_dep, war.get((b, "ps"))],
                    inc=(1 if kc == kcs - 1 else 0))
            dmm = S.dep("t")
            if bias_name:
                dv0 = S.emit("v", lambda e, b=b, oc=oc: e.tensor_scalar_add(
                    PS[b][:, :], PS[b][:, :], K[bias_name][:, oc:oc + 1]),
                    waits=[dmm, meta["const_dep"]], inc=1)
            else:
                dv0 = dmm
            dv = S.emit("v", lambda e, b=b, xb=xb: e.tensor_add(
                xnew[:, xb, :], PS[b][:, :], xold[:, xb, :]),
                waits=[dv0, dl, war.get((xb, "xn"))], inc=1)
            war[(b, "ps")] = dv
            dst = _q_dma(S, dst_dram.ap()[oc * 128:(oc + 1) * 128, ts],
                         xnew[:, xb, :], waits=[dv], slot=f"pro{xb}")
            war[(xb, "x")] = dst
            war[(xb, "xn")] = dst
    A.pop()
    return S.dep("q")


def _emit_ub_halo(meta, u_dram, ub):
    S, A = meta["S"], meta["A"]
    A.push()
    uf = A.alloc([128, 2, TT], F32, "uf")
    dz = S.emit("p", lambda e: e.memset(ub[:, :, 0:1], 0.0), inc=1)
    war = {}
    tiles = {}
    for tt in range(NTT):
        for c in range(CH):
            b = (tt * CH + c) % 2
            dl = _q_dma(S, uf[:, b, :],
                        u_dram.ap()[c * 128:(c + 1) * 128,
                                    tt * TT:(tt + 1) * TT],
                        waits=[war.get(b), meta.get("u_ready"), dz],
                        slot=f"ub{b}")
            war[b] = S.emit("v", lambda e, c=c, tt=tt, b=b: e.tensor_copy(
                ub[:, c, 1 + tt * TT:1 + (tt + 1) * TT], uf[:, b, :]),
                waits=[dl], inc=1)
        tiles[tt] = war[b]
    meta["ub_tiles"] = tiles
    A.pop()
    return S.dep("v")


def _emit_ffn_sublayer(meta, l, u_in, u_mid, res_in, extra_res, dst, tag):
    S, A, K, PS, C = meta["S"], meta["A"], meta["K"], meta["PS"], meta["C"]
    cdep = meta["const_dep"]
    A.push()
    ub = A.alloc([128, CH, T + 1], BF16, "ub")
    Of = A.alloc([128, 4, T], BF16, "Of")
    meta["u_ready"] = None
    dub = _emit_ub_halo(meta, u_in, ub)
    P = _mk_attn_bufs(meta, rope=True, fine=True)
    P["tabs"] = ("tab_f_cq", "tab_f_sq", "tab_f_ck", "tab_f_sk")
    P["l2norm"] = True
    P["ffn_l"] = l
    P["rhs_fn"] = (lambda ub: lambda kc, tt:
                   ub[:, kc, 1 + tt * TT:1 + (tt + 1) * TT] if kc < 4 else
                   ub[:, kc, tt * TT:(tt + 1) * TT])(ub)
    P["rhs_dbg"] = (lambda ub: lambda: ub[:, :, 0:TT])(ub)
    P["rhs_dep"] = dub
    P["rhs_tiles"] = meta.get("ub_tiles")
    P["scope_war"] = dub
    for i_ in range(8):
        P["bank"][i_] = dub
    S.emit("p", lambda e: e.memset(P["V"][:, :, :, 64:65], 1.0),
           waits=[dub], inc=1)
    S.wait_only("t", [S.dep("p")])
    for hp in range(4):
        P["wsrc"] = (lambda hp: lambda j: C["c_wf_qkv"].ap()[
            l, (hp, 4 + hp, 8 + hp)[j]])(hp)
        P["dbg_pair"] = (tag == "f1l0" and hp == 0)
        _emit_pair_qkv_a(meta, P, hp)
        P["qkv_dep"] = S.dep("v")
        P["norm_dep"] = S.dep("v")
        if tag == "f1l0" and hp == 0:
            S.dbg("f_qt0", [128, T], BF16, lambda: P["Qt"][:, :], [P["qkv_dep"]])
            S.dbg("f_kt0", [128, T], BF16, lambda: P["Kt"][:, :], [P["qkv_dep"]])
        P["o_dst"] = (lambda hp: lambda qb, h: Of[64 * h:64 * h + 64, hp,
                                                  qb * TT:(qb + 1) * TT])(hp)
        _emit_attention(meta, P, hp, "f")
        P["war_qt"] = S.dep("t")
        P["war_v4"] = S.dep("t")
        P["war_norm_v"] = S.dep("v")
        P["war_norm_s"] = S.dep("s")
        for nmm in ("knr_dep", "knr_war"):
            P.pop(nmm, None)
    if tag == "f1l0":
        S.dbg("f_o0", [128, 4 * T], BF16,
              lambda: Of.rearrange("p a b -> p (a b)"), [S.dep("v")])
    _emit_phase_proj_res(meta, Of, lambda oc: C["c_wf_wo"].ap()[l, oc], None,
                         u_in, u_mid, kcs=4)
    S.new_phase_sems()
    A.pop()
    if tag == "f1l0":
        S.dbg("f_u1", [D, T], F32, lambda: u_mid.ap(), [])
    A.push()
    hb = A.alloc([128, CH, T], BF16, "hb")
    _emit_ln_shift_apply(meta, u_mid, hb, l)
    if tag == "f1l0":
        S.dbg("f_hb", [128, CH * T], BF16,
              lambda: hb.rearrange("p a b -> p (a b)"), [S.dep("v")])
    _emit_w1_glu_w2(meta, l, hb, res_in, extra_res, dst, tag)
    S.new_phase_sems()
    A.pop()


def _emit_ln_shift_apply(meta, u_dram, hb, l):
    S, A, K, PS, C = meta["S"], meta["A"], meta["K"], meta["PS"], meta["C"]
    uf = A.alloc([128, 1, CH, TT + 1], F32, "uf2")
    xbt = A.alloc([128, 1, CH, TT + 1], BF16, "xbt2")
    sq = A.alloc([128, 1, 2, TT], BF16, "sq2")
    cdep = meta["const_dep"]
    war_v = {}
    prev_apply = {}
    dz = S.emit("p", lambda e: e.memset(uf[:, :, :, 0:1], 0.0), inc=1)

    def chv(buf, b, c):
        if c < 4:
            return buf[:, b, c, 1:TT + 1]
        return buf[:, b, c, 0:TT]

    for tt in range(NTT):
        b = 0
        dl = None
        for c in range(CH):
            if tt == 0:
                dl = _q_dma(S, uf[:, b, c, 1:TT + 1],
                            u_dram.ap()[c * 128:(c + 1) * 128, 0:TT],
                            waits=([war_v.get(tt - 1), dz] if c == 0 else ()),
                            slot=f"lsf{b}")
            else:
                dl = _q_dma(S, uf[:, b, c, 0:TT + 1],
                            u_dram.ap()[c * 128:(c + 1) * 128,
                                        tt * TT - 1:(tt + 1) * TT],
                            waits=([war_v.get(tt - 1), dz] if c == 0 else ()),
                            slot=f"lsf{b}")
        dcast = None
        for c in range(CH):
            dcast = S.emit("v", lambda e, c=c, b=b: e.tensor_copy(
                xbt[:, b, c, :], uf[:, b, c, :]), waits=[dl], inc=1)
        war = prev_apply.get(tt - 1, meta.get("const_v_dep"))
        sqmm = {}
        for c in range(CH):
            dsq = S.emit("s", lambda e, c=c, b=b: e.activation(
                sq[:, 0, c % 2, :], chv(xbt, b, c), AF.Square),
                waits=[dcast, war, sqmm.get(c - 2)], inc=1)
            S.emit("t", lambda e, c=c, b=b: e.matmul(
                PS[4][0:1, :], K["ones_c_bf"][:, :], chv(xbt, b, c),
                start=(c == 0), stop=(c == CH - 1)),
                waits=[dcast, cdep, war], inc=(1 if c == CH - 1 else 0))
            sqmm[c] = S.emit("t", lambda e, c=c, b=b: e.matmul(
                PS[5][0:1, :], K["ones_c_bf"][:, :], sq[:, 0, c % 2, :],
                start=(c == 0), stop=(c == CH - 1)),
                waits=[dsq, war], inc=1)
        dstat = S.dep("t")
        d1 = S.emit("v", lambda e: e.tensor_scalar_mul(
            K["mu_row"][0:1, :], PS[4][0:1, :], 1.0 / D), waits=[dstat], inc=1)
        d2 = S.emit("v", lambda e: e.tensor_scalar_mul(
            K["r_row"][0:1, :], PS[5][0:1, :], 1.0 / D), waits=[dstat], inc=1)
        d3 = S.emit("v", lambda e: e.tensor_mul(
            K["mur_row"][0:1, :], K["mu_row"][0:1, :], K["mu_row"][0:1, :]),
            waits=[d2], inc=1)
        d4 = S.emit("v", lambda e: e.tensor_sub(
            K["r_row"][0:1, :], K["r_row"][0:1, :], K["mur_row"][0:1, :]),
            waits=[d3], inc=1)
        d5 = S.emit("v", lambda e: e.tensor_scalar_max(
            K["r_row"][0:1, :], K["r_row"][0:1, :], 0.0), waits=[d4], inc=1)
        d6 = S.emit("s", lambda e: e.activation(
            K["r_row"][0:1, :], K["r_row"][0:1, :], AF.Ln,
            bias=K["eps1"][0:1, :], scale=1.0),
            waits=[d5, meta.get("eps_dep")], inc=1)
        d7 = S.emit("s", lambda e: e.activation(
            K["r_row"][0:1, :], K["r_row"][0:1, :], AF.Exp, scale=-0.5),
            waits=[d6], inc=1)
        d8 = S.emit("v", lambda e: e.tensor_mul(
            K["mur_row"][0:1, :], K["mu_row"][0:1, :], K["r_row"][0:1, :]),
            waits=[d7], inc=1)
        dr = S.emit("t", lambda e: e.matmul(
            PS[4][:, :], K["ones_r_f32"][:, :], K["r_row"][0:1, :],
            start=True, stop=True), waits=[d8], inc=1)
        dm = S.emit("t", lambda e: e.matmul(
            PS[5][:, :], K["ones_r_f32"][:, :], K["mur_row"][0:1, :],
            start=True, stop=True), waits=[d8], inc=1)
        dv = None
        for c in range(CH):
            dv1 = S.emit("v", lambda e, c=c, b=b: e.tensor_mul(
                uf[:, b, c, 1:TT + 1], chv(xbt, b, c), PS[4][:, :]),
                waits=[dr], inc=1)
            dv2 = S.emit("v", lambda e, c=c, b=b: e.tensor_sub(
                uf[:, b, c, 1:TT + 1], uf[:, b, c, 1:TT + 1], PS[5][:, :]),
                waits=[dm, dv1], inc=1)
            dv = S.emit("v", lambda e, c=c, b=b, tt=tt, l=l: e.tensor_scalar(
                hb[:, c, tt * TT:(tt + 1) * TT], uf[:, b, c, 1:TT + 1],
                K["lng"][:, l, c:c + 1], K["lnb"][:, l, c:c + 1],
                ALU.mult, ALU.add), waits=[dv2, meta["const_dep"]], inc=1)
        war_v[tt] = dv
        prev_apply[tt] = dv
        meta.setdefault("hb_tiles", {})[tt] = dv
    meta["hb_done"] = S.dep("v")


def _emit_w1_glu_w2(meta, l, hb, res_in, extra_res, dst, tag):
    S, A, K, PS, C = meta["S"], meta["A"], meta["K"], meta["PS"], meta["C"]
    cdep = meta["const_dep"]
    TH = 1024
    prod = A.alloc([128, MA, TH], BF16, "prod")
    a_sb = A.alloc([128, 2, TT], F32, "asb")
    gel = A.alloc([128, 2, TT], F32, "gel")
    w1t = A.alloc([128, 2, 2, CH, 128], BF16, "w1t")
    w2t = A.alloc([128, 1, MA, 128], BF16, "w2t")
    ures = A.alloc([128, 2, TT], F32, "ures")
    xres = A.alloc([128, 2, TT], F32, "xres") if extra_res is not None else None
    yout = A.alloc([128, 2, TT], F32, "yout")
    for th in range(T // TH):
        hs = th * TH
        hb_dep = meta["hb_tiles"][(hs + TH - 1) // TT]
        war_w1 = {}
        war_prod = {}
        for m in range(MA):
            bw = m % 2
            dwa = _q_dma(S, w1t[:, bw, 0, :, :],
                         bass.AP(C["c_wf_w1a"].ap()[l, m].tensor,
                                 C["c_wf_w1a"].ap()[l, m].offset,
                                 [[128, 128], [128 * 128, CH], [1, 128]]),
                         waits=[meta["wprep_dep"], war_w1.get(m - 2)],
                         slot=f"w1a{bw}")
            dwg = _q_dma(S, w1t[:, bw, 1, :, :],
                         bass.AP(C["c_wf_w1g"].ap()[l, m].tensor,
                                 C["c_wf_w1g"].ap()[l, m].offset,
                                 [[128, 128], [128 * 128, CH], [1, 128]]),
                         waits=[war_w1.get(m - 2)], slot=f"w1g{bw}")
            for sub in range(TH // TT):
                ss = slice(hs + sub * TT, hs + (sub + 1) * TT)
                ab = 0 if sub % 2 == 0 else 6
                gb = 1 if sub % 2 == 0 else 7
                sb2 = sub % 2
                for kc in range(CH):
                    S.emit("t", lambda e, kc=kc, bw=bw, ss=ss, ab=ab: e.matmul(
                        PS[ab][:, :], w1t[:, bw, 0, kc, :], hb[:, kc, ss],
                        start=(kc == 0), stop=(kc == CH - 1)),
                        waits=[dwa, hb_dep, war_prod.get(ab)],
                        inc=(1 if kc == CH - 1 else 0))
                da = S.dep("t")
                for kc in range(CH):
                    S.emit("t", lambda e, kc=kc, bw=bw, ss=ss, gb=gb: e.matmul(
                        PS[gb][:, :], w1t[:, bw, 1, kc, :], hb[:, kc, ss],
                        start=(kc == 0), stop=(kc == CH - 1)),
                        waits=[dwg, hb_dep, war_prod.get(gb)],
                        inc=(1 if kc == CH - 1 else 0))
                dg = S.dep("t")
                dae = S.emit("v", lambda e, m=m, ab=ab, sb2=sb2: e.tensor_scalar_add(
                    a_sb[:, sb2, :], PS[ab][:, :], K["b1a_sb"][:, l, m:m + 1]),
                    waits=[da, cdep, war_prod.get(("a", sb2))], inc=1)
                war_prod[ab] = dae
                dge = S.emit("s", lambda e, m=m, gb=gb, sb2=sb2: e.activation(
                    gel[:, sb2, :], PS[gb][:, :], AF.Gelu,
                    bias=K["b1g_sb"][:, l, m:m + 1], scale=1.0),
                    waits=[dg, cdep, war_prod.get(("g", sb2))], inc=1)
                war_prod[gb] = dge
                dp = S.emit("v", lambda e, m=m, sub=sub, sb2=sb2: e.tensor_mul(
                    prod[:, m, sub * TT:(sub + 1) * TT], a_sb[:, sb2, :],
                    gel[:, sb2, :]),
                    waits=[dae, dge], inc=1)
                war_prod[("a", sb2)] = dp
                war_prod[("g", sb2)] = dp
            war_w1[m] = dp
        dprod = S.dep("v")
        war_ps = {}
        war_res = {}
        for oc in range(CH):
            bw = 0
            dw2 = _q_dma(S, w2t[:, bw, :, :],
                         bass.AP(C["c_wf_w2"].ap()[l, oc].tensor,
                                 C["c_wf_w2"].ap()[l, oc].offset,
                                 [[128, 128], [128 * 128, MA], [1, 128]]),
                         waits=[meta["wprep_dep"], war_ps.get(("w", bw))],
                         slot=f"w2t{bw}")
            for sub in range(TH // TT):
                ss = slice(hs + sub * TT, hs + (sub + 1) * TT)
                bb = sub % 2
                rs = slice(sub * TT, (sub + 1) * TT)
                for m in range(MA):
                    S.emit("t", lambda e, m=m, bw=bw, rs=rs, bb=bb: e.matmul(
                        PS[2 + bb][:, :], w2t[:, bw, m, :],
                        prod[:, m, rs],
                        start=(m == 0), stop=(m == MA - 1)),
                        waits=[dw2, dprod, war_ps.get(bb)],
                        inc=(1 if m == MA - 1 else 0))
                dy = S.dep("t")
                dlu = _q_dma(S, ures[:, bb, :],
                             res_in.ap()[oc * 128:(oc + 1) * 128, ss],
                             waits=[war_res.get((bb, "u"))],
                             slot=f"ures{bb}")
                dv0 = S.emit("v", lambda e, bb=bb, oc=oc: e.tensor_scalar_add(
                    PS[2 + bb][:, :], PS[2 + bb][:, :],
                    K["b2b"][:, l, oc:oc + 1]), waits=[dy, cdep], inc=1)
                dv1 = S.emit("v", lambda e, bb=bb: e.tensor_add(
                    yout[:, bb, :], PS[2 + bb][:, :], ures[:, bb, :]),
                    waits=[dv0, dlu], inc=1)
                if extra_res is not None:
                    dlx = _q_dma(S, xres[:, bb, :],
                                 extra_res.ap()[oc * 128:(oc + 1) * 128, ss],
                                 waits=[war_res.get((bb, "x"))],
                                 slot=f"xres{bb}")
                    dv1 = S.emit("v", lambda e, bb=bb: e.tensor_add(
                        yout[:, bb, :], yout[:, bb, :], xres[:, bb, :]),
                        waits=[dv1, dlx], inc=1)
                war_ps[bb] = dv1
                dso = _q_dma(S, dst.ap()[oc * 128:(oc + 1) * 128, ss],
                             yout[:, bb, :], waits=[dv1], slot=f"w2o{bb}")
                war_res[(bb, "u")] = dso
                war_res[(bb, "x")] = dso
            war_ps[("w", bw)] = S.dep("t")
        S.barrier()


def _emit_pair_qkv_b(meta, P, hp):
    S, A, K, PS, C = meta["S"], meta["A"], meta["K"], meta["PS"], meta["C"]
    cdep = meta["const_dep"]
    xn = P["xn"]
    wdep = None
    for j, wn in enumerate(("c_wb_q", "c_wb_k", "c_wb_v")):
        srcw = C[wn].ap()[hp]
        sap = bass.AP(srcw.tensor, srcw.offset,
                      [[128, 128], [128 * 128, CH], [1, 128]])
        wdep = _q_dma(S, P["wq"][:, j, :, :], sap,
                      waits=[meta["wprep_dep"], P.get("scope_war")],
                      slot="wq")
    for tt in range(NTT):
        ts = slice(tt * TT, (tt + 1) * TT)
        b = tt % 2
        qbank = 0 if tt % 2 == 0 else 6
        kbank = 1 if tt % 2 == 0 else 7
        for psn, j in ((qbank, 0), (kbank, 1), (2, 2)):
            rdep = (P["rhs_tiles"][tt] if P.get("rhs_tiles")
                    else P["rhs_dep"])
            for kc in range(CH):
                S.emit("t", lambda e, psn=psn, j=j, kc=kc, tt=tt: e.matmul(
                    PS[psn][:, :], P["wq"][:, j, kc, :],
                    xn[:, kc, tt * TT:(tt + 1) * TT],
                    start=(kc == 0), stop=(kc == CH - 1)),
                    waits=[wdep, rdep, P["bank"].get(psn)],
                    inc=(1 if kc == CH - 1 else 0))
            if j == 0:
                dqp = S.dep("t")
            elif j == 1:
                dkp = S.dep("t")
            else:
                dvp = S.dep("t")
        dq = S.emit("v", lambda e, tt=tt, hp=hp, qbank=qbank: e.tensor_scalar(
            P["Qt"][:, tt * TT:(tt + 1) * TT], PS[qbank][:, :],
            K["bq"][:, hp:hp + 1], 0.125, ALU.add, ALU.mult),
            waits=[dqp, cdep, P.get("war_qt")], inc=1)
        P["bank"][qbank] = dq
        dk = S.emit("v", lambda e, tt=tt, hp=hp, kbank=kbank: e.tensor_scalar_add(
            P["Kt"][:, tt * TT:(tt + 1) * TT], PS[kbank][:, :],
            K["bk"][:, hp:hp + 1]), waits=[dkp, cdep, P.get("war_qt")], inc=1)
        P["bank"][kbank] = dk
        dvf = S.emit("v", lambda e, hp=hp: e.tensor_scalar_add(
            P["vt_f32"][:, 0, :], PS[2][:, :], K["bv"][:, hp:hp + 1]),
            waits=[dvp, cdep, P["war_vt"].get(tt - 1)], inc=1)
        P["bank"][2] = dvf
        dtr = None
        for j in range(4):
            dtr = S.emit("t", lambda e, j=j: e.transpose(
                PS[3][:, j * 128:(j + 1) * 128],
                P["vt_f32"][:, 0, j * 128:(j + 1) * 128],
                K["ident_f32"][:, :]),
                waits=[dvf, cdep, P["war_ps3"].get(tt - 1),
                       P["bank"].get(3)], inc=1)
        P["war_vt"][tt] = dtr
        dve = None
        V4b = P["V"]
        for h in range(2):
            vout = V4b[:, h, tt * 4:(tt + 1) * 4, 0:64]
            vin = bass.AP(PS[3], 64 * h, [[512, 128], [128, 4], [1, 64]])
            dve = S.emit("v", lambda e, vout=vout, vin=vin: e.tensor_copy(
                vout, vin), waits=[dtr, P.get("war_v4")], inc=1)
        P["war_ps3"][tt] = dve
        P["bank"][3] = dve
        for dst_t, nrm, dd in ((P["Qt"], P["qn"], dq), (P["Kt"], P["kn"], dk)):
            dsqn = S.emit("s", lambda e, dst_t=dst_t, tt=tt, b=b: e.activation(
                P["sqn"][:, b, :], dst_t[:, tt * TT:(tt + 1) * TT], AF.Square),
                waits=[dd, P.get("war_sqn_l")], inc=1)
            dnm = S.emit("t", lambda e, b=b: e.matmul(
                PS[5][0:2, :], K["ones2_bf"][:, :], P["sqn"][:, b, :],
                start=True, stop=True),
                waits=[dsqn, cdep, P.get("war_ps5_l"), P["bank"].get(5)],
                inc=1)
            P["war_sqn_l"] = dnm
            dnv = S.emit("v", lambda e, nrm=nrm, tt=tt: e.reduce_max(
                nrm[0:2, tt:tt + 1], PS[5][0:2, :], mybir.AxisListType.X),
                waits=[dnm], inc=1)
            P["war_ps5_l"] = dnv


def _emit_phase_attn_b(meta, xn3):
    S, A, C, K = meta["S"], meta["A"], meta["C"], meta["K"]
    O = meta["Ob"]
    A.push()
    P = _mk_attn_bufs(meta, rope=False)
    P["xn"] = xn3
    P["rhs_dep"] = meta["ln3_done"]
    P["rhs_tiles"] = meta.get("ln3_tiles")
    P["scope_war"] = meta["ln3_done"]
    for i_ in range(8):
        P["bank"][i_] = meta["ln3_done"]
    P["wband"] = A.alloc([128, 2, 2, 640], F32, "wband")
    P["war_band"] = {}
    S.emit("p", lambda e: e.memset(P["V"][:, :, :, 64:65], 1.0),
           waits=[P["scope_war"]], inc=1)
    S.wait_only("t", [S.dep("p")])
    for hp in range(CH):
        _emit_pair_qkv_b(meta, P, hp)
        P["qkv_dep"] = S.dep("v")
        P["norm_dep"] = S.dep("v")
        if hp == 0:
            S.dbg("b_qt0", [128, T], BF16, lambda: P["Qt"][:, :], [P["qkv_dep"]])
        P["o_dst"] = (lambda hp: lambda qb, h: O[64 * h:64 * h + 64, hp,
                                                 qb * TT:(qb + 1) * TT])(hp)
        _emit_attention(meta, P, hp, "b")
        P["war_qt"] = S.dep("t")
        P["war_v4"] = S.dep("t")
        P["war_norm_v"] = S.dep("v")
        P["war_norm_s"] = S.dep("s")
        for nmm in ("knr_dep", "knr_war"):
            P.pop(nmm, None)
    S.barrier()
    A.pop()
    meta["attn_b_done"] = S.dep("v")


def _build_full(dbg_names=None, phase_limit=99):
    nc, es, meta = build_program(dbg_names=dbg_names, phase_limit=phase_limit)
    S = meta["S"]
    if phase_limit < 6:
        C = meta["C"]
        srcn = {3: "x1T", 4: "x2T", 5: "x3T"}.get(phase_limit)
        if srcn:
            _q_dma(S, C["yT"].ap(), C[srcn].ap(), waits=(), slot="out")
        elif meta.get("xn") is not None:
            for c in range(CH):
                _p_dma(S, C["yT"].ap()[c * 128:(c + 1) * 128, :],
                       meta["xn"][:, c, :], waits=[meta.get("ln1_done")])
    S.barrier()
    with nc.Block() as block:
        S.replay(block)
    es.close()
    return nc, meta


def kernel(**inputs):
    consts = _host_consts()
    L = _host_layout(inputs)
    nc, meta = _build_full()
    in_maps = []
    for i in range(8):
        m = {"xT": L["xT"][i]}
        for k in ("wa_qkv", "wa_proj", "wb_q", "wb_k", "wb_v", "wb_o",
                  "wf_qkv", "wf_wo", "wf_w1a", "wf_w1g", "wf_w2",
                  "b1a", "b1g", "band_b"):
            m[k] = L[k]
        def vec_d(a):
            return np.ascontiguousarray(
                np.asarray(a, np.float32).reshape(CH, 128).T)
        def vec_ld(a):
            return np.ascontiguousarray(
                np.asarray(a, np.float32).reshape(2, CH, 128).transpose(2, 0, 1))
        m["bq"] = vec_d(inputs["L0_attn_b_bq"])
        m["bk"] = vec_d(inputs["L0_attn_b_bk"])
        m["bv"] = vec_d(inputs["L0_attn_b_bv"])
        m["bo"] = vec_d(inputs["L0_attn_b_bo"])
        m["pos_bias"] = np.asarray(inputs["L0_attn_b_pos_bias"], np.float32)
        m["b2"] = vec_ld(inputs["L0_ffn_b2"])
        m["ln_g"] = vec_ld(inputs["L0_ffn_ln_g"])
        m["ln_b"] = vec_ld(inputs["L0_ffn_ln_b"])
        m["qs"] = np.ascontiguousarray(
            np.asarray(inputs["L0_ffn_q_scale"], np.float32).T)
        m["ks"] = np.ascontiguousarray(
            np.asarray(inputs["L0_ffn_k_scale"], np.float32).T)
        m.update(consts)
        in_maps.append(m)
    res = run_bass_kernel_spmd(nc, in_maps, core_ids=list(range(8)))
    out = np.stack([np.ascontiguousarray(r["yT"].T) for r in res.results], axis=0)
    return out


# version 10
# speedup vs baseline: 1.0660x; 1.0184x over previous
import contextlib
import numpy as np
import ml_dtypes
import concourse.bass as bass
import concourse.mybir as mybir
from concourse.bass_utils import run_bass_kernel_spmd

F32 = mybir.dt.float32
BF16 = mybir.dt.bfloat16
AF = mybir.ActivationFunctionType
ALU = mybir.AluOpType
BF = ml_dtypes.bfloat16

T = 4096
D = 1024
CH = 8
TT = 512
NTT = T // TT
NQB = 8
KCN = 32
DH = 64
NEG = -1.0e30
HID = 2730
HIDP = 2816
MA = 22
OFFA = 384
WA = 896
OFFF = 384
WF = 1408
BW_B = 8064
WBB = 4480

ENGS = ("t", "v", "s", "p", "q")


def _host_consts():
    c = {}
    f32 = np.float32
    half = DH // 2
    theta = (1.0 / (10000.0 ** (np.arange(half, dtype=f32) / f32(half)))).astype(f32)
    pos = np.arange(T, dtype=f32)
    freqs = (pos[:, None] * theta[None, :]).astype(f32)
    cosr = np.repeat(np.cos(freqs, dtype=f32), 2, axis=1)
    sinr = np.repeat(np.sin(freqs, dtype=f32), 2, axis=1)
    cqa = np.tile(cosr.T, (2, 1)).astype(f32)
    sqa = np.tile(sinr.T, (2, 1)).astype(f32)
    c["tab_a_cq"] = (cqa * f32(0.125)).astype(f32)
    c["tab_a_sq"] = (sqa * f32(0.125)).astype(f32)
    c["tab_a_ck"] = cqa.copy()
    c["tab_a_sk"] = sqa.copy()
    WINDOW = 512
    half_idx = np.arange(0, DH, 2, dtype=f32)
    inv_freq = (1.0 / (10000.0 ** (half_idx / f32(DH)))).astype(f32)
    tpos = np.arange(T, dtype=f32)
    fr = (tpos[:, None] * inv_freq[None, :]).astype(f32)
    fr2 = np.concatenate([fr, fr], axis=1)
    base = ((half_idx + f32(0.4 * DH)) / f32(1.4 * DH)).astype(f32)
    power = ((tpos - f32(T // 2)) / f32(WINDOW // 2)).astype(f32)
    scale = (base[None, :] ** power[:, None]).astype(f32)
    scale2 = np.concatenate([scale, scale], axis=1)
    cosf = np.cos(fr2, dtype=f32)
    sinf = np.sin(fr2, dtype=f32)
    fcq = (cosf * scale2).astype(f32)
    fsq = (sinf * scale2).astype(f32)
    fck = (cosf / scale2).astype(f32)
    fsk = (sinf / scale2).astype(f32)
    c["tab_f_cq"] = np.tile((fcq * f32(8.0)).T, (2, 1)).astype(f32)
    c["tab_f_sq"] = np.tile((fsq * f32(8.0)).T, (2, 1)).astype(f32)
    c["tab_f_ck"] = np.tile(fck.T, (2, 1)).astype(f32)
    c["tab_f_sk"] = np.tile(fsk.T, (2, 1)).astype(f32)
    m = np.arange(WA)[None, :] - np.arange(128)[:, None] - OFFA
    c["band_a"] = np.where(m >= 0, f32(0), f32(NEG)).astype(f32)
    m = np.arange(WF)[None, :] - np.arange(128)[:, None] - OFFF
    c["band_f"] = np.where((m >= 0) & (m <= WINDOW), f32(0), f32(NEG)).astype(f32)
    c["ident_bf"] = np.eye(128, dtype=f32).astype(BF)
    c["ident_f32"] = np.eye(128, dtype=f32)
    c["perm_bf"] = _perm_matrix().astype(BF)
    c["ones_r_f32"] = np.ones((1, 128), dtype=f32)
    c["ones_c_bf"] = np.ones((128, 1), dtype=f32).astype(BF)
    on2 = np.zeros((128, 2), dtype=f32)
    on2[0:64, 0] = 1.0
    on2[64:128, 1] = 1.0
    c["ones2_bf"] = on2.astype(BF)
    b2 = np.zeros((2, 128), dtype=f32)
    b2[0, 0:64] = 1.0
    b2[1, 64:128] = 1.0
    c["b2_f32"] = b2
    return c


def _perm_matrix():
    P = np.zeros((128, 128), dtype=np.float32)
    for blk in (0, 64):
        for d in range(32):
            P[blk + d + 32, blk + d] = -1.0
        for d in range(32, 64):
            P[blk + d - 32, blk + d] = 1.0
    return P


def _host_layout(inputs):
    f32 = np.float32
    L = {}
    x = np.asarray(inputs["L0_attn_a_x"], dtype=f32)
    L["xT"] = [np.ascontiguousarray(x[i].T) for i in range(8)]
    def ocmajor(w, pad_to=None):
        w = np.asarray(w, dtype=f32)
        K, M = w.shape
        if pad_to is not None and M < pad_to:
            w = np.concatenate([w, np.zeros((K, pad_to - M), f32)], axis=1)
            M = pad_to
        n = M // 128
        return np.ascontiguousarray(w.reshape(K, n, 128).transpose(1, 0, 2))
    L["wa_qkv"] = ocmajor(inputs["L0_attn_a_w_qkv"])
    L["wa_proj"] = ocmajor(inputs["L0_attn_a_w_proj"])
    for nm, key in (("wb_q", "L0_attn_b_Wq"), ("wb_k", "L0_attn_b_Wk"),
                    ("wb_v", "L0_attn_b_Wv"), ("wb_o", "L0_attn_b_Wo")):
        L[nm] = ocmajor(np.ascontiguousarray(np.asarray(inputs[key], f32).T))
    wq = np.asarray(inputs["L0_ffn_wqkv"], f32)
    L["wf_qkv"] = np.stack([ocmajor(wq[l]) for l in range(2)])
    wo = np.asarray(inputs["L0_ffn_wo"], f32)
    L["wf_wo"] = np.stack([ocmajor(wo[l]) for l in range(2)])
    w1 = np.asarray(inputs["L0_ffn_w1"], f32)
    L["wf_w1a"] = np.stack([ocmajor(w1[l][:, :HID], pad_to=HIDP) for l in range(2)])
    L["wf_w1g"] = np.stack([ocmajor(w1[l][:, HID:], pad_to=HIDP) for l in range(2)])
    w2 = np.asarray(inputs["L0_ffn_w2"], f32)
    w2p = np.concatenate([w2, np.zeros((2, HIDP - HID, D), f32)], axis=1)
    L["wf_w2"] = np.stack([ocmajor(w2p[l]) for l in range(2)])
    b1 = np.asarray(inputs["L0_ffn_b1"], f32)
    def vec_lma(a):
        return np.ascontiguousarray(a.reshape(2, MA, 128).transpose(2, 0, 1))
    L["b1a"] = vec_lma(np.concatenate(
        [b1[:, :HID], np.zeros((2, HIDP - HID), f32)], axis=1))
    L["b1g"] = vec_lma(np.concatenate(
        [b1[:, HID:], np.zeros((2, HIDP - HID), f32)], axis=1))
    pb = np.asarray(inputs["L0_attn_b_pos_bias"], f32)
    mm = np.arange(BW_B)[None, :] - np.arange(128)[:, None] + 127
    idx = np.minimum(mm, 4094)
    L["band_b"] = np.ascontiguousarray(pb[:, idx])
    return L


class Dep:
    __slots__ = ("key", "c", "g", "k", "e")
    def __init__(self, key, c, g=0, k="cmp", e=None):
        self.key, self.c, self.g, self.k, self.e = key, c, g, k, e


class Sched:
    def __init__(self, nc, es):
        self.nc = nc
        self.es = es
        self.ops = {e: [] for e in ENGS}
        self.cnt = {e: 0 for e in ENGS}
        self.sems = {}
        self.slot_cnt = {}
        self.waited = {}
        self.semgen = 0
        self.dbg_names = None
        self.dbg_tensors = {}
        self.dbg_deps = []
        self._new_semset()

    def _new_semset(self):
        self.semgen += 1
        self.gen = self.semgen
        if self.gen == 1:
            for e in ENGS:
                self.sems[("e", e)] = self.es.enter_context(
                    self.nc.semaphore(f"sm{e}"))
                self.cnt[e] = 0

    def new_phase_sems(self):
        self.barrier()
        self._new_semset()

    def slot_sem(self, slot):
        key = ("s", slot)
        if key not in self.sems:
            self.sems[key] = self.es.enter_context(
                self.nc.semaphore("sl_" + slot))
            self.slot_cnt[key] = 0
        return key

    def dep(self, e, kind=None):
        k = kind or "cmp"
        return Dep(("e", e), self.cnt[e], self.gen, k, e)

    def dma(self, e, slot, fn, waits=()):
        key = self.slot_sem(slot)
        self.slot_cnt[key] += 16
        cnt = self.slot_cnt[key]
        sem = self.sems[key]
        wl = self._translate(e, waits)

        def op(eng):
            for s, c in wl:
                eng.wait_ge(s, c)
            fn(eng).then_inc(sem, 16)
        self.ops[e].append(op)
        return Dep(key, cnt, self.gen, "dma", e)

    def _translate(self, e, waits):
        wl = []
        for d in waits:
            if d is None or d.c <= 0:
                continue
            if d.g != self.gen:
                continue
            wkey = (e, d.key)
            if self.waited.get(wkey, 0) >= d.c:
                continue
            self.waited[wkey] = d.c
            wl.append((self.sems[d.key], d.c))
        return wl

    def emit(self, e, fn, waits=(), inc=0):
        wl = self._translate(e, waits)
        if inc:
            self.cnt[e] += inc
        sem = self.sems[("e", e)] if inc else None
        amt = inc

        def op(eng):
            for s, c in wl:
                eng.wait_ge(s, c)
            ins = fn(eng)
            if amt:
                if ins is None:
                    eng.nop().then_inc(sem, amt)
                else:
                    ins.then_inc(sem, amt)
        self.ops[e].append(op)
        return Dep(("e", e), self.cnt[e], self.gen, "cmp", e) if inc else None

    def wait_only(self, e, waits):
        self.emit(e, lambda eng: None, waits=waits, inc=0)

    def barrier(self):
        deps = [Dep(("e", e), self.cnt[e], self.gen, "bar", e)
                for e in ENGS]
        deps += [Dep(key, self.slot_cnt[key], self.gen, "bar", None)
                 for key in self.slot_cnt]
        for e in ENGS:
            self.wait_only(e, deps)

    def dbg(self, name, shape, dtype, src_ap_fn, waits):
        if self.dbg_names is None or name not in self.dbg_names:
            return
        t = self.nc.dram_tensor("dbg_" + name, list(shape), dtype,
                                kind="ExternalOutput")
        self.dbg_tensors[name] = t
        d = self.dma("q", "dbg_" + name,
                     lambda eng: eng.dma_start(out=t.ap(), in_=src_ap_fn()),
                     waits=waits)
        self.dbg_deps.append(d)
        return d

    def replay(self, block):
        for ename, attr in (("t", "tensor"), ("v", "vector"), ("s", "scalar"),
                            ("p", "gpsimd"), ("q", "sync")):
            def mk(e):
                def body(eng):
                    for f in self.ops[e]:
                        f(eng)
                return body
            getattr(block, attr)(mk(ename))


class Arena:
    def __init__(self, nc, es, nbytes):
        self.h = es.enter_context(nc.sbuf_tensor("arena", [128, nbytes],
                                                 mybir.dt.uint8))
        self.nbytes = nbytes
        self.off = 0
        self.stack = []

    def push(self):
        self.stack.append(self.off)

    def pop(self):
        self.off = self.stack.pop()

    def alloc(self, shape, dtype, name=""):
        bpe = {F32: 4, BF16: 2}[dtype]
        n = 1
        for s in shape[1:]:
            n *= s
        self.off = (self.off + 63) & ~63
        start = self.off
        self.off += n * bpe
        assert self.off <= self.nbytes, \
            f"arena overflow {name}: {self.off} > {self.nbytes}"
        base = self.h[:].bitcast(dtype)
        ap = base[:, start // bpe: start // bpe + n]
        if len(shape) > 2:
            pat = "p (" + " ".join(f"a{i}" for i in range(len(shape) - 1)) + ") -> p " + \
                  " ".join(f"a{i}" for i in range(len(shape) - 1))
            kw = {f"a{i}": shape[i + 1] for i in range(len(shape) - 1)}
            ap = ap.rearrange(pat, **kw)
        return ap


def build_program(dbg_names=None, phase_limit=99):
    nc = bass.Bass("TRN2", target_bir_lowering=False)
    es = contextlib.ExitStack()
    C = {}

    def din(name, shape, dtype=F32):
        C[name] = nc.dram_tensor(name, list(shape), dtype, kind="ExternalInput")
        return C[name]

    def dint(name, shape, dtype):
        C[name] = nc.dram_tensor(name, list(shape), dtype)
        return C[name]

    din("xT", [D, T])
    din("wa_qkv", [24, D, 128]); din("wa_proj", [8, D, 128])
    for nm in ("wb_q", "wb_k", "wb_v", "wb_o"):
        din(nm, [8, D, 128])
    din("bq", [128, CH]); din("bk", [128, CH]); din("bv", [128, CH])
    din("bo", [128, CH])
    din("pos_bias", [16, 4095])
    din("band_b", [16, 128, BW_B])
    din("wf_qkv", [2, 12, D, 128]); din("wf_wo", [2, 8, 512, 128])
    din("wf_w1a", [2, MA, D, 128]); din("wf_w1g", [2, MA, D, 128])
    din("wf_w2", [2, 8, HIDP, 128])
    din("b1a", [128, 2, MA]); din("b1g", [128, 2, MA]); din("b2", [128, 2, CH])
    din("ln_g", [128, 2, CH]); din("ln_b", [128, 2, CH])
    din("qs", [DH, 2]); din("ks", [DH, 2])
    for nm in ("tab_a_cq", "tab_a_sq", "tab_a_ck", "tab_a_sk",
               "tab_f_cq", "tab_f_sq", "tab_f_ck", "tab_f_sk"):
        din(nm, [128, T])
    din("band_a", [128, WA]); din("band_f", [128, WF])
    din("ident_bf", [128, 128], BF16); din("ident_f32", [128, 128], F32)
    din("perm_bf", [128, 128], BF16)
    din("ones_r_f32", [1, 128], F32); din("ones_c_bf", [128, 1], BF16)
    din("ones2_bf", [128, 2], BF16); din("b2_f32", [2, 128], F32)

    C["yT"] = nc.dram_tensor("yT", [D, T], F32, kind="ExternalOutput")
    dint("x1T", [D, T], F32)
    dint("x2T", [D, T], F32)
    dint("x3T", [D, T], F32)
    dint("uT", [D, T], F32)
    dint("u1T", [D, T], F32)
    dint("c_wa_qkv", [24, D, 128], BF16); dint("c_wa_proj", [8, D, 128], BF16)
    for nm in ("wb_q", "wb_k", "wb_v", "wb_o"):
        dint("c_" + nm, [8, D, 128], BF16)
    dint("c_wf_qkv", [2, 12, D, 128], BF16); dint("c_wf_wo", [2, 8, 512, 128], BF16)
    dint("c_wf_w1a", [2, MA, D, 128], BF16); dint("c_wf_w1g", [2, MA, D, 128], BF16)
    dint("c_wf_w2", [2, 8, HIDP, 128], BF16)

    S = Sched(nc, es)
    S.dbg_names = set(dbg_names) if dbg_names else None
    A = Arena(nc, es, 188 * 1024)

    K = {}
    def sball(name, shape, dtype):
        K[name] = es.enter_context(nc.sbuf_tensor(name, list(shape), dtype))
        return K[name]

    for nm, sh, dt in (
        ("ident_bf", [128, 128], BF16), ("ident_f32", [128, 128], F32),
        ("perm_bf", [128, 128], BF16), ("ones_r_f32", [1, 128], F32),
        ("ones_c_bf", [128, 1], BF16), ("ones2_bf", [128, 2], BF16),
        ("b2_f32", [2, 128], F32),
        ("band_a", [128, WA], F32), ("band_f", [128, WF], F32),
        ("bq", [128, CH], F32), ("bk", [128, CH], F32), ("bv", [128, CH], F32),
        ("bo", [128, CH], F32), ("b2b", [128, 2, CH], F32),
        ("lng", [128, 2, CH], F32), ("lnb", [128, 2, CH], F32),
        ("b1a_sb", [128, 2, MA], F32), ("b1g_sb", [128, 2, MA], F32),
        ("qs_sb", [128, 2], F32), ("ks_sb", [128, 2], F32),
        ("pbrow", [1, 16], F32),
        ("mu_row", [1, TT], F32), ("r_row", [1, TT], F32), ("mur_row", [1, TT], F32),
        ("eps1", [1, 1], F32),
    ):
        K[nm] = es.enter_context(nc.sbuf_tensor("sb_" + nm, list(sh), dt))

    PS = [es.enter_context(nc.psum_tensor(f"ps{i}", [128, 512], F32))
          for i in range(8)]

    meta = dict(nc=nc, S=S, A=A, K=K, PS=PS, C=C)

    _emit_wprep(meta)
    _emit_const_loads(meta)
    if phase_limit >= 1:
        A.push()
        xn = A.alloc([128, CH, T], BF16, "xn")
        meta["xn"] = xn
        _emit_ln_stats_apply(meta, src_dram=C["xT"], xn_out=xn, phase_tag="ln1")
    if phase_limit >= 2:
        meta["Oa"] = A.alloc([128, CH, T], BF16, "Oa")
        _emit_phase_attn_a(meta)
        S.dbg("a_o", [128, CH * T], BF16,
              lambda: meta["Oa"].rearrange("p a b -> p (a b)"),
              [meta["attn_a_done"]])
        S.new_phase_sems()
    if phase_limit >= 3:
        _emit_phase_proj_res(meta, meta["Oa"],
                             lambda oc: C["c_wa_proj"].ap()[oc], None,
                             C["xT"], C["x1T"], o_dep=None)
        S.dbg("x1", [D, T], F32, lambda: C["x1T"].ap(), [])
        S.new_phase_sems()
        A.pop()
    if phase_limit >= 4:
        _emit_ln_stats_apply(meta, C["x1T"], None, "ln2", out_dram=C["uT"])
        S.new_phase_sems()
        S.dbg("u_dram", [D, T], F32, lambda: C["uT"].ap(), [])
        _emit_ffn_sublayer(meta, 0, C["uT"], C["u1T"], C["u1T"], None,
                           C["uT"], "f1l0")
        _emit_ffn_sublayer(meta, 1, C["uT"], C["u1T"], C["u1T"], C["x1T"],
                           C["x2T"], "f1l1")
        S.new_phase_sems()
        S.dbg("x2", [D, T], F32, lambda: C["x2T"].ap(), [])
    if phase_limit >= 5:
        A.push()
        xn3 = A.alloc([128, CH, T], BF16, "xn3")
        _emit_ln_stats_apply(meta, C["x2T"], xn3, "ln3")
        meta["Ob"] = A.alloc([128, CH, T], BF16, "Ob")
        S.new_phase_sems()
        _emit_phase_attn_b(meta, xn3)
        S.new_phase_sems()
        _emit_phase_proj_res(meta, meta["Ob"],
                             lambda oc: C["c_wb_o"].ap()[oc], "bo",
                             C["x2T"], C["x3T"], o_dep=None)
        S.new_phase_sems()
        S.dbg("x3", [D, T], F32, lambda: C["x3T"].ap(), [])
        A.pop()
    if phase_limit >= 6:
        _emit_ln_stats_apply(meta, C["x3T"], None, "ln4", out_dram=C["uT"])
        S.new_phase_sems()
        _emit_ffn_sublayer(meta, 0, C["uT"], C["u1T"], C["u1T"], None,
                           C["uT"], "f2l0")
        _emit_ffn_sublayer(meta, 1, C["uT"], C["u1T"], C["u1T"], C["x3T"],
                           C["yT"], "f2l1")
        S.new_phase_sems()
    return nc, es, meta


def _q_dma(S, dst_ap, src_ap, waits=(), slot="misc"):
    return S.dma("q", slot, lambda e: e.dma_start(out=dst_ap, in_=src_ap),
                 waits=waits)


def _p_dma(S, dst_ap, src_ap, waits=(), slot="pmisc"):
    return S.dma("p", slot, lambda e: e.dma_start(out=dst_ap, in_=src_ap),
                 waits=waits)


def _emit_wprep(meta):
    S, C = meta["S"], meta["C"]
    deps = []
    pairs = [("wa_qkv", "c_wa_qkv"), ("wa_proj", "c_wa_proj"),
             ("wb_q", "c_wb_q"), ("wb_k", "c_wb_k"), ("wb_v", "c_wb_v"),
             ("wb_o", "c_wb_o"), ("wf_qkv", "c_wf_qkv"), ("wf_wo", "c_wf_wo"),
             ("wf_w1a", "c_wf_w1a"), ("wf_w1g", "c_wf_w1g"), ("wf_w2", "c_wf_w2")]
    for src, dst in pairs:
        sap = C[src].ap()
        dap = C[dst].ap()
        n = 1
        for s in sap.shape:
            n *= s
        sflat = sap.rearrange(" ".join(f"a{i}" for i in range(len(sap.shape)))
                              + " -> (" + " ".join(f"a{i}" for i in range(len(sap.shape))) + ")")
        dflat = dap.rearrange(" ".join(f"a{i}" for i in range(len(dap.shape)))
                              + " -> (" + " ".join(f"a{i}" for i in range(len(dap.shape))) + ")")
        CHUNK = 1 << 20
        for o in range(0, n, CHUNK):
            ln = min(CHUNK, n - o)
            deps.append(_p_dma(S, dflat[o:o + ln], sflat[o:o + ln],
                               slot="wprep"))
    meta["wprep_dep"] = deps[-1]


def _emit_const_loads(meta):
    S, C, K = meta["S"], meta["C"], meta["K"]
    nc = meta["nc"]
    deps = []
    for nm in ("ident_bf", "ident_f32", "perm_bf", "ones_r_f32", "ones_c_bf",
               "ones2_bf", "b2_f32", "band_a", "band_f"):
        deps.append(_q_dma(S, K[nm][:], C[nm].ap(), slot="const"))
    for nm, dn in (("bq", "bq"), ("bk", "bk"), ("bv", "bv"), ("bo", "bo")):
        deps.append(_q_dma(S, K[nm][:], C[dn].ap(), slot="const"))
    for nm, dn in (("b2b", "b2"), ("lng", "ln_g"), ("lnb", "ln_b")):
        deps.append(_q_dma(S, K[nm][:], C[dn].ap(), slot="const"))
    for nm, dn in (("b1a_sb", "b1a"), ("b1g_sb", "b1g")):
        deps.append(_q_dma(S, K[nm][:], C[dn].ap(), slot="const"))
    for nm in ("qs", "ks"):
        deps.append(_q_dma(S, K[nm + "_sb"][0:64, :], C[nm].ap(), slot="const"))
    meta["A"].push()
    pb_sb = meta["A"].alloc([128, 4095], F32, "pbtmp")
    pbm = meta["A"].alloc([128, 1], F32, "pbm")
    deps.append(_q_dma(S, pb_sb[0:16, :], C["pos_bias"].ap(), slot="const"))
    d = deps[-1]
    dcl = d
    for nm in ("qs_sb", "ks_sb"):
        S.emit("v", lambda e, t=K[nm]: e.tensor_copy(t[64:128, :], t[0:64, :]),
               waits=[d], inc=1)
    meta["eps_dep"] = S.emit("p", lambda e: e.memset(K["eps1"][:, :], 1e-5),
                             inc=1)
    dm = S.emit("v", lambda e: e.reduce_max(pbm[0:16, :], pb_sb[0:16, :],
                                            mybir.AxisListType.X),
                waits=[d], inc=1)
    PS = meta["PS"]
    dt = S.emit("t", lambda e: e.transpose(PS[5][0:1, 0:16], pbm[0:16, 0:1],
                                           K["ident_f32"][0:16, 0:16]),
                waits=[dm, dcl], inc=1)
    S.emit("v", lambda e: e.tensor_copy(K["pbrow"][0:1, 0:16], PS[5][0:1, 0:16]),
           waits=[dt], inc=1)
    meta["A"].pop()
    meta["const_dep"] = dcl
    meta["const_v_dep"] = S.dep("v")


def _emit_ln_stats_apply(meta, src_dram, xn_out, phase_tag, gains_l=None,
                         out_dram=None):
    S, A, K, PS, C = meta["S"], meta["A"], meta["K"], meta["PS"], meta["C"]
    A.push()
    xf = A.alloc([128, 2, CH, TT], F32, "xf")
    xbt = A.alloc([128, 2, CH, TT], BF16, "xbt")
    sq = A.alloc([128, 2, 2, TT], BF16, "sq")
    src = src_dram.ap()
    cdep = meta["const_dep"]
    war_v = {}
    war_t = {}
    prev_apply = {}
    for tt in range(NTT):
        b = tt % 2
        dl = None
        for c in range(CH):
            dl = _q_dma(S, xf[:, b, c, :],
                        src[c * 128:(c + 1) * 128, tt * TT:(tt + 1) * TT],
                        waits=([war_v.get(tt - 2)] if c == 0 else ()),
                        slot=f"lnxf{b}")
        dcast = None
        for c in range(CH):
            dcast = S.emit("v", lambda e, c=c, b=b: e.tensor_copy(
                xbt[:, b, c, :], xf[:, b, c, :]), waits=[dl], inc=1)
        war = prev_apply.get(tt - 1, meta.get("const_v_dep"))
        sqmm = {}
        for c in range(CH):
            dsq = S.emit("s", lambda e, c=c, b=b: e.activation(
                sq[:, b, c % 2, :], xbt[:, b, c, :], AF.Square),
                waits=[dcast, war, sqmm.get(c - 2)], inc=1)
            S.emit("t", lambda e, c=c, b=b: e.matmul(
                PS[6][0:1, :], K["ones_c_bf"][:, :], xbt[:, b, c, :],
                start=(c == 0), stop=(c == CH - 1)),
                waits=[dcast, cdep, war], inc=(1 if c == CH - 1 else 0))
            sqmm[c] = S.emit("t", lambda e, c=c, b=b: e.matmul(
                PS[7][0:1, :], K["ones_c_bf"][:, :], sq[:, b, c % 2, :],
                start=(c == 0), stop=(c == CH - 1)),
                waits=[dsq, war], inc=1)
        dstat = S.dep("t")
        d1 = S.emit("v", lambda e: e.tensor_scalar_mul(
            K["mu_row"][0:1, :], PS[6][0:1, :], 1.0 / D), waits=[dstat], inc=1)
        d2 = S.emit("v", lambda e: e.tensor_scalar_mul(
            K["r_row"][0:1, :], PS[7][0:1, :], 1.0 / D), waits=[dstat], inc=1)
        d3 = S.emit("v", lambda e: e.tensor_mul(
            K["mur_row"][0:1, :], K["mu_row"][0:1, :], K["mu_row"][0:1, :]),
            waits=[d2], inc=1)
        d4 = S.emit("v", lambda e: e.tensor_sub(
            K["r_row"][0:1, :], K["r_row"][0:1, :], K["mur_row"][0:1, :]),
            waits=[d3], inc=1)
        d5 = S.emit("v", lambda e: e.tensor_scalar_max(
            K["r_row"][0:1, :], K["r_row"][0:1, :], 0.0), waits=[d4], inc=1)
        d6 = S.emit("s", lambda e: e.activation(
            K["r_row"][0:1, :], K["r_row"][0:1, :], AF.Ln,
            bias=K["eps1"][0:1, :], scale=1.0),
            waits=[d5, meta.get("eps_dep")], inc=1)
        d7 = S.emit("s", lambda e: e.activation(
            K["r_row"][0:1, :], K["r_row"][0:1, :], AF.Exp, scale=-0.5),
            waits=[d6], inc=1)
        d8 = S.emit("v", lambda e: e.tensor_mul(
            K["mur_row"][0:1, :], K["mu_row"][0:1, :], K["r_row"][0:1, :]),
            waits=[d7], inc=1)
        if tt == 0:
            S.dbg(phase_tag + "_r", [1, TT], F32, lambda: K["r_row"][0:1, :], [d8])
            S.dbg(phase_tag + "_mur", [1, TT], F32,
                  lambda: K["mur_row"][0:1, :], [d8])
        dr = S.emit("t", lambda e: e.matmul(
            PS[6][:, :], K["ones_r_f32"][:, :], K["r_row"][0:1, :],
            start=True, stop=True), waits=[d8], inc=1)
        dm = S.emit("t", lambda e: e.matmul(
            PS[7][:, :], K["ones_r_f32"][:, :], K["mur_row"][0:1, :],
            start=True, stop=True), waits=[d8], inc=1)
        dv = None
        for c in range(CH):
            dv1 = S.emit("v", lambda e, c=c, b=b: e.tensor_mul(
                xf[:, b, c, :], xbt[:, b, c, :], PS[6][:, :]),
                waits=[dr], inc=1)
            if out_dram is not None:
                dv = S.emit("v", lambda e, c=c, b=b: e.tensor_sub(
                    xf[:, b, c, :], xf[:, b, c, :], PS[7][:, :]),
                    waits=[dm, dv1], inc=1)
                dv = _q_dma(S, out_dram.ap()[c * 128:(c + 1) * 128,
                                             tt * TT:(tt + 1) * TT],
                            xf[:, b, c, :], waits=[dv], slot=f"lnout{b}")
            elif gains_l is None:
                dv = S.emit("v", lambda e, c=c, b=b, tt=tt: e.tensor_sub(
                    xn_out[:, c, tt * TT:(tt + 1) * TT], xf[:, b, c, :],
                    PS[7][:, :]), waits=[dm, dv1], inc=1)
            else:
                dv2 = S.emit("v", lambda e, c=c, b=b: e.tensor_sub(
                    xf[:, b, c, :], xf[:, b, c, :], PS[7][:, :]),
                    waits=[dm, dv1], inc=1)
                dv = S.emit("v", lambda e, c=c, b=b, tt=tt, l=gains_l:
                    e.tensor_scalar(
                        xn_out[:, c, tt * TT:(tt + 1) * TT], xf[:, b, c, :],
                        K["lng"][:, l, c:c + 1], K["lnb"][:, l, c:c + 1],
                        ALU.mult, ALU.add), waits=[dv2], inc=1)
        war_v[tt] = dv
        prev_apply[tt] = dv
        meta.setdefault(phase_tag + "_tiles", {})[tt] = dv
    if xn_out is not None:
        S.dbg(phase_tag + "_xn", [128, CH * T], BF16,
              lambda: xn_out.rearrange("p a b -> p (a b)"), [S.dep("v")])
    A.pop()
    meta[phase_tag + "_done"] = S.dep("v")


def _emit_pair_qkv_a(meta, P, hp):
    S, A, K, PS, C = meta["S"], meta["A"], meta["K"], meta["PS"], meta["C"]
    cdep = meta["const_dep"]
    wdep = None
    for j in range(3):
        srcw = P["wsrc"](j)
        sap = bass.AP(srcw.tensor, srcw.offset,
                      [[128, 128], [128 * 128, CH], [1, 128]])
        wdep = _q_dma(S, P["wq"][:, j, :, :], sap,
                      waits=[meta["wprep_dep"], P.get("scope_war")],
                      slot="wq")
    if P.get("dbg_pair"):
        dd = S.dbg("pp_wq", [128, 3 * CH * 128], BF16,
                   lambda: P["wq"].rearrange("p a b c -> p (a b c)"), [wdep])
        dd2 = S.dbg("pp_ub0", [128, CH, TT], BF16,
                    lambda: P["rhs_dbg"](), [P["rhs_dep"]])
    V4 = P["V"]
    for tt in range(NTT):
        ts = slice(tt * TT, (tt + 1) * TT)
        b = tt % 2

        qbank = 0 if tt % 2 == 0 else 6
        kbank = 1 if tt % 2 == 0 else 7
        for psn, j in ((qbank, 0), (kbank, 1), (2, 2)):
            for kc in range(CH):
                rdep = (P["rhs_tiles"][min(tt + 1, NTT - 1)]
                        if P.get("rhs_tiles") else P["rhs_dep"])
                S.emit("t", lambda e, psn=psn, j=j, kc=kc, tt=tt: e.matmul(
                    PS[psn][:, :], P["wq"][:, j, kc, :],
                    P["rhs_fn"](kc, tt),
                    start=(kc == 0), stop=(kc == CH - 1)),
                    waits=[wdep, rdep, P["bank"].get(psn)],
                    inc=(1 if kc == CH - 1 else 0))
            if j == 0:
                dqp = S.dep("t")
            elif j == 1:
                dkp = S.dep("t")
            else:
                dvp = S.dep("t")
        dvf = S.emit("v", lambda e: e.tensor_copy(P["vt_f32"][:, 0, :],
                                                  PS[2][:, :]),
                     waits=[dvp, P["war_vt"].get(tt - 1)], inc=1)
        P["bank"][2] = dvf
        dtr = None
        for j in range(4):
            dtr = S.emit("t", lambda e, j=j, b=b: e.transpose(
                PS[3][:, j * 128:(j + 1) * 128],
                P["vt_f32"][:, 0, j * 128:(j + 1) * 128],
                K["ident_f32"][:, :]),
                waits=[dvf, cdep, P["war_ps3"].get(tt - 1),
                       P["bank"].get(3)], inc=1)
        P["war_vt"][tt] = dtr
        dve = None
        for h in range(2):
            vout = V4[:, h, tt * 4:(tt + 1) * 4, 0:64]
            vin = bass.AP(PS[3], 64 * h, [[512, 128], [128, 4], [1, 64]])
            dve = S.emit("v", lambda e, vout=vout, vin=vin: e.tensor_copy(
                vout, vin), waits=[dtr, P.get("war_v4")], inc=1)
        P["war_ps3"][tt] = dve
        P["bank"][3] = dve
        for name, psn, tcos, tsin, dst, nrm in (
                ("q", qbank, 0, None, P["Qt"], P["qn"]),
                ("k", kbank, 2, None, P["Kt"], P["kn"])):
            dep_p = dqp if name == "q" else dkp
            tdep = None
            for j in range(2):
                tdep = _q_dma(S, P["tab"][:, j, :],
                              C[P["tabs"][tcos + j]].ap()[:, ts],
                              waits=[P.get("war_tab_l"), P.get("scope_war")],
                              slot="tab")
            dcast = S.emit("v", lambda e, psn=psn, b=b: e.tensor_copy(
                P["qb"][:, b, :], PS[psn][:, :]),
                waits=[dep_p, P.get("war_qb_l")], inc=1)
            P["bank"][psn] = dcast
            if P.get("dbg_pair") and tt == 0:
                dd = S.dbg(f"pp_{name}_raw", [128, TT], BF16,
                           lambda: P["qb"][:, 0, :], [dcast])
                if dd: dcast = dd
            if P.get("l2norm"):
                dcast = _emit_l2norm(meta, P, dcast, b, name)
                if P.get("dbg_pair") and tt == 0:
                    dd = S.dbg(f"pp_{name}_l2", [128, TT], BF16,
                               lambda: P["qb"][:, 0, :], [dcast])
                    if dd: dcast = dd
            drot = S.emit("t", lambda e, b=b: e.matmul(
                PS[4][:, :], K["perm_bf"][:, :], P["qb"][:, b, :],
                start=True, stop=True),
                waits=[dcast, cdep, P.get("war_rot_l"), P["bank"].get(4)],
                inc=1)
            dt1 = S.emit("v", lambda e, b=b: e.tensor_mul(
                P["t1"][:, :], P["qb"][:, b, :], P["tab"][:, 0, :]),
                waits=[tdep, dcast], inc=1)
            dt2 = S.emit("v", lambda e, b=b: e.tensor_mul(
                P["t2"][:, :], PS[4][:, :], P["tab"][:, 1, :]),
                waits=[drot, dt1], inc=1)
            P["war_rot_l"] = dt2
            P["war_tab_l"] = dt2
            P["bank"][4] = dt2
            dqt = S.emit("v", lambda e, dst=dst, tt=tt: e.tensor_add(
                dst[:, tt * TT:(tt + 1) * TT], P["t1"][:, :], P["t2"][:, :]),
                waits=[dt2, P.get("war_qt")], inc=1)
            if P.get("dbg_pair") and tt == 0:
                dd = S.dbg(f"pp_{name}_t1", [128, TT], F32,
                           lambda: P["t1"][:, :], [dt2])
                dd = S.dbg(f"pp_{name}_t2", [128, TT], F32,
                           lambda: P["t2"][:, :], [dqt])
                if dd:
                    P["war_rot_l"] = dd
                    P["war_qb_l"] = dd
                    P["war_tab_l"] = dd
            P["war_qb_l"] = drot
            dsqn = S.emit("s", lambda e, dst=dst, tt=tt, b=b: e.activation(
                P["sqn"][:, b, :], dst[:, tt * TT:(tt + 1) * TT], AF.Square),
                waits=[dqt, P.get("war_sqn_l")], inc=1)
            dnm = S.emit("t", lambda e, b=b: e.matmul(
                PS[5][0:2, :], K["ones2_bf"][:, :], P["sqn"][:, b, :],
                start=True, stop=True),
                waits=[dsqn, cdep, P.get("war_ps5_l"), P["bank"].get(5)],
                inc=1)
            P["war_sqn_l"] = dnm
            if P.get("l2norm"):
                n32 = P["qn32"] if name == "q" else P["kn32"]
                for j4 in range(4):
                    dnv = S.emit("v", lambda e, n32=n32, tt=tt, j4=j4:
                        e.reduce_max(n32[0:2, 4 + 4 * tt + j4:5 + 4 * tt + j4],
                                     PS[5][0:2, j4 * 128:(j4 + 1) * 128],
                                     mybir.AxisListType.X),
                        waits=[dnm, P.get("war_norm_v"),
                               P.get("war_norm_s")], inc=1)
            else:
                dnv = S.emit("v", lambda e, nrm=nrm, tt=tt: e.reduce_max(
                    nrm[0:2, tt:tt + 1], PS[5][0:2, :], mybir.AxisListType.X),
                    waits=[dnm, P.get("war_norm_v"),
                           P.get("war_norm_s")], inc=1)
            P["war_ps5_l"] = dnv
            P["bank"][5] = dnv


def _emit_l2norm(meta, P, dcast, b, name):
    S, K, PS = meta["S"], meta["K"], meta["PS"]
    cdep = meta["const_dep"]
    svec = K["qs_sb"] if name == "q" else K["ks_sb"]
    l = P["ffn_l"]
    dsq = S.emit("s", lambda e, b=b: e.activation(
        P["sqn"][:, b, :], P["qb"][:, b, :], AF.Square),
        waits=[dcast, P.get("war_sqn_l")], inc=1)
    dnm = S.emit("t", lambda e, b=b: e.matmul(
        PS[5][0:2, :], K["ones2_bf"][:, :], P["sqn"][:, b, :],
        start=True, stop=True),
        waits=[dsq, cdep, P.get("war_ps5_l"), P["bank"].get(5)], inc=1)
    P["war_sqn_l"] = dnm
    dmx = S.emit("v", lambda e: e.tensor_scalar_max(
        P["ssc"][0:2, :], PS[5][0:2, :], 1e-24),
        waits=[dnm, P.get("war_ssc")], inc=1)
    P["war_ps5_l"] = dmx
    dln = S.emit("s", lambda e: e.activation(P["ssc"][0:2, :], P["ssc"][0:2, :],
                                             AF.Ln), waits=[dmx], inc=1)
    dex = S.emit("s", lambda e: e.activation(P["ssc"][0:2, :], P["ssc"][0:2, :],
                                             AF.Exp, scale=-0.5),
                 waits=[dln], inc=1)
    drb = S.emit("t", lambda e: e.matmul(
        PS[5][:, :], K["b2_f32"][0:2, :], P["ssc"][0:2, :],
        start=True, stop=True),
        waits=[dex, cdep, P.get("war_ps5_l"), P["bank"].get(5)], inc=1)
    P["war_ssc"] = drb
    dsc = S.emit("v", lambda e, b=b, svec=svec, l=l: e.scalar_tensor_tensor(
        P["qb"][:, b, :], P["qb"][:, b, :], svec[:, l:l + 1], PS[5][:, :],
        ALU.mult, ALU.mult), waits=[drb, meta["const_v_dep"]], inc=1)
    P["war_ps5_l"] = dsc
    P["bank"][5] = dsc
    return dsc


def _emit_attention(meta, P, hp, kind):
    S, A, K, PS, C = meta["S"], meta["A"], meta["K"], meta["PS"], meta["C"]
    cdep = meta["const_dep"]
    for qb in range(NQB):
        qs_ = slice(qb * TT, (qb + 1) * TT)
        dbc = {}

        def bcast_negc(dneg):
            for h in range(2):
                db = S.emit("t", lambda e, h=h: e.matmul(
                    PS[7][:, 500 + h:501 + h], K["ones_r_f32"][:, :],
                    P["negc_row"][0:1, h:h + 1], start=True, stop=True),
                    waits=[dneg, P.get("war_negsb"), P["bank"].get(7)],
                    inc=1)
                dbc[h] = S.emit("v", lambda e, h=h: e.tensor_copy(
                    P["negc_sb"][:, h:h + 1], PS[7][:, 500 + h:501 + h]),
                    waits=[db], inc=1)

        if kind != "f":
            if kind == "a":
                if qb == 0:
                    dk = S.emit("v", lambda e: e.tensor_copy(
                        P["knr"][0:2, :], P["kn"][0:2, 0:1]),
                        waits=[P["norm_dep"], P.get("war_norm_s")], inc=1)
                else:
                    dk = S.emit("v", lambda e, qb=qb: e.tensor_max(
                        P["knr"][0:2, :], P["knr"][0:2, :],
                        P["kn"][0:2, qb:qb + 1]),
                        waits=[P["norm_dep"], P.get("knr_war")], inc=1)
            else:
                if qb == 0:
                    dk = S.emit("v", lambda e: e.reduce_max(
                        P["knr"][0:2, :], P["kn"][0:2, :],
                        mybir.AxisListType.X),
                        waits=[P["norm_dep"], P.get("war_norm_s")], inc=1)
                else:
                    dk = P["knr_dep"]
            P["knr_dep"] = dk
            d1 = S.emit("s", lambda e, qb=qb: e.activation(
                P["c1"][0:2, 0:1], P["qn"][0:2, qb:qb + 1], AF.Ln),
                waits=[dk, P["norm_dep"]], inc=1)
            d2 = S.emit("s", lambda e: e.activation(
                P["c2"][0:2, 0:1], P["knr"][0:2, :], AF.Ln),
                waits=[dk, d1], inc=1)
            d3 = S.emit("v", lambda e: e.tensor_add(P["c1"][0:2, 0:1],
                                                    P["c1"][0:2, 0:1],
                                                    P["c2"][0:2, 0:1]),
                        waits=[d1, d2], inc=1)
            P["knr_war"] = d3
            d4 = S.emit("s", lambda e: e.activation(P["c1"][0:2, 0:1],
                                                    P["c1"][0:2, 0:1],
                                                    AF.Exp, scale=0.5),
                        waits=[d3], inc=1)
            dtr = S.emit("t", lambda e: e.transpose(PS[7][0:1, 504:506],
                                                    P["c1"][0:2, 0:1],
                                                    K["ident_f32"][0:2, 0:2]),
                         waits=[d4, cdep, P.get("war_negps"),
                                P["bank"].get(7)], inc=1)
            if kind == "b":
                dtr = S.emit("v", lambda e, hp=hp: e.tensor_add(
                    PS[7][0:1, 504:506], PS[7][0:1, 504:506],
                    K["pbrow"][0:1, 2 * hp:2 * hp + 2]),
                    waits=[dtr, meta["const_v_dep"]], inc=1)
            dneg = S.emit("v", lambda e: e.tensor_scalar_mul(
                P["negc_row"][0:1, 0:2], PS[7][0:1, 504:506], -1.0),
                waits=[dtr], inc=1)
            P["war_negps"] = dneg
            bcast_negc(dneg)
            P["war_negsb"] = dbc[1]
            P["bank"][7] = dbc[1]

        if kind == "a":
            kcs = [(kc, "vis" if kc < 4 * qb else "band",
                    512 * qb - 128 * kc + OFFA) for kc in range(4 * qb + 4)]
            nk = len(kcs)
            dS_l, dP_l = {}, {}
            SBK = ((0, 2, 6), (1, 3, 7))

            def emit_S(i):
                kc, kd, c0 = kcs[i]
                for h in range(2):
                    psn = SBK[h][i % 3]
                    dS_l[(i, h)] = S.emit(
                        "t", lambda e, psn=psn, h=h, kc=kc, qs_=qs_: e.matmul(
                            PS[psn][:, :], P["Kt"][64 * h:64 * (h + 1),
                                                   kc * 128:(kc + 1) * 128],
                            P["Qt"][64 * h:64 * (h + 1), qs_],
                            start=True, stop=True),
                        waits=[P["bank"].get(psn), P["qkv_dep"]], inc=1)

            def emit_PX(i):
                kc, kd, c0 = kcs[i]
                p = i % 2
                for h in range(2):
                    psn = SBK[h][i % 3]
                    dS = dS_l[(i, h)]
                    if kd == "vis":
                        dP = S.emit("s", lambda e, psn=psn, h=h, p=p:
                            e.activation(
                                P["pt"][:, 2 * h + p, :], PS[psn][:, :], AF.Exp,
                                bias=P["negc_sb"][:, h:h + 1], scale=1.0),
                            waits=[dS, dbc[h], P["war_pt"].get((h, p))], inc=1)
                    else:
                        band_ap = K["band_a"][:, c0:c0 + TT]
                        dsm = S.emit("v", lambda e, psn=psn, p=p,
                                     band_ap=band_ap:
                            e.tensor_add(P["sm"][:, p, :], PS[psn][:, :],
                                         band_ap),
                            waits=[dS, cdep, P["war_sm"].get(p)], inc=1)
                        dP = S.emit("s", lambda e, h=h, p=p: e.activation(
                            P["pt"][:, 2 * h + p, :], P["sm"][:, p, :],
                            AF.Exp, bias=P["negc_sb"][:, h:h + 1], scale=1.0),
                            waits=[dsm, dbc[h], P["war_pt"].get((h, p))], inc=1)
                        P["war_sm"][p] = dP
                    P["bank"][psn] = dP
                    dP_l[(i, h)] = dP

            def emit_AV(i):
                kc, kd, c0 = kcs[i]
                p = i % 2
                for h in range(2):
                    dav = S.emit("t", lambda e, h=h, p=p, kc=kc, i=i, nk=nk:
                        e.matmul(
                            PS[4 + h][0:65, :], P["V"][:, h, kc, :],
                            P["pt"][:, 2 * h + p, :],
                            start=(i == 0), stop=(i == nk - 1)),
                        waits=[dP_l[(i, h)], P["bank"].get(4 + h)
                               if i == 0 else None], inc=1)
                    P["war_pt"][(h, p)] = dav

            emit_S(0)
            if nk > 1:
                emit_S(1)
            for i in range(nk):
                emit_PX(i)
                if i + 2 < nk:
                    emit_S(i + 2)
                emit_AV(i)
        elif kind == "f":
            qp = qb % 2
            kcs = [(kc, 512 * qb - 128 * kc + OFFF)
                   for kc in range(max(0, 4 * (qb - 1)), 4 * (qb + 1))]
            nk = len(kcs)
            dmx = None
            for i, (kc, c0) in enumerate(kcs):
                for h in range(2):
                    psn = 2 * h + (i % 2)
                    dS = S.emit("t", lambda e, psn=psn, h=h, kc=kc, qs_=qs_:
                        e.matmul(
                            PS[psn][:, :], P["Kt"][64 * h:64 * (h + 1),
                                                   kc * 128:(kc + 1) * 128],
                            P["Qt"][64 * h:64 * (h + 1), qs_],
                            start=True, stop=True),
                        waits=[P["bank"].get(psn), P["qkv_dep"]], inc=1)
                    dsm = S.emit("v", lambda e, psn=psn, h=h, i=i, c0=c0,
                                 qp=qp:
                        e.tensor_add(P["smB"][:, qp, h, i, :], PS[psn][:, :],
                                     K["band_f"][:, c0:c0 + TT]),
                        waits=[dS, cdep, P["war_smB"].get((qp, h, i))], inc=1)
                    P["bank"][psn] = dsm
                    dmx = S.emit("v", lambda e, h=h, i=i, qp=qp: e.reduce_max(
                        P["rmx"][:, qp, h, i:i + 1], P["smB"][:, qp, h, i, :],
                        mybir.AxisListType.X), waits=[dsm], inc=1)
            for h in range(2):
                dm2 = S.emit("v", lambda e, h=h, nk=nk, qp=qp: e.reduce_max(
                    P["rm1"][:, qp, h:h + 1], P["rmx"][:, qp, h, 0:nk],
                    mybir.AxisListType.X), waits=[dmx], inc=1)
                dtr = S.emit("t", lambda e, h=h: e.transpose(
                    PS[7][0:1, 360:488],
                    P["rm1"][:, qp, h:h + 1], K["ident_f32"][:, :]),
                    waits=[dm2, cdep, P.get("war_negps"),
                           P["bank"].get(7)], inc=1)
                dcx = S.emit("v", lambda e, h=h, qp=qp: e.reduce_max(
                    P["negc_row"][0:1, h:h + 1], PS[7][0:1, 360:488],
                    mybir.AxisListType.X), waits=[dtr], inc=1)
                P["war_negps"] = dcx
                P["bank"][7] = dcx
            dneg = S.emit("v", lambda e: e.tensor_scalar_mul(
                P["negc_row"][0:1, 0:2], P["negc_row"][0:1, 0:2], -1.0),
                waits=[P["war_negps"]], inc=1)
            P["war_negps"] = dneg
            for h in range(2):
                db = S.emit("t", lambda e, h=h: e.matmul(
                    PS[7][:, 500 + h:501 + h], K["ones_r_f32"][:, :],
                    P["negc_row"][0:1, h:h + 1], start=True, stop=True),
                    waits=[dneg, P.get("war_negsb"), P["bank"].get(7)],
                    inc=1)
                dbc[h] = S.emit("v", lambda e, h=h, qp=qp: e.tensor_copy(
                    P["negc2"][:, qp, h:h + 1], PS[7][:, 500 + h:501 + h]),
                    waits=[db], inc=1)
            P["war_negsb"] = dbc[1]
            P["bank"][7] = dbc[1]
            P["fq"][qb] = (kcs, dict(dbc))
            todo = [qb - 1] if qb > 0 else []
            if qb == NQB - 1:
                todo.append(qb)
            for q2 in todo:
                kcs2, dbc2 = P["fq"][q2]
                q2p = q2 % 2
                nk2 = len(kcs2)
                for i, (kc, c0) in enumerate(kcs2):
                    for h in range(2):
                        p = i % 2
                        dP = S.emit("s", lambda e, h=h, i=i, p=p, q2p=q2p:
                            e.activation(
                                P["pt"][:, 2 * h + p, :],
                                P["smB"][:, q2p, h, i, :], AF.Exp,
                                bias=P["negc2"][:, q2p, h:h + 1], scale=1.0),
                            waits=[dbc2[h], P["war_pt"].get((h, p))], inc=1)
                        P["war_smB"][(q2p, h, i)] = dP
                        dav = S.emit("t", lambda e, h=h, p=p, kc=kc, i=i,
                                     nk2=nk2:
                            e.matmul(
                                PS[4 + h][0:65, :], P["V"][:, h, kc, :],
                                P["pt"][:, 2 * h + p, :],
                                start=(i == 0), stop=(i == nk2 - 1)),
                            waits=[dP, P["bank"].get(4 + h)
                                   if i == 0 else None], inc=1)
                        P["war_pt"][(h, p)] = dav
                davs = S.dep("t")
                for h in range(2):
                    drr = S.emit("v", lambda e, h=h: e.reciprocal(
                        P["rrow"][0:1, :], PS[4 + h][64:65, :]),
                        waits=[davs], inc=1)
                    dg = S.emit("t", lambda e: e.matmul(
                        PS[6][0:64, :], K["ones_r_f32"][0:1, 0:64],
                        P["rrow"][0:1, :], start=True, stop=True),
                        waits=[drr, P.get("war_gam"), P["bank"].get(6)],
                        inc=1)
                    dgc = S.emit("v", lambda e: e.tensor_copy(
                        P["gam"][0:64, :], PS[6][0:64, :]), waits=[dg], inc=1)
                    P["war_gam"] = dgc
                    P["bank"][6] = dgc
                    dst_ap = P["o_dst"](q2, h)
                    dout = S.emit("v", lambda e, h=h, dst_ap=dst_ap:
                        e.tensor_mul(dst_ap, PS[4 + h][0:64, :],
                                     P["gam"][0:64, :]), waits=[dgc], inc=1)
                    P["bank"][4 + h] = dout
                P["war_negsb"] = S.dep("v")
            P["o_done"] = S.dep("v")
            continue
        else:
            dS_l, dP_l, dband_l = {}, {}, {}

            def b_band(h, si):
                base = 3840 - 256 * si
                srcb = C["band_b"].ap()[2 * hp + h]
                bap = bass.AP(srcb.tensor, srcb.offset + qb * TT + base,
                              [[BW_B, 128], [1, 640]])
                dband_l[(h, si)] = _q_dma(
                    S, P["wband"][:, h, si % 2, :], bap,
                    waits=[P["war_band"].get((h, si % 2))],
                    slot=f"band{h}{si % 2}")

            SBK = ((0, 2, 6), (1, 3, 7))

            def b_S(kc):
                for h in range(2):
                    psn = SBK[h][kc % 3]
                    dS_l[(kc, h)] = S.emit(
                        "t", lambda e, psn=psn, h=h, kc=kc, qs_=qs_: e.matmul(
                            PS[psn][:, :], P["Kt"][64 * h:64 * (h + 1),
                                                   kc * 128:(kc + 1) * 128],
                            P["Qt"][64 * h:64 * (h + 1), qs_],
                            start=True, stop=True),
                        waits=[P["bank"].get(psn), P["qkv_dep"]], inc=1)

            def b_PX(kc):
                p = kc % 2
                si = kc // 2
                base = 3840 - 256 * si
                c0 = 3968 - 128 * kc - base
                for h in range(2):
                    psn = SBK[h][kc % 3]
                    dsm = S.emit("v", lambda e, psn=psn, c0=c0, h=h, si=si:
                        e.tensor_add(PS[psn][:, :], PS[psn][:, :],
                                     P["wband"][:, h, si % 2, c0:c0 + TT]),
                        waits=[dS_l[(kc, h)], dband_l[(h, si)]], inc=1)
                    P["war_band"][(h, si % 2)] = dsm
                    dP = S.emit("s", lambda e, psn=psn, h=h, p=p: e.activation(
                        P["pt"][:, 2 * h + p, :], PS[psn][:, :],
                        AF.Exp, bias=P["negc_sb"][:, h:h + 1], scale=1.0),
                        waits=[dsm, dbc[h], P["war_pt"].get((h, p))], inc=1)
                    P["bank"][psn] = dP
                    dP_l[(kc, h)] = dP

            def b_AV(kc):
                p = kc % 2
                for h in range(2):
                    dav = S.emit("t", lambda e, h=h, p=p, kc=kc: e.matmul(
                        PS[4 + h][0:65, :], P["V"][:, h, kc, :],
                        P["pt"][:, 2 * h + p, :],
                        start=(kc == 0), stop=(kc == KCN - 1)),
                        waits=[dP_l[(kc, h)], P["bank"].get(4 + h)
                               if kc == 0 else None], inc=1)
                    P["war_pt"][(h, p)] = dav

            b_band(0, 0)
            b_band(1, 0)
            b_band(0, 1)
            b_band(1, 1)
            b_S(0)
            b_S(1)
            for kc in range(KCN):
                if kc % 2 == 0 and kc >= 2 and kc // 2 + 1 <= 15:
                    b_band(0, kc // 2 + 1)
                    b_band(1, kc // 2 + 1)
                b_PX(kc)
                if kc + 2 < KCN:
                    b_S(kc + 2)
                b_AV(kc)
        davs = S.dep("t")
        for h in range(2):
            drr = S.emit("v", lambda e, h=h: e.reciprocal(
                P["rrow"][0:1, :], PS[4 + h][64:65, :]), waits=[davs], inc=1)
            dg = S.emit("t", lambda e: e.matmul(
                PS[6][0:64, :], K["ones_r_f32"][0:1, 0:64], P["rrow"][0:1, :],
                start=True, stop=True),
                waits=[drr, P.get("war_gam"), P["bank"].get(6)], inc=1)
            dgc = S.emit("v", lambda e: e.tensor_copy(P["gam"][0:64, :],
                                                      PS[6][0:64, :]),
                         waits=[dg], inc=1)
            P["war_gam"] = dgc
            P["bank"][6] = dgc
            dst_ap = P["o_dst"](qb, h)
            dout = S.emit("v", lambda e, h=h, dst_ap=dst_ap: e.tensor_mul(
                dst_ap, PS[4 + h][0:64, :], P["gam"][0:64, :]),
                waits=[dgc], inc=1)
            P["bank"][4 + h] = dout
        P["war_negsb"] = S.dep("v")
        P["o_done"] = S.dep("v")


def _mk_attn_bufs(meta, nkc=KCN, rope=True, nkin=CH, fine=False):
    A = meta["A"]
    P = {}
    if fine:
        P["smB"] = A.alloc([128, 2, 2, 8, TT], BF16, "smB")
        P["rmx"] = A.alloc([128, 2, 2, 8], F32, "rmx")
        P["rm1"] = A.alloc([128, 2, 2], F32, "rm1")
        P["negc2"] = A.alloc([128, 2, 2], F32, "negc2")
        P["war_smB"] = {}
        P["fq"] = {}
    P["wq"] = A.alloc([128, 3, nkin, 128], BF16, "wq")
    if rope:
        P["tab"] = A.alloc([128, 2, TT], F32, "tab")
        P["t1"] = A.alloc([128, TT], F32, "t1")
        P["t2"] = A.alloc([128, TT], F32, "t2")
        P["ssc"] = A.alloc([128, TT], F32, "ssc")
    P["vt_f32"] = A.alloc([128, 1, TT], F32, "vtf")
    if rope:
        P["qb"] = A.alloc([128, 2, TT], BF16, "qbb")
    P["sqn"] = A.alloc([128, 2, TT], BF16, "sqn")
    if rope:
        P["sm"] = A.alloc([128, 2, TT], F32, "sm")
    P["Qt"] = A.alloc([128, T], BF16, "Qt")
    P["Kt"] = A.alloc([128, T], BF16, "Kt")
    P["V"] = A.alloc([128, 2, nkc, 65], BF16, "Vsb")
    sm_all = A.alloc([128, 112], F32, "smalls")
    P["qn"] = sm_all[:, 0:8]
    P["kn"] = sm_all[:, 8:16]
    P["qn32"] = sm_all[:, 16:56]
    P["kn32"] = sm_all[:, 56:96]
    P["acc4"] = sm_all[:, 96:100]
    P["t4"] = sm_all[:, 100:104]
    P["knr"] = sm_all[:, 104:105]
    P["c1"] = sm_all[:, 105:106]
    P["c2"] = sm_all[:, 106:107]
    P["negc_row"] = sm_all[:, 107:109]
    P["negc_sb"] = sm_all[:, 109:111]
    P["rrow"] = A.alloc([128, TT], F32, "rrow")
    P["gam"] = A.alloc([128, TT], F32, "gam")
    P["pt"] = A.alloc([128, 4, TT], BF16, "pt")
    for nmm in ("war_tab", "bank", "war_vt", "war_ps3", "war_sm",
                "war_pt", "war_o"):
        P[nmm] = {}
    return P


def _emit_phase_attn_a(meta):
    S, A, C = meta["S"], meta["A"], meta["C"]
    O = meta["Oa"]
    A.push()
    P = _mk_attn_bufs(meta)
    P["tabs"] = ("tab_a_cq", "tab_a_sq", "tab_a_ck", "tab_a_sk")
    xn = meta["xn"]
    P["rhs_fn"] = lambda kc, tt: xn[:, kc, tt * TT:(tt + 1) * TT]
    P["rhs_dbg"] = lambda: xn[:, :, 0:TT]
    P["rhs_dep"] = meta["ln1_done"]
    P["rhs_tiles"] = meta.get("ln1_tiles")
    P["scope_war"] = meta["ln1_done"]
    for i_ in range(8):
        P["bank"][i_] = meta["ln1_done"]
    S.emit("p", lambda e: e.memset(P["V"][:, :, :, 64:65], 1.0),
           waits=[P["scope_war"]], inc=1)
    donev = S.dep("p")
    S.wait_only("t", [donev])
    for hp in range(CH):
        P["wsrc"] = (lambda hp: lambda j: C["c_wa_qkv"].ap()[
            (hp, 8 + hp, 16 + hp)[j]])(hp)
        _emit_pair_qkv_a(meta, P, hp)
        P["qkv_dep"] = S.dep("v")
        P["norm_dep"] = S.dep("v")
        if hp == 0:
            S.dbg("a_qt0", [128, T], BF16, lambda: P["Qt"][:, :], [P["qkv_dep"]])
            S.dbg("a_kt0", [128, T], BF16, lambda: P["Kt"][:, :], [P["qkv_dep"]])
            S.dbg("a_v0", [128, 2 * KCN * 65], BF16,
                  lambda: P["V"].rearrange("p a b c -> p (a b c)"),
                  [P["qkv_dep"]])
            S.dbg("a_qn0", [128, NTT], F32, lambda: P["qn"][:, :],
                  [P["norm_dep"]])
        P["o_dst"] = (lambda hp: lambda qb, h: O[64 * h:64 * h + 64, hp,
                                                 qb * TT:(qb + 1) * TT])(hp)
        _emit_attention(meta, P, hp, "a")
        P["war_qt"] = S.dep("t")
        P["war_v4"] = S.dep("t")
        P["war_norm_v"] = S.dep("v")
        P["war_norm_s"] = S.dep("s")
        P.pop("knr_dep", None)
        P.pop("knr_war", None)
    S.barrier()
    A.pop()
    meta["attn_a_done"] = S.dep("v")


def _emit_phase_proj_res(meta, O, w_oc_fn, bias_name, src_dram, dst_dram,
                         kcs=CH, o_dep=None):
    S, A, K, PS, C = meta["S"], meta["A"], meta["K"], meta["PS"], meta["C"]
    A.push()
    wp = A.alloc([128, CH, kcs, 128], BF16, "wp")
    xold = A.alloc([128, 2, TT], F32, "xold")
    xnew = A.alloc([128, 2, TT], F32, "xnew")
    wdep = None
    for oc in range(CH):
        srcw = w_oc_fn(oc)
        sap = bass.AP(srcw.tensor, srcw.offset,
                      [[128, 128], [128 * 128, kcs], [1, 128]])
        wdep = _q_dma(S, wp[:, oc, :, :], sap, waits=[meta["wprep_dep"]],
                      slot="wp")
    war = {}
    for tt in range(NTT):
        ts = slice(tt * TT, (tt + 1) * TT)
        for oc in range(CH):
            idx = tt * CH + oc
            b = idx % 4
            xb = idx % 2
            dl = _q_dma(S, xold[:, xb, :],
                        src_dram.ap()[oc * 128:(oc + 1) * 128, ts],
                        waits=[war.get((xb, "x"))], slot=f"xold{xb}")
            for kc in range(kcs):
                S.emit("t", lambda e, oc=oc, kc=kc, tt=tt, b=b: e.matmul(
                    PS[b][:, :], wp[:, oc, kc, :], O[:, kc, tt * TT:(tt + 1) * TT],
                    start=(kc == 0), stop=(kc == kcs - 1)),
                    waits=[wdep, o_dep, war.get((b, "ps"))],
                    inc=(1 if kc == kcs - 1 else 0))
            dmm = S.dep("t")
            if bias_name:
                dv0 = S.emit("v", lambda e, b=b, oc=oc: e.tensor_scalar_add(
                    PS[b][:, :], PS[b][:, :], K[bias_name][:, oc:oc + 1]),
                    waits=[dmm, meta["const_dep"]], inc=1)
            else:
                dv0 = dmm
            dv = S.emit("v", lambda e, b=b, xb=xb: e.tensor_add(
                xnew[:, xb, :], PS[b][:, :], xold[:, xb, :]),
                waits=[dv0, dl, war.get((xb, "xn"))], inc=1)
            war[(b, "ps")] = dv
            dst = _q_dma(S, dst_dram.ap()[oc * 128:(oc + 1) * 128, ts],
                         xnew[:, xb, :], waits=[dv], slot=f"pro{xb}")
            war[(xb, "x")] = dst
            war[(xb, "xn")] = dst
    A.pop()
    return S.dep("q")


def _emit_ub_halo(meta, u_dram, ub):
    S, A = meta["S"], meta["A"]
    A.push()
    uf = A.alloc([128, 2, TT], F32, "uf")
    dz = S.emit("p", lambda e: e.memset(ub[:, :, 0:1], 0.0), inc=1)
    war = {}
    tiles = {}
    for tt in range(NTT):
        for c in range(CH):
            b = (tt * CH + c) % 2
            dl = _q_dma(S, uf[:, b, :],
                        u_dram.ap()[c * 128:(c + 1) * 128,
                                    tt * TT:(tt + 1) * TT],
                        waits=[war.get(b), meta.get("u_ready"), dz],
                        slot=f"ub{b}")
            war[b] = S.emit("v", lambda e, c=c, tt=tt, b=b: e.tensor_copy(
                ub[:, c, 1 + tt * TT:1 + (tt + 1) * TT], uf[:, b, :]),
                waits=[dl], inc=1)
        tiles[tt] = war[b]
    meta["ub_tiles"] = tiles
    A.pop()
    return S.dep("v")


def _emit_ffn_sublayer(meta, l, u_in, u_mid, res_in, extra_res, dst, tag):
    S, A, K, PS, C = meta["S"], meta["A"], meta["K"], meta["PS"], meta["C"]
    cdep = meta["const_dep"]
    A.push()
    Of = A.alloc([128, 4, T], BF16, "Of")
    A.push()
    ub = A.alloc([128, CH, T + 1], BF16, "ub")
    meta["u_ready"] = None
    dub = _emit_ub_halo(meta, u_in, ub)
    P = _mk_attn_bufs(meta, rope=True, fine=True)
    P["tabs"] = ("tab_f_cq", "tab_f_sq", "tab_f_ck", "tab_f_sk")
    P["l2norm"] = True
    P["ffn_l"] = l
    P["rhs_fn"] = (lambda ub: lambda kc, tt:
                   ub[:, kc, 1 + tt * TT:1 + (tt + 1) * TT] if kc < 4 else
                   ub[:, kc, tt * TT:(tt + 1) * TT])(ub)
    P["rhs_dbg"] = (lambda ub: lambda: ub[:, :, 0:TT])(ub)
    P["rhs_dep"] = dub
    P["rhs_tiles"] = meta.get("ub_tiles")
    P["scope_war"] = dub
    for i_ in range(8):
        P["bank"][i_] = dub
    S.emit("p", lambda e: e.memset(P["V"][:, :, :, 64:65], 1.0),
           waits=[dub], inc=1)
    S.wait_only("t", [S.dep("p")])
    for hp in range(4):
        P["wsrc"] = (lambda hp: lambda j: C["c_wf_qkv"].ap()[
            l, (hp, 4 + hp, 8 + hp)[j]])(hp)
        P["dbg_pair"] = (tag == "f1l0" and hp == 0)
        _emit_pair_qkv_a(meta, P, hp)
        P["qkv_dep"] = S.dep("v")
        P["norm_dep"] = S.dep("v")
        if tag == "f1l0" and hp == 0:
            S.dbg("f_qt0", [128, T], BF16, lambda: P["Qt"][:, :], [P["qkv_dep"]])
            S.dbg("f_kt0", [128, T], BF16, lambda: P["Kt"][:, :], [P["qkv_dep"]])
        P["o_dst"] = (lambda hp: lambda qb, h: Of[64 * h:64 * h + 64, hp,
                                                  qb * TT:(qb + 1) * TT])(hp)
        _emit_attention(meta, P, hp, "f")
        P["war_qt"] = S.dep("t")
        P["war_v4"] = S.dep("t")
        P["war_norm_v"] = S.dep("v")
        P["war_norm_s"] = S.dep("s")
        for nmm in ("knr_dep", "knr_war"):
            P.pop(nmm, None)
    if tag == "f1l0":
        S.dbg("f_o0", [128, 4 * T], BF16,
              lambda: Of.rearrange("p a b -> p (a b)"), [S.dep("v")])
    S.barrier()
    A.pop()
    _emit_phase_proj_res(meta, Of, lambda oc: C["c_wf_wo"].ap()[l, oc], None,
                         u_in, u_mid, kcs=4)
    S.new_phase_sems()
    A.pop()
    if tag == "f1l0":
        S.dbg("f_u1", [D, T], F32, lambda: u_mid.ap(), [])
    A.push()
    hb = A.alloc([128, CH, T], BF16, "hb")
    _emit_ln_shift_apply(meta, u_mid, hb, l)
    if tag == "f1l0":
        S.dbg("f_hb", [128, CH * T], BF16,
              lambda: hb.rearrange("p a b -> p (a b)"), [S.dep("v")])
    _emit_w1_glu_w2(meta, l, hb, res_in, extra_res, dst, tag)
    S.new_phase_sems()
    A.pop()


def _emit_ln_shift_apply(meta, u_dram, hb, l):
    S, A, K, PS, C = meta["S"], meta["A"], meta["K"], meta["PS"], meta["C"]
    uf = A.alloc([128, 1, CH, TT + 1], F32, "uf2")
    xbt = A.alloc([128, 1, CH, TT + 1], BF16, "xbt2")
    sq = A.alloc([128, 1, 2, TT], BF16, "sq2")
    cdep = meta["const_dep"]
    war_v = {}
    prev_apply = {}
    dz = S.emit("p", lambda e: e.memset(uf[:, :, :, 0:1], 0.0), inc=1)

    def chv(buf, b, c):
        if c < 4:
            return buf[:, b, c, 1:TT + 1]
        return buf[:, b, c, 0:TT]

    for tt in range(NTT):
        b = 0
        dl = None
        for c in range(CH):
            if tt == 0:
                dl = _q_dma(S, uf[:, b, c, 1:TT + 1],
                            u_dram.ap()[c * 128:(c + 1) * 128, 0:TT],
                            waits=([war_v.get(tt - 1), dz] if c == 0 else ()),
                            slot=f"lsf{b}")
            else:
                dl = _q_dma(S, uf[:, b, c, 0:TT + 1],
                            u_dram.ap()[c * 128:(c + 1) * 128,
                                        tt * TT - 1:(tt + 1) * TT],
                            waits=([war_v.get(tt - 1), dz] if c == 0 else ()),
                            slot=f"lsf{b}")
        dcast = None
        for c in range(CH):
            dcast = S.emit("v", lambda e, c=c, b=b: e.tensor_copy(
                xbt[:, b, c, :], uf[:, b, c, :]), waits=[dl], inc=1)
        war = prev_apply.get(tt - 1, meta.get("const_v_dep"))
        sqmm = {}
        for c in range(CH):
            dsq = S.emit("s", lambda e, c=c, b=b: e.activation(
                sq[:, 0, c % 2, :], chv(xbt, b, c), AF.Square),
                waits=[dcast, war, sqmm.get(c - 2)], inc=1)
            S.emit("t", lambda e, c=c, b=b: e.matmul(
                PS[4][0:1, :], K["ones_c_bf"][:, :], chv(xbt, b, c),
                start=(c == 0), stop=(c == CH - 1)),
                waits=[dcast, cdep, war], inc=(1 if c == CH - 1 else 0))
            sqmm[c] = S.emit("t", lambda e, c=c, b=b: e.matmul(
                PS[5][0:1, :], K["ones_c_bf"][:, :], sq[:, 0, c % 2, :],
                start=(c == 0), stop=(c == CH - 1)),
                waits=[dsq, war], inc=1)
        dstat = S.dep("t")
        d1 = S.emit("v", lambda e: e.tensor_scalar_mul(
            K["mu_row"][0:1, :], PS[4][0:1, :], 1.0 / D), waits=[dstat], inc=1)
        d2 = S.emit("v", lambda e: e.tensor_scalar_mul(
            K["r_row"][0:1, :], PS[5][0:1, :], 1.0 / D), waits=[dstat], inc=1)
        d3 = S.emit("v", lambda e: e.tensor_mul(
            K["mur_row"][0:1, :], K["mu_row"][0:1, :], K["mu_row"][0:1, :]),
            waits=[d2], inc=1)
        d4 = S.emit("v", lambda e: e.tensor_sub(
            K["r_row"][0:1, :], K["r_row"][0:1, :], K["mur_row"][0:1, :]),
            waits=[d3], inc=1)
        d5 = S.emit("v", lambda e: e.tensor_scalar_max(
            K["r_row"][0:1, :], K["r_row"][0:1, :], 0.0), waits=[d4], inc=1)
        d6 = S.emit("s", lambda e: e.activation(
            K["r_row"][0:1, :], K["r_row"][0:1, :], AF.Ln,
            bias=K["eps1"][0:1, :], scale=1.0),
            waits=[d5, meta.get("eps_dep")], inc=1)
        d7 = S.emit("s", lambda e: e.activation(
            K["r_row"][0:1, :], K["r_row"][0:1, :], AF.Exp, scale=-0.5),
            waits=[d6], inc=1)
        d8 = S.emit("v", lambda e: e.tensor_mul(
            K["mur_row"][0:1, :], K["mu_row"][0:1, :], K["r_row"][0:1, :]),
            waits=[d7], inc=1)
        dr = S.emit("t", lambda e: e.matmul(
            PS[4][:, :], K["ones_r_f32"][:, :], K["r_row"][0:1, :],
            start=True, stop=True), waits=[d8], inc=1)
        dm = S.emit("t", lambda e: e.matmul(
            PS[5][:, :], K["ones_r_f32"][:, :], K["mur_row"][0:1, :],
            start=True, stop=True), waits=[d8], inc=1)
        dv = None
        for c in range(CH):
            dv1 = S.emit("v", lambda e, c=c, b=b: e.tensor_mul(
                uf[:, b, c, 1:TT + 1], chv(xbt, b, c), PS[4][:, :]),
                waits=[dr], inc=1)
            dv2 = S.emit("v", lambda e, c=c, b=b: e.tensor_sub(
                uf[:, b, c, 1:TT + 1], uf[:, b, c, 1:TT + 1], PS[5][:, :]),
                waits=[dm, dv1], inc=1)
            dv = S.emit("v", lambda e, c=c, b=b, tt=tt, l=l: e.tensor_scalar(
                hb[:, c, tt * TT:(tt + 1) * TT], uf[:, b, c, 1:TT + 1],
                K["lng"][:, l, c:c + 1], K["lnb"][:, l, c:c + 1],
                ALU.mult, ALU.add), waits=[dv2, meta["const_dep"]], inc=1)
        war_v[tt] = dv
        prev_apply[tt] = dv
        meta.setdefault("hb_tiles", {})[tt] = dv
    meta["hb_done"] = S.dep("v")


def _emit_w1_glu_w2(meta, l, hb, res_in, extra_res, dst, tag):
    S, A, K, PS, C = meta["S"], meta["A"], meta["K"], meta["PS"], meta["C"]
    cdep = meta["const_dep"]
    TH = 1024
    prod = A.alloc([128, MA, TH], BF16, "prod")
    a_sb = A.alloc([128, 2, TT], F32, "asb")
    gel = A.alloc([128, 2, TT], F32, "gel")
    w1t = A.alloc([128, 2, 2, CH, 128], BF16, "w1t")
    w2t = A.alloc([128, 1, MA, 128], BF16, "w2t")
    ures = A.alloc([128, 2, TT], F32, "ures")
    xres = A.alloc([128, 2, TT], F32, "xres") if extra_res is not None else None
    yout = A.alloc([128, 2, TT], F32, "yout")
    for th in range(T // TH):
        hs = th * TH
        hb_dep = meta["hb_tiles"][(hs + TH - 1) // TT]
        war_w1 = {}
        war_prod = {}
        for m in range(MA):
            bw = m % 2
            dwa = _q_dma(S, w1t[:, bw, 0, :, :],
                         bass.AP(C["c_wf_w1a"].ap()[l, m].tensor,
                                 C["c_wf_w1a"].ap()[l, m].offset,
                                 [[128, 128], [128 * 128, CH], [1, 128]]),
                         waits=[meta["wprep_dep"], war_w1.get(m - 2)],
                         slot=f"w1a{bw}")
            dwg = _q_dma(S, w1t[:, bw, 1, :, :],
                         bass.AP(C["c_wf_w1g"].ap()[l, m].tensor,
                                 C["c_wf_w1g"].ap()[l, m].offset,
                                 [[128, 128], [128 * 128, CH], [1, 128]]),
                         waits=[war_w1.get(m - 2)], slot=f"w1g{bw}")
            for sub in range(TH // TT):
                ss = slice(hs + sub * TT, hs + (sub + 1) * TT)
                ab = 0 if sub % 2 == 0 else 6
                gb = 1 if sub % 2 == 0 else 7
                sb2 = sub % 2
                for kc in range(CH):
                    S.emit("t", lambda e, kc=kc, bw=bw, ss=ss, ab=ab: e.matmul(
                        PS[ab][:, :], w1t[:, bw, 0, kc, :], hb[:, kc, ss],
                        start=(kc == 0), stop=(kc == CH - 1)),
                        waits=[dwa, hb_dep, war_prod.get(ab)],
                        inc=(1 if kc == CH - 1 else 0))
                da = S.dep("t")
                for kc in range(CH):
                    S.emit("t", lambda e, kc=kc, bw=bw, ss=ss, gb=gb: e.matmul(
                        PS[gb][:, :], w1t[:, bw, 1, kc, :], hb[:, kc, ss],
                        start=(kc == 0), stop=(kc == CH - 1)),
                        waits=[dwg, hb_dep, war_prod.get(gb)],
                        inc=(1 if kc == CH - 1 else 0))
                dg = S.dep("t")
                dae = S.emit("v", lambda e, m=m, ab=ab, sb2=sb2: e.tensor_scalar_add(
                    a_sb[:, sb2, :], PS[ab][:, :], K["b1a_sb"][:, l, m:m + 1]),
                    waits=[da, cdep, war_prod.get(("a", sb2))], inc=1)
                war_prod[ab] = dae
                dge = S.emit("s", lambda e, m=m, gb=gb, sb2=sb2: e.activation(
                    gel[:, sb2, :], PS[gb][:, :], AF.Gelu,
                    bias=K["b1g_sb"][:, l, m:m + 1], scale=1.0),
                    waits=[dg, cdep, war_prod.get(("g", sb2))], inc=1)
                war_prod[gb] = dge
                dp = S.emit("v", lambda e, m=m, sub=sub, sb2=sb2: e.tensor_mul(
                    prod[:, m, sub * TT:(sub + 1) * TT], a_sb[:, sb2, :],
                    gel[:, sb2, :]),
                    waits=[dae, dge], inc=1)
                war_prod[("a", sb2)] = dp
                war_prod[("g", sb2)] = dp
            war_w1[m] = dp
        dprod = S.dep("v")
        war_ps = {}
        war_res = {}
        for oc in range(CH):
            bw = 0
            dw2 = _q_dma(S, w2t[:, bw, :, :],
                         bass.AP(C["c_wf_w2"].ap()[l, oc].tensor,
                                 C["c_wf_w2"].ap()[l, oc].offset,
                                 [[128, 128], [128 * 128, MA], [1, 128]]),
                         waits=[meta["wprep_dep"], war_ps.get(("w", bw))],
                         slot=f"w2t{bw}")
            for sub in range(TH // TT):
                ss = slice(hs + sub * TT, hs + (sub + 1) * TT)
                bb = sub % 2
                rs = slice(sub * TT, (sub + 1) * TT)
                for m in range(MA):
                    S.emit("t", lambda e, m=m, bw=bw, rs=rs, bb=bb: e.matmul(
                        PS[2 + bb][:, :], w2t[:, bw, m, :],
                        prod[:, m, rs],
                        start=(m == 0), stop=(m == MA - 1)),
                        waits=[dw2, dprod, war_ps.get(bb)],
                        inc=(1 if m == MA - 1 else 0))
                dy = S.dep("t")
                dlu = _q_dma(S, ures[:, bb, :],
                             res_in.ap()[oc * 128:(oc + 1) * 128, ss],
                             waits=[war_res.get((bb, "u"))],
                             slot=f"ures{bb}")
                dv0 = S.emit("v", lambda e, bb=bb, oc=oc: e.tensor_scalar_add(
                    PS[2 + bb][:, :], PS[2 + bb][:, :],
                    K["b2b"][:, l, oc:oc + 1]), waits=[dy, cdep], inc=1)
                dv1 = S.emit("v", lambda e, bb=bb: e.tensor_add(
                    yout[:, bb, :], PS[2 + bb][:, :], ures[:, bb, :]),
                    waits=[dv0, dlu], inc=1)
                if extra_res is not None:
                    dlx = _q_dma(S, xres[:, bb, :],
                                 extra_res.ap()[oc * 128:(oc + 1) * 128, ss],
                                 waits=[war_res.get((bb, "x"))],
                                 slot=f"xres{bb}")
                    dv1 = S.emit("v", lambda e, bb=bb: e.tensor_add(
                        yout[:, bb, :], yout[:, bb, :], xres[:, bb, :]),
                        waits=[dv1, dlx], inc=1)
                war_ps[bb] = dv1
                dso = _q_dma(S, dst.ap()[oc * 128:(oc + 1) * 128, ss],
                             yout[:, bb, :], waits=[dv1], slot=f"w2o{bb}")
                war_res[(bb, "u")] = dso
                war_res[(bb, "x")] = dso
            war_ps[("w", bw)] = S.dep("t")
        S.barrier()


def _emit_pair_qkv_b(meta, P, hp):
    S, A, K, PS, C = meta["S"], meta["A"], meta["K"], meta["PS"], meta["C"]
    cdep = meta["const_dep"]
    xn = P["xn"]
    wdep = None
    for j, wn in enumerate(("c_wb_q", "c_wb_k", "c_wb_v")):
        srcw = C[wn].ap()[hp]
        sap = bass.AP(srcw.tensor, srcw.offset,
                      [[128, 128], [128 * 128, CH], [1, 128]])
        wdep = _q_dma(S, P["wq"][:, j, :, :], sap,
                      waits=[meta["wprep_dep"], P.get("scope_war")],
                      slot="wq")
    for tt in range(NTT):
        ts = slice(tt * TT, (tt + 1) * TT)
        b = tt % 2
        qbank = 0 if tt % 2 == 0 else 6
        kbank = 1 if tt % 2 == 0 else 7
        for psn, j in ((qbank, 0), (kbank, 1), (2, 2)):
            rdep = (P["rhs_tiles"][tt] if P.get("rhs_tiles")
                    else P["rhs_dep"])
            for kc in range(CH):
                S.emit("t", lambda e, psn=psn, j=j, kc=kc, tt=tt: e.matmul(
                    PS[psn][:, :], P["wq"][:, j, kc, :],
                    xn[:, kc, tt * TT:(tt + 1) * TT],
                    start=(kc == 0), stop=(kc == CH - 1)),
                    waits=[wdep, rdep, P["bank"].get(psn)],
                    inc=(1 if kc == CH - 1 else 0))
            if j == 0:
                dqp = S.dep("t")
            elif j == 1:
                dkp = S.dep("t")
            else:
                dvp = S.dep("t")
        dq = S.emit("v", lambda e, tt=tt, hp=hp, qbank=qbank: e.tensor_scalar(
            P["Qt"][:, tt * TT:(tt + 1) * TT], PS[qbank][:, :],
            K["bq"][:, hp:hp + 1], 0.125, ALU.add, ALU.mult),
            waits=[dqp, cdep, P.get("war_qt")], inc=1)
        P["bank"][qbank] = dq
        dk = S.emit("v", lambda e, tt=tt, hp=hp, kbank=kbank: e.tensor_scalar_add(
            P["Kt"][:, tt * TT:(tt + 1) * TT], PS[kbank][:, :],
            K["bk"][:, hp:hp + 1]), waits=[dkp, cdep, P.get("war_qt")], inc=1)
        P["bank"][kbank] = dk
        dvf = S.emit("v", lambda e, hp=hp: e.tensor_scalar_add(
            P["vt_f32"][:, 0, :], PS[2][:, :], K["bv"][:, hp:hp + 1]),
            waits=[dvp, cdep, P["war_vt"].get(tt - 1)], inc=1)
        P["bank"][2] = dvf
        dtr = None
        for j in range(4):
            dtr = S.emit("t", lambda e, j=j: e.transpose(
                PS[3][:, j * 128:(j + 1) * 128],
                P["vt_f32"][:, 0, j * 128:(j + 1) * 128],
                K["ident_f32"][:, :]),
                waits=[dvf, cdep, P["war_ps3"].get(tt - 1),
                       P["bank"].get(3)], inc=1)
        P["war_vt"][tt] = dtr
        dve = None
        V4b = P["V"]
        for h in range(2):
            vout = V4b[:, h, tt * 4:(tt + 1) * 4, 0:64]
            vin = bass.AP(PS[3], 64 * h, [[512, 128], [128, 4], [1, 64]])
            dve = S.emit("v", lambda e, vout=vout, vin=vin: e.tensor_copy(
                vout, vin), waits=[dtr, P.get("war_v4")], inc=1)
        P["war_ps3"][tt] = dve
        P["bank"][3] = dve
        for dst_t, nrm, dd in ((P["Qt"], P["qn"], dq), (P["Kt"], P["kn"], dk)):
            dsqn = S.emit("s", lambda e, dst_t=dst_t, tt=tt, b=b: e.activation(
                P["sqn"][:, b, :], dst_t[:, tt * TT:(tt + 1) * TT], AF.Square),
                waits=[dd, P.get("war_sqn_l")], inc=1)
            dnm = S.emit("t", lambda e, b=b: e.matmul(
                PS[5][0:2, :], K["ones2_bf"][:, :], P["sqn"][:, b, :],
                start=True, stop=True),
                waits=[dsqn, cdep, P.get("war_ps5_l"), P["bank"].get(5)],
                inc=1)
            P["war_sqn_l"] = dnm
            dnv = S.emit("v", lambda e, nrm=nrm, tt=tt: e.reduce_max(
                nrm[0:2, tt:tt + 1], PS[5][0:2, :], mybir.AxisListType.X),
                waits=[dnm], inc=1)
            P["war_ps5_l"] = dnv


def _emit_phase_attn_b(meta, xn3):
    S, A, C, K = meta["S"], meta["A"], meta["C"], meta["K"]
    O = meta["Ob"]
    A.push()
    P = _mk_attn_bufs(meta, rope=False)
    P["xn"] = xn3
    P["rhs_dep"] = meta["ln3_done"]
    P["rhs_tiles"] = meta.get("ln3_tiles")
    P["scope_war"] = meta["ln3_done"]
    for i_ in range(8):
        P["bank"][i_] = meta["ln3_done"]
    P["wband"] = A.alloc([128, 2, 2, 640], F32, "wband")
    P["war_band"] = {}
    S.emit("p", lambda e: e.memset(P["V"][:, :, :, 64:65], 1.0),
           waits=[P["scope_war"]], inc=1)
    S.wait_only("t", [S.dep("p")])
    for hp in range(CH):
        _emit_pair_qkv_b(meta, P, hp)
        P["qkv_dep"] = S.dep("v")
        P["norm_dep"] = S.dep("v")
        if hp == 0:
            S.dbg("b_qt0", [128, T], BF16, lambda: P["Qt"][:, :], [P["qkv_dep"]])
        P["o_dst"] = (lambda hp: lambda qb, h: O[64 * h:64 * h + 64, hp,
                                                 qb * TT:(qb + 1) * TT])(hp)
        _emit_attention(meta, P, hp, "b")
        P["war_qt"] = S.dep("t")
        P["war_v4"] = S.dep("t")
        P["war_norm_v"] = S.dep("v")
        P["war_norm_s"] = S.dep("s")
        for nmm in ("knr_dep", "knr_war"):
            P.pop(nmm, None)
    S.barrier()
    A.pop()
    meta["attn_b_done"] = S.dep("v")


def _build_full(dbg_names=None, phase_limit=99):
    nc, es, meta = build_program(dbg_names=dbg_names, phase_limit=phase_limit)
    S = meta["S"]
    if phase_limit < 6:
        C = meta["C"]
        srcn = {3: "x1T", 4: "x2T", 5: "x3T"}.get(phase_limit)
        if srcn:
            _q_dma(S, C["yT"].ap(), C[srcn].ap(), waits=(), slot="out")
        elif meta.get("xn") is not None:
            for c in range(CH):
                _p_dma(S, C["yT"].ap()[c * 128:(c + 1) * 128, :],
                       meta["xn"][:, c, :], waits=[meta.get("ln1_done")])
    S.barrier()
    with nc.Block() as block:
        S.replay(block)
    es.close()
    return nc, meta


def kernel(**inputs):
    consts = _host_consts()
    L = _host_layout(inputs)
    nc, meta = _build_full()
    in_maps = []
    for i in range(8):
        m = {"xT": L["xT"][i]}
        for k in ("wa_qkv", "wa_proj", "wb_q", "wb_k", "wb_v", "wb_o",
                  "wf_qkv", "wf_wo", "wf_w1a", "wf_w1g", "wf_w2",
                  "b1a", "b1g", "band_b"):
            m[k] = L[k]
        def vec_d(a):
            return np.ascontiguousarray(
                np.asarray(a, np.float32).reshape(CH, 128).T)
        def vec_ld(a):
            return np.ascontiguousarray(
                np.asarray(a, np.float32).reshape(2, CH, 128).transpose(2, 0, 1))
        m["bq"] = vec_d(inputs["L0_attn_b_bq"])
        m["bk"] = vec_d(inputs["L0_attn_b_bk"])
        m["bv"] = vec_d(inputs["L0_attn_b_bv"])
        m["bo"] = vec_d(inputs["L0_attn_b_bo"])
        m["pos_bias"] = np.asarray(inputs["L0_attn_b_pos_bias"], np.float32)
        m["b2"] = vec_ld(inputs["L0_ffn_b2"])
        m["ln_g"] = vec_ld(inputs["L0_ffn_ln_g"])
        m["ln_b"] = vec_ld(inputs["L0_ffn_ln_b"])
        m["qs"] = np.ascontiguousarray(
            np.asarray(inputs["L0_ffn_q_scale"], np.float32).T)
        m["ks"] = np.ascontiguousarray(
            np.asarray(inputs["L0_ffn_k_scale"], np.float32).T)
        m.update(consts)
        in_maps.append(m)
    res = run_bass_kernel_spmd(nc, in_maps, core_ids=list(range(8)))
    out = np.stack([np.ascontiguousarray(r["yT"].T) for r in res.results], axis=0)
    return out


# version 11
# speedup vs baseline: 1.0784x; 1.0117x over previous
import contextlib
import numpy as np
import ml_dtypes
import concourse.bass as bass
import concourse.mybir as mybir
from concourse.bass_utils import run_bass_kernel_spmd

F32 = mybir.dt.float32
BF16 = mybir.dt.bfloat16
AF = mybir.ActivationFunctionType
ALU = mybir.AluOpType
BF = ml_dtypes.bfloat16

T = 4096
D = 1024
CH = 8
TT = 512
NTT = T // TT
NQB = 8
KCN = 32
DH = 64
NEG = -1.0e30
HID = 2730
HIDP = 2816
MA = 22
OFFA = 384
WA = 896
OFFF = 384
WF = 1408
BW_B = 8064
WBB = 4480

ENGS = ("t", "v", "s", "p", "q")


def _host_consts():
    c = {}
    f32 = np.float32
    half = DH // 2
    theta = (1.0 / (10000.0 ** (np.arange(half, dtype=f32) / f32(half)))).astype(f32)
    pos = np.arange(T, dtype=f32)
    freqs = (pos[:, None] * theta[None, :]).astype(f32)
    cosr = np.repeat(np.cos(freqs, dtype=f32), 2, axis=1)
    sinr = np.repeat(np.sin(freqs, dtype=f32), 2, axis=1)
    cqa = np.tile(cosr.T, (2, 1)).astype(f32)
    sqa = np.tile(sinr.T, (2, 1)).astype(f32)
    c["tab_a_cq"] = (cqa * f32(0.125)).astype(f32)
    c["tab_a_sq"] = (sqa * f32(0.125)).astype(f32)
    c["tab_a_ck"] = cqa.copy()
    c["tab_a_sk"] = sqa.copy()
    WINDOW = 512
    half_idx = np.arange(0, DH, 2, dtype=f32)
    inv_freq = (1.0 / (10000.0 ** (half_idx / f32(DH)))).astype(f32)
    tpos = np.arange(T, dtype=f32)
    fr = (tpos[:, None] * inv_freq[None, :]).astype(f32)
    fr2 = np.concatenate([fr, fr], axis=1)
    base = ((half_idx + f32(0.4 * DH)) / f32(1.4 * DH)).astype(f32)
    power = ((tpos - f32(T // 2)) / f32(WINDOW // 2)).astype(f32)
    scale = (base[None, :] ** power[:, None]).astype(f32)
    scale2 = np.concatenate([scale, scale], axis=1)
    cosf = np.cos(fr2, dtype=f32)
    sinf = np.sin(fr2, dtype=f32)
    fcq = (cosf * scale2).astype(f32)
    fsq = (sinf * scale2).astype(f32)
    fck = (cosf / scale2).astype(f32)
    fsk = (sinf / scale2).astype(f32)
    c["tab_f_cq"] = np.tile((fcq * f32(8.0)).T, (2, 1)).astype(f32)
    c["tab_f_sq"] = np.tile((fsq * f32(8.0)).T, (2, 1)).astype(f32)
    c["tab_f_ck"] = np.tile(fck.T, (2, 1)).astype(f32)
    c["tab_f_sk"] = np.tile(fsk.T, (2, 1)).astype(f32)
    m = np.arange(WA)[None, :] - np.arange(128)[:, None] - OFFA
    c["band_a"] = np.where(m >= 0, f32(0), f32(NEG)).astype(f32)
    m = np.arange(WF)[None, :] - np.arange(128)[:, None] - OFFF
    c["band_f"] = np.where((m >= 0) & (m <= WINDOW), f32(0), f32(NEG)).astype(f32)
    c["ident_bf"] = np.eye(128, dtype=f32).astype(BF)
    c["ident_f32"] = np.eye(128, dtype=f32)
    c["perm_bf"] = _perm_matrix().astype(BF)
    c["ones_r_f32"] = np.ones((1, 128), dtype=f32)
    c["ones_c_bf"] = np.ones((128, 1), dtype=f32).astype(BF)
    on2 = np.zeros((128, 2), dtype=f32)
    on2[0:64, 0] = 1.0
    on2[64:128, 1] = 1.0
    c["ones2_bf"] = on2.astype(BF)
    b2 = np.zeros((2, 128), dtype=f32)
    b2[0, 0:64] = 1.0
    b2[1, 64:128] = 1.0
    c["b2_f32"] = b2
    return c


def _perm_matrix():
    P = np.zeros((128, 128), dtype=np.float32)
    for blk in (0, 64):
        for d in range(32):
            P[blk + d + 32, blk + d] = -1.0
        for d in range(32, 64):
            P[blk + d - 32, blk + d] = 1.0
    return P


def _host_layout(inputs):
    f32 = np.float32
    L = {}
    x = np.asarray(inputs["L0_attn_a_x"], dtype=f32)
    L["xT"] = [np.ascontiguousarray(x[i].T) for i in range(8)]
    def ocmajor(w, pad_to=None):
        w = np.asarray(w, dtype=f32)
        K, M = w.shape
        if pad_to is not None and M < pad_to:
            w = np.concatenate([w, np.zeros((K, pad_to - M), f32)], axis=1)
            M = pad_to
        n = M // 128
        return np.ascontiguousarray(w.reshape(K, n, 128).transpose(1, 0, 2))
    L["wa_qkv"] = ocmajor(inputs["L0_attn_a_w_qkv"])
    L["wa_proj"] = ocmajor(inputs["L0_attn_a_w_proj"])
    for nm, key in (("wb_q", "L0_attn_b_Wq"), ("wb_k", "L0_attn_b_Wk"),
                    ("wb_v", "L0_attn_b_Wv"), ("wb_o", "L0_attn_b_Wo")):
        L[nm] = ocmajor(np.ascontiguousarray(np.asarray(inputs[key], f32).T))
    wq = np.asarray(inputs["L0_ffn_wqkv"], f32)
    L["wf_qkv"] = np.stack([ocmajor(wq[l]) for l in range(2)])
    wo = np.asarray(inputs["L0_ffn_wo"], f32)
    L["wf_wo"] = np.stack([ocmajor(wo[l]) for l in range(2)])
    w1 = np.asarray(inputs["L0_ffn_w1"], f32)
    L["wf_w1a"] = np.stack([ocmajor(w1[l][:, :HID], pad_to=HIDP) for l in range(2)])
    L["wf_w1g"] = np.stack([ocmajor(w1[l][:, HID:], pad_to=HIDP) for l in range(2)])
    w2 = np.asarray(inputs["L0_ffn_w2"], f32)
    w2p = np.concatenate([w2, np.zeros((2, HIDP - HID, D), f32)], axis=1)
    L["wf_w2"] = np.stack([ocmajor(w2p[l]) for l in range(2)])
    b1 = np.asarray(inputs["L0_ffn_b1"], f32)
    def vec_lma(a):
        return np.ascontiguousarray(a.reshape(2, MA, 128).transpose(2, 0, 1))
    L["b1a"] = vec_lma(np.concatenate(
        [b1[:, :HID], np.zeros((2, HIDP - HID), f32)], axis=1))
    L["b1g"] = vec_lma(np.concatenate(
        [b1[:, HID:], np.zeros((2, HIDP - HID), f32)], axis=1))
    pb = np.asarray(inputs["L0_attn_b_pos_bias"], f32)
    mm = np.arange(BW_B)[None, :] - np.arange(128)[:, None] + 127
    idx = np.minimum(mm, 4094)
    L["band_b"] = np.ascontiguousarray(pb[:, idx])
    return L


class Dep:
    __slots__ = ("key", "c", "g", "k", "e")
    def __init__(self, key, c, g=0, k="cmp", e=None):
        self.key, self.c, self.g, self.k, self.e = key, c, g, k, e


class Sched:
    def __init__(self, nc, es):
        self.nc = nc
        self.es = es
        self.ops = {e: [] for e in ENGS}
        self.cnt = {e: 0 for e in ENGS}
        self.sems = {}
        self.slot_cnt = {}
        self.waited = {}
        self.semgen = 0
        self.dbg_names = None
        self.dbg_tensors = {}
        self.dbg_deps = []
        self._new_semset()

    def _new_semset(self):
        self.semgen += 1
        self.gen = self.semgen
        if self.gen == 1:
            for e in ENGS:
                self.sems[("e", e)] = self.es.enter_context(
                    self.nc.semaphore(f"sm{e}"))
                self.cnt[e] = 0

    def new_phase_sems(self):
        self.barrier()
        self._new_semset()

    def slot_sem(self, slot):
        key = ("s", slot)
        if key not in self.sems:
            self.sems[key] = self.es.enter_context(
                self.nc.semaphore("sl_" + slot))
            self.slot_cnt[key] = 0
        return key

    def dep(self, e, kind=None):
        k = kind or "cmp"
        return Dep(("e", e), self.cnt[e], self.gen, k, e)

    def dma(self, e, slot, fn, waits=()):
        key = self.slot_sem(slot)
        self.slot_cnt[key] += 16
        cnt = self.slot_cnt[key]
        sem = self.sems[key]
        wl = self._translate(e, waits)

        def op(eng):
            for s, c in wl:
                eng.wait_ge(s, c)
            fn(eng).then_inc(sem, 16)
        self.ops[e].append(op)
        return Dep(key, cnt, self.gen, "dma", e)

    def _translate(self, e, waits):
        wl = []
        for d in waits:
            if d is None or d.c <= 0:
                continue
            if d.g != self.gen:
                continue
            wkey = (e, d.key)
            if self.waited.get(wkey, 0) >= d.c:
                continue
            self.waited[wkey] = d.c
            wl.append((self.sems[d.key], d.c))
        return wl

    def emit(self, e, fn, waits=(), inc=0):
        wl = self._translate(e, waits)
        if inc:
            self.cnt[e] += inc
        sem = self.sems[("e", e)] if inc else None
        amt = inc

        def op(eng):
            for s, c in wl:
                eng.wait_ge(s, c)
            ins = fn(eng)
            if amt:
                if ins is None:
                    eng.nop().then_inc(sem, amt)
                else:
                    ins.then_inc(sem, amt)
        self.ops[e].append(op)
        return Dep(("e", e), self.cnt[e], self.gen, "cmp", e) if inc else None

    def wait_only(self, e, waits):
        self.emit(e, lambda eng: None, waits=waits, inc=0)

    def barrier(self):
        deps = [Dep(("e", e), self.cnt[e], self.gen, "bar", e)
                for e in ENGS]
        deps += [Dep(key, self.slot_cnt[key], self.gen, "bar", None)
                 for key in self.slot_cnt]
        for e in ENGS:
            self.wait_only(e, deps)

    def dbg(self, name, shape, dtype, src_ap_fn, waits):
        if self.dbg_names is None or name not in self.dbg_names:
            return
        t = self.nc.dram_tensor("dbg_" + name, list(shape), dtype,
                                kind="ExternalOutput")
        self.dbg_tensors[name] = t
        d = self.dma("q", "dbg_" + name,
                     lambda eng: eng.dma_start(out=t.ap(), in_=src_ap_fn()),
                     waits=waits)
        self.dbg_deps.append(d)
        return d

    def replay(self, block):
        for ename, attr in (("t", "tensor"), ("v", "vector"), ("s", "scalar"),
                            ("p", "gpsimd"), ("q", "sync")):
            def mk(e):
                def body(eng):
                    for f in self.ops[e]:
                        f(eng)
                return body
            getattr(block, attr)(mk(ename))


class Arena:
    def __init__(self, nc, es, nbytes):
        self.h = es.enter_context(nc.sbuf_tensor("arena", [128, nbytes],
                                                 mybir.dt.uint8))
        self.nbytes = nbytes
        self.off = 0
        self.stack = []

    def push(self):
        self.stack.append(self.off)

    def pop(self):
        self.off = self.stack.pop()

    def alloc(self, shape, dtype, name=""):
        bpe = {F32: 4, BF16: 2}[dtype]
        n = 1
        for s in shape[1:]:
            n *= s
        self.off = (self.off + 63) & ~63
        start = self.off
        self.off += n * bpe
        assert self.off <= self.nbytes, \
            f"arena overflow {name}: {self.off} > {self.nbytes}"
        base = self.h[:].bitcast(dtype)
        ap = base[:, start // bpe: start // bpe + n]
        if len(shape) > 2:
            pat = "p (" + " ".join(f"a{i}" for i in range(len(shape) - 1)) + ") -> p " + \
                  " ".join(f"a{i}" for i in range(len(shape) - 1))
            kw = {f"a{i}": shape[i + 1] for i in range(len(shape) - 1)}
            ap = ap.rearrange(pat, **kw)
        return ap


def build_program(dbg_names=None, phase_limit=99):
    nc = bass.Bass("TRN2", target_bir_lowering=False)
    es = contextlib.ExitStack()
    C = {}

    def din(name, shape, dtype=F32):
        C[name] = nc.dram_tensor(name, list(shape), dtype, kind="ExternalInput")
        return C[name]

    def dint(name, shape, dtype):
        C[name] = nc.dram_tensor(name, list(shape), dtype)
        return C[name]

    din("xT", [D, T])
    din("wa_qkv", [24, D, 128]); din("wa_proj", [8, D, 128])
    for nm in ("wb_q", "wb_k", "wb_v", "wb_o"):
        din(nm, [8, D, 128])
    din("bq", [128, CH]); din("bk", [128, CH]); din("bv", [128, CH])
    din("bo", [128, CH])
    din("pos_bias", [16, 4095])
    din("band_b", [16, 128, BW_B])
    din("wf_qkv", [2, 12, D, 128]); din("wf_wo", [2, 8, 512, 128])
    din("wf_w1a", [2, MA, D, 128]); din("wf_w1g", [2, MA, D, 128])
    din("wf_w2", [2, 8, HIDP, 128])
    din("b1a", [128, 2, MA]); din("b1g", [128, 2, MA]); din("b2", [128, 2, CH])
    din("ln_g", [128, 2, CH]); din("ln_b", [128, 2, CH])
    din("qs", [DH, 2]); din("ks", [DH, 2])
    for nm in ("tab_a_cq", "tab_a_sq", "tab_a_ck", "tab_a_sk",
               "tab_f_cq", "tab_f_sq", "tab_f_ck", "tab_f_sk"):
        din(nm, [128, T])
    din("band_a", [128, WA]); din("band_f", [128, WF])
    din("ident_bf", [128, 128], BF16); din("ident_f32", [128, 128], F32)
    din("perm_bf", [128, 128], BF16)
    din("ones_r_f32", [1, 128], F32); din("ones_c_bf", [128, 1], BF16)
    din("ones2_bf", [128, 2], BF16); din("b2_f32", [2, 128], F32)

    C["yT"] = nc.dram_tensor("yT", [D, T], F32, kind="ExternalOutput")
    dint("x1T", [D, T], F32)
    dint("x2T", [D, T], F32)
    dint("x3T", [D, T], F32)
    dint("uT", [D, T], F32)
    dint("u1T", [D, T], F32)
    dint("c_wa_qkv", [24, D, 128], BF16); dint("c_wa_proj", [8, D, 128], BF16)
    for nm in ("wb_q", "wb_k", "wb_v", "wb_o"):
        dint("c_" + nm, [8, D, 128], BF16)
    dint("c_wf_qkv", [2, 12, D, 128], BF16); dint("c_wf_wo", [2, 8, 512, 128], BF16)
    dint("c_wf_w1a", [2, MA, D, 128], BF16); dint("c_wf_w1g", [2, MA, D, 128], BF16)
    dint("c_wf_w2", [2, 8, HIDP, 128], BF16)

    S = Sched(nc, es)
    S.dbg_names = set(dbg_names) if dbg_names else None
    A = Arena(nc, es, 188 * 1024)

    K = {}
    def sball(name, shape, dtype):
        K[name] = es.enter_context(nc.sbuf_tensor(name, list(shape), dtype))
        return K[name]

    for nm, sh, dt in (
        ("ident_bf", [128, 128], BF16), ("ident_f32", [128, 128], F32),
        ("perm_bf", [128, 128], BF16), ("ones_r_f32", [1, 128], F32),
        ("ones_c_bf", [128, 1], BF16), ("ones2_bf", [128, 2], BF16),
        ("b2_f32", [2, 128], F32),
        ("band_a", [128, WA], F32), ("band_f", [128, WF], F32),
        ("bq", [128, CH], F32), ("bk", [128, CH], F32), ("bv", [128, CH], F32),
        ("bo", [128, CH], F32), ("b2b", [128, 2, CH], F32),
        ("lng", [128, 2, CH], F32), ("lnb", [128, 2, CH], F32),
        ("b1a_sb", [128, 2, MA], F32), ("b1g_sb", [128, 2, MA], F32),
        ("qs_sb", [128, 2], F32), ("ks_sb", [128, 2], F32),
        ("pbrow", [1, 16], F32),
        ("mu_row", [1, TT], F32), ("r_row", [1, TT], F32), ("mur_row", [1, TT], F32),
        ("eps1", [1, 1], F32),
    ):
        K[nm] = es.enter_context(nc.sbuf_tensor("sb_" + nm, list(sh), dt))

    PS = [es.enter_context(nc.psum_tensor(f"ps{i}", [128, 512], F32))
          for i in range(8)]

    meta = dict(nc=nc, S=S, A=A, K=K, PS=PS, C=C)

    _emit_wprep(meta)
    _emit_const_loads(meta)
    if phase_limit >= 1:
        A.push()
        xn = A.alloc([128, CH, T], BF16, "xn")
        meta["xn"] = xn
        _emit_ln_stats_apply(meta, src_dram=C["xT"], xn_out=xn, phase_tag="ln1")
    if phase_limit >= 2:
        meta["Oa"] = A.alloc([128, CH, T], BF16, "Oa")
        _emit_phase_attn_a(meta)
        S.dbg("a_o", [128, CH * T], BF16,
              lambda: meta["Oa"].rearrange("p a b -> p (a b)"),
              [meta["attn_a_done"]])
        S.new_phase_sems()
    if phase_limit >= 3:
        _emit_phase_proj_res(meta, meta["Oa"],
                             lambda oc: C["c_wa_proj"].ap()[oc], None,
                             C["xT"], C["x1T"], o_dep=None)
        S.dbg("x1", [D, T], F32, lambda: C["x1T"].ap(), [])
        S.new_phase_sems()
        A.pop()
    if phase_limit >= 4:
        _emit_ln_stats_apply(meta, C["x1T"], None, "ln2", out_dram=C["uT"])
        S.new_phase_sems()
        S.dbg("u_dram", [D, T], F32, lambda: C["uT"].ap(), [])
        _emit_ffn_sublayer(meta, 0, C["uT"], C["u1T"], C["u1T"], None,
                           C["uT"], "f1l0")
        _emit_ffn_sublayer(meta, 1, C["uT"], C["u1T"], C["u1T"], C["x1T"],
                           C["x2T"], "f1l1")
        S.new_phase_sems()
        S.dbg("x2", [D, T], F32, lambda: C["x2T"].ap(), [])
    if phase_limit >= 5:
        A.push()
        xn3 = A.alloc([128, CH, T], BF16, "xn3")
        _emit_ln_stats_apply(meta, C["x2T"], xn3, "ln3")
        meta["Ob"] = A.alloc([128, CH, T], BF16, "Ob")
        S.new_phase_sems()
        _emit_phase_attn_b(meta, xn3)
        S.new_phase_sems()
        _emit_phase_proj_res(meta, meta["Ob"],
                             lambda oc: C["c_wb_o"].ap()[oc], "bo",
                             C["x2T"], C["x3T"], o_dep=None)
        S.new_phase_sems()
        S.dbg("x3", [D, T], F32, lambda: C["x3T"].ap(), [])
        A.pop()
    if phase_limit >= 6:
        _emit_ln_stats_apply(meta, C["x3T"], None, "ln4", out_dram=C["uT"])
        S.new_phase_sems()
        _emit_ffn_sublayer(meta, 0, C["uT"], C["u1T"], C["u1T"], None,
                           C["uT"], "f2l0")
        _emit_ffn_sublayer(meta, 1, C["uT"], C["u1T"], C["u1T"], C["x3T"],
                           C["yT"], "f2l1")
        S.new_phase_sems()
    return nc, es, meta


def _q_dma(S, dst_ap, src_ap, waits=(), slot="misc"):
    return S.dma("q", slot, lambda e: e.dma_start(out=dst_ap, in_=src_ap),
                 waits=waits)


def _p_dma(S, dst_ap, src_ap, waits=(), slot="pmisc"):
    return S.dma("p", slot, lambda e: e.dma_start(out=dst_ap, in_=src_ap),
                 waits=waits)


def _emit_wprep(meta):
    S, C = meta["S"], meta["C"]
    deps = []
    pairs = [("wa_qkv", "c_wa_qkv"), ("wa_proj", "c_wa_proj"),
             ("wb_q", "c_wb_q"), ("wb_k", "c_wb_k"), ("wb_v", "c_wb_v"),
             ("wb_o", "c_wb_o"), ("wf_qkv", "c_wf_qkv"), ("wf_wo", "c_wf_wo"),
             ("wf_w1a", "c_wf_w1a"), ("wf_w1g", "c_wf_w1g"), ("wf_w2", "c_wf_w2")]
    for src, dst in pairs:
        sap = C[src].ap()
        dap = C[dst].ap()
        n = 1
        for s in sap.shape:
            n *= s
        sflat = sap.rearrange(" ".join(f"a{i}" for i in range(len(sap.shape)))
                              + " -> (" + " ".join(f"a{i}" for i in range(len(sap.shape))) + ")")
        dflat = dap.rearrange(" ".join(f"a{i}" for i in range(len(dap.shape)))
                              + " -> (" + " ".join(f"a{i}" for i in range(len(dap.shape))) + ")")
        CHUNK = 1 << 20
        for o in range(0, n, CHUNK):
            ln = min(CHUNK, n - o)
            deps.append(_p_dma(S, dflat[o:o + ln], sflat[o:o + ln],
                               slot="wprep"))
    meta["wprep_dep"] = deps[-1]


def _emit_const_loads(meta):
    S, C, K = meta["S"], meta["C"], meta["K"]
    nc = meta["nc"]
    deps = []
    for nm in ("ident_bf", "ident_f32", "perm_bf", "ones_r_f32", "ones_c_bf",
               "ones2_bf", "b2_f32", "band_a", "band_f"):
        deps.append(_q_dma(S, K[nm][:], C[nm].ap(), slot="const"))
    for nm, dn in (("bq", "bq"), ("bk", "bk"), ("bv", "bv"), ("bo", "bo")):
        deps.append(_q_dma(S, K[nm][:], C[dn].ap(), slot="const"))
    for nm, dn in (("b2b", "b2"), ("lng", "ln_g"), ("lnb", "ln_b")):
        deps.append(_q_dma(S, K[nm][:], C[dn].ap(), slot="const"))
    for nm, dn in (("b1a_sb", "b1a"), ("b1g_sb", "b1g")):
        deps.append(_q_dma(S, K[nm][:], C[dn].ap(), slot="const"))
    for nm in ("qs", "ks"):
        deps.append(_q_dma(S, K[nm + "_sb"][0:64, :], C[nm].ap(), slot="const"))
    meta["A"].push()
    pb_sb = meta["A"].alloc([128, 4095], F32, "pbtmp")
    pbm = meta["A"].alloc([128, 1], F32, "pbm")
    deps.append(_q_dma(S, pb_sb[0:16, :], C["pos_bias"].ap(), slot="const"))
    d = deps[-1]
    dcl = d
    for nm in ("qs_sb", "ks_sb"):
        S.emit("v", lambda e, t=K[nm]: e.tensor_copy(t[64:128, :], t[0:64, :]),
               waits=[d], inc=1)
    meta["eps_dep"] = S.emit("p", lambda e: e.memset(K["eps1"][:, :], 1e-5),
                             inc=1)
    dm = S.emit("v", lambda e: e.reduce_max(pbm[0:16, :], pb_sb[0:16, :],
                                            mybir.AxisListType.X),
                waits=[d], inc=1)
    PS = meta["PS"]
    dt = S.emit("t", lambda e: e.transpose(PS[5][0:1, 0:16], pbm[0:16, 0:1],
                                           K["ident_f32"][0:16, 0:16]),
                waits=[dm, dcl], inc=1)
    S.emit("v", lambda e: e.tensor_copy(K["pbrow"][0:1, 0:16], PS[5][0:1, 0:16]),
           waits=[dt], inc=1)
    meta["A"].pop()
    meta["const_dep"] = dcl
    meta["const_v_dep"] = S.dep("v")


def _emit_ln_stats_apply(meta, src_dram, xn_out, phase_tag, gains_l=None,
                         out_dram=None):
    S, A, K, PS, C = meta["S"], meta["A"], meta["K"], meta["PS"], meta["C"]
    A.push()
    xf = A.alloc([128, 2, CH, TT], F32, "xf")
    xbt = A.alloc([128, 2, CH, TT], BF16, "xbt")
    sq = A.alloc([128, 2, 2, TT], BF16, "sq")
    src = src_dram.ap()
    cdep = meta["const_dep"]
    war_v = {}
    war_t = {}
    prev_apply = {}
    for tt in range(NTT):
        b = tt % 2
        dl = None
        for c in range(CH):
            dl = _q_dma(S, xf[:, b, c, :],
                        src[c * 128:(c + 1) * 128, tt * TT:(tt + 1) * TT],
                        waits=([war_v.get(tt - 2)] if c == 0 else ()),
                        slot=f"lnxf{b}")
        dcast = None
        for c in range(CH):
            dcast = S.emit("v", lambda e, c=c, b=b: e.tensor_copy(
                xbt[:, b, c, :], xf[:, b, c, :]), waits=[dl], inc=1)
        war = prev_apply.get(tt - 1, meta.get("const_v_dep"))
        sqmm = {}
        for c in range(CH):
            dsq = S.emit("s", lambda e, c=c, b=b: e.activation(
                sq[:, b, c % 2, :], xbt[:, b, c, :], AF.Square),
                waits=[dcast, war, sqmm.get(c - 2)], inc=1)
            S.emit("t", lambda e, c=c, b=b: e.matmul(
                PS[6][0:1, :], K["ones_c_bf"][:, :], xbt[:, b, c, :],
                start=(c == 0), stop=(c == CH - 1)),
                waits=[dcast, cdep, war], inc=(1 if c == CH - 1 else 0))
            sqmm[c] = S.emit("t", lambda e, c=c, b=b: e.matmul(
                PS[7][0:1, :], K["ones_c_bf"][:, :], sq[:, b, c % 2, :],
                start=(c == 0), stop=(c == CH - 1)),
                waits=[dsq, war], inc=1)
        dstat = S.dep("t")
        d1 = S.emit("v", lambda e: e.tensor_scalar_mul(
            K["mu_row"][0:1, :], PS[6][0:1, :], 1.0 / D), waits=[dstat], inc=1)
        d2 = S.emit("v", lambda e: e.tensor_scalar_mul(
            K["r_row"][0:1, :], PS[7][0:1, :], 1.0 / D), waits=[dstat], inc=1)
        d3 = S.emit("v", lambda e: e.tensor_mul(
            K["mur_row"][0:1, :], K["mu_row"][0:1, :], K["mu_row"][0:1, :]),
            waits=[d2], inc=1)
        d4 = S.emit("v", lambda e: e.tensor_sub(
            K["r_row"][0:1, :], K["r_row"][0:1, :], K["mur_row"][0:1, :]),
            waits=[d3], inc=1)
        d5 = S.emit("v", lambda e: e.tensor_scalar_max(
            K["r_row"][0:1, :], K["r_row"][0:1, :], 0.0), waits=[d4], inc=1)
        d6 = S.emit("s", lambda e: e.activation(
            K["r_row"][0:1, :], K["r_row"][0:1, :], AF.Ln,
            bias=K["eps1"][0:1, :], scale=1.0),
            waits=[d5, meta.get("eps_dep")], inc=1)
        d7 = S.emit("s", lambda e: e.activation(
            K["r_row"][0:1, :], K["r_row"][0:1, :], AF.Exp, scale=-0.5),
            waits=[d6], inc=1)
        d8 = S.emit("v", lambda e: e.tensor_mul(
            K["mur_row"][0:1, :], K["mu_row"][0:1, :], K["r_row"][0:1, :]),
            waits=[d7], inc=1)
        if tt == 0:
            S.dbg(phase_tag + "_r", [1, TT], F32, lambda: K["r_row"][0:1, :], [d8])
            S.dbg(phase_tag + "_mur", [1, TT], F32,
                  lambda: K["mur_row"][0:1, :], [d8])
        dr = S.emit("t", lambda e: e.matmul(
            PS[6][:, :], K["ones_r_f32"][:, :], K["r_row"][0:1, :],
            start=True, stop=True), waits=[d8], inc=1)
        dm = S.emit("t", lambda e: e.matmul(
            PS[7][:, :], K["ones_r_f32"][:, :], K["mur_row"][0:1, :],
            start=True, stop=True), waits=[d8], inc=1)
        dv = None
        for c in range(CH):
            dv1 = S.emit("v", lambda e, c=c, b=b: e.tensor_mul(
                xf[:, b, c, :], xbt[:, b, c, :], PS[6][:, :]),
                waits=[dr], inc=1)
            if out_dram is not None:
                dv = S.emit("v", lambda e, c=c, b=b: e.tensor_sub(
                    xf[:, b, c, :], xf[:, b, c, :], PS[7][:, :]),
                    waits=[dm, dv1], inc=1)
                dv = _q_dma(S, out_dram.ap()[c * 128:(c + 1) * 128,
                                             tt * TT:(tt + 1) * TT],
                            xf[:, b, c, :], waits=[dv], slot=f"lnout{b}")
            elif gains_l is None:
                dv = S.emit("v", lambda e, c=c, b=b, tt=tt: e.tensor_sub(
                    xn_out[:, c, tt * TT:(tt + 1) * TT], xf[:, b, c, :],
                    PS[7][:, :]), waits=[dm, dv1], inc=1)
            else:
                dv2 = S.emit("v", lambda e, c=c, b=b: e.tensor_sub(
                    xf[:, b, c, :], xf[:, b, c, :], PS[7][:, :]),
                    waits=[dm, dv1], inc=1)
                dv = S.emit("v", lambda e, c=c, b=b, tt=tt, l=gains_l:
                    e.tensor_scalar(
                        xn_out[:, c, tt * TT:(tt + 1) * TT], xf[:, b, c, :],
                        K["lng"][:, l, c:c + 1], K["lnb"][:, l, c:c + 1],
                        ALU.mult, ALU.add), waits=[dv2], inc=1)
        war_v[tt] = dv
        prev_apply[tt] = dv
        meta.setdefault(phase_tag + "_tiles", {})[tt] = dv
    if xn_out is not None:
        S.dbg(phase_tag + "_xn", [128, CH * T], BF16,
              lambda: xn_out.rearrange("p a b -> p (a b)"), [S.dep("v")])
    A.pop()
    meta[phase_tag + "_done"] = S.dep("v")


def _emit_pair_qkv_a(meta, P, hp):
    S, A, K, PS, C = meta["S"], meta["A"], meta["K"], meta["PS"], meta["C"]
    cdep = meta["const_dep"]
    wdep = None
    for j in range(3):
        srcw = P["wsrc"](j)
        sap = bass.AP(srcw.tensor, srcw.offset,
                      [[128, 128], [128 * 128, CH], [1, 128]])
        wdep = _q_dma(S, P["wq"][:, j, :, :], sap,
                      waits=[meta["wprep_dep"], P.get("scope_war")],
                      slot="wq")
    if P.get("dbg_pair"):
        dd = S.dbg("pp_wq", [128, 3 * CH * 128], BF16,
                   lambda: P["wq"].rearrange("p a b c -> p (a b c)"), [wdep])
        dd2 = S.dbg("pp_ub0", [128, CH, TT], BF16,
                    lambda: P["rhs_dbg"](), [P["rhs_dep"]])
    V4 = P["V"]
    for tt in range(NTT):
        ts = slice(tt * TT, (tt + 1) * TT)
        b = tt % 2

        qbank = 0 if tt % 2 == 0 else 6
        kbank = 1 if tt % 2 == 0 else 7
        for psn, j in ((qbank, 0), (kbank, 1), (2, 2)):
            for kc in range(CH):
                rdep = (P["rhs_tiles"][min(tt + 1, NTT - 1)]
                        if P.get("rhs_tiles") else P["rhs_dep"])
                S.emit("t", lambda e, psn=psn, j=j, kc=kc, tt=tt: e.matmul(
                    PS[psn][:, :], P["wq"][:, j, kc, :],
                    P["rhs_fn"](kc, tt),
                    start=(kc == 0), stop=(kc == CH - 1)),
                    waits=[wdep, rdep, P["bank"].get(psn)],
                    inc=(1 if kc == CH - 1 else 0))
            if j == 0:
                dqp = S.dep("t")
            elif j == 1:
                dkp = S.dep("t")
            else:
                dvp = S.dep("t")
        dvf = S.emit("v", lambda e: e.tensor_copy(P["vt_f32"][:, 0, :],
                                                  PS[2][:, :]),
                     waits=[dvp, P["war_vt"].get(tt - 1)], inc=1)
        P["bank"][2] = dvf
        dtr = None
        for j in range(4):
            dtr = S.emit("t", lambda e, j=j, b=b: e.transpose(
                PS[3][:, j * 128:(j + 1) * 128],
                P["vt_f32"][:, 0, j * 128:(j + 1) * 128],
                K["ident_f32"][:, :]),
                waits=[dvf, cdep, P["war_ps3"].get(tt - 1),
                       P["bank"].get(3)], inc=1)
        P["war_vt"][tt] = dtr
        dve = None
        for h in range(2):
            vout = V4[:, h, tt * 4:(tt + 1) * 4, 0:64]
            vin = bass.AP(PS[3], 64 * h, [[512, 128], [128, 4], [1, 64]])
            dve = S.emit("v", lambda e, vout=vout, vin=vin: e.tensor_copy(
                vout, vin), waits=[dtr, P.get("war_v4")], inc=1)
        P["war_ps3"][tt] = dve
        P["bank"][3] = dve
        for name, psn, tcos, tsin, dst, nrm in (
                ("q", qbank, 0, None, P["Qt"], P["qn"]),
                ("k", kbank, 2, None, P["Kt"], P["kn"])):
            dep_p = dqp if name == "q" else dkp
            tdep = None
            for j in range(2):
                tdep = _q_dma(S, P["tab"][:, j, :],
                              C[P["tabs"][tcos + j]].ap()[:, ts],
                              waits=[P.get("war_tab_l"), P.get("scope_war")],
                              slot="tab")
            dcast = S.emit("v", lambda e, psn=psn, b=b: e.tensor_copy(
                P["qb"][:, b, :], PS[psn][:, :]),
                waits=[dep_p, P.get("war_qb_l")], inc=1)
            P["bank"][psn] = dcast
            if P.get("dbg_pair") and tt == 0:
                dd = S.dbg(f"pp_{name}_raw", [128, TT], BF16,
                           lambda: P["qb"][:, 0, :], [dcast])
                if dd: dcast = dd
            if P.get("l2norm"):
                dcast = _emit_l2norm(meta, P, dcast, b, name)
                if P.get("dbg_pair") and tt == 0:
                    dd = S.dbg(f"pp_{name}_l2", [128, TT], BF16,
                               lambda: P["qb"][:, 0, :], [dcast])
                    if dd: dcast = dd
            drot = S.emit("t", lambda e, b=b: e.matmul(
                PS[4][:, :], K["perm_bf"][:, :], P["qb"][:, b, :],
                start=True, stop=True),
                waits=[dcast, cdep, P.get("war_rot_l"), P["bank"].get(4)],
                inc=1)
            dt1 = S.emit("v", lambda e, b=b: e.tensor_mul(
                P["t1"][:, :], P["qb"][:, b, :], P["tab"][:, 0, :]),
                waits=[tdep, dcast], inc=1)
            dt2 = S.emit("v", lambda e, b=b: e.tensor_mul(
                P["t2"][:, :], PS[4][:, :], P["tab"][:, 1, :]),
                waits=[drot, dt1], inc=1)
            P["war_rot_l"] = dt2
            P["war_tab_l"] = dt2
            P["bank"][4] = dt2
            dqt = S.emit("v", lambda e, dst=dst, tt=tt: e.tensor_add(
                dst[:, tt * TT:(tt + 1) * TT], P["t1"][:, :], P["t2"][:, :]),
                waits=[dt2, P.get("war_qt")], inc=1)
            if P.get("dbg_pair") and tt == 0:
                dd = S.dbg(f"pp_{name}_t1", [128, TT], F32,
                           lambda: P["t1"][:, :], [dt2])
                dd = S.dbg(f"pp_{name}_t2", [128, TT], F32,
                           lambda: P["t2"][:, :], [dqt])
                if dd:
                    P["war_rot_l"] = dd
                    P["war_qb_l"] = dd
                    P["war_tab_l"] = dd
            P["war_qb_l"] = drot
            dsqn = S.emit("s", lambda e, dst=dst, tt=tt, b=b: e.activation(
                P["sqn"][:, b, :], dst[:, tt * TT:(tt + 1) * TT], AF.Square),
                waits=[dqt, P.get("war_sqn_l")], inc=1)
            dnm = S.emit("t", lambda e, b=b: e.matmul(
                PS[5][0:2, :], K["ones2_bf"][:, :], P["sqn"][:, b, :],
                start=True, stop=True),
                waits=[dsqn, cdep, P.get("war_ps5_l"), P["bank"].get(5)],
                inc=1)
            P["war_sqn_l"] = dnm
            if P.get("l2norm"):
                n32 = P["qn32"] if name == "q" else P["kn32"]
                for j4 in range(4):
                    dnv = S.emit("v", lambda e, n32=n32, tt=tt, j4=j4:
                        e.reduce_max(n32[0:2, 4 + 4 * tt + j4:5 + 4 * tt + j4],
                                     PS[5][0:2, j4 * 128:(j4 + 1) * 128],
                                     mybir.AxisListType.X),
                        waits=[dnm, P.get("war_norm_v"),
                               P.get("war_norm_s")], inc=1)
            else:
                dnv = S.emit("v", lambda e, nrm=nrm, tt=tt: e.reduce_max(
                    nrm[0:2, tt:tt + 1], PS[5][0:2, :], mybir.AxisListType.X),
                    waits=[dnm, P.get("war_norm_v"),
                           P.get("war_norm_s")], inc=1)
            P["war_ps5_l"] = dnv
            P["bank"][5] = dnv


def _emit_l2norm(meta, P, dcast, b, name):
    S, K, PS = meta["S"], meta["K"], meta["PS"]
    cdep = meta["const_dep"]
    svec = K["qs_sb"] if name == "q" else K["ks_sb"]
    l = P["ffn_l"]
    dsq = S.emit("s", lambda e, b=b: e.activation(
        P["sqn"][:, b, :], P["qb"][:, b, :], AF.Square),
        waits=[dcast, P.get("war_sqn_l")], inc=1)
    dnm = S.emit("t", lambda e, b=b: e.matmul(
        PS[5][0:2, :], K["ones2_bf"][:, :], P["sqn"][:, b, :],
        start=True, stop=True),
        waits=[dsq, cdep, P.get("war_ps5_l"), P["bank"].get(5)], inc=1)
    P["war_sqn_l"] = dnm
    dmx = S.emit("v", lambda e: e.tensor_scalar_max(
        P["ssc"][0:2, :], PS[5][0:2, :], 1e-24),
        waits=[dnm, P.get("war_ssc")], inc=1)
    P["war_ps5_l"] = dmx
    dln = S.emit("s", lambda e: e.activation(P["ssc"][0:2, :], P["ssc"][0:2, :],
                                             AF.Ln), waits=[dmx], inc=1)
    dex = S.emit("s", lambda e: e.activation(P["ssc"][0:2, :], P["ssc"][0:2, :],
                                             AF.Exp, scale=-0.5),
                 waits=[dln], inc=1)
    drb = S.emit("t", lambda e: e.matmul(
        PS[5][:, :], K["b2_f32"][0:2, :], P["ssc"][0:2, :],
        start=True, stop=True),
        waits=[dex, cdep, P.get("war_ps5_l"), P["bank"].get(5)], inc=1)
    P["war_ssc"] = drb
    dsc = S.emit("v", lambda e, b=b, svec=svec, l=l: e.scalar_tensor_tensor(
        P["qb"][:, b, :], P["qb"][:, b, :], svec[:, l:l + 1], PS[5][:, :],
        ALU.mult, ALU.mult), waits=[drb, meta["const_v_dep"]], inc=1)
    P["war_ps5_l"] = dsc
    P["bank"][5] = dsc
    return dsc


def _emit_attention(meta, P, hp, kind):
    S, A, K, PS, C = meta["S"], meta["A"], meta["K"], meta["PS"], meta["C"]
    cdep = meta["const_dep"]
    for qb in range(NQB):
        qs_ = slice(qb * TT, (qb + 1) * TT)
        dbc = {}

        def bcast_negc(dneg):
            for h in range(2):
                db = S.emit("t", lambda e, h=h: e.matmul(
                    PS[7][:, 500 + h:501 + h], K["ones_r_f32"][:, :],
                    P["negc_row"][0:1, h:h + 1], start=True, stop=True),
                    waits=[dneg, P.get("war_negsb"), P["bank"].get(7)],
                    inc=1)
                dbc[h] = S.emit("v", lambda e, h=h: e.tensor_copy(
                    P["negc_sb"][:, h:h + 1], PS[7][:, 500 + h:501 + h]),
                    waits=[db], inc=1)

        if kind != "f":
            if kind == "a":
                if qb == 0:
                    dk = S.emit("v", lambda e: e.tensor_copy(
                        P["knr"][0:2, :], P["kn"][0:2, 0:1]),
                        waits=[P["norm_dep"], P.get("war_norm_s")], inc=1)
                else:
                    dk = S.emit("v", lambda e, qb=qb: e.tensor_max(
                        P["knr"][0:2, :], P["knr"][0:2, :],
                        P["kn"][0:2, qb:qb + 1]),
                        waits=[P["norm_dep"], P.get("knr_war")], inc=1)
            else:
                if qb == 0:
                    dk = S.emit("v", lambda e: e.reduce_max(
                        P["knr"][0:2, :], P["kn"][0:2, :],
                        mybir.AxisListType.X),
                        waits=[P["norm_dep"], P.get("war_norm_s")], inc=1)
                else:
                    dk = P["knr_dep"]
            P["knr_dep"] = dk
            d1 = S.emit("s", lambda e, qb=qb: e.activation(
                P["c1"][0:2, 0:1], P["qn"][0:2, qb:qb + 1], AF.Ln),
                waits=[dk, P["norm_dep"]], inc=1)
            d2 = S.emit("s", lambda e: e.activation(
                P["c2"][0:2, 0:1], P["knr"][0:2, :], AF.Ln),
                waits=[dk, d1], inc=1)
            d3 = S.emit("v", lambda e: e.tensor_add(P["c1"][0:2, 0:1],
                                                    P["c1"][0:2, 0:1],
                                                    P["c2"][0:2, 0:1]),
                        waits=[d1, d2], inc=1)
            P["knr_war"] = d3
            d4 = S.emit("s", lambda e: e.activation(P["c1"][0:2, 0:1],
                                                    P["c1"][0:2, 0:1],
                                                    AF.Exp, scale=0.5),
                        waits=[d3], inc=1)
            dtr = S.emit("t", lambda e: e.transpose(PS[7][0:1, 504:506],
                                                    P["c1"][0:2, 0:1],
                                                    K["ident_f32"][0:2, 0:2]),
                         waits=[d4, cdep, P.get("war_negps"),
                                P["bank"].get(7)], inc=1)
            if kind == "b":
                dtr = S.emit("v", lambda e, hp=hp: e.tensor_add(
                    PS[7][0:1, 504:506], PS[7][0:1, 504:506],
                    K["pbrow"][0:1, 2 * hp:2 * hp + 2]),
                    waits=[dtr, meta["const_v_dep"]], inc=1)
            dneg = S.emit("v", lambda e: e.tensor_scalar_mul(
                P["negc_row"][0:1, 0:2], PS[7][0:1, 504:506], -1.0),
                waits=[dtr], inc=1)
            P["war_negps"] = dneg
            bcast_negc(dneg)
            P["war_negsb"] = dbc[1]
            P["bank"][7] = dbc[1]

        if kind == "a":
            kcs = [(kc, "vis" if kc < 4 * qb else "band",
                    512 * qb - 128 * kc + OFFA) for kc in range(4 * qb + 4)]
            nk = len(kcs)
            dS_l, dP_l = {}, {}
            SBK = ((0, 2, 6), (1, 3, 7))

            def emit_S(i):
                kc, kd, c0 = kcs[i]
                for h in range(2):
                    psn = SBK[h][i % 3]
                    dS_l[(i, h)] = S.emit(
                        "t", lambda e, psn=psn, h=h, kc=kc, qs_=qs_: e.matmul(
                            PS[psn][:, :], P["Kt"][64 * h:64 * (h + 1),
                                                   kc * 128:(kc + 1) * 128],
                            P["Qt"][64 * h:64 * (h + 1), qs_],
                            start=True, stop=True),
                        waits=[P["bank"].get(psn), P["qkv_dep"]], inc=1)

            def emit_PX(i):
                kc, kd, c0 = kcs[i]
                p = i % 2
                for h in range(2):
                    psn = SBK[h][i % 3]
                    dS = dS_l[(i, h)]
                    if kd == "vis":
                        dP = S.emit("s", lambda e, psn=psn, h=h, p=p:
                            e.activation(
                                P["pt"][:, 2 * h + p, :], PS[psn][:, :], AF.Exp,
                                bias=P["negc_sb"][:, h:h + 1], scale=1.0),
                            waits=[dS, dbc[h], P["war_pt"].get((h, p))], inc=1)
                    else:
                        band_ap = K["band_a"][:, c0:c0 + TT]
                        dsm = S.emit("v", lambda e, psn=psn,
                                     band_ap=band_ap:
                            e.tensor_add(PS[psn][:, :], PS[psn][:, :],
                                         band_ap),
                            waits=[dS, cdep], inc=1)
                        dP = S.emit("s", lambda e, psn=psn, h=h, p=p:
                            e.activation(
                                P["pt"][:, 2 * h + p, :], PS[psn][:, :],
                                AF.Exp, bias=P["negc_sb"][:, h:h + 1],
                                scale=1.0),
                            waits=[dsm, dbc[h], P["war_pt"].get((h, p))], inc=1)
                    P["bank"][psn] = dP
                    dP_l[(i, h)] = dP

            def emit_AV(i):
                kc, kd, c0 = kcs[i]
                p = i % 2
                for h in range(2):
                    dav = S.emit("t", lambda e, h=h, p=p, kc=kc, i=i, nk=nk:
                        e.matmul(
                            PS[4 + h][0:65, :], P["V"][:, h, kc, :],
                            P["pt"][:, 2 * h + p, :],
                            start=(i == 0), stop=(i == nk - 1)),
                        waits=[dP_l[(i, h)], P["bank"].get(4 + h)
                               if i == 0 else None], inc=1)
                    P["war_pt"][(h, p)] = dav

            emit_S(0)
            if nk > 1:
                emit_S(1)
            for i in range(nk):
                emit_PX(i)
                if i + 2 < nk:
                    emit_S(i + 2)
                emit_AV(i)
        elif kind == "f":
            qp = qb % 2
            kcs = [(kc, 512 * qb - 128 * kc + OFFF)
                   for kc in range(max(0, 4 * (qb - 1)), 4 * (qb + 1))]
            nk = len(kcs)
            dmx = None
            for i, (kc, c0) in enumerate(kcs):
                for h in range(2):
                    psn = 2 * h + (i % 2)
                    dS = S.emit("t", lambda e, psn=psn, h=h, kc=kc, qs_=qs_:
                        e.matmul(
                            PS[psn][:, :], P["Kt"][64 * h:64 * (h + 1),
                                                   kc * 128:(kc + 1) * 128],
                            P["Qt"][64 * h:64 * (h + 1), qs_],
                            start=True, stop=True),
                        waits=[P["bank"].get(psn), P["qkv_dep"]], inc=1)
                    dsm = S.emit("v", lambda e, psn=psn, h=h, i=i, c0=c0,
                                 qp=qp:
                        e.tensor_add(P["smB"][:, qp, h, i, :], PS[psn][:, :],
                                     K["band_f"][:, c0:c0 + TT]),
                        waits=[dS, cdep, P["war_smB"].get((qp, h, i))], inc=1)
                    P["bank"][psn] = dsm
                    dmx = S.emit("v", lambda e, h=h, i=i, qp=qp: e.reduce_max(
                        P["rmx"][:, qp, h, i:i + 1], P["smB"][:, qp, h, i, :],
                        mybir.AxisListType.X), waits=[dsm], inc=1)
            for h in range(2):
                dm2 = S.emit("v", lambda e, h=h, nk=nk, qp=qp: e.reduce_max(
                    P["rm1"][:, qp, h:h + 1], P["rmx"][:, qp, h, 0:nk],
                    mybir.AxisListType.X), waits=[dmx], inc=1)
                dtr = S.emit("t", lambda e, h=h: e.transpose(
                    PS[7][0:1, 360:488],
                    P["rm1"][:, qp, h:h + 1], K["ident_f32"][:, :]),
                    waits=[dm2, cdep, P.get("war_negps"),
                           P["bank"].get(7)], inc=1)
                dcx = S.emit("v", lambda e, h=h, qp=qp: e.reduce_max(
                    P["negc_row"][0:1, h:h + 1], PS[7][0:1, 360:488],
                    mybir.AxisListType.X), waits=[dtr], inc=1)
                P["war_negps"] = dcx
                P["bank"][7] = dcx
            dneg = S.emit("v", lambda e: e.tensor_scalar_mul(
                P["negc_row"][0:1, 0:2], P["negc_row"][0:1, 0:2], -1.0),
                waits=[P["war_negps"]], inc=1)
            P["war_negps"] = dneg
            for h in range(2):
                db = S.emit("t", lambda e, h=h: e.matmul(
                    PS[7][:, 500 + h:501 + h], K["ones_r_f32"][:, :],
                    P["negc_row"][0:1, h:h + 1], start=True, stop=True),
                    waits=[dneg, P.get("war_negsb"), P["bank"].get(7)],
                    inc=1)
                dbc[h] = S.emit("v", lambda e, h=h, qp=qp: e.tensor_copy(
                    P["negc2"][:, qp, h:h + 1], PS[7][:, 500 + h:501 + h]),
                    waits=[db], inc=1)
            P["war_negsb"] = dbc[1]
            P["bank"][7] = dbc[1]
            P["fq"][qb] = (kcs, dict(dbc))
            todo = [qb - 1] if qb > 0 else []
            if qb == NQB - 1:
                todo.append(qb)
            for q2 in todo:
                kcs2, dbc2 = P["fq"][q2]
                q2p = q2 % 2
                nk2 = len(kcs2)
                for i, (kc, c0) in enumerate(kcs2):
                    for h in range(2):
                        p = i % 2
                        dP = S.emit("s", lambda e, h=h, i=i, p=p, q2p=q2p:
                            e.activation(
                                P["pt"][:, 2 * h + p, :],
                                P["smB"][:, q2p, h, i, :], AF.Exp,
                                bias=P["negc2"][:, q2p, h:h + 1], scale=1.0),
                            waits=[dbc2[h], P["war_pt"].get((h, p))], inc=1)
                        P["war_smB"][(q2p, h, i)] = dP
                        dav = S.emit("t", lambda e, h=h, p=p, kc=kc, i=i,
                                     nk2=nk2:
                            e.matmul(
                                PS[4 + h][0:65, :], P["V"][:, h, kc, :],
                                P["pt"][:, 2 * h + p, :],
                                start=(i == 0), stop=(i == nk2 - 1)),
                            waits=[dP, P["bank"].get(4 + h)
                                   if i == 0 else None], inc=1)
                        P["war_pt"][(h, p)] = dav
                davs = S.dep("t")
                for h in range(2):
                    drr = S.emit("v", lambda e, h=h: e.reciprocal(
                        P["rrow"][0:1, :], PS[4 + h][64:65, :]),
                        waits=[davs], inc=1)
                    dg = S.emit("t", lambda e: e.matmul(
                        PS[6][0:64, :], K["ones_r_f32"][0:1, 0:64],
                        P["rrow"][0:1, :], start=True, stop=True),
                        waits=[drr, P.get("war_gam"), P["bank"].get(6)],
                        inc=1)
                    dgc = S.emit("v", lambda e: e.tensor_copy(
                        P["gam"][0:64, :], PS[6][0:64, :]), waits=[dg], inc=1)
                    P["war_gam"] = dgc
                    P["bank"][6] = dgc
                    dst_ap = P["o_dst"](q2, h)
                    dout = S.emit("v", lambda e, h=h, dst_ap=dst_ap:
                        e.tensor_mul(dst_ap, PS[4 + h][0:64, :],
                                     P["gam"][0:64, :]), waits=[dgc], inc=1)
                    P["bank"][4 + h] = dout
                P["war_negsb"] = S.dep("v")
            P["o_done"] = S.dep("v")
            continue
        else:
            dS_l, dP_l, dband_l = {}, {}, {}

            def b_band(h, si):
                base = 3840 - 256 * si
                srcb = C["band_b"].ap()[2 * hp + h]
                bap = bass.AP(srcb.tensor, srcb.offset + qb * TT + base,
                              [[BW_B, 128], [1, 640]])
                dband_l[(h, si)] = _q_dma(
                    S, P["wband"][:, h, si % 2, :], bap,
                    waits=[P["war_band"].get((h, si % 2))],
                    slot=f"band{h}{si % 2}")

            SBK = ((0, 2, 6), (1, 3, 7))

            def b_S(kc):
                for h in range(2):
                    psn = SBK[h][kc % 3]
                    dS_l[(kc, h)] = S.emit(
                        "t", lambda e, psn=psn, h=h, kc=kc, qs_=qs_: e.matmul(
                            PS[psn][:, :], P["Kt"][64 * h:64 * (h + 1),
                                                   kc * 128:(kc + 1) * 128],
                            P["Qt"][64 * h:64 * (h + 1), qs_],
                            start=True, stop=True),
                        waits=[P["bank"].get(psn), P["qkv_dep"]], inc=1)

            def b_PX(kc):
                p = kc % 2
                si = kc // 2
                base = 3840 - 256 * si
                c0 = 3968 - 128 * kc - base
                for h in range(2):
                    psn = SBK[h][kc % 3]
                    dsm = S.emit("v", lambda e, psn=psn, c0=c0, h=h, si=si:
                        e.tensor_add(PS[psn][:, :], PS[psn][:, :],
                                     P["wband"][:, h, si % 2, c0:c0 + TT]),
                        waits=[dS_l[(kc, h)], dband_l[(h, si)]], inc=1)
                    P["war_band"][(h, si % 2)] = dsm
                    dP = S.emit("s", lambda e, psn=psn, h=h, p=p: e.activation(
                        P["pt"][:, 2 * h + p, :], PS[psn][:, :],
                        AF.Exp, bias=P["negc_sb"][:, h:h + 1], scale=1.0),
                        waits=[dsm, dbc[h], P["war_pt"].get((h, p))], inc=1)
                    P["bank"][psn] = dP
                    dP_l[(kc, h)] = dP

            def b_AV(kc):
                p = kc % 2
                for h in range(2):
                    dav = S.emit("t", lambda e, h=h, p=p, kc=kc: e.matmul(
                        PS[4 + h][0:65, :], P["V"][:, h, kc, :],
                        P["pt"][:, 2 * h + p, :],
                        start=(kc == 0), stop=(kc == KCN - 1)),
                        waits=[dP_l[(kc, h)], P["bank"].get(4 + h)
                               if kc == 0 else None], inc=1)
                    P["war_pt"][(h, p)] = dav

            b_band(0, 0)
            b_band(1, 0)
            b_band(0, 1)
            b_band(1, 1)
            b_S(0)
            b_S(1)
            for kc in range(KCN):
                if kc % 2 == 0 and kc >= 2 and kc // 2 + 1 <= 15:
                    b_band(0, kc // 2 + 1)
                    b_band(1, kc // 2 + 1)
                b_PX(kc)
                if kc + 2 < KCN:
                    b_S(kc + 2)
                b_AV(kc)
        davs = S.dep("t")
        for h in range(2):
            drr = S.emit("v", lambda e, h=h: e.reciprocal(
                P["rrow"][0:1, :], PS[4 + h][64:65, :]), waits=[davs], inc=1)
            dg = S.emit("t", lambda e: e.matmul(
                PS[6][0:64, :], K["ones_r_f32"][0:1, 0:64], P["rrow"][0:1, :],
                start=True, stop=True),
                waits=[drr, P.get("war_gam"), P["bank"].get(6)], inc=1)
            dgc = S.emit("v", lambda e: e.tensor_copy(P["gam"][0:64, :],
                                                      PS[6][0:64, :]),
                         waits=[dg], inc=1)
            P["war_gam"] = dgc
            P["bank"][6] = dgc
            dst_ap = P["o_dst"](qb, h)
            dout = S.emit("v", lambda e, h=h, dst_ap=dst_ap: e.tensor_mul(
                dst_ap, PS[4 + h][0:64, :], P["gam"][0:64, :]),
                waits=[dgc], inc=1)
            P["bank"][4 + h] = dout
        P["war_negsb"] = S.dep("v")
        P["o_done"] = S.dep("v")


def _mk_attn_bufs(meta, nkc=KCN, rope=True, nkin=CH, fine=False):
    A = meta["A"]
    P = {}
    if fine:
        P["smB"] = A.alloc([128, 2, 2, 8, TT], BF16, "smB")
        P["rmx"] = A.alloc([128, 2, 2, 8], F32, "rmx")
        P["rm1"] = A.alloc([128, 2, 2], F32, "rm1")
        P["negc2"] = A.alloc([128, 2, 2], F32, "negc2")
        P["war_smB"] = {}
        P["fq"] = {}
    P["wq"] = A.alloc([128, 3, nkin, 128], BF16, "wq")
    if rope:
        P["tab"] = A.alloc([128, 2, TT], F32, "tab")
        P["t1"] = A.alloc([128, TT], F32, "t1")
        P["t2"] = A.alloc([128, TT], F32, "t2")
        P["ssc"] = A.alloc([128, TT], F32, "ssc")
    P["vt_f32"] = A.alloc([128, 1, TT], F32, "vtf")
    if rope:
        P["qb"] = A.alloc([128, 2, TT], BF16, "qbb")
    P["sqn"] = A.alloc([128, 2, TT], BF16, "sqn")
    if rope:
        P["sm"] = A.alloc([128, 2, TT], F32, "sm")
    P["Qt"] = A.alloc([128, T], BF16, "Qt")
    P["Kt"] = A.alloc([128, T], BF16, "Kt")
    P["V"] = A.alloc([128, 2, nkc, 65], BF16, "Vsb")
    sm_all = A.alloc([128, 112], F32, "smalls")
    P["qn"] = sm_all[:, 0:8]
    P["kn"] = sm_all[:, 8:16]
    P["qn32"] = sm_all[:, 16:56]
    P["kn32"] = sm_all[:, 56:96]
    P["acc4"] = sm_all[:, 96:100]
    P["t4"] = sm_all[:, 100:104]
    P["knr"] = sm_all[:, 104:105]
    P["c1"] = sm_all[:, 105:106]
    P["c2"] = sm_all[:, 106:107]
    P["negc_row"] = sm_all[:, 107:109]
    P["negc_sb"] = sm_all[:, 109:111]
    P["rrow"] = A.alloc([128, TT], F32, "rrow")
    P["gam"] = A.alloc([128, TT], F32, "gam")
    P["pt"] = A.alloc([128, 4, TT], BF16, "pt")
    for nmm in ("war_tab", "bank", "war_vt", "war_ps3", "war_sm",
                "war_pt", "war_o"):
        P[nmm] = {}
    return P


def _emit_phase_attn_a(meta):
    S, A, C = meta["S"], meta["A"], meta["C"]
    O = meta["Oa"]
    A.push()
    P = _mk_attn_bufs(meta)
    P["tabs"] = ("tab_a_cq", "tab_a_sq", "tab_a_ck", "tab_a_sk")
    xn = meta["xn"]
    P["rhs_fn"] = lambda kc, tt: xn[:, kc, tt * TT:(tt + 1) * TT]
    P["rhs_dbg"] = lambda: xn[:, :, 0:TT]
    P["rhs_dep"] = meta["ln1_done"]
    P["rhs_tiles"] = meta.get("ln1_tiles")
    P["scope_war"] = meta["ln1_done"]
    for i_ in range(8):
        P["bank"][i_] = meta["ln1_done"]
    S.emit("p", lambda e: e.memset(P["V"][:, :, :, 64:65], 1.0),
           waits=[P["scope_war"]], inc=1)
    donev = S.dep("p")
    S.wait_only("t", [donev])
    for hp in range(CH):
        P["wsrc"] = (lambda hp: lambda j: C["c_wa_qkv"].ap()[
            (hp, 8 + hp, 16 + hp)[j]])(hp)
        _emit_pair_qkv_a(meta, P, hp)
        P["qkv_dep"] = S.dep("v")
        P["norm_dep"] = S.dep("v")
        if hp == 0:
            S.dbg("a_qt0", [128, T], BF16, lambda: P["Qt"][:, :], [P["qkv_dep"]])
            S.dbg("a_kt0", [128, T], BF16, lambda: P["Kt"][:, :], [P["qkv_dep"]])
            S.dbg("a_v0", [128, 2 * KCN * 65], BF16,
                  lambda: P["V"].rearrange("p a b c -> p (a b c)"),
                  [P["qkv_dep"]])
            S.dbg("a_qn0", [128, NTT], F32, lambda: P["qn"][:, :],
                  [P["norm_dep"]])
        P["o_dst"] = (lambda hp: lambda qb, h: O[64 * h:64 * h + 64, hp,
                                                 qb * TT:(qb + 1) * TT])(hp)
        _emit_attention(meta, P, hp, "a")
        P["war_qt"] = S.dep("t")
        P["war_v4"] = S.dep("t")
        P["war_norm_v"] = S.dep("v")
        P["war_norm_s"] = S.dep("s")
        P.pop("knr_dep", None)
        P.pop("knr_war", None)
    S.barrier()
    A.pop()
    meta["attn_a_done"] = S.dep("v")


def _emit_phase_proj_res(meta, O, w_oc_fn, bias_name, src_dram, dst_dram,
                         kcs=CH, o_dep=None):
    S, A, K, PS, C = meta["S"], meta["A"], meta["K"], meta["PS"], meta["C"]
    A.push()
    wp = A.alloc([128, CH, kcs, 128], BF16, "wp")
    xold = A.alloc([128, 2, TT], F32, "xold")
    xnew = A.alloc([128, 2, TT], F32, "xnew")
    wdep = None
    for oc in range(CH):
        srcw = w_oc_fn(oc)
        sap = bass.AP(srcw.tensor, srcw.offset,
                      [[128, 128], [128 * 128, kcs], [1, 128]])
        wdep = _q_dma(S, wp[:, oc, :, :], sap, waits=[meta["wprep_dep"]],
                      slot="wp")
    war = {}
    for tt in range(NTT):
        ts = slice(tt * TT, (tt + 1) * TT)
        for oc in range(CH):
            idx = tt * CH + oc
            b = idx % 4
            xb = idx % 2
            dl = _q_dma(S, xold[:, xb, :],
                        src_dram.ap()[oc * 128:(oc + 1) * 128, ts],
                        waits=[war.get((xb, "x"))], slot=f"xold{xb}")
            for kc in range(kcs):
                S.emit("t", lambda e, oc=oc, kc=kc, tt=tt, b=b: e.matmul(
                    PS[b][:, :], wp[:, oc, kc, :], O[:, kc, tt * TT:(tt + 1) * TT],
                    start=(kc == 0), stop=(kc == kcs - 1)),
                    waits=[wdep, o_dep, war.get((b, "ps"))],
                    inc=(1 if kc == kcs - 1 else 0))
            dmm = S.dep("t")
            if bias_name:
                dv0 = S.emit("v", lambda e, b=b, oc=oc: e.tensor_scalar_add(
                    PS[b][:, :], PS[b][:, :], K[bias_name][:, oc:oc + 1]),
                    waits=[dmm, meta["const_dep"]], inc=1)
            else:
                dv0 = dmm
            dv = S.emit("v", lambda e, b=b, xb=xb: e.tensor_add(
                xnew[:, xb, :], PS[b][:, :], xold[:, xb, :]),
                waits=[dv0, dl, war.get((xb, "xn"))], inc=1)
            war[(b, "ps")] = dv
            dst = _q_dma(S, dst_dram.ap()[oc * 128:(oc + 1) * 128, ts],
                         xnew[:, xb, :], waits=[dv], slot=f"pro{xb}")
            war[(xb, "x")] = dst
            war[(xb, "xn")] = dst
    A.pop()
    return S.dep("q")


def _emit_ub_halo(meta, u_dram, ub):
    S, A = meta["S"], meta["A"]
    A.push()
    uf = A.alloc([128, 2, TT], F32, "uf")
    dz = S.emit("p", lambda e: e.memset(ub[:, :, 0:1], 0.0), inc=1)
    war = {}
    tiles = {}
    for tt in range(NTT):
        for c in range(CH):
            b = (tt * CH + c) % 2
            dl = _q_dma(S, uf[:, b, :],
                        u_dram.ap()[c * 128:(c + 1) * 128,
                                    tt * TT:(tt + 1) * TT],
                        waits=[war.get(b), meta.get("u_ready"), dz],
                        slot=f"ub{b}")
            war[b] = S.emit("v", lambda e, c=c, tt=tt, b=b: e.tensor_copy(
                ub[:, c, 1 + tt * TT:1 + (tt + 1) * TT], uf[:, b, :]),
                waits=[dl], inc=1)
        tiles[tt] = war[b]
    meta["ub_tiles"] = tiles
    A.pop()
    return S.dep("v")


def _emit_ffn_sublayer(meta, l, u_in, u_mid, res_in, extra_res, dst, tag):
    S, A, K, PS, C = meta["S"], meta["A"], meta["K"], meta["PS"], meta["C"]
    cdep = meta["const_dep"]
    A.push()
    Of = A.alloc([128, 4, T], BF16, "Of")
    A.push()
    ub = A.alloc([128, CH, T + 1], BF16, "ub")
    meta["u_ready"] = None
    dub = _emit_ub_halo(meta, u_in, ub)
    P = _mk_attn_bufs(meta, rope=True, fine=True)
    P["tabs"] = ("tab_f_cq", "tab_f_sq", "tab_f_ck", "tab_f_sk")
    P["l2norm"] = True
    P["ffn_l"] = l
    P["rhs_fn"] = (lambda ub: lambda kc, tt:
                   ub[:, kc, 1 + tt * TT:1 + (tt + 1) * TT] if kc < 4 else
                   ub[:, kc, tt * TT:(tt + 1) * TT])(ub)
    P["rhs_dbg"] = (lambda ub: lambda: ub[:, :, 0:TT])(ub)
    P["rhs_dep"] = dub
    P["rhs_tiles"] = meta.get("ub_tiles")
    P["scope_war"] = dub
    for i_ in range(8):
        P["bank"][i_] = dub
    S.emit("p", lambda e: e.memset(P["V"][:, :, :, 64:65], 1.0),
           waits=[dub], inc=1)
    S.wait_only("t", [S.dep("p")])
    for hp in range(4):
        P["wsrc"] = (lambda hp: lambda j: C["c_wf_qkv"].ap()[
            l, (hp, 4 + hp, 8 + hp)[j]])(hp)
        P["dbg_pair"] = (tag == "f1l0" and hp == 0)
        _emit_pair_qkv_a(meta, P, hp)
        P["qkv_dep"] = S.dep("v")
        P["norm_dep"] = S.dep("v")
        if tag == "f1l0" and hp == 0:
            S.dbg("f_qt0", [128, T], BF16, lambda: P["Qt"][:, :], [P["qkv_dep"]])
            S.dbg("f_kt0", [128, T], BF16, lambda: P["Kt"][:, :], [P["qkv_dep"]])
        P["o_dst"] = (lambda hp: lambda qb, h: Of[64 * h:64 * h + 64, hp,
                                                  qb * TT:(qb + 1) * TT])(hp)
        _emit_attention(meta, P, hp, "f")
        P["war_qt"] = S.dep("t")
        P["war_v4"] = S.dep("t")
        P["war_norm_v"] = S.dep("v")
        P["war_norm_s"] = S.dep("s")
        for nmm in ("knr_dep", "knr_war"):
            P.pop(nmm, None)
    if tag == "f1l0":
        S.dbg("f_o0", [128, 4 * T], BF16,
              lambda: Of.rearrange("p a b -> p (a b)"), [S.dep("v")])
    S.barrier()
    A.pop()
    _emit_phase_proj_res(meta, Of, lambda oc: C["c_wf_wo"].ap()[l, oc], None,
                         u_in, u_mid, kcs=4)
    S.new_phase_sems()
    A.pop()
    if tag == "f1l0":
        S.dbg("f_u1", [D, T], F32, lambda: u_mid.ap(), [])
    A.push()
    hb = A.alloc([128, CH, T], BF16, "hb")
    _emit_ln_shift_apply(meta, u_mid, hb, l)
    if tag == "f1l0":
        S.dbg("f_hb", [128, CH * T], BF16,
              lambda: hb.rearrange("p a b -> p (a b)"), [S.dep("v")])
    _emit_w1_glu_w2(meta, l, hb, res_in, extra_res, dst, tag)
    S.new_phase_sems()
    A.pop()


def _emit_ln_shift_apply(meta, u_dram, hb, l):
    S, A, K, PS, C = meta["S"], meta["A"], meta["K"], meta["PS"], meta["C"]
    uf = A.alloc([128, 1, CH, TT + 1], F32, "uf2")
    xbt = A.alloc([128, 1, CH, TT + 1], BF16, "xbt2")
    sq = A.alloc([128, 1, 2, TT], BF16, "sq2")
    cdep = meta["const_dep"]
    war_v = {}
    prev_apply = {}
    dz = S.emit("p", lambda e: e.memset(uf[:, :, :, 0:1], 0.0), inc=1)

    def chv(buf, b, c):
        if c < 4:
            return buf[:, b, c, 1:TT + 1]
        return buf[:, b, c, 0:TT]

    for tt in range(NTT):
        b = 0
        dl = None
        for c in range(CH):
            if tt == 0:
                dl = _q_dma(S, uf[:, b, c, 1:TT + 1],
                            u_dram.ap()[c * 128:(c + 1) * 128, 0:TT],
                            waits=([war_v.get(tt - 1), dz] if c == 0 else ()),
                            slot=f"lsf{b}")
            else:
                dl = _q_dma(S, uf[:, b, c, 0:TT + 1],
                            u_dram.ap()[c * 128:(c + 1) * 128,
                                        tt * TT - 1:(tt + 1) * TT],
                            waits=([war_v.get(tt - 1), dz] if c == 0 else ()),
                            slot=f"lsf{b}")
        dcast = None
        for c in range(CH):
            dcast = S.emit("v", lambda e, c=c, b=b: e.tensor_copy(
                xbt[:, b, c, :], uf[:, b, c, :]), waits=[dl], inc=1)
        war = prev_apply.get(tt - 1, meta.get("const_v_dep"))
        sqmm = {}
        for c in range(CH):
            dsq = S.emit("s", lambda e, c=c, b=b: e.activation(
                sq[:, 0, c % 2, :], chv(xbt, b, c), AF.Square),
                waits=[dcast, war, sqmm.get(c - 2)], inc=1)
            S.emit("t", lambda e, c=c, b=b: e.matmul(
                PS[4][0:1, :], K["ones_c_bf"][:, :], chv(xbt, b, c),
                start=(c == 0), stop=(c == CH - 1)),
                waits=[dcast, cdep, war], inc=(1 if c == CH - 1 else 0))
            sqmm[c] = S.emit("t", lambda e, c=c, b=b: e.matmul(
                PS[5][0:1, :], K["ones_c_bf"][:, :], sq[:, 0, c % 2, :],
                start=(c == 0), stop=(c == CH - 1)),
                waits=[dsq, war], inc=1)
        dstat = S.dep("t")
        d1 = S.emit("v", lambda e: e.tensor_scalar_mul(
            K["mu_row"][0:1, :], PS[4][0:1, :], 1.0 / D), waits=[dstat], inc=1)
        d2 = S.emit("v", lambda e: e.tensor_scalar_mul(
            K["r_row"][0:1, :], PS[5][0:1, :], 1.0 / D), waits=[dstat], inc=1)
        d3 = S.emit("v", lambda e: e.tensor_mul(
            K["mur_row"][0:1, :], K["mu_row"][0:1, :], K["mu_row"][0:1, :]),
            waits=[d2], inc=1)
        d4 = S.emit("v", lambda e: e.tensor_sub(
            K["r_row"][0:1, :], K["r_row"][0:1, :], K["mur_row"][0:1, :]),
            waits=[d3], inc=1)
        d5 = S.emit("v", lambda e: e.tensor_scalar_max(
            K["r_row"][0:1, :], K["r_row"][0:1, :], 0.0), waits=[d4], inc=1)
        d6 = S.emit("s", lambda e: e.activation(
            K["r_row"][0:1, :], K["r_row"][0:1, :], AF.Ln,
            bias=K["eps1"][0:1, :], scale=1.0),
            waits=[d5, meta.get("eps_dep")], inc=1)
        d7 = S.emit("s", lambda e: e.activation(
            K["r_row"][0:1, :], K["r_row"][0:1, :], AF.Exp, scale=-0.5),
            waits=[d6], inc=1)
        d8 = S.emit("v", lambda e: e.tensor_mul(
            K["mur_row"][0:1, :], K["mu_row"][0:1, :], K["r_row"][0:1, :]),
            waits=[d7], inc=1)
        dr = S.emit("t", lambda e: e.matmul(
            PS[4][:, :], K["ones_r_f32"][:, :], K["r_row"][0:1, :],
            start=True, stop=True), waits=[d8], inc=1)
        dm = S.emit("t", lambda e: e.matmul(
            PS[5][:, :], K["ones_r_f32"][:, :], K["mur_row"][0:1, :],
            start=True, stop=True), waits=[d8], inc=1)
        dv = None
        for c in range(CH):
            dv1 = S.emit("v", lambda e, c=c, b=b: e.tensor_mul(
                uf[:, b, c, 1:TT + 1], chv(xbt, b, c), PS[4][:, :]),
                waits=[dr], inc=1)
            dv2 = S.emit("v", lambda e, c=c, b=b: e.tensor_sub(
                uf[:, b, c, 1:TT + 1], uf[:, b, c, 1:TT + 1], PS[5][:, :]),
                waits=[dm, dv1], inc=1)
            dv = S.emit("v", lambda e, c=c, b=b, tt=tt, l=l: e.tensor_scalar(
                hb[:, c, tt * TT:(tt + 1) * TT], uf[:, b, c, 1:TT + 1],
                K["lng"][:, l, c:c + 1], K["lnb"][:, l, c:c + 1],
                ALU.mult, ALU.add), waits=[dv2, meta["const_dep"]], inc=1)
        war_v[tt] = dv
        prev_apply[tt] = dv
        meta.setdefault("hb_tiles", {})[tt] = dv
    meta["hb_done"] = S.dep("v")


def _emit_w1_glu_w2(meta, l, hb, res_in, extra_res, dst, tag):
    S, A, K, PS, C = meta["S"], meta["A"], meta["K"], meta["PS"], meta["C"]
    cdep = meta["const_dep"]
    TH = 1024
    prod = A.alloc([128, MA, TH], BF16, "prod")
    a_sb = A.alloc([128, 2, TT], F32, "asb")
    gel = A.alloc([128, 2, TT], F32, "gel")
    w1t = A.alloc([128, 2, 2, CH, 128], BF16, "w1t")
    w2t = A.alloc([128, 1, MA, 128], BF16, "w2t")
    ures = A.alloc([128, 2, TT], F32, "ures")
    xres = A.alloc([128, 2, TT], F32, "xres") if extra_res is not None else None
    yout = A.alloc([128, 2, TT], F32, "yout")
    for th in range(T // TH):
        hs = th * TH
        hb_dep = meta["hb_tiles"][(hs + TH - 1) // TT]
        war_w1 = {}
        war_prod = {}
        for m in range(MA):
            bw = m % 2
            dwa = _q_dma(S, w1t[:, bw, 0, :, :],
                         bass.AP(C["c_wf_w1a"].ap()[l, m].tensor,
                                 C["c_wf_w1a"].ap()[l, m].offset,
                                 [[128, 128], [128 * 128, CH], [1, 128]]),
                         waits=[meta["wprep_dep"], war_w1.get(m - 2)],
                         slot=f"w1a{bw}")
            dwg = _q_dma(S, w1t[:, bw, 1, :, :],
                         bass.AP(C["c_wf_w1g"].ap()[l, m].tensor,
                                 C["c_wf_w1g"].ap()[l, m].offset,
                                 [[128, 128], [128 * 128, CH], [1, 128]]),
                         waits=[war_w1.get(m - 2)], slot=f"w1g{bw}")
            for sub in range(TH // TT):
                ss = slice(hs + sub * TT, hs + (sub + 1) * TT)
                ab = 0 if sub % 2 == 0 else 6
                gb = 1 if sub % 2 == 0 else 7
                sb2 = sub % 2
                for kc in range(CH):
                    S.emit("t", lambda e, kc=kc, bw=bw, ss=ss, ab=ab: e.matmul(
                        PS[ab][:, :], w1t[:, bw, 0, kc, :], hb[:, kc, ss],
                        start=(kc == 0), stop=(kc == CH - 1)),
                        waits=[dwa, hb_dep, war_prod.get(ab)],
                        inc=(1 if kc == CH - 1 else 0))
                da = S.dep("t")
                for kc in range(CH):
                    S.emit("t", lambda e, kc=kc, bw=bw, ss=ss, gb=gb: e.matmul(
                        PS[gb][:, :], w1t[:, bw, 1, kc, :], hb[:, kc, ss],
                        start=(kc == 0), stop=(kc == CH - 1)),
                        waits=[dwg, hb_dep, war_prod.get(gb)],
                        inc=(1 if kc == CH - 1 else 0))
                dg = S.dep("t")
                dae = S.emit("v", lambda e, m=m, ab=ab, sb2=sb2: e.tensor_scalar_add(
                    a_sb[:, sb2, :], PS[ab][:, :], K["b1a_sb"][:, l, m:m + 1]),
                    waits=[da, cdep, war_prod.get(("a", sb2))], inc=1)
                war_prod[ab] = dae
                dge = S.emit("s", lambda e, m=m, gb=gb, sb2=sb2: e.activation(
                    gel[:, sb2, :], PS[gb][:, :], AF.Gelu,
                    bias=K["b1g_sb"][:, l, m:m + 1], scale=1.0),
                    waits=[dg, cdep, war_prod.get(("g", sb2))], inc=1)
                war_prod[gb] = dge
                dp = S.emit("v", lambda e, m=m, sub=sub, sb2=sb2: e.tensor_mul(
                    prod[:, m, sub * TT:(sub + 1) * TT], a_sb[:, sb2, :],
                    gel[:, sb2, :]),
                    waits=[dae, dge], inc=1)
                war_prod[("a", sb2)] = dp
                war_prod[("g", sb2)] = dp
            war_w1[m] = dp
        dprod = S.dep("v")
        war_ps = {}
        war_res = {}
        for oc in range(CH):
            bw = 0
            dw2 = _q_dma(S, w2t[:, bw, :, :],
                         bass.AP(C["c_wf_w2"].ap()[l, oc].tensor,
                                 C["c_wf_w2"].ap()[l, oc].offset,
                                 [[128, 128], [128 * 128, MA], [1, 128]]),
                         waits=[meta["wprep_dep"], war_ps.get(("w", bw))],
                         slot=f"w2t{bw}")
            for sub in range(TH // TT):
                ss = slice(hs + sub * TT, hs + (sub + 1) * TT)
                bb = sub % 2
                rs = slice(sub * TT, (sub + 1) * TT)
                for m in range(MA):
                    S.emit("t", lambda e, m=m, bw=bw, rs=rs, bb=bb: e.matmul(
                        PS[2 + bb][:, :], w2t[:, bw, m, :],
                        prod[:, m, rs],
                        start=(m == 0), stop=(m == MA - 1)),
                        waits=[dw2, dprod, war_ps.get(bb)],
                        inc=(1 if m == MA - 1 else 0))
                dy = S.dep("t")
                dlu = _q_dma(S, ures[:, bb, :],
                             res_in.ap()[oc * 128:(oc + 1) * 128, ss],
                             waits=[war_res.get((bb, "u"))],
                             slot=f"ures{bb}")
                dv0 = S.emit("v", lambda e, bb=bb, oc=oc: e.tensor_scalar_add(
                    PS[2 + bb][:, :], PS[2 + bb][:, :],
                    K["b2b"][:, l, oc:oc + 1]), waits=[dy, cdep], inc=1)
                dv1 = S.emit("v", lambda e, bb=bb: e.tensor_add(
                    yout[:, bb, :], PS[2 + bb][:, :], ures[:, bb, :]),
                    waits=[dv0, dlu], inc=1)
                if extra_res is not None:
                    dlx = _q_dma(S, xres[:, bb, :],
                                 extra_res.ap()[oc * 128:(oc + 1) * 128, ss],
                                 waits=[war_res.get((bb, "x"))],
                                 slot=f"xres{bb}")
                    dv1 = S.emit("v", lambda e, bb=bb: e.tensor_add(
                        yout[:, bb, :], yout[:, bb, :], xres[:, bb, :]),
                        waits=[dv1, dlx], inc=1)
                war_ps[bb] = dv1
                dso = _q_dma(S, dst.ap()[oc * 128:(oc + 1) * 128, ss],
                             yout[:, bb, :], waits=[dv1], slot=f"w2o{bb}")
                war_res[(bb, "u")] = dso
                war_res[(bb, "x")] = dso
            war_ps[("w", bw)] = S.dep("t")
        S.barrier()


def _emit_pair_qkv_b(meta, P, hp):
    S, A, K, PS, C = meta["S"], meta["A"], meta["K"], meta["PS"], meta["C"]
    cdep = meta["const_dep"]
    xn = P["xn"]
    wdep = None
    for j, wn in enumerate(("c_wb_q", "c_wb_k", "c_wb_v")):
        srcw = C[wn].ap()[hp]
        sap = bass.AP(srcw.tensor, srcw.offset,
                      [[128, 128], [128 * 128, CH], [1, 128]])
        wdep = _q_dma(S, P["wq"][:, j, :, :], sap,
                      waits=[meta["wprep_dep"], P.get("scope_war")],
                      slot="wq")
    for tt in range(NTT):
        ts = slice(tt * TT, (tt + 1) * TT)
        b = tt % 2
        qbank = 0 if tt % 2 == 0 else 6
        kbank = 1 if tt % 2 == 0 else 7
        for psn, j in ((qbank, 0), (kbank, 1), (2, 2)):
            rdep = (P["rhs_tiles"][tt] if P.get("rhs_tiles")
                    else P["rhs_dep"])
            for kc in range(CH):
                S.emit("t", lambda e, psn=psn, j=j, kc=kc, tt=tt: e.matmul(
                    PS[psn][:, :], P["wq"][:, j, kc, :],
                    xn[:, kc, tt * TT:(tt + 1) * TT],
                    start=(kc == 0), stop=(kc == CH - 1)),
                    waits=[wdep, rdep, P["bank"].get(psn)],
                    inc=(1 if kc == CH - 1 else 0))
            if j == 0:
                dqp = S.dep("t")
            elif j == 1:
                dkp = S.dep("t")
            else:
                dvp = S.dep("t")
        dq = S.emit("v", lambda e, tt=tt, hp=hp, qbank=qbank: e.tensor_scalar(
            P["Qt"][:, tt * TT:(tt + 1) * TT], PS[qbank][:, :],
            K["bq"][:, hp:hp + 1], 0.125, ALU.add, ALU.mult),
            waits=[dqp, cdep, P.get("war_qt")], inc=1)
        P["bank"][qbank] = dq
        dk = S.emit("v", lambda e, tt=tt, hp=hp, kbank=kbank: e.tensor_scalar_add(
            P["Kt"][:, tt * TT:(tt + 1) * TT], PS[kbank][:, :],
            K["bk"][:, hp:hp + 1]), waits=[dkp, cdep, P.get("war_qt")], inc=1)
        P["bank"][kbank] = dk
        dvf = S.emit("v", lambda e, hp=hp: e.tensor_scalar_add(
            P["vt_f32"][:, 0, :], PS[2][:, :], K["bv"][:, hp:hp + 1]),
            waits=[dvp, cdep, P["war_vt"].get(tt - 1)], inc=1)
        P["bank"][2] = dvf
        dtr = None
        for j in range(4):
            dtr = S.emit("t", lambda e, j=j: e.transpose(
                PS[3][:, j * 128:(j + 1) * 128],
                P["vt_f32"][:, 0, j * 128:(j + 1) * 128],
                K["ident_f32"][:, :]),
                waits=[dvf, cdep, P["war_ps3"].get(tt - 1),
                       P["bank"].get(3)], inc=1)
        P["war_vt"][tt] = dtr
        dve = None
        V4b = P["V"]
        for h in range(2):
            vout = V4b[:, h, tt * 4:(tt + 1) * 4, 0:64]
            vin = bass.AP(PS[3], 64 * h, [[512, 128], [128, 4], [1, 64]])
            dve = S.emit("v", lambda e, vout=vout, vin=vin: e.tensor_copy(
                vout, vin), waits=[dtr, P.get("war_v4")], inc=1)
        P["war_ps3"][tt] = dve
        P["bank"][3] = dve
        for dst_t, nrm, dd in ((P["Qt"], P["qn"], dq), (P["Kt"], P["kn"], dk)):
            dsqn = S.emit("s", lambda e, dst_t=dst_t, tt=tt, b=b: e.activation(
                P["sqn"][:, b, :], dst_t[:, tt * TT:(tt + 1) * TT], AF.Square),
                waits=[dd, P.get("war_sqn_l")], inc=1)
            dnm = S.emit("t", lambda e, b=b: e.matmul(
                PS[5][0:2, :], K["ones2_bf"][:, :], P["sqn"][:, b, :],
                start=True, stop=True),
                waits=[dsqn, cdep, P.get("war_ps5_l"), P["bank"].get(5)],
                inc=1)
            P["war_sqn_l"] = dnm
            dnv = S.emit("v", lambda e, nrm=nrm, tt=tt: e.reduce_max(
                nrm[0:2, tt:tt + 1], PS[5][0:2, :], mybir.AxisListType.X),
                waits=[dnm], inc=1)
            P["war_ps5_l"] = dnv


def _emit_phase_attn_b(meta, xn3):
    S, A, C, K = meta["S"], meta["A"], meta["C"], meta["K"]
    O = meta["Ob"]
    A.push()
    P = _mk_attn_bufs(meta, rope=False)
    P["xn"] = xn3
    P["rhs_dep"] = meta["ln3_done"]
    P["rhs_tiles"] = meta.get("ln3_tiles")
    P["scope_war"] = meta["ln3_done"]
    for i_ in range(8):
        P["bank"][i_] = meta["ln3_done"]
    P["wband"] = A.alloc([128, 2, 2, 640], F32, "wband")
    P["war_band"] = {}
    S.emit("p", lambda e: e.memset(P["V"][:, :, :, 64:65], 1.0),
           waits=[P["scope_war"]], inc=1)
    S.wait_only("t", [S.dep("p")])
    for hp in range(CH):
        _emit_pair_qkv_b(meta, P, hp)
        P["qkv_dep"] = S.dep("v")
        P["norm_dep"] = S.dep("v")
        if hp == 0:
            S.dbg("b_qt0", [128, T], BF16, lambda: P["Qt"][:, :], [P["qkv_dep"]])
        P["o_dst"] = (lambda hp: lambda qb, h: O[64 * h:64 * h + 64, hp,
                                                 qb * TT:(qb + 1) * TT])(hp)
        _emit_attention(meta, P, hp, "b")
        P["war_qt"] = S.dep("t")
        P["war_v4"] = S.dep("t")
        P["war_norm_v"] = S.dep("v")
        P["war_norm_s"] = S.dep("s")
        for nmm in ("knr_dep", "knr_war"):
            P.pop(nmm, None)
    S.barrier()
    A.pop()
    meta["attn_b_done"] = S.dep("v")


def _build_full(dbg_names=None, phase_limit=99):
    nc, es, meta = build_program(dbg_names=dbg_names, phase_limit=phase_limit)
    S = meta["S"]
    if phase_limit < 6:
        C = meta["C"]
        srcn = {3: "x1T", 4: "x2T", 5: "x3T"}.get(phase_limit)
        if srcn:
            _q_dma(S, C["yT"].ap(), C[srcn].ap(), waits=(), slot="out")
        elif meta.get("xn") is not None:
            for c in range(CH):
                _p_dma(S, C["yT"].ap()[c * 128:(c + 1) * 128, :],
                       meta["xn"][:, c, :], waits=[meta.get("ln1_done")])
    S.barrier()
    with nc.Block() as block:
        S.replay(block)
    es.close()
    return nc, meta


def kernel(**inputs):
    consts = _host_consts()
    L = _host_layout(inputs)
    nc, meta = _build_full()
    in_maps = []
    for i in range(8):
        m = {"xT": L["xT"][i]}
        for k in ("wa_qkv", "wa_proj", "wb_q", "wb_k", "wb_v", "wb_o",
                  "wf_qkv", "wf_wo", "wf_w1a", "wf_w1g", "wf_w2",
                  "b1a", "b1g", "band_b"):
            m[k] = L[k]
        def vec_d(a):
            return np.ascontiguousarray(
                np.asarray(a, np.float32).reshape(CH, 128).T)
        def vec_ld(a):
            return np.ascontiguousarray(
                np.asarray(a, np.float32).reshape(2, CH, 128).transpose(2, 0, 1))
        m["bq"] = vec_d(inputs["L0_attn_b_bq"])
        m["bk"] = vec_d(inputs["L0_attn_b_bk"])
        m["bv"] = vec_d(inputs["L0_attn_b_bv"])
        m["bo"] = vec_d(inputs["L0_attn_b_bo"])
        m["pos_bias"] = np.asarray(inputs["L0_attn_b_pos_bias"], np.float32)
        m["b2"] = vec_ld(inputs["L0_ffn_b2"])
        m["ln_g"] = vec_ld(inputs["L0_ffn_ln_g"])
        m["ln_b"] = vec_ld(inputs["L0_ffn_ln_b"])
        m["qs"] = np.ascontiguousarray(
            np.asarray(inputs["L0_ffn_q_scale"], np.float32).T)
        m["ks"] = np.ascontiguousarray(
            np.asarray(inputs["L0_ffn_k_scale"], np.float32).T)
        m.update(consts)
        in_maps.append(m)
    res = run_bass_kernel_spmd(nc, in_maps, core_ids=list(range(8)))
    out = np.stack([np.ascontiguousarray(r["yT"].T) for r in res.results], axis=0)
    return out
